# version 04
# baseline (speedup 1.0000x reference)
import numpy as np
import ml_dtypes
from contextlib import ExitStack

import concourse.bass as bass
import concourse.mybir as mybir
from concourse.bass_utils import run_bass_kernel_spmd

F32 = mybir.dt.float32
BF16 = mybir.dt.bfloat16
F16 = mybir.dt.float16
I32 = mybir.dt.int32
AF = mybir.ActivationFunctionType
ALU = mybir.AluOpType

LN_EPS = 1e-5
RMS_EPS = 1.1920928955078125e-07
FFN_EPS = 1e-06
LAMBDA_INIT = 0.8
SM_SCALE = float(1.0 / np.sqrt(128.0))

REAL = dict(S=4096, D=2048, HG=16, HC=8, I=5632)


def _cfg(cfg):
    S, D, HG, HC, I = cfg["S"], cfg["D"], cfg["HG"], cfg["HC"], cfg["I"]
    assert D == HG * 128 and S == 2 * D and I % 512 == 0
    c = dict(cfg)
    c.update(
        BLK=S // 128, SH=2 * HC, DCH=D // 128, KCH=S // 128, SCH=S // 128,
        QT=512, NQT=S // 512, U=2 * HC, JT=min(512, D), NJ=D // min(512, D),
        OG=I // 512, ICH=I // 128, NGU=D // 128, NGD=I // 128, CC=(S // 2) // 128,
    )
    return c


class Stream:

    def __init__(self, block, engname, sem):
        self.block = block
        self.engname = engname
        self.sem = sem
        self.n = 0

    def raw(self, f):
        getattr(self.block, self.engname)(f)

    def op(self, f, inc=1):
        def g(e):
            f(e).then_inc(self.sem, inc)
        self.raw(g)
        self.n += inc
        return self.n

    def opn(self, f):
        cnt = []

        def g(e):
            inss = f(e)
            for ins in inss:
                ins.then_inc(self.sem, 1)
            cnt.append(len(inss))
        self.raw(g)
        self.n += cnt[0]
        return self.n

    def dman(self, f):
        cnt = []

        def g(e):
            inss = f(e)
            for ins in inss:
                ins.then_inc(self.sem, 16)
            cnt.append(len(inss))
        self.raw(g)
        self.n += 16 * cnt[0]
        return self.n

    def mmgroup(self, f):
        def g(e):
            f(e)[-1].then_inc(self.sem, 1)
        self.raw(g)
        self.n += 1
        return self.n

    def wait(self, other, tok=None):
        t = other.n if tok is None else tok
        if t > 0:
            self.raw(lambda e: e.wait_ge(other.sem, t))

    def selfwait(self):
        if self.n > 0:
            self.raw(lambda e: e.wait_ge(self.sem, self.n))


def build_nc(cfg, debug=False):
    c = _cfg(cfg)
    S, D, HC, I = c["S"], c["D"], c["HC"], c["I"]
    DCH, KCH, SCH, QT, NQT = c["DCH"], c["KCH"], c["SCH"], c["QT"], c["NQT"]
    U, NJ, JT, OG, ICH = c["U"], c["NJ"], c["JT"], c["OG"], c["ICH"]
    NGU, NGD, CC, BLK = c["NGU"], c["NGD"], c["CC"], c["BLK"]
    I2 = 2 * I
    WI_GU = D // 8
    WI_DN = I // 8
    NOT_GU = I2 // 128
    NOT_DN = D // 128
    NT = QT // 128

    nc = bass.Bass("TRN2", target_bir_lowering=False)
    dk = "ExternalOutput" if debug else "Internal"

    x_in = nc.dram_tensor("x_in", [S, D], F32, kind="ExternalInput").ap()
    wq_in = nc.dram_tensor("wq_in", [HC * 128, D], F32, kind="ExternalInput").ap()
    wk_in = nc.dram_tensor("wk_in", [HC * 128, D], F32, kind="ExternalInput").ap()
    wv_in = nc.dram_tensor("wv_in", [HC * 128, D], F32, kind="ExternalInput").ap()
    wo_in = nc.dram_tensor("wo_in", [D, D], F32, kind="ExternalInput").ap()
    lq1 = nc.dram_tensor("lq1", [1, 128], F32, kind="ExternalInput").ap()
    lq2 = nc.dram_tensor("lq2", [1, 128], F32, kind="ExternalInput").ap()
    lk1 = nc.dram_tensor("lk1", [1, 128], F32, kind="ExternalInput").ap()
    lk2 = nc.dram_tensor("lk2", [1, 128], F32, kind="ExternalInput").ap()
    gup = nc.dram_tensor("gup", [I2, WI_GU], I32, kind="ExternalInput").ap()
    gus = nc.dram_tensor("gus", [NGU, I2], F16, kind="ExternalInput").ap()
    dnp = nc.dram_tensor("dnp", [D, WI_DN], I32, kind="ExternalInput").ap()
    dns = nc.dram_tensor("dns", [NGD, D], F16, kind="ExternalInput").ap()
    out_c = nc.dram_tensor("out_c", [U * 128, D], F32, kind="ExternalOutput").ap()

    wq_bf = nc.dram_tensor("wq_bf", [HC * 128, D], F16, kind="Internal").ap()
    wk_bf = nc.dram_tensor("wk_bf", [HC * 128, D], F16, kind="Internal").ap()
    wv_bf = nc.dram_tensor("wv_bf", [HC * 128, D], F16, kind="Internal").ap()
    wo_bf = nc.dram_tensor("wo_bf", [D, D], F16, kind="Internal").ap()
    xn_nat = nc.dram_tensor("xn_nat", [S, D], F16, kind=dk).ap()
    xnT_nat = nc.dram_tensor("xnT_nat", [D, S], F16, kind=dk).ap()
    s_stash = nc.dram_tensor("s_stash", [U * 128, D], F32, kind=dk).ap()
    wguT = nc.dram_tensor("wguT", [DCH, 128, I2], F16, kind=dk).ap()
    wdnT = nc.dram_tensor("wdnT", [ICH, 128, D], F16, kind=dk).ap()
    y_dbg = nc.dram_tensor("y_dbg", [U * 128, D], F32, kind=dk).ap() if debug else None

    with ExitStack() as top:
        E = top.enter_context
        ident = E(nc.sbuf_tensor("ident", [128, 128], BF16))
        identh = E(nc.sbuf_tensor("identh", [128, 128], F16))
        identf = E(nc.sbuf_tensor("identf", [128, 128], F32))
        identi = E(nc.sbuf_tensor("identi", [128, 128], I32))
        onesb = E(nc.sbuf_tensor("onesb", [128, 1], F16))
        ones1 = E(nc.sbuf_tensor("ones1", [1, 128], F32))
        epsLN = E(nc.sbuf_tensor("epsLN", [128, 1], F32))
        epsRMS = E(nc.sbuf_tensor("epsRMS", [128, 1], F32))
        epsFFN = E(nc.sbuf_tensor("epsFFN", [128, 1], F32))
        lamt = E(nc.sbuf_tensor("lamt", [1, 1], F32))
        ltmp = E(nc.sbuf_tensor("ltmp", [1, 768], F32))
        lred = E(nc.sbuf_tensor("lred", [1, 4], F32))

        PP = E(nc.psum_tensor("PP", [128, 4096], F32))
        PPb = PP.bitcast(BF16)
        PPh = PP.bitcast(F16)

        sems = {k: E(nc.semaphore(k)) for k in ["pe", "ac", "dv", "gp", "sp"]}
        block = E(nc.Block())

        PE = Stream(block, "tensor", sems["pe"])
        AC = Stream(block, "scalar", sems["ac"])
        DV = Stream(block, "vector", sems["dv"])
        GP = Stream(block, "gpsimd", sems["gp"])
        SP = Stream(block, "sync", sems["sp"])

        def barrier():
            streams = [PE, AC, DV, GP, SP]
            for s in streams:
                for o in streams:
                    if o is not s:
                        s.wait(o)

        GP.dman(lambda g: [
            g.dma_start(out=wq_bf, in_=wq_in),
            g.dma_start(out=wk_bf, in_=wk_in),
            g.dma_start(out=wv_bf, in_=wv_in),
            g.dma_start(out=wo_bf, in_=wo_in),
        ])
        tok_wcast = GP.n
        GP.op(lambda g: g.iota(identi[:, :], pattern=[[-1, 128]], base=0, channel_multiplier=1))
        tok_iota = GP.n
        SP.dman(lambda s: [
            s.dma_start(out=ltmp[:, 0:128], in_=lq1),
            s.dma_start(out=ltmp[:, 128:256], in_=lk1),
            s.dma_start(out=ltmp[:, 256:384], in_=lq2),
            s.dma_start(out=ltmp[:, 384:512], in_=lk2),
        ])
        tok_lload = SP.n

        DV.wait(GP, tok_iota)
        DV.opn(lambda v: [
            v.memset(onesb[:, :], 1.0),
            v.memset(ones1[:, :], 1.0),
            v.memset(epsLN[:, :], LN_EPS),
            v.memset(epsRMS[:, :], RMS_EPS),
            v.memset(epsFFN[:, :], FFN_EPS / 4096.0),
            v.tensor_scalar(ident[:, :], identi[:, :], 0, None, op0=ALU.is_equal),
            v.tensor_scalar(identh[:, :], identi[:, :], 0, None, op0=ALU.is_equal),
            v.tensor_scalar(identf[:, :], identi[:, :], 0, None, op0=ALU.is_equal),
        ])
        DV.wait(SP, tok_lload)
        DV.wait(GP, tok_wcast)
        DV.opn(lambda v: [
            v.tensor_tensor(ltmp[:, 512:640], ltmp[:, 0:128], ltmp[:, 128:256], op=ALU.mult),
            v.tensor_tensor(ltmp[:, 640:768], ltmp[:, 256:384], ltmp[:, 384:512], op=ALU.mult),
        ])
        DV.selfwait()
        DV.opn(lambda v: [
            v.tensor_reduce(lred[:, 0:1], ltmp[:, 512:640], op=ALU.add, axis=mybir.AxisListType.X),
            v.tensor_reduce(lred[:, 1:2], ltmp[:, 640:768], op=ALU.add, axis=mybir.AxisListType.X),
        ])
        AC.wait(DV)
        AC.opn(lambda s: [s.activation(lred[:, 2:3], lred[:, 0:1], AF.Exp),
                          s.activation(lred[:, 3:4], lred[:, 1:2], AF.Exp)])
        DV.wait(AC)
        DV.op(lambda v: v.tensor_tensor(lamt[:, :], lred[:, 2:3], lred[:, 3:4], op=ALU.subtract))
        DV.selfwait()
        DV.op(lambda v: v.tensor_scalar(lamt[:, :], lamt[:, :], LAMBDA_INIT, None, op0=ALU.add))

        with ExitStack() as ph:
            Ep = ph.enter_context
            xt = [Ep(nc.sbuf_tensor(f"xt{i}", [128, D], F32)) for i in range(2)]
            xnt = [Ep(nc.sbuf_tensor(f"xnt{i}", [128, D], F16)) for i in range(2)]
            NST = max(D // 512, 1)
            stats = Ep(nc.sbuf_tensor("stats", [128, 6 * NST], F32))
            mv = Ep(nc.sbuf_tensor("mv", [128, 2], F32))
            negm = Ep(nc.sbuf_tensor("negm", [128, 1], F32))
            lnv = Ep(nc.sbuf_tensor("lnv", [128, 1], F32))
            rstd = Ep(nc.sbuf_tensor("rstd", [128, 1], F32))
            txp = [Ep(nc.sbuf_tensor(f"txp{i}", [128, 2048], F16)) for i in range(2)]

            tok_xn_dv = {}
            tok_xn_store = {}
            tok_xload = {}

            def emit_xload(si):
                if si > SCH:
                    return
                if si >= 2:
                    GP.wait(DV, tok_xn_dv[si - 2])
                row = 128 * (si % SCH)
                GP.op(lambda g, row=row, p=si % 2: g.dma_start(
                    out=xt[p][:, :], in_=x_in[row:row + 128, :]), inc=16)
                tok_xload[si] = GP.n

            emit_xload(0)
            for si in range(SCH):
                p = si % 2
                emit_xload(si + 1)
                DV.wait(GP, tok_xload[min(si + 1, SCH)])
                if si >= 2:
                    DV.wait(SP, tok_xn_store[si - 2])
                DV.opn(lambda v, p=p: [
                    v.bn_stats(stats[:, 6 * i:6 * i + 6], xt[p][:, 512 * i:512 * i + min(512, D)])
                    for i in range(NST)])
                DV.selfwait()
                DV.op(lambda v: v.bn_aggr(mv[:, :], stats[:, :]))
                DV.selfwait()
                DV.op(lambda v: v.tensor_scalar(negm[:, :], mv[:, 0:1], -1.0, None, op0=ALU.mult))
                AC.wait(DV)
                AC.op(lambda s: s.activation(lnv[:, :], mv[:, 1:2], AF.Ln, bias=epsLN[:, :]))
                AC.selfwait()
                AC.op(lambda s: s.activation(rstd[:, :], lnv[:, :], AF.Exp, scale=-0.5))
                DV.wait(AC)
                tok_xn_dv[si] = DV.op(lambda v, p=p: v.tensor_scalar(
                    xnt[p][:, :], xt[p][:, :], negm[:, :], rstd[:, :], op0=ALU.add, op1=ALU.mult))
                SP.wait(DV, tok_xn_dv[si])
                tok_xn_store[si] = SP.op(lambda s, si=si, p=p: s.dma_start(
                    out=xn_nat[128 * si:128 * si + 128, :], in_=xnt[p][:, :]), inc=16)

            SP.selfwait()
            HW = min(2048, S)
            NHALF = S // HW
            for dc in range(DCH):
                for hf in range(NHALF):
                    p = (dc * NHALF + hf) % 2
                    SP.op(lambda s, dc=dc, hf=hf, p=p: s.dma_start_transpose(
                        txp[p][:, 0:HW],
                        xn_nat[HW * hf:HW * hf + HW, 128 * dc:128 * dc + 128]), inc=16)
                    SP.selfwait()
                    SP.op(lambda s, dc=dc, hf=hf, p=p: s.dma_start(
                        out=xnT_nat[128 * dc:128 * dc + 128, HW * hf:HW * hf + HW],
                        in_=txp[p][:, 0:HW]), inc=16)
            SP.selfwait()
            barrier()

        with ExitStack() as ph:
            Ep = ph.enter_context
            wqT = [Ep(nc.sbuf_tensor(f"wqT{i}", [128, DCH * 128], F16)) for i in range(2)]
            wkT = [Ep(nc.sbuf_tensor(f"wkT{i}", [128, DCH * 128], F16)) for i in range(2)]
            wvT = [Ep(nc.sbuf_tensor(f"wvT{i}", [128, DCH * 128], F16)) for i in range(2)]
            xnS = [Ep(nc.sbuf_tensor(f"xnS{i}", [128, DCH * 512], F16)) for i in range(2)]
            QTs = Ep(nc.sbuf_tensor("QTs", [128, S], F16))
            KTs = Ep(nc.sbuf_tensor("KTs", [128, S], F16))
            Vs = Ep(nc.sbuf_tensor("Vs", [128, S], F16))
            Et = [Ep(nc.sbuf_tensor(f"Et{i}", [128, 2 * QT], F16)) for i in range(2)]
            a1s = Ep(nc.sbuf_tensor("a1s", [1, QT], F32))
            a2s = Ep(nc.sbuf_tensor("a2s", [1, QT], F32))
            a2b = Ep(nc.sbuf_tensor("a2b", [1, QT], F32))
            bc1 = Ep(nc.sbuf_tensor("bc1", [128, QT], F32))
            bc2 = Ep(nc.sbuf_tensor("bc2", [128, QT], F32))
            t1s = Ep(nc.sbuf_tensor("t1s", [128, QT], F32))
            t2s = Ep(nc.sbuf_tensor("t2s", [128, QT], F32))
            vcb = Ep(nc.sbuf_tensor("vcb", [128, QT], F32))
            sqs = Ep(nc.sbuf_tensor("sqs", [128, 512], F32))
            ssA = Ep(nc.sbuf_tensor("ssA", [128, 4], F32))
            lnA = Ep(nc.sbuf_tensor("lnA", [128, 4], F32))
            rsA = Ep(nc.sbuf_tensor("rsA", [128, 4], F32))
            nrm = Ep(nc.sbuf_tensor("nrm", [128, 2 * SCH * 128], F16))
            woT = [Ep(nc.sbuf_tensor(f"woT{i}", [128, D], F16)) for i in range(3)]
            sev = Ep(nc.sbuf_tensor("sev", [128, D], F32))

            S0, S1c = 0, 1024
            AV1, AV2 = 2048, 2560
            D1C, D2C = 3072, 3584
            QC, KC, VC = 0, 512, 1024
            TRC = 1024

            xn_use = {}
            xn_tok = {}
            xcnt = 0
            tok_wT = {}
            wo_use = {}
            tok_nrm_last = {}
            tok_wo_mm = {}

            for pr in range(HC):
                pp = pr % 2
                if pr >= 2:
                    SP.wait(PE, tok_wT[("consumed", pr - 2)])
                SP.wait(GP, tok_wcast)
                SP.dman(lambda s, pr=pr, pp=pp: [
                    s.dma_start_transpose(
                        w_t[pp][:, 128 * dc:128 * dc + 128],
                        w_b[128 * pr:128 * pr + 128, 128 * dc:128 * dc + 128])
                    for (w_t, w_b) in ((wqT, wq_bf), (wkT, wk_bf), (wvT, wv_bf))
                    for dc in range(DCH)])
                SP.op(lambda s: s.dma_start(out=ltmp[:, 0:128], in_=lq1), inc=16)
                tok_wT[pr] = SP.n

                tok_qkv_evac = 0
                for st in range(NQT):
                    sp2 = xcnt % 2
                    for k in (xcnt, xcnt + 1):
                        if k not in xn_tok and k <= HC * NQT:
                            if k >= 2:
                                GP.wait(PE, xn_use.get(k - 2, 0))
                            st_k = k % NQT
                            GP.op(lambda g, st_k=st_k, p2=k % 2: g.dma_start(
                                out=xnS[p2][:, :].rearrange("p (c s) -> p c s", c=DCH),
                                in_=xnT_nat.rearrange("(c p) s -> p c s", p=128)[:, :, QT * st_k:QT * st_k + QT]),
                            inc=16)
                            xn_tok[k] = GP.n
                    PE.wait(GP, xn_tok[min(xcnt + 1, HC * NQT)])
                    PE.wait(SP, tok_wT[pr])
                    if st == 0:
                        PE.wait(DV)
                        PE.wait(AC)
                    else:
                        PE.wait(DV, tok_qkv_evac)
                    PE.mmgroup(lambda t, pp=pp, sp2=sp2: [
                        t.matmul(PP[:, QC:QC + 512], wqT[pp][:, 128 * dc:128 * dc + 128],
                                 xnS[sp2][:, 512 * dc:512 * dc + 512],
                                 start=(dc == 0), stop=(dc == DCH - 1))
                        for dc in range(DCH)])
                    PE.mmgroup(lambda t, pp=pp, sp2=sp2: [
                        t.matmul(PP[:, KC:KC + 512], wkT[pp][:, 128 * dc:128 * dc + 128],
                                 xnS[sp2][:, 512 * dc:512 * dc + 512],
                                 start=(dc == 0), stop=(dc == DCH - 1))
                        for dc in range(DCH)])
                    PE.mmgroup(lambda t, pp=pp, sp2=sp2: [
                        t.matmul(PP[:, VC + 128 * q:VC + 128 * q + 128],
                                 xnS[sp2][:, 512 * dc + 128 * q:512 * dc + 128 * q + 128],
                                 wvT[pp][:, 128 * dc:128 * dc + 128],
                                 start=(dc == 0), stop=(dc == DCH - 1))
                        for q in range(4) for dc in range(DCH)])
                    xn_use[xcnt] = PE.n
                    xcnt += 1
                    if st == NQT - 1:
                        tok_wT[("consumed", pr)] = PE.n
                    DV.wait(PE)
                    tok_qkv_evac = DV.opn(lambda v, st=st: [
                        v.tensor_copy(QTs[:, QT * st:QT * st + QT], PP[:, QC:QC + 512]),
                        v.tensor_copy(KTs[:, QT * st:QT * st + QT], PP[:, KC:KC + 512]),
                        v.tensor_copy(Vs[:, QT * st:QT * st + QT], PP[:, VC:VC + 512]),
                    ])
                tok_qkv_dv = DV.n

                for qt in range(NQT):
                    tok_scores = {}
                    tok_exp = {}
                    tok_cons = {}
                    for kc in range(KCH):
                        sp2 = kc % 2
                        if kc == 0:
                            PE.wait(DV, tok_qkv_dv)
                            if qt > 0:
                                PE.wait(DV, tok_nrm_last[(pr, qt - 1)])
                        if kc >= 2:
                            PE.wait(AC, tok_exp[kc - 2])
                        PE.mmgroup(lambda t, kc=kc, qt=qt, sp2=sp2: [
                            t.matmul(PP[:, 1024 * sp2:1024 * sp2 + 512],
                                     KTs[0:64, 128 * kc:128 * kc + 128],
                                     QTs[0:64, QT * qt:QT * qt + QT],
                                     start=True, stop=True, tile_position=(0, 0)),
                            t.matmul(PP[:, 1024 * sp2 + 512:1024 * sp2 + 1024],
                                     KTs[64:128, 128 * kc:128 * kc + 128],
                                     QTs[64:128, QT * qt:QT * qt + QT],
                                     start=True, stop=True, tile_position=(64, 0)),
                        ])
                        tok_scores[kc] = PE.n
                        AC.wait(PE, tok_scores[kc])
                        if kc >= 2:
                            AC.wait(PE, tok_cons[kc - 2])
                        tok_exp[kc] = AC.op(lambda s, sp2=sp2: s.activation(
                            Et[sp2][:, :], PP[:, 1024 * sp2:1024 * sp2 + 2 * QT],
                            AF.Exp, scale=SM_SCALE))
                        PE.wait(AC, tok_exp[kc])
                        PE.mmgroup(lambda t, kc=kc, sp2=sp2: [
                            t.matmul(PP[0:1, D1C:D1C + QT], onesb[:, :], Et[sp2][:, 0:QT],
                                     start=(kc == 0), stop=(kc == KCH - 1)),
                            t.matmul(PP[0:1, D2C:D2C + QT], onesb[:, :], Et[sp2][:, QT:2 * QT],
                                     start=(kc == 0), stop=(kc == KCH - 1)),
                            t.matmul(PP[:, AV1:AV1 + QT], Vs[:, 128 * kc:128 * kc + 128],
                                     Et[sp2][:, 0:QT], start=(kc == 0), stop=(kc == KCH - 1)),
                            t.matmul(PP[:, AV2:AV2 + QT], Vs[:, 128 * kc:128 * kc + 128],
                                     Et[sp2][:, QT:2 * QT], start=(kc == 0), stop=(kc == KCH - 1)),
                        ])
                        tok_cons[kc] = PE.n

                    DV.wait(PE, tok_cons[KCH - 1])
                    DV.opn(lambda v: [
                        v.reciprocal(a1s[:, :], PP[0:1, D1C:D1C + QT]),
                        v.reciprocal(a2s[:, :], PP[0:1, D2C:D2C + QT]),
                    ])
                    DV.selfwait()
                    DV.op(lambda v: v.tensor_scalar(a2b[:, :], a2s[:, :], lamt[0:1, 0:1], -1.0,
                                                    op0=ALU.mult, op1=ALU.mult))
                    tok_alpha = DV.n
                    PE.wait(DV, tok_alpha)
                    PE.wait(AC, tok_exp[KCH - 1])
                    PE.mmgroup(lambda t: [
                        t.matmul(PP[:, S0:S0 + QT], ones1[0:1, :], a1s[:, :], start=True, stop=True)])
                    PE.mmgroup(lambda t: [
                        t.matmul(PP[:, S0 + QT:S0 + 2 * QT], ones1[0:1, :], a2b[:, :], start=True, stop=True)])
                    DV.wait(PE)
                    DV.opn(lambda v: [
                        v.tensor_copy(bc1[:, :], PP[:, S0:S0 + QT]),
                        v.tensor_copy(bc2[:, :], PP[:, S0 + QT:S0 + 2 * QT]),
                    ])
                    DV.selfwait()
                    DV.opn(lambda v: [
                        v.tensor_tensor(t1s[:, :], PP[:, AV1:AV1 + QT], bc1[:, :], op=ALU.mult),
                        v.tensor_tensor(t2s[:, :], PP[:, AV2:AV2 + QT], bc2[:, :], op=ALU.mult),
                    ])
                    DV.selfwait()
                    tok_comb = DV.op(lambda v: v.tensor_tensor(vcb[:, :], t1s[:, :], t2s[:, :], op=ALU.add))
                    PE.wait(DV, tok_comb)
                    PE.mmgroup(lambda t: [
                        t.transpose(PP[:, TRC + 128 * ti:TRC + 128 * ti + 128],
                                    vcb[:, 128 * ti:128 * ti + 128], identf[:, :])
                        for ti in range(NT)])
                    AC.wait(PE)
                    AC.opn(lambda s: [
                        s.activation(sqs[:, 128 * ti:128 * ti + 128],
                                     PP[:, TRC + 128 * ti:TRC + 128 * ti + 128],
                                     AF.Square, accum_out=ssA[:, ti:ti + 1])
                        for ti in range(NT)])
                    AC.selfwait()
                    AC.opn(lambda s: [
                        s.activation(lnA[:, ti:ti + 1], ssA[:, ti:ti + 1], AF.Ln,
                                     scale=1.0 / 128.0, bias=epsRMS[:, :])
                        for ti in range(NT)])
                    AC.selfwait()
                    AC.opn(lambda s: [
                        s.activation(rsA[:, ti:ti + 1], lnA[:, ti:ti + 1], AF.Exp, scale=-0.5)
                        for ti in range(NT)])
                    DV.wait(AC)
                    if qt == 0 and pr >= 2:
                        DV.wait(PE, tok_wo_mm[(pr - 2) | 1])
                    po = (pr % 2) * SCH * 128
                    tok_nrm_last[(pr, qt)] = DV.opn(lambda v, qt=qt, po=po: [
                        v.tensor_scalar(nrm[:, po + (qt * NT + ti) * 128:po + (qt * NT + ti) * 128 + 128],
                                        PP[:, TRC + 128 * ti:TRC + 128 * ti + 128],
                                        rsA[:, ti:ti + 1], 1.0 - LAMBDA_INIT,
                                        op0=ALU.mult, op1=ALU.mult)
                        for ti in range(NT)])

                if pr % 2 == 1 or pr == HC - 1:
                    duo = (pr - 1, pr) if pr % 2 == 1 else (pr,)
                    combos = [(pa, sh) for pa in duo for sh in (0, 1)]
                    ngrp = (len(combos) + 1) // 2
                    nwo = ngrp * CC
                    wotok = {}

                    def emit_woload(k, nwo=nwo):
                        if k > nwo or k in wotok:
                            return
                        if k >= 3:
                            SP.wait(PE, wo_use[k - 3])
                        cck = k % CC
                        SP.op(lambda s, cck=cck, wp=k % 3: s.dma_start_transpose(
                            woT[wp][:, :], wo_bf[0:D, 128 * cck:128 * cck + 128]), inc=16)
                        wotok[k] = SP.n

                    woci = 0
                    prev_sstore = 0
                    emit_woload(0)
                    for g0 in range(0, len(combos), 2):
                        sub = combos[g0:g0 + 2]
                        for cc in range(CC):
                            wp = woci % 3
                            emit_woload(woci + 1)
                            PE.wait(SP, wotok[min(woci + 1, nwo)])
                            if cc == 0:
                                PE.wait(DV)
                            def womms(t, sub=sub, cc=cc, wp=wp):
                                ins = []
                                for ci, (pa, sh) in enumerate(sub):
                                    po2 = (pa % 2) * SCH * 128
                                    lhs = nrm[:, po2 + (sh * CC + cc) * 128:po2 + (sh * CC + cc) * 128 + 128]
                                    for j in range(NJ):
                                        ins.append(t.matmul(
                                            PP[:, 2048 * ci + JT * j:2048 * ci + JT * j + JT],
                                            lhs, woT[wp][:, JT * j:JT * j + JT],
                                            start=(cc == 0), stop=(cc == CC - 1)))
                                return ins
                            PE.mmgroup(womms)
                            wo_use[woci] = PE.n
                            woci += 1
                        tok_wo_mm[pr] = PE.n
                        for ci, (pa, sh) in enumerate(sub):
                            ul = 2 * pa + sh
                            DV.wait(PE)
                            DV.wait(GP, prev_sstore)
                            DV.op(lambda v, ci=ci: v.tensor_copy(sev[:, :], PP[:, 2048 * ci:2048 * ci + D]))
                            GP.wait(DV)
                            prev_sstore = GP.op(lambda g, ul=ul: g.dma_start(
                                out=s_stash[128 * ul:128 * ul + 128, :], in_=sev[:, :]), inc=16)
            barrier()

        with ExitStack() as ph:
            Ep = ph.enter_context
            pk = [Ep(nc.sbuf_tensor(f"pk{i}", [128, 256], I32)) for i in range(2)]
            bA = Ep(nc.sbuf_tensor("bA", [128, 2048], I32))
            bB = Ep(nc.sbuf_tensor("bB", [128, 2048], I32))
            bC = Ep(nc.sbuf_tensor("bC", [128, 2048], I32))
            bD = Ep(nc.sbuf_tensor("bD", [128, 2048], F32))
            Wf = Ep(nc.sbuf_tensor("Wf", [128, 2048], F32))
            dense = Ep(nc.sbuf_tensor("dense", [128, 2048], F16))
            scT = Ep(nc.sbuf_tensor("scT", [128, 64], F32))
            stg = Ep(nc.sbuf_tensor("stg", [128, max(DCH, ICH) * 512], F16))
            dnsb = Ep(nc.sbuf_tensor("dnsb", [NGD, D], F16))
            gusb = Ep(nc.sbuf_tensor("gusb", [NGU, I2], F16))

            bAf = bA.bitcast(F32)
            bBf = bB.bitcast(F32)
            bDi = bD.bitcast(I32)
            WfR = Wf.bitcast(I32).rearrange("p (w e) -> p e w", e=8)

            def dequant_strip(nw, src_pk):
                DV.opn(lambda v: [
                    v.tensor_scalar(bA[:, 256 * j:256 * j + nw], src_pk[:, 0:nw],
                                    4 * j, 7, op0=ALU.logical_shift_right, op1=ALU.bitwise_and)
                    for j in range(8)] + [
                    v.tensor_scalar(bC[:, 256 * j:256 * j + nw], src_pk[:, 0:nw],
                                    28 - 4 * j, -2147483648,
                                    op0=ALU.logical_shift_left, op1=ALU.bitwise_and)
                    for j in range(8)])
                tok_rd = DV.n
                DV.selfwait()
                DV.opn(lambda v: [
                    v.tensor_scalar(bB[:, 256 * j:256 * j + nw], bA[:, 256 * j:256 * j + nw],
                                    252.0, float(1 << 22), op0=ALU.add, op1=ALU.mult)
                    for j in range(8)])
                DV.selfwait()
                DV.opn(lambda v: [
                    v.tensor_scalar(bAf[:, 256 * j:256 * j + nw], bBf[:, 256 * j:256 * j + nw],
                                    2.0, 1.0, op0=ALU.mult, op1=ALU.subtract)
                    for j in range(8)])
                DV.selfwait()
                DV.opn(lambda v: [
                    v.tensor_tensor(bD[:, 256 * j:256 * j + nw], bAf[:, 256 * j:256 * j + nw],
                                    bBf[:, 256 * j:256 * j + nw], op=ALU.min)
                    for j in range(8)])
                DV.selfwait()
                DV.opn(lambda v: [
                    v.tensor_tensor(WfR[:, j, 0:nw], bDi[:, 256 * j:256 * j + nw],
                                    bC[:, 256 * j:256 * j + nw], op=ALU.bitwise_or)
                    for j in range(8)])
                DV.selfwait()
                return tok_rd

            tok_pk_rd = {}
            tok_pe_tr = {}
            tok_evac = {}
            tok_stg_store = 0
            scnt = 0

            pk_sched = [("g", ot, 0, WI_GU) for ot in range(NOT_GU)]
            for ot_ in range(NOT_DN):
                for sti_ in range((WI_DN + 255) // 256):
                    w0_ = 256 * sti_
                    pk_sched.append(("d", ot_, w0_, min(256, WI_DN - w0_)))
            pk_sched.append(("g", 0, 0, WI_GU))
            pktok = {}
            pkemit = [0]

            def emit_pk(upto):
                while pkemit[0] <= min(upto, len(pk_sched) - 1):
                    k = pkemit[0]
                    if k >= 2:
                        GP.wait(DV, tok_pk_rd.get(k - 2, 0))
                    kind, ot_, w0_, nw_ = pk_sched[k]
                    if kind == "g":
                        GP.op(lambda g, ot_=ot_, p2=k % 2: g.dma_start(
                            out=pk[p2][:, 0:WI_GU], in_=gup[128 * ot_:128 * ot_ + 128, :]), inc=16)
                    else:
                        GP.op(lambda g, ot_=ot_, w0_=w0_, nw_=nw_, p2=k % 2: g.dma_start(
                            out=pk[p2][:, 0:nw_], in_=dnp[128 * ot_:128 * ot_ + 128, w0_:w0_ + nw_]), inc=16)
                    pktok[k] = GP.n
                    pkemit[0] += 1

            def pk_wait(k):
                emit_pk(k + 1)
                DV.wait(GP, pktok[min(k + 1, len(pk_sched) - 1)])

            SP.op(lambda s: s.dma_start(out=gusb[:, :], in_=gus), inc=16)
            SP.op(lambda s: s.dma_start(out=dnsb[:, :], in_=dns), inc=16)
            tok_gus = SP.n
            tok_scg_used = {}
            for ot in range(NOT_GU):
                p2 = scnt % 2
                PE.wait(SP, tok_gus)
                if ot > 0:
                    PE.wait(DV, tok_scg_used[ot - 1])
                PE.mmgroup(lambda t, ot=ot: [
                    t.transpose(PPh[:, 2 * 3584:2 * 3584 + NGU],
                                gusb[0:NGU, 128 * ot:128 * ot + 128], identh[0:NGU, 0:NGU])])
                DV.wait(PE)
                DV.op(lambda v: v.tensor_copy(scT[:, 0:NGU], PPh[:, 2 * 3584:2 * 3584 + NGU]))
                DV.selfwait()
                pk_wait(scnt)
                tok_pk_rd[scnt] = dequant_strip(WI_GU, pk[p2])
                scnt += 1
                if ot >= 1:
                    DV.wait(PE, tok_pe_tr.get(ot - 1, 0))
                if ot % 4 == 0 and ot > 0:
                    DV.wait(GP, tok_stg_store)
                tok_scg_used[ot] = DV.opn(lambda v: [
                    v.tensor_scalar(dense[:, 128 * g2:128 * g2 + 128], Wf[:, 128 * g2:128 * g2 + 128],
                                    scT[:, g2:g2 + 1], None, op0=ALU.mult)
                    for g2 in range(NGU)])
                DV.selfwait()
                PE.wait(DV)
                if ot >= 2:
                    PE.wait(DV, tok_evac[ot - 2])
                pb = (ot % 2) * 1024
                PE.mmgroup(lambda t, pb=pb: [
                    t.transpose(PPh[:, 2 * pb + 128 * ci2:2 * pb + 128 * ci2 + 128],
                                dense[:, 128 * ci2:128 * ci2 + 128], identh[:, :])
                    for ci2 in range(DCH)])
                tok_pe_tr[ot] = PE.n
                DV.wait(PE)
                ob = (ot % 4) * 128
                tok_evac[ot] = DV.opn(lambda v, pb=pb, ob=ob: [
                    v.tensor_copy(stg[:, 512 * ci2 + ob:512 * ci2 + ob + 128],
                                  PPh[:, 2 * pb + 128 * ci2:2 * pb + 128 * ci2 + 128])
                    for ci2 in range(DCH)])
                if ot % 4 == 3:
                    GP.wait(DV, tok_evac[ot])
                    blk0 = 128 * (ot - 3)
                    tok_stg_store = GP.dman(lambda g, blk0=blk0: [
                        g.dma_start(out=wguT[ci2, :, blk0:blk0 + 512],
                                    in_=stg[:, 512 * ci2:512 * ci2 + 512])
                        for ci2 in range(DCH)])

            tok_scT_used = {}
            for ot in range(NOT_DN):
                nstrip = (WI_DN + 255) // 256
                if ot > 0:
                    PE.wait(DV, tok_scT_used[ot - 1])
                PE.mmgroup(lambda t, ot=ot: [
                    t.transpose(PPh[:, 2 * 3584:2 * 3584 + NGD],
                                dnsb[0:NGD, 128 * ot:128 * ot + 128], identh[0:NGD, 0:NGD])])
                DV.wait(PE)
                DV.op(lambda v: v.tensor_copy(scT[:, 0:NGD], PPh[:, 2 * 3584:2 * 3584 + NGD]))
                DV.selfwait()
                for sti in range(nstrip):
                    w0 = 256 * sti
                    nw = min(256, WI_DN - w0)
                    p2 = scnt % 2
                    pk_wait(scnt)
                    tok_pk_rd[scnt] = dequant_strip(nw, pk[p2])
                    scnt += 1
                    g0 = (8 * w0) // 128
                    ng = (8 * nw) // 128
                    key = ("dn", ot, sti)
                    prevkey = ("dn", ot, sti - 1) if sti > 0 else ("dn", ot - 1, nstrip - 1)
                    if prevkey in tok_pe_tr:
                        DV.wait(PE, tok_pe_tr[prevkey])
                    if (ot * nstrip + sti) % 4 == 0 and ot + sti > 0:
                        DV.wait(GP, tok_stg_store)
                    DV.opn(lambda v, g0=g0, ng=ng: [
                        v.tensor_scalar(dense[:, 128 * g2:128 * g2 + 128],
                                        Wf[:, 128 * g2:128 * g2 + 128],
                                        scT[:, g0 + g2:g0 + g2 + 1], None, op0=ALU.mult)
                        for g2 in range(ng)])
                    DV.selfwait()
                    tok_scT_used[ot] = DV.n
                    PE.wait(DV)
                    prevkey2 = ("dn", ot, sti - 2) if sti >= 2 else None
                    if prevkey2 in tok_evac:
                        PE.wait(DV, tok_evac[prevkey2])
                    pb = (scnt % 2) * 1024
                    PE.mmgroup(lambda t, pb=pb, ng=ng: [
                        t.transpose(PPh[:, 2 * pb + 128 * ci2:2 * pb + 128 * ci2 + 128],
                                    dense[:, 128 * ci2:128 * ci2 + 128], identh[:, :])
                        for ci2 in range(ng)])
                    tok_pe_tr[key] = PE.n
                    DV.wait(PE)
                    ob = (ot % 4) * 128
                    tok_evac[key] = DV.opn(lambda v, pb=pb, ob=ob, g0=g0, ng=ng: [
                        v.tensor_copy(stg[:, 512 * (g0 + ci2) + ob:512 * (g0 + ci2) + ob + 128],
                                      PPh[:, 2 * pb + 128 * ci2:2 * pb + 128 * ci2 + 128])
                        for ci2 in range(ng)])
                if ot % 4 == 3 or ot == NOT_DN - 1:
                    nblk = (ot % 4) + 1
                    blk0 = 128 * (ot - nblk + 1)
                    GP.wait(DV, tok_evac[("dn", ot, nstrip - 1)])
                    tok_stg_store = GP.dman(lambda g, blk0=blk0, nblk=nblk: [
                        g.dma_start(out=wdnT[ci2, :, blk0:blk0 + 128 * nblk],
                                    in_=stg[:, 512 * ci2:512 * ci2 + 128 * nblk])
                        for ci2 in range(ICH)])
            emit_pk(len(pk_sched) - 1)
            GP.selfwait()
            barrier()

        with ExitStack() as ph:
            Ep = ph.enter_context
            xnU = Ep(nc.sbuf_tensor("xnU", [128, 4 * DCH * 128], F16))
            wsl = [Ep(nc.sbuf_tensor(f"wsl{i}", [128, 512], F16)) for i in range(4)]
            gsil = Ep(nc.sbuf_tensor("gsil", [128, 4 * 512], F16))
            hsb = Ep(nc.sbuf_tensor("hsb", [128, 4 * 512], F16))
            hT = Ep(nc.sbuf_tensor("hT", [128, 4 * ICH * 128], F16))
            ssF = Ep(nc.sbuf_tensor("ssF", [128, 1], F32))
            lnF = Ep(nc.sbuf_tensor("lnF", [128, 1], F32))
            rsF = Ep(nc.sbuf_tensor("rsF", [128, 1], F32))
            ybuf = [Ep(nc.sbuf_tensor(f"ybuf{i}", [128, D], F32)) for i in range(4)]
            gbuf = Ep(nc.sbuf_tensor("gbuf", [128, D], F32))
            sbuf2 = [Ep(nc.sbuf_tensor(f"sbuf2{i}", [128, D], F32)) for i in range(2)]
            xbuf = [Ep(nc.sbuf_tensor(f"xbuf{i}", [128, D], F32)) for i in range(2)]
            obuf = Ep(nc.sbuf_tensor("obuf", [128, D], F32))

            NUG = (U + 3) // 4
            load_fns = []
            loadtok = {}
            wtoks = {}
            emitted = [0]
            LA = 3

            def emit_load(idx):
                while emitted[0] <= min(idx, len(load_fns) - 1):
                    k = emitted[0]
                    if k >= 4:
                        GP.wait(PE, wtoks[k - 4])
                    load_fns[k]()
                    loadtok[k] = GP.n
                    emitted[0] += 1

            def mk_gu_load(dc, rb, wp):
                def f():
                    GP.op(lambda g: g.dma_start(out=wsl[wp][:, :], in_=wguT[dc, :, rb:rb + 512]), inc=16)
                return f

            def mk_dn_load(ic, j, wp):
                def f():
                    GP.op(lambda g: g.dma_start(out=wsl[wp][:, 0:JT], in_=wdnT[ic, :, JT * j:JT * j + JT]), inc=16)
                return f

            sched = []
            li = 0
            for ug in range(NUG):
                for og in range(OG):
                    for ph2 in (0, 1):
                        rb = 512 * og + (I if ph2 else 0)
                        for dc in range(DCH):
                            load_fns.append(mk_gu_load(dc, rb, li % 4)); li += 1
                for j in range(NJ):
                    for ic in range(ICH):
                        load_fns.append(mk_dn_load(ic, j, li % 4)); li += 1
            load_fns.append(mk_dn_load(0, 0, li % 4))
            NLOAD = li

            def pe_wait_load(k):
                emit_load(k + LA)
                PE.wait(GP, loadtok[min(k + 1, len(load_fns) - 1)])

            tok_xnU_used = 0
            tok_g_ac = 0
            tok_gbuf_free = 0
            tok_out_dv = 0
            tok_obuf_store = 0
            sxtok = {}
            sx_guard = {}
            tok_ydump = 0
            gi = 0

            for ug in range(NUG):
                us = list(range(4 * ug, min(4 * ug + 4, U)))
                nu = len(us)
                if ug > 0:
                    SP.wait(PE, tok_xnU_used)
                SP.dman(lambda s, us=us: [
                    s.dma_start_transpose(
                        xnU[:, (ui * DCH + dc) * 128:(ui * DCH + dc) * 128 + 128],
                        xn_nat.rearrange("(m blk) d -> blk m d", blk=BLK)[u, :, 128 * dc:128 * dc + 128])
                    for ui, u in enumerate(us) for dc in range(DCH)])
                tok_xnU = SP.n

                tok_h_dv = 0
                tok_hTdv = 0
                tok_hT_pe = 0
                tok_gsil_ac = 0
                tok_yevac = {}
                for og in range(OG):
                    for ph2 in (0, 1):
                        for dc in range(DCH):
                            pe_wait_load(gi)
                            if dc == 0:
                                PE.wait(SP, tok_xnU)
                                if ph2 == 0 and (og > 0 or ug > 0):
                                    PE.wait(DV, tok_hTdv if og > 0 else tok_ylast)
                                if ph2 == 1:
                                    PE.wait(AC, tok_gsil_ac)
                            pbase = 2048 * ph2
                            wp = gi % 4
                            PE.mmgroup(lambda t, us=us, dc=dc, wp=wp, pbase=pbase: [
                                t.matmul(PP[:, pbase + 512 * ui:pbase + 512 * ui + 512],
                                         xnU[:, (ui * DCH + dc) * 128:(ui * DCH + dc) * 128 + 128],
                                         wsl[wp][:, :], start=(dc == 0), stop=(dc == DCH - 1))
                                for ui in range(len(us))])
                            wtoks[gi] = PE.n
                            gi += 1
                        if ph2 == 0:
                            AC.wait(PE)
                            AC.wait(DV, tok_h_dv)
                            tok_gsil_ac = AC.opn(lambda s, nu=nu: [
                                s.activation(gsil[:, 512 * ui:512 * ui + 512],
                                             PP[:, 512 * ui:512 * ui + 512], AF.Silu)
                                for ui in range(nu)])
                        else:
                            DV.wait(PE)
                            DV.wait(AC, tok_gsil_ac)
                            DV.wait(PE, tok_hT_pe)
                            tok_h_dv = DV.opn(lambda v, nu=nu: [
                                v.scalar_tensor_tensor(
                                    hsb[:, 512 * ui:512 * ui + 512],
                                    PP[:, 2048 + 512 * ui:2048 + 512 * ui + 512], 1.0 / 64.0,
                                    gsil[:, 512 * ui:512 * ui + 512],
                                    op0=ALU.mult, op1=ALU.mult)
                                for ui in range(nu)])
                    PE.wait(DV, tok_h_dv)
                    PE.mmgroup(lambda t, nu=nu: [
                        t.transpose(PPh[:, 128 * (4 * ui + ti):128 * (4 * ui + ti) + 128],
                                    hsb[:, 512 * ui + 128 * ti:512 * ui + 128 * ti + 128], identh[:, :])
                        for ui in range(nu) for ti in range(4)])
                    tok_hT_pe = PE.n
                    DV.wait(PE, tok_hT_pe)
                    tok_hTdv = DV.opn(lambda v, og=og, nu=nu: [
                        v.tensor_copy(hT[:, (ui * ICH + og * 4 + ti) * 128:(ui * ICH + og * 4 + ti) * 128 + 128],
                                      PPh[:, 128 * (4 * ui + ti):128 * (4 * ui + ti) + 128])
                        for ui in range(nu) for ti in range(4)])
                tok_xnU_used = PE.n

                for j in range(NJ):
                    q = j % 2
                    for ic in range(ICH):
                        pe_wait_load(gi)
                        if ic == 0:
                            if j == 0:
                                PE.wait(DV, tok_hTdv)
                            if j >= 2:
                                PE.wait(DV, tok_yevac[j - 2])
                            if ug > 0 and j < 2:
                                PE.wait(AC, tok_g_ac)
                        wp = gi % 4
                        PE.mmgroup(lambda t, us=us, j=j, ic=ic, wp=wp, q=q, nu=nu: [
                            t.matmul(PP[:, 2048 * q + 512 * ui:2048 * q + 512 * ui + JT],
                                     hT[:, (ui * ICH + ic) * 128:(ui * ICH + ic) * 128 + 128],
                                     wsl[wp][:, 0:JT], start=(ic == 0), stop=(ic == ICH - 1))
                            for ui in range(nu)])
                        wtoks[gi] = PE.n
                        gi += 1
                    DV.wait(PE)
                    if ug > 0 and j == 0:
                        DV.wait(AC, tok_g_ac)
                        DV.wait(SP, tok_ydump)
                    tok_yevac[j] = DV.opn(lambda v, j=j, q=q, nu=nu: [
                        v.tensor_copy(ybuf[ui][:, JT * j:JT * j + JT],
                                      PP[:, 2048 * q + 512 * ui:2048 * q + 512 * ui + JT])
                        for ui in range(nu)])
                tok_ylast = tok_yevac[NJ - 1]

                if debug:
                    SP.wait(DV, tok_ylast)
                    SP.dman(lambda s, us=us: [
                        s.dma_start(out=y_dbg[128 * u:128 * u + 128, :], in_=ybuf[ui][:, :])
                        for ui, u in enumerate(us)])
                    tok_ydump = SP.n

                for ui, u in enumerate(us):
                    ul = u
                    for uk in (u, u + 1):
                        if uk not in sxtok and uk <= U:
                            SP.wait(DV, sx_guard.get(uk - 2, 0))
                            row = 128 * (uk % U)
                            SP.op(lambda s, row=row, p=uk % 2: s.dma_start(
                                out=sbuf2[p][:, :], in_=s_stash[row:row + 128, :]), inc=16)
                            SP.op(lambda s, p=uk % 2, blk=uk % U: s.dma_start(
                                out=xbuf[p][:, :],
                                in_=x_in.rearrange("(m blk) d -> blk m d", blk=BLK)[blk, :, :]), inc=16)
                            sxtok[uk] = SP.n
                    AC.wait(DV, tok_ylast)
                    AC.wait(DV, tok_gbuf_free)
                    AC.op(lambda s, ui=ui: s.activation(
                        gbuf[:, :], ybuf[ui][:, :], AF.Square, accum_out=ssF[:, :]))
                    AC.selfwait()
                    AC.op(lambda s: s.activation(lnF[:, :], ssF[:, :], AF.Ln,
                                                 scale=1.0 / D, bias=epsFFN[:, :]))
                    AC.selfwait()
                    AC.op(lambda s: s.activation(rsF[:, :], lnF[:, :], AF.Exp, scale=-0.5))
                    AC.selfwait()
                    tok_g_ac = AC.op(lambda s, ui=ui: s.activation(
                        gbuf[:, :], ybuf[ui][:, :], AF.Silu, scale=rsF[:, :]))
                    DV.wait(AC, tok_g_ac)
                    DV.wait(SP, sxtok[u + 1])
                    DV.wait(SP, tok_obuf_store)
                    tok_gbuf_free = DV.op(lambda v: v.tensor_tensor(
                        obuf[:, :], gbuf[:, :], sbuf2[u % 2][:, :], op=ALU.mult))
                    DV.selfwait()
                    tok_out_dv = DV.op(lambda v: v.tensor_tensor(
                        obuf[:, :], obuf[:, :], xbuf[u % 2][:, :], op=ALU.add))
                    sx_guard[u] = tok_out_dv
                    SP.wait(DV, tok_out_dv)
                    tok_obuf_store = SP.op(lambda s, ul=ul: s.dma_start(
                        out=out_c[128 * ul:128 * ul + 128, :], in_=obuf[:, :]), inc=16)
            emit_load(len(load_fns) - 1)
            GP.selfwait()
            GP.wait(SP)
            SP.selfwait()
    return nc



def shard_inputs(cfg, inputs, n_cores=8):
    c = _cfg(cfg)
    S, D, HC, BLK, SH = c["S"], c["D"], c["HC"], c["BLK"], c["SH"]
    x = np.ascontiguousarray(inputs["L0_attn_x"])
    wq, wk, wv, wo = (inputs["L0_attn_wq"], inputs["L0_attn_wk"],
                      inputs["L0_attn_wv"], inputs["L0_attn_wo"])
    maps = []
    for core in range(n_cores):
        b = core // 2
        side = core % 2
        q = np.arange(S) // BLK
        k = np.arange(S) % BLK
        src = q * BLK + (k + side * SH) % BLK
        cp = np.arange(D)
        colperm = (cp // BLK) * BLK + (cp % BLK + side * SH) % BLK
        h0 = HC * side
        m = {
            "x_in": x[b][src],
            "wq_in": wq[128 * h0:128 * (h0 + HC)],
            "wk_in": wk[128 * h0:128 * (h0 + HC)],
            "wv_in": wv[128 * h0:128 * (h0 + HC)],
            "wo_in": wo[:, colperm],
            "lq1": np.asarray(inputs["L0_attn_lambda_q1"]).reshape(1, 128),
            "lq2": np.asarray(inputs["L0_attn_lambda_q2"]).reshape(1, 128),
            "lk1": np.asarray(inputs["L0_attn_lambda_k1"]).reshape(1, 128),
            "lk2": np.asarray(inputs["L0_attn_lambda_k2"]).reshape(1, 128),
            "gup": inputs["L0_ffn_gate_up_packed"],
            "gus": inputs["L0_ffn_gate_up_scales"],
            "dnp": inputs["L0_ffn_down_packed"],
            "dns": inputs["L0_ffn_down_scales"],
        }
        maps.append({k2: np.ascontiguousarray(v) for k2, v in m.items()})
    return maps


def unshard_output(cfg, outs, n_cores=8):
    c = _cfg(cfg)
    S, D, U, BLK, SH = c["S"], c["D"], c["U"], c["BLK"], c["SH"]
    B = n_cores // 2
    out = np.empty((B, S, D), dtype=np.float32)
    m = np.arange(128)
    for core in range(n_cores):
        b = core // 2
        side = core % 2
        oc = outs[core]
        for ul in range(U):
            rows = BLK * m + ul + side * SH
            out[b, rows, :] = oc[128 * ul:128 * ul + 128, :]
    return out


def kernel(**inputs):
    cfg = REAL
    in_maps = shard_inputs(cfg, inputs)
    nc = build_nc(cfg)
    res = run_bass_kernel_spmd(nc, in_maps, core_ids=list(range(8)))
    return unshard_output(cfg, [r["out_c"] for r in res.results])


# version 06
# speedup vs baseline: 1.2823x; 1.2823x over previous
import numpy as np
import ml_dtypes
from contextlib import ExitStack

import concourse.bass as bass
import concourse.mybir as mybir
from concourse.bass_utils import run_bass_kernel_spmd

F32 = mybir.dt.float32
BF16 = mybir.dt.bfloat16
F16 = mybir.dt.float16
I32 = mybir.dt.int32
AF = mybir.ActivationFunctionType
ALU = mybir.AluOpType

LN_EPS = 1e-5
RMS_EPS = 1.1920928955078125e-07
FFN_EPS = 1e-06
LAMBDA_INIT = 0.8
SM_SCALE = float(1.0 / np.sqrt(128.0))

REAL = dict(S=4096, D=2048, HG=16, HC=8, I=5632)


def _cfg(cfg):
    S, D, HG, HC, I = cfg["S"], cfg["D"], cfg["HG"], cfg["HC"], cfg["I"]
    assert D == HG * 128 and S == 2 * D and I % 512 == 0
    c = dict(cfg)
    c.update(
        BLK=S // 128, SH=2 * HC, DCH=D // 128, KCH=S // 128, SCH=S // 128,
        QT=512, NQT=S // 512, U=2 * HC, JT=min(512, D), NJ=D // min(512, D),
        OG=I // 512, ICH=I // 128, NGU=D // 128, NGD=I // 128, CC=(S // 2) // 128,
    )
    return c


class Stream:

    CHUNK = 400

    def __init__(self, block, engname, sem):
        self.block = block
        self.engname = engname
        self.sem = sem
        self.n = 0
        self.pend = []

    def raw(self, f):
        self.pend.append(f)

    def flush(self):
        while self.pend:
            chunk = self.pend[:self.CHUNK]
            self.pend = self.pend[self.CHUNK:]
            getattr(self.block, self.engname)(
                lambda e, chunk=chunk: [g(e) for g in chunk])

    def op(self, f, inc=1):
        def g(e):
            f(e).then_inc(self.sem, inc)
        self.raw(g)
        self.n += inc
        return self.n

    def opn(self, f, n):
        def g(e):
            inss = f(e)
            assert len(inss) == n, (self.engname, len(inss), n)
            for ins in inss:
                ins.then_inc(self.sem, 1)
        self.raw(g)
        self.n += n
        return self.n

    def dman(self, f, n):
        def g(e):
            inss = f(e)
            assert len(inss) == n, (self.engname, len(inss), n)
            for ins in inss:
                ins.then_inc(self.sem, 16)
        self.raw(g)
        self.n += 16 * n
        return self.n

    def mmgroup(self, f):
        def g(e):
            f(e)[-1].then_inc(self.sem, 1)
        self.raw(g)
        self.n += 1
        return self.n

    def wait(self, other, tok=None):
        t = other.n if tok is None else tok
        if t > 0:
            self.raw(lambda e: e.wait_ge(other.sem, t))

    def selfwait(self):
        if self.n > 0:
            self.raw(lambda e, n=self.n: e.wait_ge(self.sem, n))


def build_nc(cfg, debug=False):
    c = _cfg(cfg)
    S, D, HC, I = c["S"], c["D"], c["HC"], c["I"]
    DCH, KCH, SCH, QT, NQT = c["DCH"], c["KCH"], c["SCH"], c["QT"], c["NQT"]
    U, NJ, JT, OG, ICH = c["U"], c["NJ"], c["JT"], c["OG"], c["ICH"]
    NGU, NGD, CC, BLK = c["NGU"], c["NGD"], c["CC"], c["BLK"]
    I2 = 2 * I
    WI_GU = D // 8
    WI_DN = I // 8
    NOT_GU = I2 // 128
    NOT_DN = D // 128
    NT = QT // 128

    nc = bass.Bass("TRN2", target_bir_lowering=False)
    dk = "ExternalOutput" if debug else "Internal"

    x_in = nc.dram_tensor("x_in", [S, D], F32, kind="ExternalInput").ap()
    wq_in = nc.dram_tensor("wq_in", [HC * 128, D], F32, kind="ExternalInput").ap()
    wk_in = nc.dram_tensor("wk_in", [HC * 128, D], F32, kind="ExternalInput").ap()
    wv_in = nc.dram_tensor("wv_in", [HC * 128, D], F32, kind="ExternalInput").ap()
    wo_in = nc.dram_tensor("wo_in", [D, D], F32, kind="ExternalInput").ap()
    lq1 = nc.dram_tensor("lq1", [1, 128], F32, kind="ExternalInput").ap()
    lq2 = nc.dram_tensor("lq2", [1, 128], F32, kind="ExternalInput").ap()
    lk1 = nc.dram_tensor("lk1", [1, 128], F32, kind="ExternalInput").ap()
    lk2 = nc.dram_tensor("lk2", [1, 128], F32, kind="ExternalInput").ap()
    gup = nc.dram_tensor("gup", [I2, WI_GU], I32, kind="ExternalInput").ap()
    gus = nc.dram_tensor("gus", [NGU, I2], F16, kind="ExternalInput").ap()
    dnp = nc.dram_tensor("dnp", [D, WI_DN], I32, kind="ExternalInput").ap()
    dns = nc.dram_tensor("dns", [NGD, D], F16, kind="ExternalInput").ap()
    out_c = nc.dram_tensor("out_c", [U * 128, D], F32, kind="ExternalOutput").ap()

    wq_bf = nc.dram_tensor("wq_bf", [HC * 128, D], F16, kind="Internal").ap()
    wk_bf = nc.dram_tensor("wk_bf", [HC * 128, D], F16, kind="Internal").ap()
    wv_bf = nc.dram_tensor("wv_bf", [HC * 128, D], F16, kind="Internal").ap()
    wo_bf = nc.dram_tensor("wo_bf", [D, D], F16, kind="Internal").ap()
    xn_nat = nc.dram_tensor("xn_nat", [S, D], F16, kind=dk).ap()
    xnT_nat = nc.dram_tensor("xnT_nat", [D, S], F16, kind=dk).ap()
    s_stash = nc.dram_tensor("s_stash", [U * 128, D], F32, kind=dk).ap()
    wguT = nc.dram_tensor("wguT", [DCH, 128, I2], F16, kind=dk).ap()
    wdnT = nc.dram_tensor("wdnT", [ICH, 128, D], F16, kind=dk).ap()
    y_dbg = nc.dram_tensor("y_dbg", [U * 128, D], F32, kind=dk).ap() if debug else None

    with ExitStack() as top:
        E = top.enter_context
        ident = E(nc.sbuf_tensor("ident", [128, 128], BF16))
        identh = E(nc.sbuf_tensor("identh", [128, 128], F16))
        identf = E(nc.sbuf_tensor("identf", [128, 128], F32))
        identi = E(nc.sbuf_tensor("identi", [128, 128], I32))
        onesb = E(nc.sbuf_tensor("onesb", [128, 1], F16))
        ones1 = E(nc.sbuf_tensor("ones1", [1, 128], F32))
        epsLN = E(nc.sbuf_tensor("epsLN", [128, 1], F32))
        epsRMS = E(nc.sbuf_tensor("epsRMS", [128, 1], F32))
        epsFFN = E(nc.sbuf_tensor("epsFFN", [128, 1], F32))
        lamt = E(nc.sbuf_tensor("lamt", [1, 1], F32))
        ltmp = E(nc.sbuf_tensor("ltmp", [1, 768], F32))
        lred = E(nc.sbuf_tensor("lred", [1, 4], F32))

        PP = E(nc.psum_tensor("PP", [128, 4096], F32))
        PPb = PP.bitcast(BF16)
        PPh = PP.bitcast(F16)

        sems = {k: E(nc.semaphore(k)) for k in ["pe", "ac", "dv", "gp", "sp"]}
        block = E(nc.Block())

        PE = Stream(block, "tensor", sems["pe"])
        AC = Stream(block, "scalar", sems["ac"])
        DV = Stream(block, "vector", sems["dv"])
        GP = Stream(block, "gpsimd", sems["gp"])
        SP = Stream(block, "sync", sems["sp"])

        ALLS = None

        def flush_all():
            for s in (PE, AC, DV, GP, SP):
                s.flush()

        def barrier():
            streams = [PE, AC, DV, GP, SP]
            for s in streams:
                for o in streams:
                    if o is not s:
                        s.wait(o)
            flush_all()

        GP.dman(lambda g: [
            g.dma_start(out=wq_bf, in_=wq_in),
            g.dma_start(out=wk_bf, in_=wk_in),
            g.dma_start(out=wv_bf, in_=wv_in),
            g.dma_start(out=wo_bf, in_=wo_in),
        ], 4)
        tok_wcast = GP.n
        GP.op(lambda g: g.iota(identi[:, :], pattern=[[-1, 128]], base=0, channel_multiplier=1))
        tok_iota = GP.n
        SP.dman(lambda s: [
            s.dma_start(out=ltmp[:, 0:128], in_=lq1),
            s.dma_start(out=ltmp[:, 128:256], in_=lk1),
            s.dma_start(out=ltmp[:, 256:384], in_=lq2),
            s.dma_start(out=ltmp[:, 384:512], in_=lk2),
        ], 4)
        tok_lload = SP.n

        DV.wait(GP, tok_iota)
        DV.opn(lambda v: [
            v.memset(onesb[:, :], 1.0),
            v.memset(ones1[:, :], 1.0),
            v.memset(epsLN[:, :], LN_EPS),
            v.memset(epsRMS[:, :], RMS_EPS),
            v.memset(epsFFN[:, :], FFN_EPS / 4096.0),
            v.tensor_scalar(ident[:, :], identi[:, :], 0, None, op0=ALU.is_equal),
            v.tensor_scalar(identh[:, :], identi[:, :], 0, None, op0=ALU.is_equal),
            v.tensor_scalar(identf[:, :], identi[:, :], 0, None, op0=ALU.is_equal),
        ], 8)
        DV.wait(SP, tok_lload)
        DV.wait(GP, tok_wcast)
        DV.opn(lambda v: [
            v.tensor_tensor(ltmp[:, 512:640], ltmp[:, 0:128], ltmp[:, 128:256], op=ALU.mult),
            v.tensor_tensor(ltmp[:, 640:768], ltmp[:, 256:384], ltmp[:, 384:512], op=ALU.mult),
        ], 2)
        DV.selfwait()
        DV.opn(lambda v: [
            v.tensor_reduce(lred[:, 0:1], ltmp[:, 512:640], op=ALU.add, axis=mybir.AxisListType.X),
            v.tensor_reduce(lred[:, 1:2], ltmp[:, 640:768], op=ALU.add, axis=mybir.AxisListType.X),
        ], 2)
        AC.wait(DV)
        AC.opn(lambda s: [s.activation(lred[:, 2:3], lred[:, 0:1], AF.Exp),
                          s.activation(lred[:, 3:4], lred[:, 1:2], AF.Exp)], 2)
        DV.wait(AC)
        DV.op(lambda v: v.tensor_tensor(lamt[:, :], lred[:, 2:3], lred[:, 3:4], op=ALU.subtract))
        DV.selfwait()
        DV.op(lambda v: v.tensor_scalar(lamt[:, :], lamt[:, :], LAMBDA_INIT, None, op0=ALU.add))

        with ExitStack() as ph:
            Ep = ph.enter_context
            xt = [Ep(nc.sbuf_tensor(f"xt{i}", [128, D], F32)) for i in range(2)]
            xnt = [Ep(nc.sbuf_tensor(f"xnt{i}", [128, D], F16)) for i in range(2)]
            NST = max(D // 512, 1)
            stats = Ep(nc.sbuf_tensor("stats", [128, 6 * NST], F32))
            mv = Ep(nc.sbuf_tensor("mv", [128, 2], F32))
            negm = Ep(nc.sbuf_tensor("negm", [128, 1], F32))
            lnv = Ep(nc.sbuf_tensor("lnv", [128, 1], F32))
            rstd = Ep(nc.sbuf_tensor("rstd", [128, 1], F32))
            txp = [Ep(nc.sbuf_tensor(f"txp{i}", [128, 2048], F16)) for i in range(2)]

            tok_xn_dv = {}
            tok_xn_store = {}
            tok_xload = {}

            def emit_xload(si):
                if si > SCH:
                    return
                if si >= 2:
                    GP.wait(DV, tok_xn_dv[si - 2])
                row = 128 * (si % SCH)
                GP.op(lambda g, row=row, p=si % 2: g.dma_start(
                    out=xt[p][:, :], in_=x_in[row:row + 128, :]), inc=16)
                tok_xload[si] = GP.n

            emit_xload(0)
            for si in range(SCH):
                p = si % 2
                emit_xload(si + 1)
                DV.wait(GP, tok_xload[min(si + 1, SCH)])
                if si >= 2:
                    DV.wait(SP, tok_xn_store[si - 2])
                DV.opn(lambda v, p=p: [
                    v.bn_stats(stats[:, 6 * i:6 * i + 6], xt[p][:, 512 * i:512 * i + min(512, D)])
                    for i in range(NST)], NST)
                DV.selfwait()
                DV.op(lambda v: v.bn_aggr(mv[:, :], stats[:, :]))
                DV.selfwait()
                DV.op(lambda v: v.tensor_scalar(negm[:, :], mv[:, 0:1], -1.0, None, op0=ALU.mult))
                AC.wait(DV)
                AC.op(lambda s: s.activation(lnv[:, :], mv[:, 1:2], AF.Ln, bias=epsLN[:, :]))
                AC.selfwait()
                AC.op(lambda s: s.activation(rstd[:, :], lnv[:, :], AF.Exp, scale=-0.5))
                DV.wait(AC)
                tok_xn_dv[si] = DV.op(lambda v, p=p: v.tensor_scalar(
                    xnt[p][:, :], xt[p][:, :], negm[:, :], rstd[:, :], op0=ALU.add, op1=ALU.mult))
                SP.wait(DV, tok_xn_dv[si])
                tok_xn_store[si] = SP.op(lambda s, si=si, p=p: s.dma_start(
                    out=xn_nat[128 * si:128 * si + 128, :], in_=xnt[p][:, :]), inc=16)

            SP.selfwait()
            HW = min(2048, S)
            NHALF = S // HW
            for dc in range(DCH):
                for hf in range(NHALF):
                    p = (dc * NHALF + hf) % 2
                    SP.op(lambda s, dc=dc, hf=hf, p=p: s.dma_start_transpose(
                        txp[p][:, 0:HW],
                        xn_nat[HW * hf:HW * hf + HW, 128 * dc:128 * dc + 128]), inc=16)
                    SP.selfwait()
                    SP.op(lambda s, dc=dc, hf=hf, p=p: s.dma_start(
                        out=xnT_nat[128 * dc:128 * dc + 128, HW * hf:HW * hf + HW],
                        in_=txp[p][:, 0:HW]), inc=16)
            SP.selfwait()
            barrier()

        with ExitStack() as ph:
            Ep = ph.enter_context
            wqT = [Ep(nc.sbuf_tensor(f"wqT{i}", [128, DCH * 128], F16)) for i in range(2)]
            wkT = [Ep(nc.sbuf_tensor(f"wkT{i}", [128, DCH * 128], F16)) for i in range(2)]
            wvT = [Ep(nc.sbuf_tensor(f"wvT{i}", [128, DCH * 128], F16)) for i in range(2)]
            xnS = [Ep(nc.sbuf_tensor(f"xnS{i}", [128, DCH * 512], F16)) for i in range(2)]
            QTs = Ep(nc.sbuf_tensor("QTs", [128, S], F16))
            KTs = Ep(nc.sbuf_tensor("KTs", [128, S], F16))
            Vs = Ep(nc.sbuf_tensor("Vs", [128, S], F16))
            Et = [Ep(nc.sbuf_tensor(f"Et{i}", [128, 2 * QT], F16)) for i in range(2)]
            a1s = Ep(nc.sbuf_tensor("a1s", [1, QT], F32))
            a2s = Ep(nc.sbuf_tensor("a2s", [1, QT], F32))
            a2b = Ep(nc.sbuf_tensor("a2b", [1, QT], F32))
            bc1 = Ep(nc.sbuf_tensor("bc1", [128, QT], F32))
            bc2 = Ep(nc.sbuf_tensor("bc2", [128, QT], F32))
            t1s = Ep(nc.sbuf_tensor("t1s", [128, QT], F32))
            t2s = Ep(nc.sbuf_tensor("t2s", [128, QT], F32))
            vcb = Ep(nc.sbuf_tensor("vcb", [128, QT], F32))
            sqs = Ep(nc.sbuf_tensor("sqs", [128, 512], F32))
            ssA = Ep(nc.sbuf_tensor("ssA", [128, 4], F32))
            lnA = Ep(nc.sbuf_tensor("lnA", [128, 4], F32))
            rsA = Ep(nc.sbuf_tensor("rsA", [128, 4], F32))
            nrm = Ep(nc.sbuf_tensor("nrm", [128, 2 * SCH * 128], F16))
            woT = [Ep(nc.sbuf_tensor(f"woT{i}", [128, D], F16)) for i in range(3)]
            sev = Ep(nc.sbuf_tensor("sev", [128, D], F32))

            S0, S1c = 0, 1024
            AV1, AV2 = 2048, 2560
            D1C, D2C = 3072, 3584
            QC, KC, VC = 0, 512, 1024
            TRC = 1024

            xn_use = {}
            xn_tok = {}
            xcnt = 0
            tok_wT = {}
            wo_use = {}
            tok_nrm_last = {}
            tok_wo_mm = {}

            for pr in range(HC):
                pp = pr % 2
                if pr >= 2:
                    SP.wait(PE, tok_wT[("consumed", pr - 2)])
                SP.wait(GP, tok_wcast)
                SP.dman(lambda s, pr=pr, pp=pp: [
                    s.dma_start_transpose(
                        w_t[pp][:, 128 * dc:128 * dc + 128],
                        w_b[128 * pr:128 * pr + 128, 128 * dc:128 * dc + 128])
                    for (w_t, w_b) in ((wqT, wq_bf), (wkT, wk_bf), (wvT, wv_bf))
                    for dc in range(DCH)], 3 * DCH)
                SP.op(lambda s: s.dma_start(out=ltmp[:, 0:128], in_=lq1), inc=16)
                tok_wT[pr] = SP.n

                tok_qkv_evac = 0
                for st in range(NQT):
                    sp2 = xcnt % 2
                    for k in (xcnt, xcnt + 1):
                        if k not in xn_tok and k <= HC * NQT:
                            if k >= 2:
                                GP.wait(PE, xn_use.get(k - 2, 0))
                            st_k = k % NQT
                            GP.op(lambda g, st_k=st_k, p2=k % 2: g.dma_start(
                                out=xnS[p2][:, :].rearrange("p (c s) -> p c s", c=DCH),
                                in_=xnT_nat.rearrange("(c p) s -> p c s", p=128)[:, :, QT * st_k:QT * st_k + QT]),
                            inc=16)
                            xn_tok[k] = GP.n
                    PE.wait(GP, xn_tok[min(xcnt + 1, HC * NQT)])
                    PE.wait(SP, tok_wT[pr])
                    if st == 0:
                        PE.wait(DV)
                        PE.wait(AC)
                    else:
                        PE.wait(DV, tok_qkv_evac)
                    PE.mmgroup(lambda t, pp=pp, sp2=sp2: [
                        t.matmul(PP[:, QC:QC + 512], wqT[pp][:, 128 * dc:128 * dc + 128],
                                 xnS[sp2][:, 512 * dc:512 * dc + 512],
                                 start=(dc == 0), stop=(dc == DCH - 1))
                        for dc in range(DCH)])
                    PE.mmgroup(lambda t, pp=pp, sp2=sp2: [
                        t.matmul(PP[:, KC:KC + 512], wkT[pp][:, 128 * dc:128 * dc + 128],
                                 xnS[sp2][:, 512 * dc:512 * dc + 512],
                                 start=(dc == 0), stop=(dc == DCH - 1))
                        for dc in range(DCH)])
                    PE.mmgroup(lambda t, pp=pp, sp2=sp2: [
                        t.matmul(PP[:, VC + 128 * q:VC + 128 * q + 128],
                                 xnS[sp2][:, 512 * dc + 128 * q:512 * dc + 128 * q + 128],
                                 wvT[pp][:, 128 * dc:128 * dc + 128],
                                 start=(dc == 0), stop=(dc == DCH - 1))
                        for q in range(4) for dc in range(DCH)])
                    xn_use[xcnt] = PE.n
                    xcnt += 1
                    if st == NQT - 1:
                        tok_wT[("consumed", pr)] = PE.n
                    DV.wait(PE)
                    tok_qkv_evac = DV.opn(lambda v, st=st: [
                        v.tensor_copy(QTs[:, QT * st:QT * st + QT], PP[:, QC:QC + 512]),
                        v.tensor_copy(KTs[:, QT * st:QT * st + QT], PP[:, KC:KC + 512]),
                        v.tensor_copy(Vs[:, QT * st:QT * st + QT], PP[:, VC:VC + 512]),
                    ], 3)
                tok_qkv_dv = DV.n

                for qt in range(NQT):
                    tok_scores = {}
                    tok_exp = {}
                    tok_cons = {}
                    for kc in range(KCH):
                        sp2 = kc % 2
                        if kc == 0:
                            PE.wait(DV, tok_qkv_dv)
                            if qt > 0:
                                PE.wait(DV, tok_nrm_last[(pr, qt - 1)])
                        if kc >= 2:
                            PE.wait(AC, tok_exp[kc - 2])
                        PE.mmgroup(lambda t, kc=kc, qt=qt, sp2=sp2: [
                            t.matmul(PP[:, 1024 * sp2:1024 * sp2 + 512],
                                     KTs[0:64, 128 * kc:128 * kc + 128],
                                     QTs[0:64, QT * qt:QT * qt + QT],
                                     start=True, stop=True, tile_position=(0, 0)),
                            t.matmul(PP[:, 1024 * sp2 + 512:1024 * sp2 + 1024],
                                     KTs[64:128, 128 * kc:128 * kc + 128],
                                     QTs[64:128, QT * qt:QT * qt + QT],
                                     start=True, stop=True, tile_position=(64, 0)),
                        ])
                        tok_scores[kc] = PE.n
                        AC.wait(PE, tok_scores[kc])
                        if kc >= 2:
                            AC.wait(PE, tok_cons[kc - 2])
                        tok_exp[kc] = AC.op(lambda s, sp2=sp2: s.activation(
                            Et[sp2][:, :], PP[:, 1024 * sp2:1024 * sp2 + 2 * QT],
                            AF.Exp, scale=SM_SCALE))
                        PE.wait(AC, tok_exp[kc])
                        PE.mmgroup(lambda t, kc=kc, sp2=sp2: [
                            t.matmul(PP[0:1, D1C:D1C + QT], onesb[:, :], Et[sp2][:, 0:QT],
                                     start=(kc == 0), stop=(kc == KCH - 1)),
                            t.matmul(PP[0:1, D2C:D2C + QT], onesb[:, :], Et[sp2][:, QT:2 * QT],
                                     start=(kc == 0), stop=(kc == KCH - 1)),
                            t.matmul(PP[:, AV1:AV1 + QT], Vs[:, 128 * kc:128 * kc + 128],
                                     Et[sp2][:, 0:QT], start=(kc == 0), stop=(kc == KCH - 1)),
                            t.matmul(PP[:, AV2:AV2 + QT], Vs[:, 128 * kc:128 * kc + 128],
                                     Et[sp2][:, QT:2 * QT], start=(kc == 0), stop=(kc == KCH - 1)),
                        ])
                        tok_cons[kc] = PE.n

                    DV.wait(PE, tok_cons[KCH - 1])
                    DV.opn(lambda v: [
                        v.reciprocal(a1s[:, :], PP[0:1, D1C:D1C + QT]),
                        v.reciprocal(a2s[:, :], PP[0:1, D2C:D2C + QT]),
                    ], 2)
                    DV.selfwait()
                    DV.op(lambda v: v.tensor_scalar(a2b[:, :], a2s[:, :], lamt[0:1, 0:1], -1.0,
                                                    op0=ALU.mult, op1=ALU.mult))
                    tok_alpha = DV.n
                    PE.wait(DV, tok_alpha)
                    PE.wait(AC, tok_exp[KCH - 1])
                    PE.mmgroup(lambda t: [
                        t.matmul(PP[:, S0:S0 + QT], ones1[0:1, :], a1s[:, :], start=True, stop=True)])
                    PE.mmgroup(lambda t: [
                        t.matmul(PP[:, S0 + QT:S0 + 2 * QT], ones1[0:1, :], a2b[:, :], start=True, stop=True)])
                    DV.wait(PE)
                    DV.opn(lambda v: [
                        v.tensor_copy(bc1[:, :], PP[:, S0:S0 + QT]),
                        v.tensor_copy(bc2[:, :], PP[:, S0 + QT:S0 + 2 * QT]),
                    ], 2)
                    DV.selfwait()
                    DV.opn(lambda v: [
                        v.tensor_tensor(t1s[:, :], PP[:, AV1:AV1 + QT], bc1[:, :], op=ALU.mult),
                        v.tensor_tensor(t2s[:, :], PP[:, AV2:AV2 + QT], bc2[:, :], op=ALU.mult),
                    ], 2)
                    DV.selfwait()
                    tok_comb = DV.op(lambda v: v.tensor_tensor(vcb[:, :], t1s[:, :], t2s[:, :], op=ALU.add))
                    PE.wait(DV, tok_comb)
                    PE.mmgroup(lambda t: [
                        t.transpose(PP[:, TRC + 128 * ti:TRC + 128 * ti + 128],
                                    vcb[:, 128 * ti:128 * ti + 128], identf[:, :])
                        for ti in range(NT)])
                    AC.wait(PE)
                    AC.opn(lambda s: [
                        s.activation(sqs[:, 128 * ti:128 * ti + 128],
                                     PP[:, TRC + 128 * ti:TRC + 128 * ti + 128],
                                     AF.Square, accum_out=ssA[:, ti:ti + 1])
                        for ti in range(NT)], NT)
                    AC.selfwait()
                    AC.opn(lambda s: [
                        s.activation(lnA[:, ti:ti + 1], ssA[:, ti:ti + 1], AF.Ln,
                                     scale=1.0 / 128.0, bias=epsRMS[:, :])
                        for ti in range(NT)], NT)
                    AC.selfwait()
                    AC.opn(lambda s: [
                        s.activation(rsA[:, ti:ti + 1], lnA[:, ti:ti + 1], AF.Exp, scale=-0.5)
                        for ti in range(NT)], NT)
                    DV.wait(AC)
                    if qt == 0 and pr >= 2:
                        DV.wait(PE, tok_wo_mm[(pr - 2) | 1])
                    po = (pr % 2) * SCH * 128
                    tok_nrm_last[(pr, qt)] = DV.opn(lambda v, qt=qt, po=po: [
                        v.tensor_scalar(nrm[:, po + (qt * NT + ti) * 128:po + (qt * NT + ti) * 128 + 128],
                                        PP[:, TRC + 128 * ti:TRC + 128 * ti + 128],
                                        rsA[:, ti:ti + 1], 1.0 - LAMBDA_INIT,
                                        op0=ALU.mult, op1=ALU.mult)
                        for ti in range(NT)], NT)

                if pr % 2 == 1 or pr == HC - 1:
                    duo = (pr - 1, pr) if pr % 2 == 1 else (pr,)
                    combos = [(pa, sh) for pa in duo for sh in (0, 1)]
                    ngrp = (len(combos) + 1) // 2
                    nwo = ngrp * CC
                    wotok = {}

                    def emit_woload(k, nwo=nwo):
                        if k > nwo or k in wotok:
                            return
                        if k >= 3:
                            SP.wait(PE, wo_use[k - 3])
                        cck = k % CC
                        SP.op(lambda s, cck=cck, wp=k % 3: s.dma_start_transpose(
                            woT[wp][:, :], wo_bf[0:D, 128 * cck:128 * cck + 128]), inc=16)
                        wotok[k] = SP.n

                    woci = 0
                    prev_sstore = 0
                    emit_woload(0)
                    for g0 in range(0, len(combos), 2):
                        sub = combos[g0:g0 + 2]
                        for cc in range(CC):
                            wp = woci % 3
                            emit_woload(woci + 1)
                            PE.wait(SP, wotok[min(woci + 1, nwo)])
                            if cc == 0:
                                PE.wait(DV)
                            def womms(t, sub=sub, cc=cc, wp=wp):
                                ins = []
                                for ci, (pa, sh) in enumerate(sub):
                                    po2 = (pa % 2) * SCH * 128
                                    lhs = nrm[:, po2 + (sh * CC + cc) * 128:po2 + (sh * CC + cc) * 128 + 128]
                                    for j in range(NJ):
                                        ins.append(t.matmul(
                                            PP[:, 2048 * ci + JT * j:2048 * ci + JT * j + JT],
                                            lhs, woT[wp][:, JT * j:JT * j + JT],
                                            start=(cc == 0), stop=(cc == CC - 1)))
                                return ins
                            PE.mmgroup(womms)
                            wo_use[woci] = PE.n
                            woci += 1
                        tok_wo_mm[pr] = PE.n
                        for ci, (pa, sh) in enumerate(sub):
                            ul = 2 * pa + sh
                            DV.wait(PE)
                            DV.wait(GP, prev_sstore)
                            DV.op(lambda v, ci=ci: v.tensor_copy(sev[:, :], PP[:, 2048 * ci:2048 * ci + D]))
                            GP.wait(DV)
                            prev_sstore = GP.op(lambda g, ul=ul: g.dma_start(
                                out=s_stash[128 * ul:128 * ul + 128, :], in_=sev[:, :]), inc=16)
            barrier()

        with ExitStack() as ph:
            Ep = ph.enter_context
            pk = [Ep(nc.sbuf_tensor(f"pk{i}", [128, 256], I32)) for i in range(2)]
            bA = Ep(nc.sbuf_tensor("bA", [128, 2048], I32))
            bB = Ep(nc.sbuf_tensor("bB", [128, 2048], I32))
            bC = Ep(nc.sbuf_tensor("bC", [128, 2048], I32))
            bD = Ep(nc.sbuf_tensor("bD", [128, 2048], F32))
            Wf = Ep(nc.sbuf_tensor("Wf", [128, 2048], F32))
            dense = Ep(nc.sbuf_tensor("dense", [128, 2048], F16))
            scT = Ep(nc.sbuf_tensor("scT", [128, 64], F32))
            stg = Ep(nc.sbuf_tensor("stg", [128, max(DCH, ICH) * 512], F16))
            dnsb = Ep(nc.sbuf_tensor("dnsb", [NGD, D], F16))
            gusb = Ep(nc.sbuf_tensor("gusb", [NGU, I2], F16))

            bAf = bA.bitcast(F32)
            bBf = bB.bitcast(F32)
            bDi = bD.bitcast(I32)
            WfR = Wf.bitcast(I32).rearrange("p (w e) -> p e w", e=8)

            def dequant_strip(nw, src_pk):
                DV.opn(lambda v: [
                    v.tensor_scalar(bA[:, 256 * j:256 * j + nw], src_pk[:, 0:nw],
                                    4 * j, 7, op0=ALU.logical_shift_right, op1=ALU.bitwise_and)
                    for j in range(8)] + [
                    v.tensor_scalar(bC[:, 256 * j:256 * j + nw], src_pk[:, 0:nw],
                                    28 - 4 * j, -2147483648,
                                    op0=ALU.logical_shift_left, op1=ALU.bitwise_and)
                    for j in range(8)], 16)
                tok_rd = DV.n
                DV.selfwait()
                DV.opn(lambda v: [
                    v.tensor_scalar(bB[:, 256 * j:256 * j + nw], bA[:, 256 * j:256 * j + nw],
                                    252.0, float(1 << 22), op0=ALU.add, op1=ALU.mult)
                    for j in range(8)], 8)
                DV.selfwait()
                DV.opn(lambda v: [
                    v.tensor_scalar(bAf[:, 256 * j:256 * j + nw], bBf[:, 256 * j:256 * j + nw],
                                    2.0, 1.0, op0=ALU.mult, op1=ALU.subtract)
                    for j in range(8)], 8)
                DV.selfwait()
                DV.opn(lambda v: [
                    v.tensor_tensor(bD[:, 256 * j:256 * j + nw], bAf[:, 256 * j:256 * j + nw],
                                    bBf[:, 256 * j:256 * j + nw], op=ALU.min)
                    for j in range(8)], 8)
                DV.selfwait()
                DV.opn(lambda v: [
                    v.tensor_tensor(WfR[:, j, 0:nw], bDi[:, 256 * j:256 * j + nw],
                                    bC[:, 256 * j:256 * j + nw], op=ALU.bitwise_or)
                    for j in range(8)], 8)
                DV.selfwait()
                return tok_rd

            tok_pk_rd = {}
            tok_pe_tr = {}
            tok_evac = {}
            tok_stg_store = 0
            scnt = 0

            pk_sched = [("g", ot, 0, WI_GU) for ot in range(NOT_GU)]
            for ot_ in range(NOT_DN):
                for sti_ in range((WI_DN + 255) // 256):
                    w0_ = 256 * sti_
                    pk_sched.append(("d", ot_, w0_, min(256, WI_DN - w0_)))
            pk_sched.append(("g", 0, 0, WI_GU))
            pktok = {}
            pkemit = [0]

            def emit_pk(upto):
                while pkemit[0] <= min(upto, len(pk_sched) - 1):
                    k = pkemit[0]
                    if k >= 2:
                        GP.wait(DV, tok_pk_rd.get(k - 2, 0))
                    kind, ot_, w0_, nw_ = pk_sched[k]
                    if kind == "g":
                        GP.op(lambda g, ot_=ot_, p2=k % 2: g.dma_start(
                            out=pk[p2][:, 0:WI_GU], in_=gup[128 * ot_:128 * ot_ + 128, :]), inc=16)
                    else:
                        GP.op(lambda g, ot_=ot_, w0_=w0_, nw_=nw_, p2=k % 2: g.dma_start(
                            out=pk[p2][:, 0:nw_], in_=dnp[128 * ot_:128 * ot_ + 128, w0_:w0_ + nw_]), inc=16)
                    pktok[k] = GP.n
                    pkemit[0] += 1

            def pk_wait(k):
                emit_pk(k + 1)
                DV.wait(GP, pktok[min(k + 1, len(pk_sched) - 1)])

            SP.op(lambda s: s.dma_start(out=gusb[:, :], in_=gus), inc=16)
            SP.op(lambda s: s.dma_start(out=dnsb[:, :], in_=dns), inc=16)
            tok_gus = SP.n
            tok_scg_used = {}
            for ot in range(NOT_GU):
                p2 = scnt % 2
                PE.wait(SP, tok_gus)
                if ot > 0:
                    PE.wait(DV, tok_scg_used[ot - 1])
                PE.mmgroup(lambda t, ot=ot: [
                    t.transpose(PPh[:, 2 * 3584:2 * 3584 + NGU],
                                gusb[0:NGU, 128 * ot:128 * ot + 128], identh[0:NGU, 0:NGU])])
                DV.wait(PE)
                DV.op(lambda v: v.tensor_copy(scT[:, 0:NGU], PPh[:, 2 * 3584:2 * 3584 + NGU]))
                DV.selfwait()
                pk_wait(scnt)
                tok_pk_rd[scnt] = dequant_strip(WI_GU, pk[p2])
                scnt += 1
                if ot >= 1:
                    DV.wait(PE, tok_pe_tr.get(ot - 1, 0))
                if ot % 4 == 0 and ot > 0:
                    DV.wait(GP, tok_stg_store)
                tok_scg_used[ot] = DV.opn(lambda v: [
                    v.tensor_scalar(dense[:, 128 * g2:128 * g2 + 128], Wf[:, 128 * g2:128 * g2 + 128],
                                    scT[:, g2:g2 + 1], None, op0=ALU.mult)
                    for g2 in range(NGU)], NGU)
                DV.selfwait()
                PE.wait(DV)
                if ot >= 2:
                    PE.wait(DV, tok_evac[ot - 2])
                pb = (ot % 2) * 1024
                PE.mmgroup(lambda t, pb=pb: [
                    t.transpose(PPh[:, 2 * pb + 128 * ci2:2 * pb + 128 * ci2 + 128],
                                dense[:, 128 * ci2:128 * ci2 + 128], identh[:, :])
                    for ci2 in range(DCH)])
                tok_pe_tr[ot] = PE.n
                DV.wait(PE)
                ob = (ot % 4) * 128
                tok_evac[ot] = DV.opn(lambda v, pb=pb, ob=ob: [
                    v.tensor_copy(stg[:, 512 * ci2 + ob:512 * ci2 + ob + 128],
                                  PPh[:, 2 * pb + 128 * ci2:2 * pb + 128 * ci2 + 128])
                    for ci2 in range(DCH)], DCH)
                if ot % 4 == 3:
                    GP.wait(DV, tok_evac[ot])
                    blk0 = 128 * (ot - 3)
                    tok_stg_store = GP.dman(lambda g, blk0=blk0: [
                        g.dma_start(out=wguT[ci2, :, blk0:blk0 + 512],
                                    in_=stg[:, 512 * ci2:512 * ci2 + 512])
                        for ci2 in range(DCH)], DCH)

            tok_scT_used = {}
            for ot in range(NOT_DN):
                nstrip = (WI_DN + 255) // 256
                if ot > 0:
                    PE.wait(DV, tok_scT_used[ot - 1])
                PE.mmgroup(lambda t, ot=ot: [
                    t.transpose(PPh[:, 2 * 3584:2 * 3584 + NGD],
                                dnsb[0:NGD, 128 * ot:128 * ot + 128], identh[0:NGD, 0:NGD])])
                DV.wait(PE)
                DV.op(lambda v: v.tensor_copy(scT[:, 0:NGD], PPh[:, 2 * 3584:2 * 3584 + NGD]))
                DV.selfwait()
                for sti in range(nstrip):
                    w0 = 256 * sti
                    nw = min(256, WI_DN - w0)
                    p2 = scnt % 2
                    pk_wait(scnt)
                    tok_pk_rd[scnt] = dequant_strip(nw, pk[p2])
                    scnt += 1
                    g0 = (8 * w0) // 128
                    ng = (8 * nw) // 128
                    key = ("dn", ot, sti)
                    prevkey = ("dn", ot, sti - 1) if sti > 0 else ("dn", ot - 1, nstrip - 1)
                    if prevkey in tok_pe_tr:
                        DV.wait(PE, tok_pe_tr[prevkey])
                    if (ot * nstrip + sti) % 4 == 0 and ot + sti > 0:
                        DV.wait(GP, tok_stg_store)
                    DV.opn(lambda v, g0=g0, ng=ng: [
                        v.tensor_scalar(dense[:, 128 * g2:128 * g2 + 128],
                                        Wf[:, 128 * g2:128 * g2 + 128],
                                        scT[:, g0 + g2:g0 + g2 + 1], None, op0=ALU.mult)
                        for g2 in range(ng)], ng)
                    DV.selfwait()
                    tok_scT_used[ot] = DV.n
                    PE.wait(DV)
                    prevkey2 = ("dn", ot, sti - 2) if sti >= 2 else None
                    if prevkey2 in tok_evac:
                        PE.wait(DV, tok_evac[prevkey2])
                    pb = (scnt % 2) * 1024
                    PE.mmgroup(lambda t, pb=pb, ng=ng: [
                        t.transpose(PPh[:, 2 * pb + 128 * ci2:2 * pb + 128 * ci2 + 128],
                                    dense[:, 128 * ci2:128 * ci2 + 128], identh[:, :])
                        for ci2 in range(ng)])
                    tok_pe_tr[key] = PE.n
                    DV.wait(PE)
                    ob = (ot % 4) * 128
                    tok_evac[key] = DV.opn(lambda v, pb=pb, ob=ob, g0=g0, ng=ng: [
                        v.tensor_copy(stg[:, 512 * (g0 + ci2) + ob:512 * (g0 + ci2) + ob + 128],
                                      PPh[:, 2 * pb + 128 * ci2:2 * pb + 128 * ci2 + 128])
                        for ci2 in range(ng)], ng)
                if ot % 4 == 3 or ot == NOT_DN - 1:
                    nblk = (ot % 4) + 1
                    blk0 = 128 * (ot - nblk + 1)
                    GP.wait(DV, tok_evac[("dn", ot, nstrip - 1)])
                    tok_stg_store = GP.dman(lambda g, blk0=blk0, nblk=nblk: [
                        g.dma_start(out=wdnT[ci2, :, blk0:blk0 + 128 * nblk],
                                    in_=stg[:, 512 * ci2:512 * ci2 + 128 * nblk])
                        for ci2 in range(ICH)], ICH)
            emit_pk(len(pk_sched) - 1)
            GP.selfwait()
            barrier()

        with ExitStack() as ph:
            Ep = ph.enter_context
            xnU = Ep(nc.sbuf_tensor("xnU", [128, 4 * DCH * 128], F16))
            wsl = [Ep(nc.sbuf_tensor(f"wsl{i}", [128, 512], F16)) for i in range(4)]
            gsil = Ep(nc.sbuf_tensor("gsil", [128, 4 * 512], F16))
            hsb = Ep(nc.sbuf_tensor("hsb", [128, 4 * 512], F16))
            hT = Ep(nc.sbuf_tensor("hT", [128, 4 * ICH * 128], F16))
            ssF = Ep(nc.sbuf_tensor("ssF", [128, 1], F32))
            lnF = Ep(nc.sbuf_tensor("lnF", [128, 1], F32))
            rsF = Ep(nc.sbuf_tensor("rsF", [128, 1], F32))
            ybuf = [Ep(nc.sbuf_tensor(f"ybuf{i}", [128, D], F32)) for i in range(4)]
            gbuf = Ep(nc.sbuf_tensor("gbuf", [128, D], F32))
            sbuf2 = [Ep(nc.sbuf_tensor(f"sbuf2{i}", [128, D], F32)) for i in range(2)]
            xbuf = [Ep(nc.sbuf_tensor(f"xbuf{i}", [128, D], F32)) for i in range(2)]
            obuf = Ep(nc.sbuf_tensor("obuf", [128, D], F32))

            NUG = (U + 3) // 4
            load_fns = []
            loadtok = {}
            wtoks = {}
            emitted = [0]
            LA = 3

            def emit_load(idx):
                while emitted[0] <= min(idx, len(load_fns) - 1):
                    k = emitted[0]
                    if k >= 4:
                        GP.wait(PE, wtoks[k - 4])
                    load_fns[k]()
                    loadtok[k] = GP.n
                    emitted[0] += 1

            def mk_gu_load(dc, rb, wp):
                def f():
                    GP.op(lambda g: g.dma_start(out=wsl[wp][:, :], in_=wguT[dc, :, rb:rb + 512]), inc=16)
                return f

            def mk_dn_load(ic, j, wp):
                def f():
                    GP.op(lambda g: g.dma_start(out=wsl[wp][:, 0:JT], in_=wdnT[ic, :, JT * j:JT * j + JT]), inc=16)
                return f

            sched = []
            li = 0
            for ug in range(NUG):
                for og in range(OG):
                    for ph2 in (0, 1):
                        rb = 512 * og + (I if ph2 else 0)
                        for dc in range(DCH):
                            load_fns.append(mk_gu_load(dc, rb, li % 4)); li += 1
                for j in range(NJ):
                    for ic in range(ICH):
                        load_fns.append(mk_dn_load(ic, j, li % 4)); li += 1
            load_fns.append(mk_dn_load(0, 0, li % 4))
            NLOAD = li

            def pe_wait_load(k):
                emit_load(k + LA)
                PE.wait(GP, loadtok[min(k + 1, len(load_fns) - 1)])

            tok_xnU_used = 0
            tok_g_ac = 0
            tok_gbuf_free = 0
            tok_out_dv = 0
            tok_obuf_store = 0
            sxtok = {}
            sx_guard = {}
            tok_ydump = 0
            gi = 0

            for ug in range(NUG):
                us = list(range(4 * ug, min(4 * ug + 4, U)))
                nu = len(us)
                if ug > 0:
                    SP.wait(PE, tok_xnU_used)
                SP.dman(lambda s, us=us: [
                    s.dma_start_transpose(
                        xnU[:, (ui * DCH + dc) * 128:(ui * DCH + dc) * 128 + 128],
                        xn_nat.rearrange("(m blk) d -> blk m d", blk=BLK)[u, :, 128 * dc:128 * dc + 128])
                    for ui, u in enumerate(us) for dc in range(DCH)], nu * DCH)
                tok_xnU = SP.n

                tok_h_dv = 0
                tok_hTdv = 0
                tok_hT_pe = 0
                tok_gsil_ac = 0
                tok_yevac = {}
                for og in range(OG):
                    for ph2 in (0, 1):
                        for dc in range(DCH):
                            pe_wait_load(gi)
                            if dc == 0:
                                PE.wait(SP, tok_xnU)
                                if ph2 == 0 and (og > 0 or ug > 0):
                                    PE.wait(DV, tok_hTdv if og > 0 else tok_ylast)
                                if ph2 == 1:
                                    PE.wait(AC, tok_gsil_ac)
                            pbase = 2048 * ph2
                            wp = gi % 4
                            PE.mmgroup(lambda t, us=us, dc=dc, wp=wp, pbase=pbase: [
                                t.matmul(PP[:, pbase + 512 * ui:pbase + 512 * ui + 512],
                                         xnU[:, (ui * DCH + dc) * 128:(ui * DCH + dc) * 128 + 128],
                                         wsl[wp][:, :], start=(dc == 0), stop=(dc == DCH - 1))
                                for ui in range(len(us))])
                            wtoks[gi] = PE.n
                            gi += 1
                        if ph2 == 0:
                            AC.wait(PE)
                            AC.wait(DV, tok_h_dv)
                            tok_gsil_ac = AC.opn(lambda s, nu=nu: [
                                s.activation(gsil[:, 512 * ui:512 * ui + 512],
                                             PP[:, 512 * ui:512 * ui + 512], AF.Silu)
                                for ui in range(nu)], nu)
                        else:
                            DV.wait(PE)
                            DV.wait(AC, tok_gsil_ac)
                            DV.wait(PE, tok_hT_pe)
                            tok_h_dv = DV.opn(lambda v, nu=nu: [
                                v.scalar_tensor_tensor(
                                    hsb[:, 512 * ui:512 * ui + 512],
                                    PP[:, 2048 + 512 * ui:2048 + 512 * ui + 512], 1.0 / 64.0,
                                    gsil[:, 512 * ui:512 * ui + 512],
                                    op0=ALU.mult, op1=ALU.mult)
                                for ui in range(nu)], nu)
                    PE.wait(DV, tok_h_dv)
                    PE.mmgroup(lambda t, nu=nu: [
                        t.transpose(PPh[:, 128 * (4 * ui + ti):128 * (4 * ui + ti) + 128],
                                    hsb[:, 512 * ui + 128 * ti:512 * ui + 128 * ti + 128], identh[:, :])
                        for ui in range(nu) for ti in range(4)])
                    tok_hT_pe = PE.n
                    DV.wait(PE, tok_hT_pe)
                    tok_hTdv = DV.opn(lambda v, og=og, nu=nu: [
                        v.tensor_copy(hT[:, (ui * ICH + og * 4 + ti) * 128:(ui * ICH + og * 4 + ti) * 128 + 128],
                                      PPh[:, 128 * (4 * ui + ti):128 * (4 * ui + ti) + 128])
                        for ui in range(nu) for ti in range(4)], nu * 4)
                tok_xnU_used = PE.n

                for j in range(NJ):
                    q = j % 2
                    for ic in range(ICH):
                        pe_wait_load(gi)
                        if ic == 0:
                            if j == 0:
                                PE.wait(DV, tok_hTdv)
                            if j >= 2:
                                PE.wait(DV, tok_yevac[j - 2])
                            if ug > 0 and j < 2:
                                PE.wait(AC, tok_g_ac)
                        wp = gi % 4
                        PE.mmgroup(lambda t, us=us, j=j, ic=ic, wp=wp, q=q, nu=nu: [
                            t.matmul(PP[:, 2048 * q + 512 * ui:2048 * q + 512 * ui + JT],
                                     hT[:, (ui * ICH + ic) * 128:(ui * ICH + ic) * 128 + 128],
                                     wsl[wp][:, 0:JT], start=(ic == 0), stop=(ic == ICH - 1))
                            for ui in range(nu)])
                        wtoks[gi] = PE.n
                        gi += 1
                    DV.wait(PE)
                    if ug > 0 and j == 0:
                        DV.wait(AC, tok_g_ac)
                        DV.wait(SP, tok_ydump)
                    tok_yevac[j] = DV.opn(lambda v, j=j, q=q, nu=nu: [
                        v.tensor_copy(ybuf[ui][:, JT * j:JT * j + JT],
                                      PP[:, 2048 * q + 512 * ui:2048 * q + 512 * ui + JT])
                        for ui in range(nu)], nu)
                tok_ylast = tok_yevac[NJ - 1]

                if debug:
                    SP.wait(DV, tok_ylast)
                    SP.dman(lambda s, us=us: [
                        s.dma_start(out=y_dbg[128 * u:128 * u + 128, :], in_=ybuf[ui][:, :])
                        for ui, u in enumerate(us)], nu)
                    tok_ydump = SP.n

                for ui, u in enumerate(us):
                    ul = u
                    for uk in (u, u + 1):
                        if uk not in sxtok and uk <= U:
                            SP.wait(DV, sx_guard.get(uk - 2, 0))
                            row = 128 * (uk % U)
                            SP.op(lambda s, row=row, p=uk % 2: s.dma_start(
                                out=sbuf2[p][:, :], in_=s_stash[row:row + 128, :]), inc=16)
                            SP.op(lambda s, p=uk % 2, blk=uk % U: s.dma_start(
                                out=xbuf[p][:, :],
                                in_=x_in.rearrange("(m blk) d -> blk m d", blk=BLK)[blk, :, :]), inc=16)
                            sxtok[uk] = SP.n
                    AC.wait(DV, tok_ylast)
                    AC.wait(DV, tok_gbuf_free)
                    AC.op(lambda s, ui=ui: s.activation(
                        gbuf[:, :], ybuf[ui][:, :], AF.Square, accum_out=ssF[:, :]))
                    AC.selfwait()
                    AC.op(lambda s: s.activation(lnF[:, :], ssF[:, :], AF.Ln,
                                                 scale=1.0 / D, bias=epsFFN[:, :]))
                    AC.selfwait()
                    AC.op(lambda s: s.activation(rsF[:, :], lnF[:, :], AF.Exp, scale=-0.5))
                    AC.selfwait()
                    tok_g_ac = AC.op(lambda s, ui=ui: s.activation(
                        gbuf[:, :], ybuf[ui][:, :], AF.Silu, scale=rsF[:, :]))
                    DV.wait(AC, tok_g_ac)
                    DV.wait(SP, sxtok[u + 1])
                    DV.wait(SP, tok_obuf_store)
                    tok_gbuf_free = DV.op(lambda v, p=u % 2: v.tensor_tensor(
                        obuf[:, :], gbuf[:, :], sbuf2[p][:, :], op=ALU.mult))
                    DV.selfwait()
                    tok_out_dv = DV.op(lambda v, p=u % 2: v.tensor_tensor(
                        obuf[:, :], obuf[:, :], xbuf[p][:, :], op=ALU.add))
                    sx_guard[u] = tok_out_dv
                    SP.wait(DV, tok_out_dv)
                    tok_obuf_store = SP.op(lambda s, ul=ul: s.dma_start(
                        out=out_c[128 * ul:128 * ul + 128, :], in_=obuf[:, :]), inc=16)
            emit_load(len(load_fns) - 1)
            GP.selfwait()
            GP.wait(SP)
            SP.selfwait()
            flush_all()
    return nc



def shard_inputs(cfg, inputs, n_cores=8):
    c = _cfg(cfg)
    S, D, HC, BLK, SH = c["S"], c["D"], c["HC"], c["BLK"], c["SH"]
    x = np.ascontiguousarray(inputs["L0_attn_x"])
    wq, wk, wv, wo = (inputs["L0_attn_wq"], inputs["L0_attn_wk"],
                      inputs["L0_attn_wv"], inputs["L0_attn_wo"])
    maps = []
    for core in range(n_cores):
        b = core // 2
        side = core % 2
        q = np.arange(S) // BLK
        k = np.arange(S) % BLK
        src = q * BLK + (k + side * SH) % BLK
        cp = np.arange(D)
        colperm = (cp // BLK) * BLK + (cp % BLK + side * SH) % BLK
        h0 = HC * side
        m = {
            "x_in": x[b][src],
            "wq_in": wq[128 * h0:128 * (h0 + HC)],
            "wk_in": wk[128 * h0:128 * (h0 + HC)],
            "wv_in": wv[128 * h0:128 * (h0 + HC)],
            "wo_in": wo[:, colperm],
            "lq1": np.asarray(inputs["L0_attn_lambda_q1"]).reshape(1, 128),
            "lq2": np.asarray(inputs["L0_attn_lambda_q2"]).reshape(1, 128),
            "lk1": np.asarray(inputs["L0_attn_lambda_k1"]).reshape(1, 128),
            "lk2": np.asarray(inputs["L0_attn_lambda_k2"]).reshape(1, 128),
            "gup": inputs["L0_ffn_gate_up_packed"],
            "gus": inputs["L0_ffn_gate_up_scales"],
            "dnp": inputs["L0_ffn_down_packed"],
            "dns": inputs["L0_ffn_down_scales"],
        }
        maps.append({k2: np.ascontiguousarray(v) for k2, v in m.items()})
    return maps


def unshard_output(cfg, outs, n_cores=8):
    c = _cfg(cfg)
    S, D, U, BLK, SH = c["S"], c["D"], c["U"], c["BLK"], c["SH"]
    B = n_cores // 2
    out = np.empty((B, S, D), dtype=np.float32)
    m = np.arange(128)
    for core in range(n_cores):
        b = core // 2
        side = core % 2
        oc = outs[core]
        for ul in range(U):
            rows = BLK * m + ul + side * SH
            out[b, rows, :] = oc[128 * ul:128 * ul + 128, :]
    return out


def kernel(**inputs):
    cfg = REAL
    in_maps = shard_inputs(cfg, inputs)
    nc = build_nc(cfg)
    res = run_bass_kernel_spmd(nc, in_maps, core_ids=list(range(8)))
    return unshard_output(cfg, [r["out_c"] for r in res.results])


# version 07
# speedup vs baseline: 1.7496x; 1.3644x over previous
import numpy as np
import ml_dtypes
from contextlib import ExitStack

import concourse.bass as bass
import concourse.mybir as mybir
from concourse.bass_utils import run_bass_kernel_spmd

F32 = mybir.dt.float32
BF16 = mybir.dt.bfloat16
F16 = mybir.dt.float16
I32 = mybir.dt.int32
AF = mybir.ActivationFunctionType
ALU = mybir.AluOpType

LN_EPS = 1e-5
RMS_EPS = 1.1920928955078125e-07
FFN_EPS = 1e-06
LAMBDA_INIT = 0.8
SM_SCALE = float(1.0 / np.sqrt(128.0))

REAL = dict(S=4096, D=2048, HG=16, HC=8, I=5632)


def _cfg(cfg):
    S, D, HG, HC, I = cfg["S"], cfg["D"], cfg["HG"], cfg["HC"], cfg["I"]
    assert D == HG * 128 and S == 2 * D and I % 512 == 0
    c = dict(cfg)
    c.update(
        BLK=S // 128, SH=2 * HC, DCH=D // 128, KCH=S // 128, SCH=S // 128,
        QT=512, NQT=S // 512, U=2 * HC, JT=min(512, D), NJ=D // min(512, D),
        OG=I // 512, ICH=I // 128, NGU=D // 128, NGD=I // 128, CC=(S // 2) // 128,
    )
    return c


class Stream:

    CHUNK = 400

    def __init__(self, block, engname, sem):
        self.block = block
        self.engname = engname
        self.sem = sem
        self.n = 0
        self.pend = []

    def raw(self, f):
        self.pend.append(f)

    def flush(self):
        while self.pend:
            chunk = self.pend[:self.CHUNK]
            self.pend = self.pend[self.CHUNK:]
            getattr(self.block, self.engname)(
                lambda e, chunk=chunk: [g(e) for g in chunk])

    def op(self, f, inc=1):
        def g(e):
            f(e).then_inc(self.sem, inc)
        self.raw(g)
        self.n += inc
        return self.n

    def opn(self, f, n):
        def g(e):
            inss = f(e)
            assert len(inss) == n, (self.engname, len(inss), n)
            for ins in inss:
                ins.then_inc(self.sem, 1)
        self.raw(g)
        self.n += n
        return self.n

    def dman(self, f, n):
        def g(e):
            inss = f(e)
            assert len(inss) == n, (self.engname, len(inss), n)
            for ins in inss:
                ins.then_inc(self.sem, 16)
        self.raw(g)
        self.n += 16 * n
        return self.n

    def mmgroup(self, f):
        def g(e):
            f(e)[-1].then_inc(self.sem, 1)
        self.raw(g)
        self.n += 1
        return self.n

    def wait(self, other, tok=None):
        t = other.n if tok is None else tok
        if t > 0:
            self.raw(lambda e: e.wait_ge(other.sem, t))

    def selfwait(self):
        if self.n > 0:
            self.raw(lambda e, n=self.n: e.wait_ge(self.sem, n))


def build_nc(cfg, debug=False):
    c = _cfg(cfg)
    S, D, HC, I = c["S"], c["D"], c["HC"], c["I"]
    DCH, KCH, SCH, QT, NQT = c["DCH"], c["KCH"], c["SCH"], c["QT"], c["NQT"]
    U, NJ, JT, OG, ICH = c["U"], c["NJ"], c["JT"], c["OG"], c["ICH"]
    NGU, NGD, CC, BLK = c["NGU"], c["NGD"], c["CC"], c["BLK"]
    I2 = 2 * I
    WI_GU = D // 8
    WI_DN = I // 8
    NOT_GU = I2 // 128
    NOT_DN = D // 128
    NT = QT // 128

    nc = bass.Bass("TRN2", target_bir_lowering=False)
    dk = "ExternalOutput" if debug else "Internal"

    x_in = nc.dram_tensor("x_in", [S, D], F32, kind="ExternalInput").ap()
    wq_in = nc.dram_tensor("wq_in", [HC * 128, D], F32, kind="ExternalInput").ap()
    wk_in = nc.dram_tensor("wk_in", [HC * 128, D], F32, kind="ExternalInput").ap()
    wv_in = nc.dram_tensor("wv_in", [HC * 128, D], F32, kind="ExternalInput").ap()
    wo_in = nc.dram_tensor("wo_in", [D, D], F32, kind="ExternalInput").ap()
    lq1 = nc.dram_tensor("lq1", [1, 128], F32, kind="ExternalInput").ap()
    lq2 = nc.dram_tensor("lq2", [1, 128], F32, kind="ExternalInput").ap()
    lk1 = nc.dram_tensor("lk1", [1, 128], F32, kind="ExternalInput").ap()
    lk2 = nc.dram_tensor("lk2", [1, 128], F32, kind="ExternalInput").ap()
    gup = nc.dram_tensor("gup", [I2, WI_GU], I32, kind="ExternalInput").ap()
    gus = nc.dram_tensor("gus", [NGU, I2], F16, kind="ExternalInput").ap()
    dnp = nc.dram_tensor("dnp", [D, WI_DN], I32, kind="ExternalInput").ap()
    dns = nc.dram_tensor("dns", [NGD, D], F16, kind="ExternalInput").ap()
    out_c = nc.dram_tensor("out_c", [U * 128, D], F32, kind="ExternalOutput").ap()

    wq_bf = nc.dram_tensor("wq_bf", [HC * 128, D], F16, kind="Internal").ap()
    wk_bf = nc.dram_tensor("wk_bf", [HC * 128, D], F16, kind="Internal").ap()
    wv_bf = nc.dram_tensor("wv_bf", [HC * 128, D], F16, kind="Internal").ap()
    wo_bf = nc.dram_tensor("wo_bf", [D, D], F16, kind="Internal").ap()
    xn_nat = nc.dram_tensor("xn_nat", [S, D], F16, kind=dk).ap()
    xnT_nat = nc.dram_tensor("xnT_nat", [D, S], F16, kind=dk).ap()
    s_stash = nc.dram_tensor("s_stash", [U * 128, D], F32, kind=dk).ap()
    wguT = nc.dram_tensor("wguT", [DCH, 128, I2], F16, kind=dk).ap()
    wdnT = nc.dram_tensor("wdnT", [ICH, 128, D], F16, kind=dk).ap()
    y_dbg = nc.dram_tensor("y_dbg", [U * 128, D], F32, kind=dk).ap() if debug else None

    with ExitStack() as top:
        E = top.enter_context
        ident = E(nc.sbuf_tensor("ident", [128, 128], BF16))
        identh = E(nc.sbuf_tensor("identh", [128, 128], F16))
        identf = E(nc.sbuf_tensor("identf", [128, 128], F32))
        identi = E(nc.sbuf_tensor("identi", [128, 128], I32))
        onesb = E(nc.sbuf_tensor("onesb", [128, 1], F16))
        ones128 = E(nc.sbuf_tensor("ones128", [128, 128], F16))
        nlam128 = E(nc.sbuf_tensor("nlam128", [128, 1], F32))
        ones1 = E(nc.sbuf_tensor("ones1", [1, 128], F32))
        epsLN = E(nc.sbuf_tensor("epsLN", [128, 1], F32))
        epsRMS = E(nc.sbuf_tensor("epsRMS", [128, 1], F32))
        epsFFN = E(nc.sbuf_tensor("epsFFN", [128, 1], F32))
        lamt = E(nc.sbuf_tensor("lamt", [1, 1], F32))
        ltmp = E(nc.sbuf_tensor("ltmp", [1, 768], F32))
        lred = E(nc.sbuf_tensor("lred", [1, 4], F32))

        PP = E(nc.psum_tensor("PP", [128, 4096], F32))
        PPb = PP.bitcast(BF16)
        PPh = PP.bitcast(F16)

        sems = {k: E(nc.semaphore(k)) for k in ["pe", "ac", "dv", "gp", "sp"]}
        block = E(nc.Block())

        PE = Stream(block, "tensor", sems["pe"])
        AC = Stream(block, "scalar", sems["ac"])
        DV = Stream(block, "vector", sems["dv"])
        GP = Stream(block, "gpsimd", sems["gp"])
        SP = Stream(block, "sync", sems["sp"])

        ALLS = None

        def flush_all():
            for s in (PE, AC, DV, GP, SP):
                s.flush()

        def barrier():
            streams = [PE, AC, DV, GP, SP]
            for s in streams:
                for o in streams:
                    if o is not s:
                        s.wait(o)
            flush_all()

        GP.dman(lambda g: [
            g.dma_start(out=wq_bf, in_=wq_in),
            g.dma_start(out=wk_bf, in_=wk_in),
            g.dma_start(out=wv_bf, in_=wv_in),
            g.dma_start(out=wo_bf, in_=wo_in),
        ], 4)
        tok_wcast = GP.n
        GP.op(lambda g: g.iota(identi[:, :], pattern=[[-1, 128]], base=0, channel_multiplier=1))
        tok_iota = GP.n
        SP.dman(lambda s: [
            s.dma_start(out=ltmp[:, 0:128], in_=lq1),
            s.dma_start(out=ltmp[:, 128:256], in_=lk1),
            s.dma_start(out=ltmp[:, 256:384], in_=lq2),
            s.dma_start(out=ltmp[:, 384:512], in_=lk2),
        ], 4)
        tok_lload = SP.n

        DV.wait(GP, tok_iota)
        DV.opn(lambda v: [
            v.memset(onesb[:, :], 1.0),
            v.memset(ones128[:, :], 1.0),
            v.memset(ones1[:, :], 1.0),
            v.memset(epsLN[:, :], LN_EPS),
            v.memset(epsRMS[:, :], RMS_EPS),
            v.memset(epsFFN[:, :], FFN_EPS / 4096.0),
            v.tensor_scalar(ident[:, :], identi[:, :], 0, None, op0=ALU.is_equal),
            v.tensor_scalar(identh[:, :], identi[:, :], 0, None, op0=ALU.is_equal),
            v.tensor_scalar(identf[:, :], identi[:, :], 0, None, op0=ALU.is_equal),
        ], 9)
        DV.wait(SP, tok_lload)
        DV.wait(GP, tok_wcast)
        DV.opn(lambda v: [
            v.tensor_tensor(ltmp[:, 512:640], ltmp[:, 0:128], ltmp[:, 128:256], op=ALU.mult),
            v.tensor_tensor(ltmp[:, 640:768], ltmp[:, 256:384], ltmp[:, 384:512], op=ALU.mult),
        ], 2)
        DV.selfwait()
        DV.opn(lambda v: [
            v.tensor_reduce(lred[:, 0:1], ltmp[:, 512:640], op=ALU.add, axis=mybir.AxisListType.X),
            v.tensor_reduce(lred[:, 1:2], ltmp[:, 640:768], op=ALU.add, axis=mybir.AxisListType.X),
        ], 2)
        AC.wait(DV)
        AC.opn(lambda s: [s.activation(lred[:, 2:3], lred[:, 0:1], AF.Exp),
                          s.activation(lred[:, 3:4], lred[:, 1:2], AF.Exp)], 2)
        DV.wait(AC)
        DV.op(lambda v: v.tensor_tensor(lamt[:, :], lred[:, 2:3], lred[:, 3:4], op=ALU.subtract))
        DV.selfwait()
        DV.op(lambda v: v.tensor_scalar(lamt[:, :], lamt[:, :], LAMBDA_INIT, None, op0=ALU.add))
        PE.wait(DV)
        PE.mmgroup(lambda t: [t.matmul(PP[:, 0:1], ones1[0:1, :], lamt[0:1, 0:1], start=True, stop=True)])
        DV.wait(PE)
        DV.op(lambda v: v.tensor_scalar(nlam128[:, :], PP[:, 0:1], -1.0, None, op0=ALU.mult))
        tok_nlam = DV.n

        with ExitStack() as ph:
            Ep = ph.enter_context
            xt = [Ep(nc.sbuf_tensor(f"xt{i}", [128, D], F32)) for i in range(2)]
            xnt = [Ep(nc.sbuf_tensor(f"xnt{i}", [128, D], F16)) for i in range(2)]
            NST = max(D // 512, 1)
            stats = Ep(nc.sbuf_tensor("stats", [128, 6 * NST], F32))
            mv = Ep(nc.sbuf_tensor("mv", [128, 2], F32))
            negm = Ep(nc.sbuf_tensor("negm", [128, 1], F32))
            lnv = Ep(nc.sbuf_tensor("lnv", [128, 1], F32))
            rstd = Ep(nc.sbuf_tensor("rstd", [128, 1], F32))
            txp = [Ep(nc.sbuf_tensor(f"txp{i}", [128, 2048], F16)) for i in range(2)]

            tok_xn_dv = {}
            tok_xn_store = {}
            tok_xload = {}

            def emit_xload(si):
                if si > SCH:
                    return
                if si >= 2:
                    GP.wait(DV, tok_xn_dv[si - 2])
                row = 128 * (si % SCH)
                GP.op(lambda g, row=row, p=si % 2: g.dma_start(
                    out=xt[p][:, :], in_=x_in[row:row + 128, :]), inc=16)
                tok_xload[si] = GP.n

            emit_xload(0)
            for si in range(SCH):
                p = si % 2
                emit_xload(si + 1)
                DV.wait(GP, tok_xload[min(si + 1, SCH)])
                if si >= 2:
                    DV.wait(SP, tok_xn_store[si - 2])
                DV.opn(lambda v, p=p: [
                    v.bn_stats(stats[:, 6 * i:6 * i + 6], xt[p][:, 512 * i:512 * i + min(512, D)])
                    for i in range(NST)], NST)
                DV.selfwait()
                DV.op(lambda v: v.bn_aggr(mv[:, :], stats[:, :]))
                DV.selfwait()
                DV.op(lambda v: v.tensor_scalar(negm[:, :], mv[:, 0:1], -1.0, None, op0=ALU.mult))
                AC.wait(DV)
                AC.op(lambda s: s.activation(lnv[:, :], mv[:, 1:2], AF.Ln, bias=epsLN[:, :]))
                AC.selfwait()
                AC.op(lambda s: s.activation(rstd[:, :], lnv[:, :], AF.Exp, scale=-0.5))
                DV.wait(AC)
                tok_xn_dv[si] = DV.op(lambda v, p=p: v.tensor_scalar(
                    xnt[p][:, :], xt[p][:, :], negm[:, :], rstd[:, :], op0=ALU.add, op1=ALU.mult))
                SP.wait(DV, tok_xn_dv[si])
                tok_xn_store[si] = SP.op(lambda s, si=si, p=p: s.dma_start(
                    out=xn_nat[128 * si:128 * si + 128, :], in_=xnt[p][:, :]), inc=16)

            SP.selfwait()
            HW = min(2048, S)
            NHALF = S // HW
            for dc in range(DCH):
                for hf in range(NHALF):
                    p = (dc * NHALF + hf) % 2
                    SP.op(lambda s, dc=dc, hf=hf, p=p: s.dma_start_transpose(
                        txp[p][:, 0:HW],
                        xn_nat[HW * hf:HW * hf + HW, 128 * dc:128 * dc + 128]), inc=16)
                    SP.selfwait()
                    SP.op(lambda s, dc=dc, hf=hf, p=p: s.dma_start(
                        out=xnT_nat[128 * dc:128 * dc + 128, HW * hf:HW * hf + HW],
                        in_=txp[p][:, 0:HW]), inc=16)
            SP.selfwait()
            barrier()

        with ExitStack() as ph:
            Ep = ph.enter_context
            wqT = [Ep(nc.sbuf_tensor(f"wqT{i}", [128, DCH * 128], F16)) for i in range(2)]
            wkT = [Ep(nc.sbuf_tensor(f"wkT{i}", [128, DCH * 128], F16)) for i in range(2)]
            wvT = [Ep(nc.sbuf_tensor(f"wvT{i}", [128, DCH * 128], F16)) for i in range(2)]
            xnS = [Ep(nc.sbuf_tensor(f"xnS{i}", [128, DCH * 512], F16)) for i in range(2)]
            QTs = Ep(nc.sbuf_tensor("QTs", [128, S], F16))
            KTs = Ep(nc.sbuf_tensor("KTs", [128, S], F16))
            Vs = Ep(nc.sbuf_tensor("Vs", [128, S], F16))
            Et = [Ep(nc.sbuf_tensor(f"Et{i}", [128, 2 * QT], F16)) for i in range(4)]
            lnA1 = Ep(nc.sbuf_tensor("lnA1", [128, QT], F32))
            lnA2 = Ep(nc.sbuf_tensor("lnA2", [128, QT], F32))
            bc1 = Ep(nc.sbuf_tensor("bc1", [128, QT], F32))
            bc2 = Ep(nc.sbuf_tensor("bc2", [128, QT], F32))
            t1s = Ep(nc.sbuf_tensor("t1s", [128, QT], F32))
            t2s = Ep(nc.sbuf_tensor("t2s", [128, QT], F32))
            vcb = Ep(nc.sbuf_tensor("vcb", [128, QT], F32))
            sqs = Ep(nc.sbuf_tensor("sqs", [128, 512], F32))
            ssA = Ep(nc.sbuf_tensor("ssA", [128, 4], F32))
            lnA = Ep(nc.sbuf_tensor("lnA", [128, 4], F32))
            rsA = Ep(nc.sbuf_tensor("rsA", [128, 4], F32))
            nrm = Ep(nc.sbuf_tensor("nrm", [128, 2 * SCH * 128], F16))
            woT = [Ep(nc.sbuf_tensor(f"woT{i}", [128, D], F16)) for i in range(3)]
            sev = Ep(nc.sbuf_tensor("sev", [128, D], F32))

            S0, S1c = 0, 1024
            AV1, AV2 = 2048, 2560
            D1C, D2C = 3072, 3584
            QC, KC, VC = 0, 512, 1024
            TRC = 1024

            xn_use = {}
            xn_tok = {}
            xcnt = 0
            tok_wT = {}
            wo_use = {}
            tok_nrm_last = {}
            tok_wo_mm = {}

            for pr in range(HC):
                pp = pr % 2
                if pr >= 2:
                    SP.wait(PE, tok_wT[("consumed", pr - 2)])
                SP.wait(GP, tok_wcast)
                SP.dman(lambda s, pr=pr, pp=pp: [
                    s.dma_start_transpose(
                        w_t[pp][:, 128 * dc:128 * dc + 128],
                        w_b[128 * pr:128 * pr + 128, 128 * dc:128 * dc + 128])
                    for (w_t, w_b) in ((wqT, wq_bf), (wkT, wk_bf), (wvT, wv_bf))
                    for dc in range(DCH)], 3 * DCH)
                SP.op(lambda s: s.dma_start(out=ltmp[:, 0:128], in_=lq1), inc=16)
                tok_wT[pr] = SP.n

                tok_qkv_evac = 0
                for st in range(NQT):
                    sp2 = xcnt % 2
                    for k in (xcnt, xcnt + 1):
                        if k not in xn_tok and k <= HC * NQT:
                            if k >= 2:
                                GP.wait(PE, xn_use.get(k - 2, 0))
                            st_k = k % NQT
                            GP.op(lambda g, st_k=st_k, p2=k % 2: g.dma_start(
                                out=xnS[p2][:, :].rearrange("p (c s) -> p c s", c=DCH),
                                in_=xnT_nat.rearrange("(c p) s -> p c s", p=128)[:, :, QT * st_k:QT * st_k + QT]),
                            inc=16)
                            xn_tok[k] = GP.n
                    PE.wait(GP, xn_tok[min(xcnt + 1, HC * NQT)])
                    PE.wait(SP, tok_wT[pr])
                    if st == 0:
                        PE.wait(DV)
                        PE.wait(AC)
                    else:
                        PE.wait(DV, tok_qkv_evac)
                    PE.mmgroup(lambda t, pp=pp, sp2=sp2: [
                        t.matmul(PP[:, QC:QC + 512], wqT[pp][:, 128 * dc:128 * dc + 128],
                                 xnS[sp2][:, 512 * dc:512 * dc + 512],
                                 start=(dc == 0), stop=(dc == DCH - 1))
                        for dc in range(DCH)])
                    PE.mmgroup(lambda t, pp=pp, sp2=sp2: [
                        t.matmul(PP[:, KC:KC + 512], wkT[pp][:, 128 * dc:128 * dc + 128],
                                 xnS[sp2][:, 512 * dc:512 * dc + 512],
                                 start=(dc == 0), stop=(dc == DCH - 1))
                        for dc in range(DCH)])
                    PE.mmgroup(lambda t, pp=pp, sp2=sp2: [
                        t.matmul(PP[:, VC + 128 * q:VC + 128 * q + 128],
                                 xnS[sp2][:, 512 * dc + 128 * q:512 * dc + 128 * q + 128],
                                 wvT[pp][:, 128 * dc:128 * dc + 128],
                                 start=(dc == 0), stop=(dc == DCH - 1))
                        for q in range(4) for dc in range(DCH)])
                    xn_use[xcnt] = PE.n
                    xcnt += 1
                    if st == NQT - 1:
                        tok_wT[("consumed", pr)] = PE.n
                    DV.wait(PE)
                    tok_qkv_evac = DV.opn(lambda v, st=st: [
                        v.tensor_copy(QTs[:, QT * st:QT * st + QT], PP[:, QC:QC + 512]),
                        v.tensor_copy(KTs[:, QT * st:QT * st + QT], PP[:, KC:KC + 512]),
                        v.tensor_copy(Vs[:, QT * st:QT * st + QT], PP[:, VC:VC + 512]),
                    ], 3)
                tok_qkv_dv = DV.n

                for qt in range(NQT):
                    tok_scores = {}
                    tok_exp = {}
                    tok_cons = {}

                    def emit_dav(kc, tok_exp=tok_exp, tok_cons=tok_cons):
                        PE.wait(AC, tok_exp[kc])
                        ep4 = kc % 4
                        PE.mmgroup(lambda t, kc=kc, ep4=ep4: [
                            t.matmul(PP[:, D1C:D1C + QT], ones128[:, :], Et[ep4][:, 0:QT],
                                     start=(kc == 0), stop=(kc == KCH - 1)),
                            t.matmul(PP[:, D2C:D2C + QT], ones128[:, :], Et[ep4][:, QT:2 * QT],
                                     start=(kc == 0), stop=(kc == KCH - 1)),
                            t.matmul(PP[:, AV1:AV1 + QT], Vs[:, 128 * kc:128 * kc + 128],
                                     Et[ep4][:, 0:QT], start=(kc == 0), stop=(kc == KCH - 1)),
                            t.matmul(PP[:, AV2:AV2 + QT], Vs[:, 128 * kc:128 * kc + 128],
                                     Et[ep4][:, QT:2 * QT], start=(kc == 0), stop=(kc == KCH - 1)),
                        ])
                        tok_cons[kc] = PE.n

                    for kc in range(KCH):
                        sp2 = kc % 2
                        if kc == 0:
                            PE.wait(DV, tok_qkv_dv)
                            if qt > 0:
                                PE.wait(DV, tok_nrm_last[(pr, qt - 1)])
                        if kc >= 2:
                            PE.wait(AC, tok_exp[kc - 2])
                        PE.mmgroup(lambda t, kc=kc, qt=qt, sp2=sp2: [
                            t.matmul(PP[:, 1024 * sp2:1024 * sp2 + 512],
                                     KTs[0:64, 128 * kc:128 * kc + 128],
                                     QTs[0:64, QT * qt:QT * qt + QT],
                                     start=True, stop=True, tile_position=(0, 0)),
                            t.matmul(PP[:, 1024 * sp2 + 512:1024 * sp2 + 1024],
                                     KTs[64:128, 128 * kc:128 * kc + 128],
                                     QTs[64:128, QT * qt:QT * qt + QT],
                                     start=True, stop=True, tile_position=(64, 0)),
                        ])
                        tok_scores[kc] = PE.n
                        AC.wait(PE, tok_scores[kc])
                        if kc >= 4:
                            AC.wait(PE, tok_cons[kc - 4])
                        tok_exp[kc] = AC.op(lambda s, sp2=sp2, ep4=kc % 4: s.activation(
                            Et[ep4][:, :], PP[:, 1024 * sp2:1024 * sp2 + 2 * QT],
                            AF.Exp, scale=SM_SCALE))
                        if kc >= 1:
                            emit_dav(kc - 1)
                    emit_dav(KCH - 1)

                    AC.wait(PE, tok_cons[KCH - 1])
                    AC.opn(lambda s: [
                        s.activation(lnA1[:, :], PP[:, D1C:D1C + QT], AF.Ln),
                        s.activation(lnA2[:, :], PP[:, D2C:D2C + QT], AF.Ln),
                    ], 2)
                    AC.selfwait()
                    AC.opn(lambda s: [
                        s.activation(bc1[:, :], lnA1[:, :], AF.Exp, scale=-1.0),
                        s.activation(bc2[:, :], lnA2[:, :], AF.Exp, scale=-1.0),
                    ], 2)
                    tok_alpha = AC.n
                    DV.wait(AC, tok_alpha)
                    DV.wait(DV, tok_nlam)
                    DV.opn(lambda v: [
                        v.tensor_tensor(t1s[:, :], PP[:, AV1:AV1 + QT], bc1[:, :], op=ALU.mult),
                        v.scalar_tensor_tensor(t2s[:, :], PP[:, AV2:AV2 + QT], nlam128[:, :],
                                               bc2[:, :], op0=ALU.mult, op1=ALU.mult),
                    ], 2)
                    DV.selfwait()
                    tok_comb = DV.op(lambda v: v.tensor_tensor(vcb[:, :], t1s[:, :], t2s[:, :], op=ALU.add))
                    PE.wait(DV, tok_comb)
                    PE.mmgroup(lambda t: [
                        t.transpose(PP[:, TRC + 128 * ti:TRC + 128 * ti + 128],
                                    vcb[:, 128 * ti:128 * ti + 128], identf[:, :])
                        for ti in range(NT)])
                    AC.wait(PE)
                    AC.opn(lambda s: [
                        s.activation(sqs[:, 128 * ti:128 * ti + 128],
                                     PP[:, TRC + 128 * ti:TRC + 128 * ti + 128],
                                     AF.Square, accum_out=ssA[:, ti:ti + 1])
                        for ti in range(NT)], NT)
                    AC.selfwait()
                    AC.opn(lambda s: [
                        s.activation(lnA[:, ti:ti + 1], ssA[:, ti:ti + 1], AF.Ln,
                                     scale=1.0 / 128.0, bias=epsRMS[:, :])
                        for ti in range(NT)], NT)
                    AC.selfwait()
                    AC.opn(lambda s: [
                        s.activation(rsA[:, ti:ti + 1], lnA[:, ti:ti + 1], AF.Exp, scale=-0.5)
                        for ti in range(NT)], NT)
                    DV.wait(AC)
                    if qt == 0 and pr >= 2:
                        DV.wait(PE, tok_wo_mm[(pr - 2) | 1])
                    po = (pr % 2) * SCH * 128
                    tok_nrm_last[(pr, qt)] = DV.opn(lambda v, qt=qt, po=po: [
                        v.tensor_scalar(nrm[:, po + (qt * NT + ti) * 128:po + (qt * NT + ti) * 128 + 128],
                                        PP[:, TRC + 128 * ti:TRC + 128 * ti + 128],
                                        rsA[:, ti:ti + 1], 1.0 - LAMBDA_INIT,
                                        op0=ALU.mult, op1=ALU.mult)
                        for ti in range(NT)], NT)

                if pr % 2 == 1 or pr == HC - 1:
                    duo = (pr - 1, pr) if pr % 2 == 1 else (pr,)
                    combos = [(pa, sh) for pa in duo for sh in (0, 1)]
                    ngrp = (len(combos) + 1) // 2
                    nwo = ngrp * CC
                    wotok = {}

                    def emit_woload(k, nwo=nwo):
                        if k > nwo or k in wotok:
                            return
                        if k >= 3:
                            SP.wait(PE, wo_use[k - 3])
                        cck = k % CC
                        SP.op(lambda s, cck=cck, wp=k % 3: s.dma_start_transpose(
                            woT[wp][:, :], wo_bf[0:D, 128 * cck:128 * cck + 128]), inc=16)
                        wotok[k] = SP.n

                    woci = 0
                    prev_sstore = 0
                    emit_woload(0)
                    for g0 in range(0, len(combos), 2):
                        sub = combos[g0:g0 + 2]
                        for cc in range(CC):
                            wp = woci % 3
                            emit_woload(woci + 1)
                            PE.wait(SP, wotok[min(woci + 1, nwo)])
                            if cc == 0:
                                PE.wait(DV)
                            def womms(t, sub=sub, cc=cc, wp=wp):
                                ins = []
                                for ci, (pa, sh) in enumerate(sub):
                                    po2 = (pa % 2) * SCH * 128
                                    lhs = nrm[:, po2 + (sh * CC + cc) * 128:po2 + (sh * CC + cc) * 128 + 128]
                                    for j in range(NJ):
                                        ins.append(t.matmul(
                                            PP[:, 2048 * ci + JT * j:2048 * ci + JT * j + JT],
                                            lhs, woT[wp][:, JT * j:JT * j + JT],
                                            start=(cc == 0), stop=(cc == CC - 1)))
                                return ins
                            PE.mmgroup(womms)
                            wo_use[woci] = PE.n
                            woci += 1
                        tok_wo_mm[pr] = PE.n
                        for ci, (pa, sh) in enumerate(sub):
                            ul = 2 * pa + sh
                            DV.wait(PE)
                            DV.wait(GP, prev_sstore)
                            DV.op(lambda v, ci=ci: v.tensor_copy(sev[:, :], PP[:, 2048 * ci:2048 * ci + D]))
                            GP.wait(DV)
                            prev_sstore = GP.op(lambda g, ul=ul: g.dma_start(
                                out=s_stash[128 * ul:128 * ul + 128, :], in_=sev[:, :]), inc=16)
            barrier()

        with ExitStack() as ph:
            Ep = ph.enter_context
            pk = [Ep(nc.sbuf_tensor(f"pk{i}", [128, 256], I32)) for i in range(2)]
            bA = Ep(nc.sbuf_tensor("bA", [128, 2048], I32))
            bB = Ep(nc.sbuf_tensor("bB", [128, 2048], I32))
            bC = Ep(nc.sbuf_tensor("bC", [128, 2048], I32))
            bD = Ep(nc.sbuf_tensor("bD", [128, 2048], F32))
            Wf = Ep(nc.sbuf_tensor("Wf", [128, 2048], F32))
            dense = Ep(nc.sbuf_tensor("dense", [128, 2048], F16))
            scT = Ep(nc.sbuf_tensor("scT", [128, 64], F32))
            stg = Ep(nc.sbuf_tensor("stg", [128, max(DCH, ICH) * 512], F16))
            dnsb = Ep(nc.sbuf_tensor("dnsb", [NGD, D], F16))
            gusb = Ep(nc.sbuf_tensor("gusb", [NGU, I2], F16))

            bAf = bA.bitcast(F32)
            bBf = bB.bitcast(F32)
            bDi = bD.bitcast(I32)
            WfR = Wf.bitcast(I32).rearrange("p (w e) -> p e w", e=8)

            def dequant_strip(nw, src_pk):
                DV.opn(lambda v: [
                    v.tensor_scalar(bA[:, 256 * j:256 * j + nw], src_pk[:, 0:nw],
                                    4 * j, 7, op0=ALU.logical_shift_right, op1=ALU.bitwise_and)
                    for j in range(8)] + [
                    v.tensor_scalar(bC[:, 256 * j:256 * j + nw], src_pk[:, 0:nw],
                                    28 - 4 * j, -2147483648,
                                    op0=ALU.logical_shift_left, op1=ALU.bitwise_and)
                    for j in range(8)], 16)
                tok_rd = DV.n
                DV.selfwait()
                DV.opn(lambda v: [
                    v.tensor_scalar(bB[:, 256 * j:256 * j + nw], bA[:, 256 * j:256 * j + nw],
                                    252.0, float(1 << 22), op0=ALU.add, op1=ALU.mult)
                    for j in range(8)], 8)
                DV.selfwait()
                DV.opn(lambda v: [
                    v.tensor_scalar(bAf[:, 256 * j:256 * j + nw], bBf[:, 256 * j:256 * j + nw],
                                    2.0, 1.0, op0=ALU.mult, op1=ALU.subtract)
                    for j in range(8)], 8)
                DV.selfwait()
                DV.opn(lambda v: [
                    v.tensor_tensor(bD[:, 256 * j:256 * j + nw], bAf[:, 256 * j:256 * j + nw],
                                    bBf[:, 256 * j:256 * j + nw], op=ALU.min)
                    for j in range(8)], 8)
                DV.selfwait()
                DV.opn(lambda v: [
                    v.tensor_tensor(WfR[:, j, 0:nw], bDi[:, 256 * j:256 * j + nw],
                                    bC[:, 256 * j:256 * j + nw], op=ALU.bitwise_or)
                    for j in range(8)], 8)
                DV.selfwait()
                return tok_rd

            tok_pk_rd = {}
            tok_pe_tr = {}
            tok_evac = {}
            tok_stg_store = 0
            scnt = 0

            pk_sched = [("g", ot, 0, WI_GU) for ot in range(NOT_GU)]
            for ot_ in range(NOT_DN):
                for sti_ in range((WI_DN + 255) // 256):
                    w0_ = 256 * sti_
                    pk_sched.append(("d", ot_, w0_, min(256, WI_DN - w0_)))
            pk_sched.append(("g", 0, 0, WI_GU))
            pktok = {}
            pkemit = [0]

            def emit_pk(upto):
                while pkemit[0] <= min(upto, len(pk_sched) - 1):
                    k = pkemit[0]
                    if k >= 2:
                        GP.wait(DV, tok_pk_rd.get(k - 2, 0))
                    kind, ot_, w0_, nw_ = pk_sched[k]
                    if kind == "g":
                        GP.op(lambda g, ot_=ot_, p2=k % 2: g.dma_start(
                            out=pk[p2][:, 0:WI_GU], in_=gup[128 * ot_:128 * ot_ + 128, :]), inc=16)
                    else:
                        GP.op(lambda g, ot_=ot_, w0_=w0_, nw_=nw_, p2=k % 2: g.dma_start(
                            out=pk[p2][:, 0:nw_], in_=dnp[128 * ot_:128 * ot_ + 128, w0_:w0_ + nw_]), inc=16)
                    pktok[k] = GP.n
                    pkemit[0] += 1

            def pk_wait(k):
                emit_pk(k + 1)
                DV.wait(GP, pktok[min(k + 1, len(pk_sched) - 1)])

            SP.op(lambda s: s.dma_start(out=gusb[:, :], in_=gus), inc=16)
            SP.op(lambda s: s.dma_start(out=dnsb[:, :], in_=dns), inc=16)
            tok_gus = SP.n
            tok_scg_used = {}
            for ot in range(NOT_GU):
                p2 = scnt % 2
                PE.wait(SP, tok_gus)
                if ot > 0:
                    PE.wait(DV, tok_scg_used[ot - 1])
                PE.mmgroup(lambda t, ot=ot: [
                    t.transpose(PPh[:, 2 * 3584:2 * 3584 + NGU],
                                gusb[0:NGU, 128 * ot:128 * ot + 128], identh[0:NGU, 0:NGU])])
                DV.wait(PE)
                DV.op(lambda v: v.tensor_copy(scT[:, 0:NGU], PPh[:, 2 * 3584:2 * 3584 + NGU]))
                DV.selfwait()
                pk_wait(scnt)
                tok_pk_rd[scnt] = dequant_strip(WI_GU, pk[p2])
                scnt += 1
                if ot >= 1:
                    DV.wait(PE, tok_pe_tr.get(ot - 1, 0))
                if ot % 4 == 0 and ot > 0:
                    DV.wait(GP, tok_stg_store)
                tok_scg_used[ot] = DV.opn(lambda v: [
                    v.tensor_scalar(dense[:, 128 * g2:128 * g2 + 128], Wf[:, 128 * g2:128 * g2 + 128],
                                    scT[:, g2:g2 + 1], None, op0=ALU.mult)
                    for g2 in range(NGU)], NGU)
                DV.selfwait()
                PE.wait(DV)
                if ot >= 2:
                    PE.wait(DV, tok_evac[ot - 2])
                pb = (ot % 2) * 1024
                PE.mmgroup(lambda t, pb=pb: [
                    t.transpose(PPh[:, 2 * pb + 128 * ci2:2 * pb + 128 * ci2 + 128],
                                dense[:, 128 * ci2:128 * ci2 + 128], identh[:, :])
                    for ci2 in range(DCH)])
                tok_pe_tr[ot] = PE.n
                DV.wait(PE)
                ob = (ot % 4) * 128
                tok_evac[ot] = DV.opn(lambda v, pb=pb, ob=ob: [
                    v.tensor_copy(stg[:, 512 * ci2 + ob:512 * ci2 + ob + 128],
                                  PPh[:, 2 * pb + 128 * ci2:2 * pb + 128 * ci2 + 128])
                    for ci2 in range(DCH)], DCH)
                if ot % 4 == 3:
                    GP.wait(DV, tok_evac[ot])
                    blk0 = 128 * (ot - 3)
                    tok_stg_store = GP.dman(lambda g, blk0=blk0: [
                        g.dma_start(out=wguT[ci2, :, blk0:blk0 + 512],
                                    in_=stg[:, 512 * ci2:512 * ci2 + 512])
                        for ci2 in range(DCH)], DCH)

            tok_scT_used = {}
            for ot in range(NOT_DN):
                nstrip = (WI_DN + 255) // 256
                if ot > 0:
                    PE.wait(DV, tok_scT_used[ot - 1])
                PE.mmgroup(lambda t, ot=ot: [
                    t.transpose(PPh[:, 2 * 3584:2 * 3584 + NGD],
                                dnsb[0:NGD, 128 * ot:128 * ot + 128], identh[0:NGD, 0:NGD])])
                DV.wait(PE)
                DV.op(lambda v: v.tensor_copy(scT[:, 0:NGD], PPh[:, 2 * 3584:2 * 3584 + NGD]))
                DV.selfwait()
                for sti in range(nstrip):
                    w0 = 256 * sti
                    nw = min(256, WI_DN - w0)
                    p2 = scnt % 2
                    pk_wait(scnt)
                    tok_pk_rd[scnt] = dequant_strip(nw, pk[p2])
                    scnt += 1
                    g0 = (8 * w0) // 128
                    ng = (8 * nw) // 128
                    key = ("dn", ot, sti)
                    prevkey = ("dn", ot, sti - 1) if sti > 0 else ("dn", ot - 1, nstrip - 1)
                    if prevkey in tok_pe_tr:
                        DV.wait(PE, tok_pe_tr[prevkey])
                    if (ot * nstrip + sti) % 4 == 0 and ot + sti > 0:
                        DV.wait(GP, tok_stg_store)
                    DV.opn(lambda v, g0=g0, ng=ng: [
                        v.tensor_scalar(dense[:, 128 * g2:128 * g2 + 128],
                                        Wf[:, 128 * g2:128 * g2 + 128],
                                        scT[:, g0 + g2:g0 + g2 + 1], None, op0=ALU.mult)
                        for g2 in range(ng)], ng)
                    DV.selfwait()
                    tok_scT_used[ot] = DV.n
                    PE.wait(DV)
                    prevkey2 = ("dn", ot, sti - 2) if sti >= 2 else None
                    if prevkey2 in tok_evac:
                        PE.wait(DV, tok_evac[prevkey2])
                    pb = (scnt % 2) * 1024
                    PE.mmgroup(lambda t, pb=pb, ng=ng: [
                        t.transpose(PPh[:, 2 * pb + 128 * ci2:2 * pb + 128 * ci2 + 128],
                                    dense[:, 128 * ci2:128 * ci2 + 128], identh[:, :])
                        for ci2 in range(ng)])
                    tok_pe_tr[key] = PE.n
                    DV.wait(PE)
                    ob = (ot % 4) * 128
                    tok_evac[key] = DV.opn(lambda v, pb=pb, ob=ob, g0=g0, ng=ng: [
                        v.tensor_copy(stg[:, 512 * (g0 + ci2) + ob:512 * (g0 + ci2) + ob + 128],
                                      PPh[:, 2 * pb + 128 * ci2:2 * pb + 128 * ci2 + 128])
                        for ci2 in range(ng)], ng)
                if ot % 4 == 3 or ot == NOT_DN - 1:
                    nblk = (ot % 4) + 1
                    blk0 = 128 * (ot - nblk + 1)
                    GP.wait(DV, tok_evac[("dn", ot, nstrip - 1)])
                    tok_stg_store = GP.dman(lambda g, blk0=blk0, nblk=nblk: [
                        g.dma_start(out=wdnT[ci2, :, blk0:blk0 + 128 * nblk],
                                    in_=stg[:, 512 * ci2:512 * ci2 + 128 * nblk])
                        for ci2 in range(ICH)], ICH)
            emit_pk(len(pk_sched) - 1)
            GP.selfwait()
            barrier()

        with ExitStack() as ph:
            Ep = ph.enter_context
            xnU = Ep(nc.sbuf_tensor("xnU", [128, 4 * DCH * 128], F16))
            wsl = [Ep(nc.sbuf_tensor(f"wsl{i}", [128, 512], F16)) for i in range(4)]
            gsil = Ep(nc.sbuf_tensor("gsil", [128, 4 * 512], F16))
            hsb = Ep(nc.sbuf_tensor("hsb", [128, 4 * 512], F16))
            hT = Ep(nc.sbuf_tensor("hT", [128, 4 * ICH * 128], F16))
            ssF = Ep(nc.sbuf_tensor("ssF", [128, 1], F32))
            lnF = Ep(nc.sbuf_tensor("lnF", [128, 1], F32))
            rsF = Ep(nc.sbuf_tensor("rsF", [128, 1], F32))
            ybuf = [Ep(nc.sbuf_tensor(f"ybuf{i}", [128, D], F32)) for i in range(4)]
            gbuf = Ep(nc.sbuf_tensor("gbuf", [128, D], F32))
            sbuf2 = [Ep(nc.sbuf_tensor(f"sbuf2{i}", [128, D], F32)) for i in range(2)]
            xbuf = [Ep(nc.sbuf_tensor(f"xbuf{i}", [128, D], F32)) for i in range(2)]
            obuf = Ep(nc.sbuf_tensor("obuf", [128, D], F32))

            NUG = (U + 3) // 4
            load_fns = []
            loadtok = {}
            wtoks = {}
            emitted = [0]
            LA = 3

            def emit_load(idx):
                while emitted[0] <= min(idx, len(load_fns) - 1):
                    k = emitted[0]
                    if k >= 4:
                        GP.wait(PE, wtoks[k - 4])
                    load_fns[k]()
                    loadtok[k] = GP.n
                    emitted[0] += 1

            def mk_gu_load(dc, rb, wp):
                def f():
                    GP.op(lambda g: g.dma_start(out=wsl[wp][:, :], in_=wguT[dc, :, rb:rb + 512]), inc=16)
                return f

            def mk_dn_load(ic, j, wp):
                def f():
                    GP.op(lambda g: g.dma_start(out=wsl[wp][:, 0:JT], in_=wdnT[ic, :, JT * j:JT * j + JT]), inc=16)
                return f

            sched = []
            li = 0
            for ug in range(NUG):
                for og in range(OG):
                    for ph2 in (0, 1):
                        rb = 512 * og + (I if ph2 else 0)
                        for dc in range(DCH):
                            load_fns.append(mk_gu_load(dc, rb, li % 4)); li += 1
                for j in range(NJ):
                    for ic in range(ICH):
                        load_fns.append(mk_dn_load(ic, j, li % 4)); li += 1
            load_fns.append(mk_dn_load(0, 0, li % 4))
            NLOAD = li

            def pe_wait_load(k):
                emit_load(k + LA)
                PE.wait(GP, loadtok[min(k + 1, len(load_fns) - 1)])

            tok_xnU_used = 0
            tok_g_ac = 0
            tok_gbuf_free = 0
            tok_out_dv = 0
            tok_obuf_store = 0
            sxtok = {}
            sx_guard = {}
            tok_ydump = 0
            gi = 0

            for ug in range(NUG):
                us = list(range(4 * ug, min(4 * ug + 4, U)))
                nu = len(us)
                if ug > 0:
                    SP.wait(PE, tok_xnU_used)
                SP.dman(lambda s, us=us: [
                    s.dma_start_transpose(
                        xnU[:, (ui * DCH + dc) * 128:(ui * DCH + dc) * 128 + 128],
                        xn_nat.rearrange("(m blk) d -> blk m d", blk=BLK)[u, :, 128 * dc:128 * dc + 128])
                    for ui, u in enumerate(us) for dc in range(DCH)], nu * DCH)
                tok_xnU = SP.n

                tok_h_dv = 0
                tok_hTdv = 0
                tok_hT_pe = 0
                tok_gsil_ac = 0
                tok_yevac = {}
                for og in range(OG):
                    for ph2 in (0, 1):
                        for dc in range(DCH):
                            pe_wait_load(gi)
                            if dc == 0:
                                PE.wait(SP, tok_xnU)
                                if ph2 == 0 and (og > 0 or ug > 0):
                                    PE.wait(DV, tok_hTdv if og > 0 else tok_ylast)
                                if ph2 == 1:
                                    PE.wait(AC, tok_gsil_ac)
                            pbase = 2048 * ph2
                            wp = gi % 4
                            PE.mmgroup(lambda t, us=us, dc=dc, wp=wp, pbase=pbase: [
                                t.matmul(PP[:, pbase + 512 * ui:pbase + 512 * ui + 512],
                                         xnU[:, (ui * DCH + dc) * 128:(ui * DCH + dc) * 128 + 128],
                                         wsl[wp][:, :], start=(dc == 0), stop=(dc == DCH - 1))
                                for ui in range(len(us))])
                            wtoks[gi] = PE.n
                            gi += 1
                        if ph2 == 0:
                            AC.wait(PE)
                            AC.wait(DV, tok_h_dv)
                            tok_gsil_ac = AC.opn(lambda s, nu=nu: [
                                s.activation(gsil[:, 512 * ui:512 * ui + 512],
                                             PP[:, 512 * ui:512 * ui + 512], AF.Silu)
                                for ui in range(nu)], nu)
                        else:
                            DV.wait(PE)
                            DV.wait(AC, tok_gsil_ac)
                            DV.wait(PE, tok_hT_pe)
                            tok_h_dv = DV.opn(lambda v, nu=nu: [
                                v.scalar_tensor_tensor(
                                    hsb[:, 512 * ui:512 * ui + 512],
                                    PP[:, 2048 + 512 * ui:2048 + 512 * ui + 512], 1.0 / 64.0,
                                    gsil[:, 512 * ui:512 * ui + 512],
                                    op0=ALU.mult, op1=ALU.mult)
                                for ui in range(nu)], nu)
                    PE.wait(DV, tok_h_dv)
                    PE.mmgroup(lambda t, nu=nu: [
                        t.transpose(PPh[:, 128 * (4 * ui + ti):128 * (4 * ui + ti) + 128],
                                    hsb[:, 512 * ui + 128 * ti:512 * ui + 128 * ti + 128], identh[:, :])
                        for ui in range(nu) for ti in range(4)])
                    tok_hT_pe = PE.n
                    DV.wait(PE, tok_hT_pe)
                    tok_hTdv = DV.opn(lambda v, og=og, nu=nu: [
                        v.tensor_copy(hT[:, (ui * ICH + og * 4 + ti) * 128:(ui * ICH + og * 4 + ti) * 128 + 128],
                                      PPh[:, 128 * (4 * ui + ti):128 * (4 * ui + ti) + 128])
                        for ui in range(nu) for ti in range(4)], nu * 4)
                tok_xnU_used = PE.n

                for j in range(NJ):
                    q = j % 2
                    for ic in range(ICH):
                        pe_wait_load(gi)
                        if ic == 0:
                            if j == 0:
                                PE.wait(DV, tok_hTdv)
                            if j >= 2:
                                PE.wait(DV, tok_yevac[j - 2])
                            if ug > 0 and j < 2:
                                PE.wait(AC, tok_g_ac)
                        wp = gi % 4
                        PE.mmgroup(lambda t, us=us, j=j, ic=ic, wp=wp, q=q, nu=nu: [
                            t.matmul(PP[:, 2048 * q + 512 * ui:2048 * q + 512 * ui + JT],
                                     hT[:, (ui * ICH + ic) * 128:(ui * ICH + ic) * 128 + 128],
                                     wsl[wp][:, 0:JT], start=(ic == 0), stop=(ic == ICH - 1))
                            for ui in range(nu)])
                        wtoks[gi] = PE.n
                        gi += 1
                    DV.wait(PE)
                    if ug > 0 and j == 0:
                        DV.wait(AC, tok_g_ac)
                        DV.wait(SP, tok_ydump)
                    tok_yevac[j] = DV.opn(lambda v, j=j, q=q, nu=nu: [
                        v.tensor_copy(ybuf[ui][:, JT * j:JT * j + JT],
                                      PP[:, 2048 * q + 512 * ui:2048 * q + 512 * ui + JT])
                        for ui in range(nu)], nu)
                tok_ylast = tok_yevac[NJ - 1]

                if debug:
                    SP.wait(DV, tok_ylast)
                    SP.dman(lambda s, us=us: [
                        s.dma_start(out=y_dbg[128 * u:128 * u + 128, :], in_=ybuf[ui][:, :])
                        for ui, u in enumerate(us)], nu)
                    tok_ydump = SP.n

                for ui, u in enumerate(us):
                    ul = u
                    for uk in (u, u + 1):
                        if uk not in sxtok and uk <= U:
                            SP.wait(DV, sx_guard.get(uk - 2, 0))
                            row = 128 * (uk % U)
                            SP.op(lambda s, row=row, p=uk % 2: s.dma_start(
                                out=sbuf2[p][:, :], in_=s_stash[row:row + 128, :]), inc=16)
                            SP.op(lambda s, p=uk % 2, blk=uk % U: s.dma_start(
                                out=xbuf[p][:, :],
                                in_=x_in.rearrange("(m blk) d -> blk m d", blk=BLK)[blk, :, :]), inc=16)
                            sxtok[uk] = SP.n
                    AC.wait(DV, tok_ylast)
                    AC.wait(DV, tok_gbuf_free)
                    AC.op(lambda s, ui=ui: s.activation(
                        gbuf[:, :], ybuf[ui][:, :], AF.Square, accum_out=ssF[:, :]))
                    AC.selfwait()
                    AC.op(lambda s: s.activation(lnF[:, :], ssF[:, :], AF.Ln,
                                                 scale=1.0 / D, bias=epsFFN[:, :]))
                    AC.selfwait()
                    AC.op(lambda s: s.activation(rsF[:, :], lnF[:, :], AF.Exp, scale=-0.5))
                    AC.selfwait()
                    tok_g_ac = AC.op(lambda s, ui=ui: s.activation(
                        gbuf[:, :], ybuf[ui][:, :], AF.Silu, scale=rsF[:, :]))
                    DV.wait(AC, tok_g_ac)
                    DV.wait(SP, sxtok[u + 1])
                    DV.wait(SP, tok_obuf_store)
                    tok_gbuf_free = DV.op(lambda v, p=u % 2: v.tensor_tensor(
                        obuf[:, :], gbuf[:, :], sbuf2[p][:, :], op=ALU.mult))
                    DV.selfwait()
                    tok_out_dv = DV.op(lambda v, p=u % 2: v.tensor_tensor(
                        obuf[:, :], obuf[:, :], xbuf[p][:, :], op=ALU.add))
                    sx_guard[u] = tok_out_dv
                    SP.wait(DV, tok_out_dv)
                    tok_obuf_store = SP.op(lambda s, ul=ul: s.dma_start(
                        out=out_c[128 * ul:128 * ul + 128, :], in_=obuf[:, :]), inc=16)
            emit_load(len(load_fns) - 1)
            GP.selfwait()
            GP.wait(SP)
            SP.selfwait()
            flush_all()
    return nc



def shard_inputs(cfg, inputs, n_cores=8):
    c = _cfg(cfg)
    S, D, HC, BLK, SH = c["S"], c["D"], c["HC"], c["BLK"], c["SH"]
    x = np.ascontiguousarray(inputs["L0_attn_x"])
    wq, wk, wv, wo = (inputs["L0_attn_wq"], inputs["L0_attn_wk"],
                      inputs["L0_attn_wv"], inputs["L0_attn_wo"])
    maps = []
    for core in range(n_cores):
        b = core // 2
        side = core % 2
        q = np.arange(S) // BLK
        k = np.arange(S) % BLK
        src = q * BLK + (k + side * SH) % BLK
        cp = np.arange(D)
        colperm = (cp // BLK) * BLK + (cp % BLK + side * SH) % BLK
        h0 = HC * side
        m = {
            "x_in": x[b][src],
            "wq_in": wq[128 * h0:128 * (h0 + HC)],
            "wk_in": wk[128 * h0:128 * (h0 + HC)],
            "wv_in": wv[128 * h0:128 * (h0 + HC)],
            "wo_in": wo[:, colperm],
            "lq1": np.asarray(inputs["L0_attn_lambda_q1"]).reshape(1, 128),
            "lq2": np.asarray(inputs["L0_attn_lambda_q2"]).reshape(1, 128),
            "lk1": np.asarray(inputs["L0_attn_lambda_k1"]).reshape(1, 128),
            "lk2": np.asarray(inputs["L0_attn_lambda_k2"]).reshape(1, 128),
            "gup": inputs["L0_ffn_gate_up_packed"],
            "gus": inputs["L0_ffn_gate_up_scales"],
            "dnp": inputs["L0_ffn_down_packed"],
            "dns": inputs["L0_ffn_down_scales"],
        }
        maps.append({k2: np.ascontiguousarray(v) for k2, v in m.items()})
    return maps


def unshard_output(cfg, outs, n_cores=8):
    c = _cfg(cfg)
    S, D, U, BLK, SH = c["S"], c["D"], c["U"], c["BLK"], c["SH"]
    B = n_cores // 2
    out = np.empty((B, S, D), dtype=np.float32)
    m = np.arange(128)
    for core in range(n_cores):
        b = core // 2
        side = core % 2
        oc = outs[core]
        for ul in range(U):
            rows = BLK * m + ul + side * SH
            out[b, rows, :] = oc[128 * ul:128 * ul + 128, :]
    return out


def kernel(**inputs):
    cfg = REAL
    in_maps = shard_inputs(cfg, inputs)
    nc = build_nc(cfg)
    res = run_bass_kernel_spmd(nc, in_maps, core_ids=list(range(8)))
    return unshard_output(cfg, [r["out_c"] for r in res.results])


# version 11
# speedup vs baseline: 1.9301x; 1.1032x over previous
import numpy as np
import ml_dtypes
from contextlib import ExitStack

import concourse.bass as bass
import concourse.mybir as mybir
from concourse.bass_utils import run_bass_kernel_spmd

F32 = mybir.dt.float32
BF16 = mybir.dt.bfloat16
F16 = mybir.dt.float16
I32 = mybir.dt.int32
AF = mybir.ActivationFunctionType
ALU = mybir.AluOpType

LN_EPS = 1e-5
RMS_EPS = 1.1920928955078125e-07
FFN_EPS = 1e-06
LAMBDA_INIT = 0.8
SM_SCALE = float(1.0 / np.sqrt(128.0))

REAL = dict(S=4096, D=2048, HG=16, HC=8, I=5632)


def _cfg(cfg):
    S, D, HG, HC, I = cfg["S"], cfg["D"], cfg["HG"], cfg["HC"], cfg["I"]
    assert D == HG * 128 and S == 2 * D and I % 512 == 0
    c = dict(cfg)
    NC = cfg.get("NC", 8)
    WD = min(512, D)
    NGUU = (2 * I) // 512
    NDU = D // WD
    c.update(
        BLK=S // 128, SH=2 * HC, DCH=D // 128, KCH=S // 128, SCH=S // 128,
        QT=512, NQT=S // 512, U=2 * HC, JT=min(512, D), NJ=D // min(512, D),
        OG=I // 512, ICH=I // 128, NGU=D // 128, NGD=I // 128, CC=(S // 2) // 128,
        NC=NC, WD=WD, NGUU=NGUU, NDU=NDU,
        S_G=-(-NGUU // NC), S_D=-(-NDU // NC),
    )
    return c


class Stream:

    CHUNK = 400

    def __init__(self, block, engname, sem):
        self.block = block
        self.engname = engname
        self.sem = sem
        self.n = 0
        self.pend = []

    def raw(self, f):
        self.pend.append(f)

    def flush(self):
        while self.pend:
            chunk = self.pend[:self.CHUNK]
            self.pend = self.pend[self.CHUNK:]
            getattr(self.block, self.engname)(
                lambda e, chunk=chunk: [g(e) for g in chunk])

    def op(self, f, inc=1):
        def g(e):
            f(e).then_inc(self.sem, inc)
        self.raw(g)
        self.n += inc
        return self.n

    def opn(self, f, n):
        def g(e):
            inss = f(e)
            assert len(inss) == n, (self.engname, len(inss), n)
            for ins in inss:
                ins.then_inc(self.sem, 1)
        self.raw(g)
        self.n += n
        return self.n

    def dman(self, f, n):
        def g(e):
            inss = f(e)
            assert len(inss) == n, (self.engname, len(inss), n)
            for ins in inss:
                ins.then_inc(self.sem, 16)
        self.raw(g)
        self.n += 16 * n
        return self.n

    def mmgroup(self, f):
        def g(e):
            f(e)[-1].then_inc(self.sem, 1)
        self.raw(g)
        self.n += 1
        return self.n

    def wait(self, other, tok=None):
        t = other.n if tok is None else tok
        if t > 0:
            self.raw(lambda e: e.wait_ge(other.sem, t))

    def selfwait(self):
        if self.n > 0:
            self.raw(lambda e, n=self.n: e.wait_ge(self.sem, n))


def build_nc(cfg, debug=False):
    c = _cfg(cfg)
    S, D, HC, I = c["S"], c["D"], c["HC"], c["I"]
    DCH, KCH, SCH, QT, NQT = c["DCH"], c["KCH"], c["SCH"], c["QT"], c["NQT"]
    U, NJ, JT, OG, ICH = c["U"], c["NJ"], c["JT"], c["OG"], c["ICH"]
    NGU, NGD, CC, BLK = c["NGU"], c["NGD"], c["CC"], c["BLK"]
    NC, WD, S_G, S_D = c["NC"], c["WD"], c["S_G"], c["S_D"]
    I2 = 2 * I
    WI_GU = D // 8
    WI_DN = I // 8
    NOT_GU = I2 // 128
    NOT_DN = D // 128
    NT = QT // 128

    nc = bass.Bass("TRN2", target_bir_lowering=False)
    dk = "ExternalOutput" if debug else "Internal"

    x_in = nc.dram_tensor("x_in", [S, D], F32, kind="ExternalInput").ap()
    wq_in = nc.dram_tensor("wq_in", [D, HC * 128], F32, kind="ExternalInput").ap()
    wk_in = nc.dram_tensor("wk_in", [D, HC * 128], F32, kind="ExternalInput").ap()
    wv_in = nc.dram_tensor("wv_in", [D, HC * 128], F32, kind="ExternalInput").ap()
    wo_in = nc.dram_tensor("wo_in", [D, D], F32, kind="ExternalInput").ap()
    lq1 = nc.dram_tensor("lq1", [1, 128], F32, kind="ExternalInput").ap()
    lq2 = nc.dram_tensor("lq2", [1, 128], F32, kind="ExternalInput").ap()
    lk1 = nc.dram_tensor("lk1", [1, 128], F32, kind="ExternalInput").ap()
    lk2 = nc.dram_tensor("lk2", [1, 128], F32, kind="ExternalInput").ap()
    gup = nc.dram_tensor("gup", [I2, WI_GU], I32, kind="ExternalInput").ap()
    gusT = nc.dram_tensor("gusT", [I2, NGU], F16, kind="ExternalInput").ap()
    dnp = nc.dram_tensor("dnp", [D, WI_DN], I32, kind="ExternalInput").ap()
    dnsT = nc.dram_tensor("dnsT", [D, NGD], F16, kind="ExternalInput").ap()
    out_c = nc.dram_tensor("out_c", [U * 128, D], F32, kind="ExternalOutput").ap()

    wq_bf = nc.dram_tensor("wq_bf", [D, HC * 128], F16, kind="Internal").ap()
    wk_bf = nc.dram_tensor("wk_bf", [D, HC * 128], F16, kind="Internal").ap()
    wv_bf = nc.dram_tensor("wv_bf", [D, HC * 128], F16, kind="Internal").ap()
    wo_bf = nc.dram_tensor("wo_bf", [D, D], F16, kind="Internal").ap()
    xn_nat = nc.dram_tensor("xn_nat", [S, D], F16, kind=dk).ap()
    xnT_nat = nc.dram_tensor("xnT_nat", [D, S], F16, kind=dk).ap()
    s_stash = nc.dram_tensor("s_stash", [U * 128, D], F32, kind=dk).ap()
    wgu_dense = nc.dram_tensor("wgu_dense", [I2, D], F16, kind=dk).ap()
    wdn_dense = nc.dram_tensor("wdn_dense", [D, I], F16, kind=dk).ap()
    y_dbg = nc.dram_tensor("y_dbg", [U * 128, D], F32, kind=dk).ap() if debug else None
    h_dbg = nc.dram_tensor("h_dbg", [4 * 128, ICH * 128], F16, kind=dk).ap() if debug else None

    with ExitStack() as top:
        E = top.enter_context
        ident = E(nc.sbuf_tensor("ident", [128, 128], BF16))
        identh = E(nc.sbuf_tensor("identh", [128, 128], F16))
        identf = E(nc.sbuf_tensor("identf", [128, 128], F32))
        identi = E(nc.sbuf_tensor("identi", [128, 128], I32))
        onesb = E(nc.sbuf_tensor("onesb", [128, 1], F16))
        ones128 = E(nc.sbuf_tensor("ones128", [128, 128], F16))
        nlam128 = E(nc.sbuf_tensor("nlam128", [128, 1], F32))
        ones1 = E(nc.sbuf_tensor("ones1", [1, 128], F32))
        epsLN = E(nc.sbuf_tensor("epsLN", [128, 1], F32))
        epsRMS = E(nc.sbuf_tensor("epsRMS", [128, 1], F32))
        epsFFN = E(nc.sbuf_tensor("epsFFN", [128, 1], F32))
        lamt = E(nc.sbuf_tensor("lamt", [1, 1], F32))
        ltmp = E(nc.sbuf_tensor("ltmp", [1, 768], F32))
        lred = E(nc.sbuf_tensor("lred", [1, 4], F32))

        PP = E(nc.psum_tensor("PP", [128, 4096], F32))
        PPb = PP.bitcast(BF16)
        PPh = PP.bitcast(F16)

        sems = {k: E(nc.semaphore(k)) for k in ["pe", "ac", "dv", "gp", "sp", "ag"]}
        block = E(nc.Block())

        PE = Stream(block, "tensor", sems["pe"])
        AC = Stream(block, "scalar", sems["ac"])
        DV = Stream(block, "vector", sems["dv"])
        GP = Stream(block, "gpsimd", sems["gp"])
        SP = Stream(block, "sync", sems["sp"])

        ALLS = None

        def flush_all():
            for s in (PE, AC, DV, GP, SP):
                s.flush()

        def barrier():
            streams = [PE, AC, DV, GP, SP]
            for s in streams:
                for o in streams:
                    if o is not s:
                        s.wait(o)
            flush_all()

        GP.dman(lambda g: [
            g.dma_start(out=wq_bf, in_=wq_in),
            g.dma_start(out=wk_bf, in_=wk_in),
            g.dma_start(out=wv_bf, in_=wv_in),
            g.dma_start(out=wo_bf, in_=wo_in),
        ], 4)
        tok_wcast = GP.n
        GP.op(lambda g: g.iota(identi[:, :], pattern=[[-1, 128]], base=0, channel_multiplier=1))
        tok_iota = GP.n
        SP.dman(lambda s: [
            s.dma_start(out=ltmp[:, 0:128], in_=lq1),
            s.dma_start(out=ltmp[:, 128:256], in_=lk1),
            s.dma_start(out=ltmp[:, 256:384], in_=lq2),
            s.dma_start(out=ltmp[:, 384:512], in_=lk2),
        ], 4)
        tok_lload = SP.n

        DV.wait(GP, tok_iota)
        DV.opn(lambda v: [
            v.memset(onesb[:, :], 1.0),
            v.memset(ones128[:, :], 1.0),
            v.memset(ones1[:, :], 1.0),
            v.memset(epsLN[:, :], LN_EPS),
            v.memset(epsRMS[:, :], RMS_EPS),
            v.memset(epsFFN[:, :], FFN_EPS / 4096.0),
            v.tensor_scalar(ident[:, :], identi[:, :], 0, None, op0=ALU.is_equal),
            v.tensor_scalar(identh[:, :], identi[:, :], 0, None, op0=ALU.is_equal),
            v.tensor_scalar(identf[:, :], identi[:, :], 0, None, op0=ALU.is_equal),
        ], 9)
        DV.wait(SP, tok_lload)
        DV.wait(GP, tok_wcast)
        DV.opn(lambda v: [
            v.tensor_tensor(ltmp[:, 512:640], ltmp[:, 0:128], ltmp[:, 128:256], op=ALU.mult),
            v.tensor_tensor(ltmp[:, 640:768], ltmp[:, 256:384], ltmp[:, 384:512], op=ALU.mult),
        ], 2)
        DV.selfwait()
        DV.opn(lambda v: [
            v.tensor_reduce(lred[:, 0:1], ltmp[:, 512:640], op=ALU.add, axis=mybir.AxisListType.X),
            v.tensor_reduce(lred[:, 1:2], ltmp[:, 640:768], op=ALU.add, axis=mybir.AxisListType.X),
        ], 2)
        AC.wait(DV)
        AC.opn(lambda s: [s.activation(lred[:, 2:3], lred[:, 0:1], AF.Exp),
                          s.activation(lred[:, 3:4], lred[:, 1:2], AF.Exp)], 2)
        DV.wait(AC)
        DV.op(lambda v: v.tensor_tensor(lamt[:, :], lred[:, 2:3], lred[:, 3:4], op=ALU.subtract))
        DV.selfwait()
        DV.op(lambda v: v.tensor_scalar(lamt[:, :], lamt[:, :], LAMBDA_INIT, None, op0=ALU.add))
        PE.wait(DV)
        PE.mmgroup(lambda t: [t.matmul(PP[:, 0:1], ones1[0:1, :], lamt[0:1, 0:1], start=True, stop=True)])
        DV.wait(PE)
        DV.op(lambda v: v.tensor_scalar(nlam128[:, :], PP[:, 0:1], -1.0, None, op0=ALU.mult))
        tok_nlam = DV.n

        with ExitStack() as ph:
            Ep = ph.enter_context
            pk = [Ep(nc.sbuf_tensor(f"pk{i}", [128, 256], I32)) for i in range(2)]
            bA = Ep(nc.sbuf_tensor("bA", [128, 2048], I32))
            bB = Ep(nc.sbuf_tensor("bB", [128, 2048], I32))
            bC = Ep(nc.sbuf_tensor("bC", [128, 2048], I32))
            bD = Ep(nc.sbuf_tensor("bD", [128, 2048], F32))
            Wf = Ep(nc.sbuf_tensor("Wf", [128, 2048], F32))
            dgu = [Ep(nc.sbuf_tensor(f"dgu{i}", [128, D], F16)) for i in range(2)]
            ddn = [Ep(nc.sbuf_tensor(f"ddn{i}", [128, I], F16)) for i in range(2)]
            scg = [Ep(nc.sbuf_tensor(f"scg{i}", [128, 64], F32)) for i in range(2)]
            scd = [Ep(nc.sbuf_tensor(f"scd{i}", [128, 64], F32)) for i in range(2)]

            bAf = bA.bitcast(F32)
            bBf = bB.bitcast(F32)
            bDi = bD.bitcast(I32)
            WfR = Wf.bitcast(I32).rearrange("p (w e) -> p e w", e=8)

            def dequant_strip(nw, src_pk):
                DV.opn(lambda v: [
                    v.tensor_scalar(bA[:, 256 * j:256 * j + nw], src_pk[:, 0:nw],
                                    4 * j, 7, op0=ALU.logical_shift_right, op1=ALU.bitwise_and)
                    for j in range(8)] + [
                    v.tensor_scalar(bC[:, 256 * j:256 * j + nw], src_pk[:, 0:nw],
                                    28 - 4 * j, -2147483648,
                                    op0=ALU.logical_shift_left, op1=ALU.bitwise_and)
                    for j in range(8)], 16)
                tok_rd = DV.n
                DV.selfwait()
                DV.opn(lambda v: [
                    v.tensor_scalar(bB[:, 256 * j:256 * j + nw], bA[:, 256 * j:256 * j + nw],
                                    252.0, float(1 << 22), op0=ALU.add, op1=ALU.mult)
                    for j in range(8)], 8)
                DV.selfwait()
                DV.opn(lambda v: [
                    v.tensor_scalar(bAf[:, 256 * j:256 * j + nw], bBf[:, 256 * j:256 * j + nw],
                                    2.0, 1.0, op0=ALU.mult, op1=ALU.subtract)
                    for j in range(8)], 8)
                DV.selfwait()
                DV.opn(lambda v: [
                    v.tensor_tensor(bD[:, 256 * j:256 * j + nw], bAf[:, 256 * j:256 * j + nw],
                                    bBf[:, 256 * j:256 * j + nw], op=ALU.min)
                    for j in range(8)], 8)
                DV.selfwait()
                DV.opn(lambda v: [
                    v.tensor_tensor(WfR[:, j, 0:nw], bDi[:, 256 * j:256 * j + nw],
                                    bC[:, 256 * j:256 * j + nw], op=ALU.bitwise_or)
                    for j in range(8)], 8)
                DV.selfwait()
                return tok_rd

            pk_sched = [("g", ot, 0, WI_GU) for ot in range(NOT_GU)]
            NSTRIP_DN = (WI_DN + 255) // 256
            for ot_ in range(NOT_DN):
                for sti_ in range(NSTRIP_DN):
                    w0_ = 256 * sti_
                    pk_sched.append(("d", ot_, w0_, min(256, WI_DN - w0_)))
            pk_sched.append(("g", 0, 0, WI_GU))
            pktok = {}
            pkemit = [0]
            tok_pk_rd = {}
            tok_sc_used = {}

            def emit_pk(upto):
                while pkemit[0] <= min(upto, len(pk_sched) - 1):
                    k = pkemit[0]
                    if k >= 2:
                        GP.wait(DV, tok_pk_rd.get(k - 2, 0))
                    kind, ot_, w0_, nw_ = pk_sched[k]
                    if kind == "g":
                        GP.op(lambda g, ot_=ot_, p2=k % 2: g.dma_start(
                            out=pk[p2][:, 0:WI_GU], in_=gup[128 * ot_:128 * ot_ + 128, :]), inc=16)
                        if k < NOT_GU:
                            GP.wait(DV, tok_sc_used.get(("g", ot_ - 2), 0))
                            GP.op(lambda g, ot_=ot_, ps=ot_ % 2: g.dma_start(
                                out=scg[ps][:, 0:NGU], in_=gusT[128 * ot_:128 * ot_ + 128, :]), inc=16)
                    else:
                        GP.op(lambda g, ot_=ot_, w0_=w0_, nw_=nw_, p2=k % 2: g.dma_start(
                            out=pk[p2][:, 0:nw_], in_=dnp[128 * ot_:128 * ot_ + 128, w0_:w0_ + nw_]), inc=16)
                        if w0_ == 0:
                            GP.wait(DV, tok_sc_used.get(("d", ot_ - 2), 0))
                            GP.op(lambda g, ot_=ot_, ps=ot_ % 2: g.dma_start(
                                out=scd[ps][:, 0:NGD], in_=dnsT[128 * ot_:128 * ot_ + 128, :]), inc=16)
                    pktok[k] = GP.n
                    pkemit[0] += 1

            def pk_wait(k):
                emit_pk(k + 1)
                DV.wait(GP, pktok[min(k + 1, len(pk_sched) - 1)])

            scnt = 0
            tok_store = {}
            for ot in range(NOT_GU):
                p2 = scnt % 2
                pk_wait(scnt)
                tok_pk_rd[scnt] = dequant_strip(WI_GU, pk[p2])
                if ot >= 2:
                    DV.wait(GP, tok_store.get(("g", ot - 2), 0))
                tok_sc_used[("g", ot)] = DV.opn(lambda v, p2=p2, ps=ot % 2: [
                    v.tensor_scalar(dgu[p2][:, 128 * g2:128 * g2 + 128], Wf[:, 128 * g2:128 * g2 + 128],
                                    scg[ps][:, g2:g2 + 1], None, op0=ALU.mult)
                    for g2 in range(NGU)], NGU)
                DV.selfwait()
                scnt += 1
                GP.wait(DV, tok_sc_used[("g", ot)])
                tok_store[("g", ot)] = GP.op(lambda g, ot=ot, p2=p2: g.dma_start(
                    out=wgu_dense[128 * ot:128 * ot + 128, :], in_=dgu[p2][:, :]), inc=16)
            for ot in range(NOT_DN):
                p2d = ot % 2
                for sti in range(NSTRIP_DN):
                    w0 = 256 * sti
                    nw = min(256, WI_DN - w0)
                    p2 = scnt % 2
                    pk_wait(scnt)
                    tok_pk_rd[scnt] = dequant_strip(nw, pk[p2])
                    g0 = (8 * w0) // 128
                    ng = (8 * nw) // 128
                    if sti == 0 and ot >= 2:
                        DV.wait(GP, tok_store.get(("d", ot - 2), 0))
                    DV.opn(lambda v, p2d=p2d, g0=g0, ng=ng: [
                        v.tensor_scalar(ddn[p2d][:, 128 * (g0 + g2):128 * (g0 + g2) + 128],
                                        Wf[:, 128 * g2:128 * g2 + 128],
                                        scd[p2d][:, g0 + g2:g0 + g2 + 1], None, op0=ALU.mult)
                        for g2 in range(ng)], ng)
                    DV.selfwait()
                    if sti == NSTRIP_DN - 1:
                        tok_sc_used[("d", ot)] = DV.n
                    scnt += 1
                GP.wait(DV)
                tok_store[("d", ot)] = GP.op(lambda g, ot=ot, p2d=p2d: g.dma_start(
                    out=wdn_dense[128 * ot:128 * ot + 128, :], in_=ddn[p2d][:, :]), inc=16)
            emit_pk(len(pk_sched) - 1)
            GP.selfwait()
            barrier()

        with ExitStack() as ph:
            Ep = ph.enter_context
            xt = [Ep(nc.sbuf_tensor(f"xt{i}", [128, D], F32)) for i in range(2)]
            xnt = [Ep(nc.sbuf_tensor(f"xnt{i}", [128, D], F16)) for i in range(2)]
            NST = max(D // 512, 1)
            stats = Ep(nc.sbuf_tensor("stats", [128, 6 * NST], F32))
            mv = Ep(nc.sbuf_tensor("mv", [128, 2], F32))
            negm = Ep(nc.sbuf_tensor("negm", [128, 1], F32))
            lnv = Ep(nc.sbuf_tensor("lnv", [128, 1], F32))
            rstd = Ep(nc.sbuf_tensor("rstd", [128, 1], F32))
            txp = [Ep(nc.sbuf_tensor(f"txp{i}", [128, 2048], F16)) for i in range(2)]

            tok_xn_dv = {}
            tok_xn_store = {}
            tok_xload = {}

            def emit_xload(si):
                if si > SCH:
                    return
                if si >= 2:
                    GP.wait(DV, tok_xn_dv[si - 2])
                row = 128 * (si % SCH)
                GP.op(lambda g, row=row, p=si % 2: g.dma_start(
                    out=xt[p][:, :], in_=x_in[row:row + 128, :]), inc=16)
                tok_xload[si] = GP.n

            emit_xload(0)
            for si in range(SCH):
                p = si % 2
                emit_xload(si + 1)
                DV.wait(GP, tok_xload[min(si + 1, SCH)])
                if si >= 2:
                    DV.wait(SP, tok_xn_store[si - 2])
                DV.opn(lambda v, p=p: [
                    v.bn_stats(stats[:, 6 * i:6 * i + 6], xt[p][:, 512 * i:512 * i + min(512, D)])
                    for i in range(NST)], NST)
                DV.selfwait()
                DV.op(lambda v: v.bn_aggr(mv[:, :], stats[:, :]))
                DV.selfwait()
                DV.op(lambda v: v.tensor_scalar(negm[:, :], mv[:, 0:1], -1.0, None, op0=ALU.mult))
                AC.wait(DV)
                AC.op(lambda s: s.activation(lnv[:, :], mv[:, 1:2], AF.Ln, bias=epsLN[:, :]))
                AC.selfwait()
                AC.op(lambda s: s.activation(rstd[:, :], lnv[:, :], AF.Exp, scale=-0.5))
                DV.wait(AC)
                tok_xn_dv[si] = DV.op(lambda v, p=p: v.tensor_scalar(
                    xnt[p][:, :], xt[p][:, :], negm[:, :], rstd[:, :], op0=ALU.add, op1=ALU.mult))
                SP.wait(DV, tok_xn_dv[si])
                tok_xn_store[si] = SP.op(lambda s, si=si, p=p: s.dma_start(
                    out=xn_nat[128 * si:128 * si + 128, :], in_=xnt[p][:, :]), inc=16)

            SP.selfwait()
            HW = min(2048, S)
            NHALF = S // HW
            for dc in range(DCH):
                for hf in range(NHALF):
                    p = (dc * NHALF + hf) % 2
                    SP.op(lambda s, dc=dc, hf=hf, p=p: s.dma_start_transpose(
                        txp[p][:, 0:HW],
                        xn_nat[HW * hf:HW * hf + HW, 128 * dc:128 * dc + 128]), inc=16)
                    SP.selfwait()
                    SP.op(lambda s, dc=dc, hf=hf, p=p: s.dma_start(
                        out=xnT_nat[128 * dc:128 * dc + 128, HW * hf:HW * hf + HW],
                        in_=txp[p][:, 0:HW]), inc=16)
            SP.selfwait()
            barrier()

        with ExitStack() as ph:
            Ep = ph.enter_context
            wqT = [Ep(nc.sbuf_tensor(f"wqT{i}", [128, DCH * 128], F16)) for i in range(2)]
            wkT = [Ep(nc.sbuf_tensor(f"wkT{i}", [128, DCH * 128], F16)) for i in range(2)]
            wvT = [Ep(nc.sbuf_tensor(f"wvT{i}", [128, DCH * 128], F16)) for i in range(2)]
            xnS = [Ep(nc.sbuf_tensor(f"xnS{i}", [128, DCH * 512], F16)) for i in range(2)]
            QTs = Ep(nc.sbuf_tensor("QTs", [128, S], F16))
            KTs = Ep(nc.sbuf_tensor("KTs", [128, S], F16))
            Vs = Ep(nc.sbuf_tensor("Vs", [128, S], F16))
            Et = [Ep(nc.sbuf_tensor(f"Et{i}", [128, 2 * QT], F16)) for i in range(4)]
            lnA1 = Ep(nc.sbuf_tensor("lnA1", [128, QT], F32))
            lnA2 = Ep(nc.sbuf_tensor("lnA2", [128, QT], F32))
            bc1 = Ep(nc.sbuf_tensor("bc1", [128, QT], F32))
            bc2 = Ep(nc.sbuf_tensor("bc2", [128, QT], F32))
            t1s = Ep(nc.sbuf_tensor("t1s", [128, QT], F32))
            t2s = Ep(nc.sbuf_tensor("t2s", [128, QT], F32))
            vcb = Ep(nc.sbuf_tensor("vcb", [128, QT], F32))
            sqs = Ep(nc.sbuf_tensor("sqs", [128, 512], F32))
            ssA = Ep(nc.sbuf_tensor("ssA", [128, 4], F32))
            lnA = Ep(nc.sbuf_tensor("lnA", [128, 4], F32))
            rsA = Ep(nc.sbuf_tensor("rsA", [128, 4], F32))
            nrm = Ep(nc.sbuf_tensor("nrm", [128, 2 * SCH * 128], F16))
            woT = [Ep(nc.sbuf_tensor(f"woT{i}", [128, D], F16)) for i in range(3)]
            sev = Ep(nc.sbuf_tensor("sev", [128, D], F32))

            S0, S1c = 0, 1024
            AV1, AV2 = 2048, 2560
            D1C, D2C = 3072, 3584
            QC, KC, VC = 0, 512, 1024
            TRC = 1024

            xn_use = {}
            xn_tok = {}
            xcnt = 0
            tok_wT = {}
            wo_use = {}
            tok_nrm_last = {}
            tok_wo_mm = {}

            for pr in range(HC):
                pp = pr % 2
                if pr >= 2:
                    SP.wait(PE, tok_wT[("consumed", pr - 2)])
                SP.wait(GP, tok_wcast)
                SP.dman(lambda s, pr=pr, pp=pp: [
                    s.dma_start(
                        out=w_t[pp][:, 128 * dc:128 * dc + 128],
                        in_=w_b[128 * dc:128 * dc + 128, 128 * pr:128 * pr + 128])
                    for (w_t, w_b) in ((wqT, wq_bf), (wkT, wk_bf), (wvT, wv_bf))
                    for dc in range(DCH)], 3 * DCH)
                SP.op(lambda s: s.dma_start(out=ltmp[:, 0:128], in_=lq1), inc=16)
                tok_wT[pr] = SP.n

                tok_qkv_evac = 0
                for st in range(NQT):
                    sp2 = xcnt % 2
                    for k in (xcnt, xcnt + 1):
                        if k not in xn_tok and k <= HC * NQT:
                            if k >= 2:
                                GP.wait(PE, xn_use.get(k - 2, 0))
                            st_k = k % NQT
                            GP.op(lambda g, st_k=st_k, p2=k % 2: g.dma_start(
                                out=xnS[p2][:, :].rearrange("p (c s) -> p c s", c=DCH),
                                in_=xnT_nat.rearrange("(c p) s -> p c s", p=128)[:, :, QT * st_k:QT * st_k + QT]),
                            inc=16)
                            xn_tok[k] = GP.n
                    PE.wait(GP, xn_tok[min(xcnt + 1, HC * NQT)])
                    PE.wait(SP, tok_wT[pr])
                    if st == 0:
                        PE.wait(DV)
                        PE.wait(AC)
                    else:
                        PE.wait(DV, tok_qkv_evac)
                    PE.mmgroup(lambda t, pp=pp, sp2=sp2: [
                        t.matmul(PP[:, QC:QC + 512], wqT[pp][:, 128 * dc:128 * dc + 128],
                                 xnS[sp2][:, 512 * dc:512 * dc + 512],
                                 start=(dc == 0), stop=(dc == DCH - 1))
                        for dc in range(DCH)])
                    PE.mmgroup(lambda t, pp=pp, sp2=sp2: [
                        t.matmul(PP[:, KC:KC + 512], wkT[pp][:, 128 * dc:128 * dc + 128],
                                 xnS[sp2][:, 512 * dc:512 * dc + 512],
                                 start=(dc == 0), stop=(dc == DCH - 1))
                        for dc in range(DCH)])
                    PE.mmgroup(lambda t, pp=pp, sp2=sp2: [
                        t.matmul(PP[:, VC + 128 * q:VC + 128 * q + 128],
                                 xnS[sp2][:, 512 * dc + 128 * q:512 * dc + 128 * q + 128],
                                 wvT[pp][:, 128 * dc:128 * dc + 128],
                                 start=(dc == 0), stop=(dc == DCH - 1))
                        for q in range(4) for dc in range(DCH)])
                    xn_use[xcnt] = PE.n
                    xcnt += 1
                    if st == NQT - 1:
                        tok_wT[("consumed", pr)] = PE.n
                    DV.wait(PE)
                    tok_qkv_evac = DV.opn(lambda v, st=st: [
                        v.tensor_copy(QTs[:, QT * st:QT * st + QT], PP[:, QC:QC + 512]),
                        v.tensor_copy(KTs[:, QT * st:QT * st + QT], PP[:, KC:KC + 512]),
                        v.tensor_copy(Vs[:, QT * st:QT * st + QT], PP[:, VC:VC + 512]),
                    ], 3)
                tok_qkv_dv = DV.n

                for qt in range(NQT):
                    tok_scores = {}
                    tok_exp = {}
                    tok_cons = {}

                    def emit_dav(kc, tok_exp=tok_exp, tok_cons=tok_cons):
                        PE.wait(AC, tok_exp[kc])
                        ep4 = kc % 4
                        PE.mmgroup(lambda t, kc=kc, ep4=ep4: [
                            t.matmul(PP[:, D1C:D1C + QT], ones128[:, :], Et[ep4][:, 0:QT],
                                     start=(kc == 0), stop=(kc == KCH - 1)),
                            t.matmul(PP[:, D2C:D2C + QT], ones128[:, :], Et[ep4][:, QT:2 * QT],
                                     start=(kc == 0), stop=(kc == KCH - 1)),
                            t.matmul(PP[:, AV1:AV1 + QT], Vs[:, 128 * kc:128 * kc + 128],
                                     Et[ep4][:, 0:QT], start=(kc == 0), stop=(kc == KCH - 1)),
                            t.matmul(PP[:, AV2:AV2 + QT], Vs[:, 128 * kc:128 * kc + 128],
                                     Et[ep4][:, QT:2 * QT], start=(kc == 0), stop=(kc == KCH - 1)),
                        ])
                        tok_cons[kc] = PE.n

                    for kc in range(KCH):
                        sp2 = kc % 2
                        if kc == 0:
                            PE.wait(DV, tok_qkv_dv)
                            if qt > 0:
                                PE.wait(DV, tok_nrm_last[(pr, qt - 1)])
                        if kc >= 2:
                            PE.wait(AC, tok_exp[kc - 2])
                        PE.mmgroup(lambda t, kc=kc, qt=qt, sp2=sp2: [
                            t.matmul(PP[:, 1024 * sp2:1024 * sp2 + 512],
                                     KTs[0:64, 128 * kc:128 * kc + 128],
                                     QTs[0:64, QT * qt:QT * qt + QT],
                                     start=True, stop=True, tile_position=(0, 0)),
                            t.matmul(PP[:, 1024 * sp2 + 512:1024 * sp2 + 1024],
                                     KTs[64:128, 128 * kc:128 * kc + 128],
                                     QTs[64:128, QT * qt:QT * qt + QT],
                                     start=True, stop=True, tile_position=(64, 0)),
                        ])
                        tok_scores[kc] = PE.n
                        AC.wait(PE, tok_scores[kc])
                        if kc >= 4:
                            AC.wait(PE, tok_cons[kc - 4])
                        tok_exp[kc] = AC.op(lambda s, sp2=sp2, ep4=kc % 4: s.activation(
                            Et[ep4][:, :], PP[:, 1024 * sp2:1024 * sp2 + 2 * QT],
                            AF.Exp, scale=SM_SCALE))
                        if kc >= 1:
                            emit_dav(kc - 1)
                    emit_dav(KCH - 1)

                    AC.wait(PE, tok_cons[KCH - 1])
                    AC.opn(lambda s: [
                        s.activation(lnA1[:, :], PP[:, D1C:D1C + QT], AF.Ln),
                        s.activation(lnA2[:, :], PP[:, D2C:D2C + QT], AF.Ln),
                    ], 2)
                    AC.selfwait()
                    AC.opn(lambda s: [
                        s.activation(bc1[:, :], lnA1[:, :], AF.Exp, scale=-1.0),
                        s.activation(bc2[:, :], lnA2[:, :], AF.Exp, scale=-1.0),
                    ], 2)
                    tok_alpha = AC.n
                    DV.wait(AC, tok_alpha)
                    DV.wait(DV, tok_nlam)
                    DV.opn(lambda v: [
                        v.tensor_tensor(t1s[:, :], PP[:, AV1:AV1 + QT], bc1[:, :], op=ALU.mult),
                        v.scalar_tensor_tensor(t2s[:, :], PP[:, AV2:AV2 + QT], nlam128[:, :],
                                               bc2[:, :], op0=ALU.mult, op1=ALU.mult),
                    ], 2)
                    DV.selfwait()
                    tok_comb = DV.op(lambda v: v.tensor_tensor(vcb[:, :], t1s[:, :], t2s[:, :], op=ALU.add))
                    PE.wait(DV, tok_comb)
                    PE.mmgroup(lambda t: [
                        t.transpose(PP[:, TRC + 128 * ti:TRC + 128 * ti + 128],
                                    vcb[:, 128 * ti:128 * ti + 128], identf[:, :])
                        for ti in range(NT)])
                    AC.wait(PE)
                    AC.opn(lambda s: [
                        s.activation(sqs[:, 128 * ti:128 * ti + 128],
                                     PP[:, TRC + 128 * ti:TRC + 128 * ti + 128],
                                     AF.Square, accum_out=ssA[:, ti:ti + 1])
                        for ti in range(NT)], NT)
                    AC.selfwait()
                    AC.opn(lambda s: [
                        s.activation(lnA[:, ti:ti + 1], ssA[:, ti:ti + 1], AF.Ln,
                                     scale=1.0 / 128.0, bias=epsRMS[:, :])
                        for ti in range(NT)], NT)
                    AC.selfwait()
                    AC.opn(lambda s: [
                        s.activation(rsA[:, ti:ti + 1], lnA[:, ti:ti + 1], AF.Exp, scale=-0.5)
                        for ti in range(NT)], NT)
                    DV.wait(AC)
                    if qt == 0 and pr >= 2:
                        DV.wait(PE, tok_wo_mm[(pr - 2) | 1])
                    po = (pr % 2) * SCH * 128
                    tok_nrm_last[(pr, qt)] = DV.opn(lambda v, qt=qt, po=po: [
                        v.tensor_scalar(nrm[:, po + (qt * NT + ti) * 128:po + (qt * NT + ti) * 128 + 128],
                                        PP[:, TRC + 128 * ti:TRC + 128 * ti + 128],
                                        rsA[:, ti:ti + 1], 1.0 - LAMBDA_INIT,
                                        op0=ALU.mult, op1=ALU.mult)
                        for ti in range(NT)], NT)

                if pr % 2 == 1 or pr == HC - 1:
                    duo = (pr - 1, pr) if pr % 2 == 1 else (pr,)
                    combos = [(pa, sh) for pa in duo for sh in (0, 1)]
                    ngrp = (len(combos) + 1) // 2
                    nwo = ngrp * CC
                    wotok = {}

                    def emit_woload(k, nwo=nwo):
                        if k > nwo or k in wotok:
                            return
                        if k >= 3:
                            SP.wait(PE, wo_use[k - 3])
                        cck = k % CC
                        SP.op(lambda s, cck=cck, wp=k % 3: s.dma_start(
                            out=woT[wp][:, :], in_=wo_bf[128 * cck:128 * cck + 128, :]), inc=16)
                        wotok[k] = SP.n

                    woci = 0
                    prev_sstore = 0
                    emit_woload(0)
                    for g0 in range(0, len(combos), 2):
                        sub = combos[g0:g0 + 2]
                        for cc in range(CC):
                            wp = woci % 3
                            emit_woload(woci + 1)
                            PE.wait(SP, wotok[min(woci + 1, nwo)])
                            if cc == 0:
                                PE.wait(DV)
                            def womms(t, sub=sub, cc=cc, wp=wp):
                                ins = []
                                for ci, (pa, sh) in enumerate(sub):
                                    po2 = (pa % 2) * SCH * 128
                                    lhs = nrm[:, po2 + (sh * CC + cc) * 128:po2 + (sh * CC + cc) * 128 + 128]
                                    for j in range(NJ):
                                        ins.append(t.matmul(
                                            PP[:, 2048 * ci + JT * j:2048 * ci + JT * j + JT],
                                            lhs, woT[wp][:, JT * j:JT * j + JT],
                                            start=(cc == 0), stop=(cc == CC - 1)))
                                return ins
                            PE.mmgroup(womms)
                            wo_use[woci] = PE.n
                            woci += 1
                        tok_wo_mm[pr] = PE.n
                        for ci, (pa, sh) in enumerate(sub):
                            ul = 2 * pa + sh
                            DV.wait(PE)
                            DV.wait(GP, prev_sstore)
                            DV.op(lambda v, ci=ci: v.tensor_copy(sev[:, :], PP[:, 2048 * ci:2048 * ci + D]))
                            GP.wait(DV)
                            prev_sstore = GP.op(lambda g, ul=ul: g.dma_start(
                                out=s_stash[128 * ul:128 * ul + 128, :], in_=sev[:, :]), inc=16)
            barrier()

        with ExitStack() as ph:
            Ep = ph.enter_context
            xnU = Ep(nc.sbuf_tensor("xnU", [128, 4 * DCH * 128], F16))
            wsl = [Ep(nc.sbuf_tensor(f"wsl{i}", [128, 512], F16)) for i in range(4)]
            gsil = Ep(nc.sbuf_tensor("gsil", [128, 4 * 512], F16))
            hsb = Ep(nc.sbuf_tensor("hsb", [128, 4 * 512], F16))
            hT = Ep(nc.sbuf_tensor("hT", [128, 4 * ICH * 128], F16))
            ssF = Ep(nc.sbuf_tensor("ssF", [128, 1], F32))
            lnF = Ep(nc.sbuf_tensor("lnF", [128, 1], F32))
            rsF = Ep(nc.sbuf_tensor("rsF", [128, 1], F32))
            ybuf = [Ep(nc.sbuf_tensor(f"ybuf{i}", [128, D], F32)) for i in range(4)]
            gbuf = Ep(nc.sbuf_tensor("gbuf", [128, D], F32))
            sbuf2 = [Ep(nc.sbuf_tensor(f"sbuf2{i}", [128, D], F32)) for i in range(2)]
            xbuf = [Ep(nc.sbuf_tensor(f"xbuf{i}", [128, D], F32)) for i in range(2)]
            obuf = Ep(nc.sbuf_tensor("obuf", [128, D], F32))

            NUG = (U + 3) // 4
            load_fns = []
            loadtok = {}
            consumed = {}
            emitted = [0]
            LA = 3

            def emit_load(idx):
                while emitted[0] <= min(idx, len(load_fns) - 1):
                    k = emitted[0]
                    if k >= 4:
                        SP.wait(PE, consumed[k - 4])
                    load_fns[k]()
                    loadtok[k] = SP.n
                    emitted[0] += 1

            def mk_gu_load(dc, rb, wp):
                def f():
                    SP.op(lambda s: s.dma_start_transpose(
                        wsl[wp][:, :], wgu_dense[rb:rb + 512, 128 * dc:128 * dc + 128]), inc=16)
                return f

            def mk_dn_load(ic, j, wp):
                def f():
                    SP.op(lambda s: s.dma_start_transpose(
                        wsl[wp][:, 0:JT], wdn_dense[JT * j:JT * j + JT, 128 * ic:128 * ic + 128]), inc=16)
                return f

            li = 0
            for ug_ in range(NUG):
                for og_ in range(OG):
                    for ph2_ in (0, 1):
                        rb_ = 512 * og_ + (I if ph2_ else 0)
                        for dc_ in range(DCH):
                            load_fns.append(mk_gu_load(dc_, rb_, li % 4)); li += 1
                for j_ in range(NJ):
                    for ic_ in range(ICH):
                        load_fns.append(mk_dn_load(ic_, j_, li % 4)); li += 1
            NLOAD = li
            load_fns.append(mk_dn_load(0, 0, li % 4)); li += 1

            def pe_wait_load(k):
                emit_load(k + LA)
                PE.wait(SP, loadtok[min(k + 1, NLOAD)])

            tok_xnU_used = 0
            tok_g_ac = 0
            tok_gbuf_free = 0
            tok_out_dv = 0
            tok_obuf_store = 0
            sxtok = {}
            sx_guard = {}
            tok_ydump = 0
            gi = 0

            for ug in range(NUG):
                us = list(range(4 * ug, min(4 * ug + 4, U)))
                nu = len(us)
                if ug > 0:
                    SP.wait(PE, tok_xnU_used)
                SP.dman(lambda s, us=us: [
                    s.dma_start_transpose(
                        xnU[:, (ui * DCH + dc) * 128:(ui * DCH + dc) * 128 + 128],
                        xn_nat.rearrange("(m blk) d -> blk m d", blk=BLK)[u, :, 128 * dc:128 * dc + 128])
                    for ui, u in enumerate(us) for dc in range(DCH)], nu * DCH)
                tok_xnU = SP.n

                tok_h_dv = 0
                tok_hTdv = 0
                tok_hT_pe = 0
                tok_gsil_ac = 0
                tok_yevac = {}
                for og in range(OG):
                    for ph2 in (0, 1):
                        for dc in range(DCH):
                            pe_wait_load(gi)
                            if dc == 0:
                                PE.wait(SP, tok_xnU)
                                if ph2 == 0 and (og > 0 or ug > 0):
                                    PE.wait(DV, tok_hTdv if og > 0 else tok_ylast)
                                if ph2 == 1:
                                    PE.wait(AC, tok_gsil_ac)
                            pbase = 2048 * ph2
                            wp = gi % 4
                            PE.mmgroup(lambda t, us=us, dc=dc, wp=wp, pbase=pbase: [
                                t.matmul(PP[:, pbase + 512 * ui:pbase + 512 * ui + 512],
                                         xnU[:, (ui * DCH + dc) * 128:(ui * DCH + dc) * 128 + 128],
                                         wsl[wp][:, :], start=(dc == 0), stop=(dc == DCH - 1))
                                for ui in range(len(us))])
                            consumed[gi] = PE.n
                            gi += 1
                        if ph2 == 0:
                            AC.wait(PE)
                            AC.wait(DV, tok_h_dv)
                            tok_gsil_ac = AC.opn(lambda s, nu=nu: [
                                s.activation(gsil[:, 512 * ui:512 * ui + 512],
                                             PP[:, 512 * ui:512 * ui + 512], AF.Silu)
                                for ui in range(nu)], nu)
                        else:
                            DV.wait(PE)
                            DV.wait(AC, tok_gsil_ac)
                            DV.wait(PE, tok_hT_pe)
                            tok_h_dv = DV.opn(lambda v, nu=nu: [
                                v.scalar_tensor_tensor(
                                    hsb[:, 512 * ui:512 * ui + 512],
                                    PP[:, 2048 + 512 * ui:2048 + 512 * ui + 512], 1.0 / 64.0,
                                    gsil[:, 512 * ui:512 * ui + 512],
                                    op0=ALU.mult, op1=ALU.mult)
                                for ui in range(nu)], nu)
                    PE.wait(DV, tok_h_dv)
                    PE.mmgroup(lambda t, nu=nu: [
                        t.transpose(PPh[:, 128 * (4 * ui + ti):128 * (4 * ui + ti) + 128],
                                    hsb[:, 512 * ui + 128 * ti:512 * ui + 128 * ti + 128], identh[:, :])
                        for ui in range(nu) for ti in range(4)])
                    tok_hT_pe = PE.n
                    DV.wait(PE, tok_hT_pe)
                    tok_hTdv = DV.opn(lambda v, og=og, nu=nu: [
                        v.tensor_copy(hT[:, (ui * ICH + og * 4 + ti) * 128:(ui * ICH + og * 4 + ti) * 128 + 128],
                                      PPh[:, 128 * (4 * ui + ti):128 * (4 * ui + ti) + 128])
                        for ui in range(nu) for ti in range(4)], nu * 4)
                tok_xnU_used = PE.n
                if debug and ug == 0:
                    SP.wait(DV, tok_hTdv)
                    SP.dman(lambda s, nu=nu: [
                        s.dma_start(out=h_dbg[128 * ui:128 * ui + 128, :], in_=hT[:, ui * ICH * 128:(ui + 1) * ICH * 128])
                        for ui in range(nu)], nu)

                for j in range(NJ):
                    q = j % 2
                    for ic in range(ICH):
                        pe_wait_load(gi)
                        if ic == 0:
                            if j == 0:
                                PE.wait(DV, tok_hTdv)
                            if j >= 2:
                                PE.wait(DV, tok_yevac[j - 2])
                            if ug > 0 and j < 2:
                                PE.wait(AC, tok_g_ac)
                        wp = gi % 4
                        PE.mmgroup(lambda t, us=us, j=j, ic=ic, wp=wp, q=q, nu=nu: [
                            t.matmul(PP[:, 2048 * q + 512 * ui:2048 * q + 512 * ui + JT],
                                     hT[:, (ui * ICH + ic) * 128:(ui * ICH + ic) * 128 + 128],
                                     wsl[wp][:, 0:JT], start=(ic == 0), stop=(ic == ICH - 1))
                            for ui in range(nu)])
                        consumed[gi] = PE.n
                        gi += 1
                    DV.wait(PE)
                    if ug > 0 and j == 0:
                        DV.wait(AC, tok_g_ac)
                        DV.wait(SP, tok_ydump)
                    tok_yevac[j] = DV.opn(lambda v, j=j, q=q, nu=nu: [
                        v.tensor_copy(ybuf[ui][:, JT * j:JT * j + JT],
                                      PP[:, 2048 * q + 512 * ui:2048 * q + 512 * ui + JT])
                        for ui in range(nu)], nu)
                tok_ylast = tok_yevac[NJ - 1]

                if debug:
                    SP.wait(DV, tok_ylast)
                    SP.dman(lambda s, us=us: [
                        s.dma_start(out=y_dbg[128 * u:128 * u + 128, :], in_=ybuf[ui][:, :])
                        for ui, u in enumerate(us)], nu)
                    tok_ydump = SP.n

                for ui, u in enumerate(us):
                    ul = u
                    for uk in (u, u + 1):
                        if uk not in sxtok and uk <= U:
                            GP.wait(DV, sx_guard.get(uk - 2, 0))
                            row = 128 * (uk % U)
                            GP.op(lambda g, row=row, p=uk % 2: g.dma_start(
                                out=sbuf2[p][:, :], in_=s_stash[row:row + 128, :]), inc=16)
                            GP.op(lambda g, p=uk % 2, blk=uk % U: g.dma_start(
                                out=xbuf[p][:, :],
                                in_=x_in.rearrange("(m blk) d -> blk m d", blk=BLK)[blk, :, :]), inc=16)
                            sxtok[uk] = GP.n
                    AC.wait(DV, tok_ylast)
                    AC.wait(DV, tok_gbuf_free)
                    AC.op(lambda s, ui=ui: s.activation(
                        gbuf[:, :], ybuf[ui][:, :], AF.Square, accum_out=ssF[:, :]))
                    AC.selfwait()
                    AC.op(lambda s: s.activation(lnF[:, :], ssF[:, :], AF.Ln,
                                                 scale=1.0 / D, bias=epsFFN[:, :]))
                    AC.selfwait()
                    AC.op(lambda s: s.activation(rsF[:, :], lnF[:, :], AF.Exp, scale=-0.5))
                    AC.selfwait()
                    tok_g_ac = AC.op(lambda s, ui=ui: s.activation(
                        gbuf[:, :], ybuf[ui][:, :], AF.Silu, scale=rsF[:, :]))
                    DV.wait(AC, tok_g_ac)
                    DV.wait(GP, sxtok[u + 1])
                    DV.wait(GP, tok_obuf_store)
                    tok_gbuf_free = DV.op(lambda v, p=u % 2: v.tensor_tensor(
                        obuf[:, :], gbuf[:, :], sbuf2[p][:, :], op=ALU.mult))
                    DV.selfwait()
                    tok_out_dv = DV.op(lambda v, p=u % 2: v.tensor_tensor(
                        obuf[:, :], obuf[:, :], xbuf[p][:, :], op=ALU.add))
                    sx_guard[u] = tok_out_dv
                    GP.wait(DV, tok_out_dv)
                    tok_obuf_store = GP.op(lambda g, ul=ul: g.dma_start(
                        out=out_c[128 * ul:128 * ul + 128, :], in_=obuf[:, :]), inc=16)
            emit_load(len(load_fns) - 1)
            GP.selfwait()
            GP.wait(SP)
            SP.selfwait()
            flush_all()
    return nc



def shard_inputs(cfg, inputs, n_cores=8):
    c = _cfg(cfg)
    S, D, HC, BLK, SH = c["S"], c["D"], c["HC"], c["BLK"], c["SH"]
    x = np.ascontiguousarray(inputs["L0_attn_x"])
    wq, wk, wv, wo = (inputs["L0_attn_wq"], inputs["L0_attn_wk"],
                      inputs["L0_attn_wv"], inputs["L0_attn_wo"])
    maps = []
    for core in range(n_cores):
        b = core // 2
        side = core % 2
        q = np.arange(S) // BLK
        k = np.arange(S) % BLK
        src = q * BLK + (k + side * SH) % BLK
        cp = np.arange(D)
        colperm = (cp // BLK) * BLK + (cp % BLK + side * SH) % BLK
        h0 = HC * side
        m = {
            "x_in": x[b][src],
            "wq_in": wq[128 * h0:128 * (h0 + HC)].T,
            "wk_in": wk[128 * h0:128 * (h0 + HC)].T,
            "wv_in": wv[128 * h0:128 * (h0 + HC)].T,
            "wo_in": wo[:, colperm].T,
            "lq1": np.asarray(inputs["L0_attn_lambda_q1"]).reshape(1, 128),
            "lq2": np.asarray(inputs["L0_attn_lambda_q2"]).reshape(1, 128),
            "lk1": np.asarray(inputs["L0_attn_lambda_k1"]).reshape(1, 128),
            "lk2": np.asarray(inputs["L0_attn_lambda_k2"]).reshape(1, 128),
            "gup": inputs["L0_ffn_gate_up_packed"],
            "gusT": inputs["L0_ffn_gate_up_scales"].T,
            "dnp": inputs["L0_ffn_down_packed"],
            "dnsT": inputs["L0_ffn_down_scales"].T,
        }
        maps.append({k2: np.ascontiguousarray(v) for k2, v in m.items()})
    return maps


def unshard_output(cfg, outs, n_cores=8):
    c = _cfg(cfg)
    S, D, U, BLK, SH = c["S"], c["D"], c["U"], c["BLK"], c["SH"]
    B = n_cores // 2
    out = np.empty((B, S, D), dtype=np.float32)
    m = np.arange(128)
    for core in range(n_cores):
        b = core // 2
        side = core % 2
        oc = outs[core]
        for ul in range(U):
            rows = BLK * m + ul + side * SH
            out[b, rows, :] = oc[128 * ul:128 * ul + 128, :]
    return out


def kernel(**inputs):
    cfg = REAL
    in_maps = shard_inputs(cfg, inputs)
    nc = build_nc(cfg)
    res = run_bass_kernel_spmd(nc, in_maps, core_ids=list(range(8)))
    return unshard_output(cfg, [r["out_c"] for r in res.results])


# version 12
# speedup vs baseline: 2.3104x; 1.1970x over previous
import numpy as np
import ml_dtypes
from contextlib import ExitStack

import concourse.bass as bass
import concourse.mybir as mybir
from concourse.bass_utils import run_bass_kernel_spmd

F32 = mybir.dt.float32
BF16 = mybir.dt.bfloat16
F16 = mybir.dt.float16
I32 = mybir.dt.int32
AF = mybir.ActivationFunctionType
ALU = mybir.AluOpType

LN_EPS = 1e-5
RMS_EPS = 1.1920928955078125e-07
FFN_EPS = 1e-06
LAMBDA_INIT = 0.8
SM_SCALE = float(1.0 / np.sqrt(128.0))

REAL = dict(S=4096, D=2048, HG=16, HC=8, I=5632)


def _cfg(cfg):
    S, D, HG, HC, I = cfg["S"], cfg["D"], cfg["HG"], cfg["HC"], cfg["I"]
    assert D == HG * 128 and S == 2 * D and I % 512 == 0
    c = dict(cfg)
    NC = cfg.get("NC", 8)
    WD = min(512, D)
    NGUU = (2 * I) // 512
    NDU = D // WD
    c.update(
        BLK=S // 128, SH=2 * HC, DCH=D // 128, KCH=S // 128, SCH=S // 128,
        QT=512, NQT=S // 512, U=2 * HC, JT=min(512, D), NJ=D // min(512, D),
        OG=I // 512, ICH=I // 128, NGU=D // 128, NGD=I // 128, CC=(S // 2) // 128,
        NC=NC, WD=WD, NGUU=NGUU, NDU=NDU,
        S_G=-(-NGUU // NC), S_D=-(-NDU // NC),
    )
    return c


class Stream:

    CHUNK = 400

    def __init__(self, block, engname, sem):
        self.block = block
        self.engname = engname
        self.sem = sem
        self.n = 0
        self.pend = []

    def raw(self, f):
        self.pend.append(f)

    def flush(self):
        while self.pend:
            chunk = self.pend[:self.CHUNK]
            self.pend = self.pend[self.CHUNK:]
            getattr(self.block, self.engname)(
                lambda e, chunk=chunk: [g(e) for g in chunk])

    def op(self, f, inc=1):
        def g(e):
            f(e).then_inc(self.sem, inc)
        self.raw(g)
        self.n += inc
        return self.n

    def opn(self, f, n):
        def g(e):
            inss = f(e)
            assert len(inss) == n, (self.engname, len(inss), n)
            for ins in inss:
                ins.then_inc(self.sem, 1)
        self.raw(g)
        self.n += n
        return self.n

    def dman(self, f, n):
        def g(e):
            inss = f(e)
            assert len(inss) == n, (self.engname, len(inss), n)
            for ins in inss:
                ins.then_inc(self.sem, 16)
        self.raw(g)
        self.n += 16 * n
        return self.n

    def mmgroup(self, f):
        def g(e):
            f(e)[-1].then_inc(self.sem, 1)
        self.raw(g)
        self.n += 1
        return self.n

    def wait(self, other, tok=None):
        t = other.n if tok is None else tok
        if t > 0:
            self.raw(lambda e: e.wait_ge(other.sem, t))

    def selfwait(self):
        if self.n > 0:
            self.raw(lambda e, n=self.n: e.wait_ge(self.sem, n))


def build_nc(cfg, debug=False):
    c = _cfg(cfg)
    S, D, HC, I = c["S"], c["D"], c["HC"], c["I"]
    DCH, KCH, SCH, QT, NQT = c["DCH"], c["KCH"], c["SCH"], c["QT"], c["NQT"]
    U, NJ, JT, OG, ICH = c["U"], c["NJ"], c["JT"], c["OG"], c["ICH"]
    NGU, NGD, CC, BLK = c["NGU"], c["NGD"], c["CC"], c["BLK"]
    NC, WD, S_G, S_D = c["NC"], c["WD"], c["S_G"], c["S_D"]
    I2 = 2 * I
    WI_GU = D // 8
    WI_DN = I // 8
    NOT_GU = I2 // 128
    NOT_DN = D // 128
    NT = QT // 128

    nc = bass.Bass("TRN2", target_bir_lowering=False)
    dk = "ExternalOutput" if debug else "Internal"

    x_in = nc.dram_tensor("x_in", [S, D], F32, kind="ExternalInput").ap()
    wq_in = nc.dram_tensor("wq_in", [D, HC * 128], F32, kind="ExternalInput").ap()
    wk_in = nc.dram_tensor("wk_in", [D, HC * 128], F32, kind="ExternalInput").ap()
    wv_in = nc.dram_tensor("wv_in", [D, HC * 128], F32, kind="ExternalInput").ap()
    wo_in = nc.dram_tensor("wo_in", [D, D], F32, kind="ExternalInput").ap()
    lq1 = nc.dram_tensor("lq1", [1, 128], F32, kind="ExternalInput").ap()
    lq2 = nc.dram_tensor("lq2", [1, 128], F32, kind="ExternalInput").ap()
    lk1 = nc.dram_tensor("lk1", [1, 128], F32, kind="ExternalInput").ap()
    lk2 = nc.dram_tensor("lk2", [1, 128], F32, kind="ExternalInput").ap()
    gup = nc.dram_tensor("gup", [I2, WI_GU], I32, kind="ExternalInput").ap()
    gusT = nc.dram_tensor("gusT", [I2, NGU], F16, kind="ExternalInput").ap()
    dnp = nc.dram_tensor("dnp", [D, WI_DN], I32, kind="ExternalInput").ap()
    dnsT = nc.dram_tensor("dnsT", [D, NGD], F16, kind="ExternalInput").ap()
    out_c = nc.dram_tensor("out_c", [U * 128, D], F32, kind="ExternalOutput").ap()

    wq_bf = nc.dram_tensor("wq_bf", [D, HC * 128], F16, kind="Internal").ap()
    wk_bf = nc.dram_tensor("wk_bf", [D, HC * 128], F16, kind="Internal").ap()
    wv_bf = nc.dram_tensor("wv_bf", [D, HC * 128], F16, kind="Internal").ap()
    wo_bf = nc.dram_tensor("wo_bf", [D, D], F16, kind="Internal").ap()
    xn_nat = nc.dram_tensor("xn_nat", [S, D], F16, kind=dk).ap()
    xnT_nat = nc.dram_tensor("xnT_nat", [D, S], F16, kind=dk).ap()
    s_stash = nc.dram_tensor("s_stash", [U * 128, D], F32, kind=dk).ap()
    wgu_dense = nc.dram_tensor("wgu_dense", [I2, D], F16, kind=dk).ap()
    wdn_dense = nc.dram_tensor("wdn_dense", [D, I], F16, kind=dk).ap()
    y_dbg = nc.dram_tensor("y_dbg", [U * 128, D], F32, kind=dk).ap() if debug else None
    h_dbg = nc.dram_tensor("h_dbg", [4 * 128, ICH * 128], F16, kind=dk).ap() if debug else None

    with ExitStack() as top:
        E = top.enter_context
        ident = E(nc.sbuf_tensor("ident", [128, 128], BF16))
        identh = E(nc.sbuf_tensor("identh", [128, 128], F16))
        identf = E(nc.sbuf_tensor("identf", [128, 128], F32))
        identi = E(nc.sbuf_tensor("identi", [128, 128], I32))
        onesb = E(nc.sbuf_tensor("onesb", [128, 1], F16))
        ones128 = E(nc.sbuf_tensor("ones128", [128, 128], F16))
        nlam128 = E(nc.sbuf_tensor("nlam128", [128, 1], F32))
        ones1 = E(nc.sbuf_tensor("ones1", [1, 128], F32))
        epsLN = E(nc.sbuf_tensor("epsLN", [128, 1], F32))
        epsRMS = E(nc.sbuf_tensor("epsRMS", [128, 1], F32))
        epsFFN = E(nc.sbuf_tensor("epsFFN", [128, 1], F32))
        lamt = E(nc.sbuf_tensor("lamt", [1, 1], F32))
        ltmp = E(nc.sbuf_tensor("ltmp", [1, 768], F32))
        lred = E(nc.sbuf_tensor("lred", [1, 4], F32))

        PP = E(nc.psum_tensor("PP", [128, 4096], F32))
        PPb = PP.bitcast(BF16)
        PPh = PP.bitcast(F16)

        sems = {k: E(nc.semaphore(k)) for k in ["pe", "ac", "dv", "gp", "sp", "ag"]}
        block = E(nc.Block())

        PE = Stream(block, "tensor", sems["pe"])
        AC = Stream(block, "scalar", sems["ac"])
        DV = Stream(block, "vector", sems["dv"])
        GP = Stream(block, "gpsimd", sems["gp"])
        SP = Stream(block, "sync", sems["sp"])

        ALLS = None

        def flush_all():
            for s in (PE, AC, DV, GP, SP):
                s.flush()

        def barrier():
            streams = [PE, AC, DV, GP, SP]
            for s in streams:
                for o in streams:
                    if o is not s:
                        s.wait(o)
            flush_all()

        GP.dman(lambda g: [
            g.dma_start(out=wq_bf, in_=wq_in),
            g.dma_start(out=wk_bf, in_=wk_in),
            g.dma_start(out=wv_bf, in_=wv_in),
            g.dma_start(out=wo_bf, in_=wo_in),
        ], 4)
        tok_wcast = GP.n
        GP.op(lambda g: g.iota(identi[:, :], pattern=[[-1, 128]], base=0, channel_multiplier=1))
        tok_iota = GP.n
        SP.dman(lambda s: [
            s.dma_start(out=ltmp[:, 0:128], in_=lq1),
            s.dma_start(out=ltmp[:, 128:256], in_=lk1),
            s.dma_start(out=ltmp[:, 256:384], in_=lq2),
            s.dma_start(out=ltmp[:, 384:512], in_=lk2),
        ], 4)
        tok_lload = SP.n

        DV.wait(GP, tok_iota)
        DV.opn(lambda v: [
            v.memset(onesb[:, :], 1.0),
            v.memset(ones128[:, :], 1.0),
            v.memset(ones1[:, :], 1.0),
            v.memset(epsLN[:, :], LN_EPS),
            v.memset(epsRMS[:, :], RMS_EPS),
            v.memset(epsFFN[:, :], FFN_EPS / 4096.0),
            v.tensor_scalar(ident[:, :], identi[:, :], 0, None, op0=ALU.is_equal),
            v.tensor_scalar(identh[:, :], identi[:, :], 0, None, op0=ALU.is_equal),
            v.tensor_scalar(identf[:, :], identi[:, :], 0, None, op0=ALU.is_equal),
        ], 9)
        DV.wait(SP, tok_lload)
        DV.wait(GP, tok_wcast)
        DV.opn(lambda v: [
            v.tensor_tensor(ltmp[:, 512:640], ltmp[:, 0:128], ltmp[:, 128:256], op=ALU.mult),
            v.tensor_tensor(ltmp[:, 640:768], ltmp[:, 256:384], ltmp[:, 384:512], op=ALU.mult),
        ], 2)
        DV.selfwait()
        DV.opn(lambda v: [
            v.tensor_reduce(lred[:, 0:1], ltmp[:, 512:640], op=ALU.add, axis=mybir.AxisListType.X),
            v.tensor_reduce(lred[:, 1:2], ltmp[:, 640:768], op=ALU.add, axis=mybir.AxisListType.X),
        ], 2)
        AC.wait(DV)
        AC.opn(lambda s: [s.activation(lred[:, 2:3], lred[:, 0:1], AF.Exp),
                          s.activation(lred[:, 3:4], lred[:, 1:2], AF.Exp)], 2)
        DV.wait(AC)
        DV.op(lambda v: v.tensor_tensor(lamt[:, :], lred[:, 2:3], lred[:, 3:4], op=ALU.subtract))
        DV.selfwait()
        DV.op(lambda v: v.tensor_scalar(lamt[:, :], lamt[:, :], LAMBDA_INIT, None, op0=ALU.add))
        PE.wait(DV)
        PE.mmgroup(lambda t: [t.matmul(PP[:, 0:1], ones1[0:1, :], lamt[0:1, 0:1], start=True, stop=True)])
        DV.wait(PE)
        DV.op(lambda v: v.tensor_scalar(nlam128[:, :], PP[:, 0:1], -1.0, None, op0=ALU.mult))
        tok_nlam = DV.n

        with ExitStack() as ph:
            Ep = ph.enter_context
            xt = [Ep(nc.sbuf_tensor(f"xt{i}", [128, D], F32)) for i in range(2)]
            xnt = [Ep(nc.sbuf_tensor(f"xnt{i}", [128, D], F16)) for i in range(2)]
            NST = max(D // 512, 1)
            stats = Ep(nc.sbuf_tensor("stats", [128, 6 * NST], F32))
            mv = Ep(nc.sbuf_tensor("mv", [128, 2], F32))
            negm = Ep(nc.sbuf_tensor("negm", [128, 1], F32))
            lnv = Ep(nc.sbuf_tensor("lnv", [128, 1], F32))
            rstd = Ep(nc.sbuf_tensor("rstd", [128, 1], F32))
            txp = [Ep(nc.sbuf_tensor(f"txp{i}", [128, 2048], F16)) for i in range(2)]

            tok_xn_dv = {}
            tok_xn_store = {}
            tok_xload = {}

            def emit_xload(si):
                if si > SCH:
                    return
                if si >= 2:
                    GP.wait(DV, tok_xn_dv[si - 2])
                row = 128 * (si % SCH)
                GP.op(lambda g, row=row, p=si % 2: g.dma_start(
                    out=xt[p][:, :], in_=x_in[row:row + 128, :]), inc=16)
                tok_xload[si] = GP.n

            emit_xload(0)
            for si in range(SCH):
                p = si % 2
                emit_xload(si + 1)
                DV.wait(GP, tok_xload[min(si + 1, SCH)])
                if si >= 2:
                    DV.wait(SP, tok_xn_store[si - 2])
                DV.opn(lambda v, p=p: [
                    v.bn_stats(stats[:, 6 * i:6 * i + 6], xt[p][:, 512 * i:512 * i + min(512, D)])
                    for i in range(NST)], NST)
                DV.selfwait()
                DV.op(lambda v: v.bn_aggr(mv[:, :], stats[:, :]))
                DV.selfwait()
                DV.op(lambda v: v.tensor_scalar(negm[:, :], mv[:, 0:1], -1.0, None, op0=ALU.mult))
                AC.wait(DV)
                AC.op(lambda s: s.activation(lnv[:, :], mv[:, 1:2], AF.Ln, bias=epsLN[:, :]))
                AC.selfwait()
                AC.op(lambda s: s.activation(rstd[:, :], lnv[:, :], AF.Exp, scale=-0.5))
                DV.wait(AC)
                tok_xn_dv[si] = DV.op(lambda v, p=p: v.tensor_scalar(
                    xnt[p][:, :], xt[p][:, :], negm[:, :], rstd[:, :], op0=ALU.add, op1=ALU.mult))
                SP.wait(DV, tok_xn_dv[si])
                tok_xn_store[si] = SP.op(lambda s, si=si, p=p: s.dma_start(
                    out=xn_nat[128 * si:128 * si + 128, :], in_=xnt[p][:, :]), inc=16)

            SP.selfwait()
            HW = min(2048, S)
            NHALF = S // HW
            for dc in range(DCH):
                for hf in range(NHALF):
                    p = (dc * NHALF + hf) % 2
                    SP.op(lambda s, dc=dc, hf=hf, p=p: s.dma_start_transpose(
                        txp[p][:, 0:HW],
                        xn_nat[HW * hf:HW * hf + HW, 128 * dc:128 * dc + 128]), inc=16)
                    SP.selfwait()
                    SP.op(lambda s, dc=dc, hf=hf, p=p: s.dma_start(
                        out=xnT_nat[128 * dc:128 * dc + 128, HW * hf:HW * hf + HW],
                        in_=txp[p][:, 0:HW]), inc=16)
            SP.selfwait()
            barrier()

        with ExitStack() as ph:
            Ep = ph.enter_context
            wqT = [Ep(nc.sbuf_tensor(f"wqT{i}", [128, DCH * 128], F16)) for i in range(2)]
            wkT = [Ep(nc.sbuf_tensor(f"wkT{i}", [128, DCH * 128], F16)) for i in range(2)]
            wvT = [Ep(nc.sbuf_tensor(f"wvT{i}", [128, DCH * 128], F16)) for i in range(2)]
            xnS = [Ep(nc.sbuf_tensor(f"xnS{i}", [128, DCH * 512], F16)) for i in range(2)]
            QTs = Ep(nc.sbuf_tensor("QTs", [128, S], F16))
            KTs = Ep(nc.sbuf_tensor("KTs", [128, S], F16))
            Vs = Ep(nc.sbuf_tensor("Vs", [128, S], F16))
            Et = [Ep(nc.sbuf_tensor(f"Et{i}", [128, 2 * QT], F16)) for i in range(4)]
            lnA1 = Ep(nc.sbuf_tensor("lnA1", [128, QT], F32))
            lnA2 = Ep(nc.sbuf_tensor("lnA2", [128, QT], F32))
            bc1 = Ep(nc.sbuf_tensor("bc1", [128, QT], F32))
            bc2 = Ep(nc.sbuf_tensor("bc2", [128, QT], F32))
            t1s = Ep(nc.sbuf_tensor("t1s", [128, QT], F32))
            t2s = Ep(nc.sbuf_tensor("t2s", [128, QT], F32))
            vcb = Ep(nc.sbuf_tensor("vcb", [128, QT], F32))
            sqs = Ep(nc.sbuf_tensor("sqs", [128, 512], F32))
            ssA = Ep(nc.sbuf_tensor("ssA", [128, 4], F32))
            lnA = Ep(nc.sbuf_tensor("lnA", [128, 4], F32))
            rsA = Ep(nc.sbuf_tensor("rsA", [128, 4], F32))
            nrm = Ep(nc.sbuf_tensor("nrm", [128, 2 * SCH * 128], F16))
            woT = [Ep(nc.sbuf_tensor(f"woT{i}", [128, D], F16)) for i in range(3)]
            sev = Ep(nc.sbuf_tensor("sev", [128, D], F32))
            pk = [Ep(nc.sbuf_tensor(f"pk{i}", [128, 256], I32)) for i in range(2)]
            bA = Ep(nc.sbuf_tensor("bA", [128, 2048], I32))
            bB = Ep(nc.sbuf_tensor("bB", [128, 2048], I32))
            bC = Ep(nc.sbuf_tensor("bC", [128, 2048], I32))
            bD = Ep(nc.sbuf_tensor("bD", [128, 2048], F32))
            Wf = Ep(nc.sbuf_tensor("Wf", [128, 2048], F32))
            dgu = [Ep(nc.sbuf_tensor(f"dgu{i}", [128, D], F16)) for i in range(2)]
            ddn0 = Ep(nc.sbuf_tensor("ddn0", [128, I], F16))
            ddn = [ddn0, ddn0]
            scg = [Ep(nc.sbuf_tensor(f"scg{i}", [128, 64], F32)) for i in range(2)]
            scd = [Ep(nc.sbuf_tensor(f"scd{i}", [128, 64], F32)) for i in range(2)]

            bAf = bA.bitcast(F32)
            bBf = bB.bitcast(F32)
            bDi = bD.bitcast(I32)
            WfR = Wf.bitcast(I32).rearrange("p (w e) -> p e w", e=8)

            def dequant_strip(nw, src_pk):
                DV.opn(lambda v: [
                    v.tensor_scalar(bA[:, 256 * j:256 * j + nw], src_pk[:, 0:nw],
                                    4 * j, 7, op0=ALU.logical_shift_right, op1=ALU.bitwise_and)
                    for j in range(8)] + [
                    v.tensor_scalar(bC[:, 256 * j:256 * j + nw], src_pk[:, 0:nw],
                                    28 - 4 * j, -2147483648,
                                    op0=ALU.logical_shift_left, op1=ALU.bitwise_and)
                    for j in range(8)], 16)
                tok_rd = DV.n
                DV.selfwait()
                DV.opn(lambda v: [
                    v.tensor_scalar(bB[:, 256 * j:256 * j + nw], bA[:, 256 * j:256 * j + nw],
                                    252.0, float(1 << 22), op0=ALU.add, op1=ALU.mult)
                    for j in range(8)], 8)
                DV.selfwait()
                DV.opn(lambda v: [
                    v.tensor_scalar(bAf[:, 256 * j:256 * j + nw], bBf[:, 256 * j:256 * j + nw],
                                    2.0, 1.0, op0=ALU.mult, op1=ALU.subtract)
                    for j in range(8)], 8)
                DV.selfwait()
                DV.opn(lambda v: [
                    v.tensor_tensor(bD[:, 256 * j:256 * j + nw], bAf[:, 256 * j:256 * j + nw],
                                    bBf[:, 256 * j:256 * j + nw], op=ALU.min)
                    for j in range(8)], 8)
                DV.selfwait()
                DV.opn(lambda v: [
                    v.tensor_tensor(WfR[:, j, 0:nw], bDi[:, 256 * j:256 * j + nw],
                                    bC[:, 256 * j:256 * j + nw], op=ALU.bitwise_or)
                    for j in range(8)], 8)
                DV.selfwait()
                return tok_rd

            pk_sched = [("g", ot, 0, WI_GU) for ot in range(NOT_GU)]
            NSTRIP_DN = (WI_DN + 255) // 256
            for ot_ in range(NOT_DN):
                for sti_ in range(NSTRIP_DN):
                    w0_ = 256 * sti_
                    pk_sched.append(("d", ot_, w0_, min(256, WI_DN - w0_)))
            pk_sched.append(("g", 0, 0, WI_GU))
            pktok = {}
            pkemit = [0]
            tok_pk_rd = {}
            tok_sc_used = {}

            def emit_pk(upto):
                while pkemit[0] <= min(upto, len(pk_sched) - 1):
                    k = pkemit[0]
                    if k >= 2:
                        GP.wait(DV, tok_pk_rd.get(k - 2, 0))
                    kind, ot_, w0_, nw_ = pk_sched[k]
                    if kind == "g":
                        GP.op(lambda g, ot_=ot_, p2=k % 2: g.dma_start(
                            out=pk[p2][:, 0:WI_GU], in_=gup[128 * ot_:128 * ot_ + 128, :]), inc=16)
                        if k < NOT_GU:
                            GP.wait(DV, tok_sc_used.get(("g", ot_ - 2), 0))
                            GP.op(lambda g, ot_=ot_, ps=ot_ % 2: g.dma_start(
                                out=scg[ps][:, 0:NGU], in_=gusT[128 * ot_:128 * ot_ + 128, :]), inc=16)
                    else:
                        GP.op(lambda g, ot_=ot_, w0_=w0_, nw_=nw_, p2=k % 2: g.dma_start(
                            out=pk[p2][:, 0:nw_], in_=dnp[128 * ot_:128 * ot_ + 128, w0_:w0_ + nw_]), inc=16)
                        if w0_ == 0:
                            GP.wait(DV, tok_sc_used.get(("d", ot_ - 2), 0))
                            GP.op(lambda g, ot_=ot_, ps=ot_ % 2: g.dma_start(
                                out=scd[ps][:, 0:NGD], in_=dnsT[128 * ot_:128 * ot_ + 128, :]), inc=16)
                    pktok[k] = GP.n
                    pkemit[0] += 1

            def pk_wait(k):
                emit_pk(k + 1)
                DV.wait(GP, pktok[min(k + 1, len(pk_sched) - 1)])

            scnt = [0]
            tok_store = {}

            def p3_steps():
              sc = scnt
              for ot in range(NOT_GU):
                p2 = sc[0] % 2
                pk_wait(sc[0])
                tok_pk_rd[sc[0]] = dequant_strip(WI_GU, pk[p2])
                if ot >= 2:
                    DV.wait(GP, tok_store.get(("g", ot - 2), 0))
                tok_sc_used[("g", ot)] = DV.opn(lambda v, p2=p2, ps=ot % 2: [
                    v.tensor_scalar(dgu[p2][:, 128 * g2:128 * g2 + 128], Wf[:, 128 * g2:128 * g2 + 128],
                                    scg[ps][:, g2:g2 + 1], None, op0=ALU.mult)
                    for g2 in range(NGU)], NGU)
                DV.selfwait()
                sc[0] += 1
                GP.wait(DV, tok_sc_used[("g", ot)])
                tok_store[("g", ot)] = GP.op(lambda g, ot=ot, p2=p2: g.dma_start(
                    out=wgu_dense[128 * ot:128 * ot + 128, :], in_=dgu[p2][:, :]), inc=16)
                yield
              for ot in range(NOT_DN):
                p2d = ot % 2
                for sti in range(NSTRIP_DN):
                    w0 = 256 * sti
                    nw = min(256, WI_DN - w0)
                    p2 = sc[0] % 2
                    pk_wait(sc[0])
                    tok_pk_rd[sc[0]] = dequant_strip(nw, pk[p2])
                    g0 = (8 * w0) // 128
                    ng = (8 * nw) // 128
                    if sti == 0 and ot >= 1:
                        DV.wait(GP, tok_store.get(("d", ot - 1), 0))
                    DV.opn(lambda v, p2d=p2d, g0=g0, ng=ng: [
                        v.tensor_scalar(ddn[p2d][:, 128 * (g0 + g2):128 * (g0 + g2) + 128],
                                        Wf[:, 128 * g2:128 * g2 + 128],
                                        scd[p2d][:, g0 + g2:g0 + g2 + 1], None, op0=ALU.mult)
                        for g2 in range(ng)], ng)
                    DV.selfwait()
                    if sti == NSTRIP_DN - 1:
                        tok_sc_used[("d", ot)] = DV.n
                    sc[0] += 1
                    yield
                GP.wait(DV)
                tok_store[("d", ot)] = GP.op(lambda g, ot=ot, p2d=p2d: g.dma_start(
                    out=wdn_dense[128 * ot:128 * ot + 128, :], in_=ddn[p2d][:, :]), inc=16)

            p3_iter = p3_steps()

            def p3_pump(n):
                for _ in range(n):
                    if next(p3_iter, "END") == "END":
                        return

            S0, S1c = 0, 1024
            AV1, AV2 = 2048, 2560
            D1C, D2C = 3072, 3584
            QC, KC, VC = 0, 512, 1024
            TRC = 1024

            xn_use = {}
            xn_tok = {}
            xcnt = 0
            tok_wT = {}
            wo_use = {}
            tok_nrm_last = {}
            tok_wo_mm = {}

            for pr in range(HC):
                pp = pr % 2
                if pr >= 2:
                    SP.wait(PE, tok_wT[("consumed", pr - 2)])
                SP.wait(GP, tok_wcast)
                SP.dman(lambda s, pr=pr, pp=pp: [
                    s.dma_start(
                        out=w_t[pp][:, 128 * dc:128 * dc + 128],
                        in_=w_b[128 * dc:128 * dc + 128, 128 * pr:128 * pr + 128])
                    for (w_t, w_b) in ((wqT, wq_bf), (wkT, wk_bf), (wvT, wv_bf))
                    for dc in range(DCH)], 3 * DCH)
                SP.op(lambda s: s.dma_start(out=ltmp[:, 0:128], in_=lq1), inc=16)
                tok_wT[pr] = SP.n

                tok_qkv_evac = 0
                for st in range(NQT):
                    sp2 = xcnt % 2
                    for k in (xcnt, xcnt + 1):
                        if k not in xn_tok and k <= HC * NQT:
                            if k >= 2:
                                GP.wait(PE, xn_use.get(k - 2, 0))
                            st_k = k % NQT
                            GP.op(lambda g, st_k=st_k, p2=k % 2: g.dma_start(
                                out=xnS[p2][:, :].rearrange("p (c s) -> p c s", c=DCH),
                                in_=xnT_nat.rearrange("(c p) s -> p c s", p=128)[:, :, QT * st_k:QT * st_k + QT]),
                            inc=16)
                            xn_tok[k] = GP.n
                    PE.wait(GP, xn_tok[min(xcnt + 1, HC * NQT)])
                    PE.wait(SP, tok_wT[pr])
                    if st == 0:
                        PE.wait(DV)
                        PE.wait(AC)
                    else:
                        PE.wait(DV, tok_qkv_evac)
                    PE.mmgroup(lambda t, pp=pp, sp2=sp2: [
                        t.matmul(PP[:, QC:QC + 512], wqT[pp][:, 128 * dc:128 * dc + 128],
                                 xnS[sp2][:, 512 * dc:512 * dc + 512],
                                 start=(dc == 0), stop=(dc == DCH - 1))
                        for dc in range(DCH)])
                    PE.mmgroup(lambda t, pp=pp, sp2=sp2: [
                        t.matmul(PP[:, KC:KC + 512], wkT[pp][:, 128 * dc:128 * dc + 128],
                                 xnS[sp2][:, 512 * dc:512 * dc + 512],
                                 start=(dc == 0), stop=(dc == DCH - 1))
                        for dc in range(DCH)])
                    PE.mmgroup(lambda t, pp=pp, sp2=sp2: [
                        t.matmul(PP[:, VC + 128 * q:VC + 128 * q + 128],
                                 xnS[sp2][:, 512 * dc + 128 * q:512 * dc + 128 * q + 128],
                                 wvT[pp][:, 128 * dc:128 * dc + 128],
                                 start=(dc == 0), stop=(dc == DCH - 1))
                        for q in range(4) for dc in range(DCH)])
                    xn_use[xcnt] = PE.n
                    xcnt += 1
                    if st == NQT - 1:
                        tok_wT[("consumed", pr)] = PE.n
                    DV.wait(PE)
                    tok_qkv_evac = DV.opn(lambda v, st=st: [
                        v.tensor_copy(QTs[:, QT * st:QT * st + QT], PP[:, QC:QC + 512]),
                        v.tensor_copy(KTs[:, QT * st:QT * st + QT], PP[:, KC:KC + 512]),
                        v.tensor_copy(Vs[:, QT * st:QT * st + QT], PP[:, VC:VC + 512]),
                    ], 3)
                tok_qkv_dv = DV.n

                for qt in range(NQT):
                    tok_scores = {}
                    tok_exp = {}
                    tok_cons = {}

                    def emit_dav(kc, tok_exp=tok_exp, tok_cons=tok_cons):
                        PE.wait(AC, tok_exp[kc])
                        ep4 = kc % 4
                        PE.mmgroup(lambda t, kc=kc, ep4=ep4: [
                            t.matmul(PP[:, D1C:D1C + QT], ones128[:, :], Et[ep4][:, 0:QT],
                                     start=(kc == 0), stop=(kc == KCH - 1)),
                            t.matmul(PP[:, D2C:D2C + QT], ones128[:, :], Et[ep4][:, QT:2 * QT],
                                     start=(kc == 0), stop=(kc == KCH - 1)),
                            t.matmul(PP[:, AV1:AV1 + QT], Vs[:, 128 * kc:128 * kc + 128],
                                     Et[ep4][:, 0:QT], start=(kc == 0), stop=(kc == KCH - 1)),
                            t.matmul(PP[:, AV2:AV2 + QT], Vs[:, 128 * kc:128 * kc + 128],
                                     Et[ep4][:, QT:2 * QT], start=(kc == 0), stop=(kc == KCH - 1)),
                        ])
                        tok_cons[kc] = PE.n

                    for kc in range(KCH):
                        sp2 = kc % 2
                        if kc == 0:
                            PE.wait(DV, tok_qkv_dv)
                            if qt > 0:
                                PE.wait(DV, tok_nrm_last[(pr, qt - 1)])
                        if kc >= 2:
                            PE.wait(AC, tok_exp[kc - 2])
                        PE.mmgroup(lambda t, kc=kc, qt=qt, sp2=sp2: [
                            t.matmul(PP[:, 1024 * sp2:1024 * sp2 + 512],
                                     KTs[0:64, 128 * kc:128 * kc + 128],
                                     QTs[0:64, QT * qt:QT * qt + QT],
                                     start=True, stop=True, tile_position=(0, 0)),
                            t.matmul(PP[:, 1024 * sp2 + 512:1024 * sp2 + 1024],
                                     KTs[64:128, 128 * kc:128 * kc + 128],
                                     QTs[64:128, QT * qt:QT * qt + QT],
                                     start=True, stop=True, tile_position=(64, 0)),
                        ])
                        tok_scores[kc] = PE.n
                        AC.wait(PE, tok_scores[kc])
                        if kc >= 4:
                            AC.wait(PE, tok_cons[kc - 4])
                        tok_exp[kc] = AC.op(lambda s, sp2=sp2, ep4=kc % 4: s.activation(
                            Et[ep4][:, :], PP[:, 1024 * sp2:1024 * sp2 + 2 * QT],
                            AF.Exp, scale=SM_SCALE))
                        if kc >= 1:
                            emit_dav(kc - 1)
                    emit_dav(KCH - 1)

                    AC.wait(PE, tok_cons[KCH - 1])
                    AC.opn(lambda s: [
                        s.activation(lnA1[:, :], PP[:, D1C:D1C + QT], AF.Ln),
                        s.activation(lnA2[:, :], PP[:, D2C:D2C + QT], AF.Ln),
                    ], 2)
                    AC.selfwait()
                    AC.opn(lambda s: [
                        s.activation(bc1[:, :], lnA1[:, :], AF.Exp, scale=-1.0),
                        s.activation(bc2[:, :], lnA2[:, :], AF.Exp, scale=-1.0),
                    ], 2)
                    tok_alpha = AC.n
                    DV.wait(AC, tok_alpha)
                    DV.wait(DV, tok_nlam)
                    DV.opn(lambda v: [
                        v.tensor_tensor(t1s[:, :], PP[:, AV1:AV1 + QT], bc1[:, :], op=ALU.mult),
                        v.scalar_tensor_tensor(t2s[:, :], PP[:, AV2:AV2 + QT], nlam128[:, :],
                                               bc2[:, :], op0=ALU.mult, op1=ALU.mult),
                    ], 2)
                    DV.selfwait()
                    tok_comb = DV.op(lambda v: v.tensor_tensor(vcb[:, :], t1s[:, :], t2s[:, :], op=ALU.add))
                    PE.wait(DV, tok_comb)
                    PE.mmgroup(lambda t: [
                        t.transpose(PP[:, TRC + 128 * ti:TRC + 128 * ti + 128],
                                    vcb[:, 128 * ti:128 * ti + 128], identf[:, :])
                        for ti in range(NT)])
                    AC.wait(PE)
                    AC.opn(lambda s: [
                        s.activation(sqs[:, 128 * ti:128 * ti + 128],
                                     PP[:, TRC + 128 * ti:TRC + 128 * ti + 128],
                                     AF.Square, accum_out=ssA[:, ti:ti + 1])
                        for ti in range(NT)], NT)
                    AC.selfwait()
                    AC.opn(lambda s: [
                        s.activation(lnA[:, ti:ti + 1], ssA[:, ti:ti + 1], AF.Ln,
                                     scale=1.0 / 128.0, bias=epsRMS[:, :])
                        for ti in range(NT)], NT)
                    AC.selfwait()
                    AC.opn(lambda s: [
                        s.activation(rsA[:, ti:ti + 1], lnA[:, ti:ti + 1], AF.Exp, scale=-0.5)
                        for ti in range(NT)], NT)
                    DV.wait(AC)
                    if qt == 0 and pr >= 2:
                        DV.wait(PE, tok_wo_mm[(pr - 2) | 1])
                    po = (pr % 2) * SCH * 128
                    tok_nrm_last[(pr, qt)] = DV.opn(lambda v, qt=qt, po=po: [
                        v.tensor_scalar(nrm[:, po + (qt * NT + ti) * 128:po + (qt * NT + ti) * 128 + 128],
                                        PP[:, TRC + 128 * ti:TRC + 128 * ti + 128],
                                        rsA[:, ti:ti + 1], 1.0 - LAMBDA_INIT,
                                        op0=ALU.mult, op1=ALU.mult)
                        for ti in range(NT)], NT)
                    p3_pump(2)

                if pr % 2 == 1 or pr == HC - 1:
                    duo = (pr - 1, pr) if pr % 2 == 1 else (pr,)
                    combos = [(pa, sh) for pa in duo for sh in (0, 1)]
                    ngrp = (len(combos) + 1) // 2
                    nwo = ngrp * CC
                    wotok = {}

                    def emit_woload(k, nwo=nwo):
                        if k > nwo or k in wotok:
                            return
                        if k >= 3:
                            SP.wait(PE, wo_use[k - 3])
                        cck = k % CC
                        SP.op(lambda s, cck=cck, wp=k % 3: s.dma_start(
                            out=woT[wp][:, :], in_=wo_bf[128 * cck:128 * cck + 128, :]), inc=16)
                        wotok[k] = SP.n

                    woci = 0
                    prev_sstore = 0
                    emit_woload(0)
                    for g0 in range(0, len(combos), 2):
                        sub = combos[g0:g0 + 2]
                        for cc in range(CC):
                            wp = woci % 3
                            emit_woload(woci + 1)
                            PE.wait(SP, wotok[min(woci + 1, nwo)])
                            if cc == 0:
                                PE.wait(DV)
                            def womms(t, sub=sub, cc=cc, wp=wp):
                                ins = []
                                for ci, (pa, sh) in enumerate(sub):
                                    po2 = (pa % 2) * SCH * 128
                                    lhs = nrm[:, po2 + (sh * CC + cc) * 128:po2 + (sh * CC + cc) * 128 + 128]
                                    for j in range(NJ):
                                        ins.append(t.matmul(
                                            PP[:, 2048 * ci + JT * j:2048 * ci + JT * j + JT],
                                            lhs, woT[wp][:, JT * j:JT * j + JT],
                                            start=(cc == 0), stop=(cc == CC - 1)))
                                return ins
                            PE.mmgroup(womms)
                            wo_use[woci] = PE.n
                            woci += 1
                        tok_wo_mm[pr] = PE.n
                        for ci, (pa, sh) in enumerate(sub):
                            ul = 2 * pa + sh
                            DV.wait(PE)
                            DV.wait(GP, prev_sstore)
                            DV.op(lambda v, ci=ci: v.tensor_copy(sev[:, :], PP[:, 2048 * ci:2048 * ci + D]))
                            GP.wait(DV)
                            prev_sstore = GP.op(lambda g, ul=ul: g.dma_start(
                                out=s_stash[128 * ul:128 * ul + 128, :], in_=sev[:, :]), inc=16)
            p3_pump(10 ** 9)
            emit_pk(len(pk_sched) - 1)
            GP.selfwait()
            barrier()

        with ExitStack() as ph:
            Ep = ph.enter_context
            xnU = Ep(nc.sbuf_tensor("xnU", [128, 4 * DCH * 128], F16))
            wsl = [Ep(nc.sbuf_tensor(f"wsl{i}", [128, 512], F16)) for i in range(4)]
            gsil = Ep(nc.sbuf_tensor("gsil", [128, 4 * 512], F16))
            hsb = Ep(nc.sbuf_tensor("hsb", [128, 4 * 512], F16))
            hT = Ep(nc.sbuf_tensor("hT", [128, 4 * ICH * 128], F16))
            ssF = Ep(nc.sbuf_tensor("ssF", [128, 1], F32))
            lnF = Ep(nc.sbuf_tensor("lnF", [128, 1], F32))
            rsF = Ep(nc.sbuf_tensor("rsF", [128, 1], F32))
            ybuf = [Ep(nc.sbuf_tensor(f"ybuf{i}", [128, D], F32)) for i in range(4)]
            gbuf = Ep(nc.sbuf_tensor("gbuf", [128, D], F32))
            sbuf2 = [Ep(nc.sbuf_tensor(f"sbuf2{i}", [128, D], F32)) for i in range(2)]
            xbuf = [Ep(nc.sbuf_tensor(f"xbuf{i}", [128, D], F32)) for i in range(2)]
            obuf = Ep(nc.sbuf_tensor("obuf", [128, D], F32))

            NUG = (U + 3) // 4
            load_fns = []
            loadtok = {}
            consumed = {}
            emitted = [0]
            LA = 3

            def emit_load(idx):
                while emitted[0] <= min(idx, len(load_fns) - 1):
                    k = emitted[0]
                    if k >= 4:
                        SP.wait(PE, consumed[k - 4])
                    load_fns[k]()
                    loadtok[k] = SP.n
                    emitted[0] += 1

            def mk_gu_load(dc, rb, wp):
                def f():
                    SP.op(lambda s: s.dma_start_transpose(
                        wsl[wp][:, :], wgu_dense[rb:rb + 512, 128 * dc:128 * dc + 128]), inc=16)
                return f

            def mk_dn_load(ic, j, wp):
                def f():
                    SP.op(lambda s: s.dma_start_transpose(
                        wsl[wp][:, 0:JT], wdn_dense[JT * j:JT * j + JT, 128 * ic:128 * ic + 128]), inc=16)
                return f

            li = 0
            for ug_ in range(NUG):
                for og_ in range(OG):
                    for ph2_ in (0, 1):
                        rb_ = 512 * og_ + (I if ph2_ else 0)
                        for dc_ in range(DCH):
                            load_fns.append(mk_gu_load(dc_, rb_, li % 4)); li += 1
                for j_ in range(NJ):
                    for ic_ in range(ICH):
                        load_fns.append(mk_dn_load(ic_, j_, li % 4)); li += 1
            NLOAD = li
            load_fns.append(mk_dn_load(0, 0, li % 4)); li += 1

            def pe_wait_load(k):
                emit_load(k + LA)
                PE.wait(SP, loadtok[min(k + 1, NLOAD)])

            tok_xnU_used = 0
            tok_g_ac = 0
            tok_gbuf_free = 0
            tok_out_dv = 0
            tok_obuf_store = 0
            sxtok = {}
            sx_guard = {}
            tok_ydump = 0
            gi = 0

            for ug in range(NUG):
                us = list(range(4 * ug, min(4 * ug + 4, U)))
                nu = len(us)
                if ug > 0:
                    SP.wait(PE, tok_xnU_used)
                SP.dman(lambda s, us=us: [
                    s.dma_start_transpose(
                        xnU[:, (ui * DCH + dc) * 128:(ui * DCH + dc) * 128 + 128],
                        xn_nat.rearrange("(m blk) d -> blk m d", blk=BLK)[u, :, 128 * dc:128 * dc + 128])
                    for ui, u in enumerate(us) for dc in range(DCH)], nu * DCH)
                tok_xnU = SP.n

                tok_h_dv = 0
                tok_hTdv = 0
                tok_hT_pe = 0
                tok_gsil_ac = 0
                tok_yevac = {}
                for og in range(OG):
                    for ph2 in (0, 1):
                        for dc in range(DCH):
                            pe_wait_load(gi)
                            if dc == 0:
                                PE.wait(SP, tok_xnU)
                                if ph2 == 0 and (og > 0 or ug > 0):
                                    PE.wait(DV, tok_hTdv if og > 0 else tok_ylast)
                                if ph2 == 1:
                                    PE.wait(AC, tok_gsil_ac)
                            pbase = 2048 * ph2
                            wp = gi % 4
                            PE.mmgroup(lambda t, us=us, dc=dc, wp=wp, pbase=pbase: [
                                t.matmul(PP[:, pbase + 512 * ui:pbase + 512 * ui + 512],
                                         xnU[:, (ui * DCH + dc) * 128:(ui * DCH + dc) * 128 + 128],
                                         wsl[wp][:, :], start=(dc == 0), stop=(dc == DCH - 1))
                                for ui in range(len(us))])
                            consumed[gi] = PE.n
                            gi += 1
                        if ph2 == 0:
                            AC.wait(PE)
                            AC.wait(DV, tok_h_dv)
                            tok_gsil_ac = AC.opn(lambda s, nu=nu: [
                                s.activation(gsil[:, 512 * ui:512 * ui + 512],
                                             PP[:, 512 * ui:512 * ui + 512], AF.Silu)
                                for ui in range(nu)], nu)
                        else:
                            DV.wait(PE)
                            DV.wait(AC, tok_gsil_ac)
                            DV.wait(PE, tok_hT_pe)
                            tok_h_dv = DV.opn(lambda v, nu=nu: [
                                v.scalar_tensor_tensor(
                                    hsb[:, 512 * ui:512 * ui + 512],
                                    PP[:, 2048 + 512 * ui:2048 + 512 * ui + 512], 1.0 / 64.0,
                                    gsil[:, 512 * ui:512 * ui + 512],
                                    op0=ALU.mult, op1=ALU.mult)
                                for ui in range(nu)], nu)
                    PE.wait(DV, tok_h_dv)
                    PE.mmgroup(lambda t, nu=nu: [
                        t.transpose(PPh[:, 128 * (4 * ui + ti):128 * (4 * ui + ti) + 128],
                                    hsb[:, 512 * ui + 128 * ti:512 * ui + 128 * ti + 128], identh[:, :])
                        for ui in range(nu) for ti in range(4)])
                    tok_hT_pe = PE.n
                    DV.wait(PE, tok_hT_pe)
                    tok_hTdv = DV.opn(lambda v, og=og, nu=nu: [
                        v.tensor_copy(hT[:, (ui * ICH + og * 4 + ti) * 128:(ui * ICH + og * 4 + ti) * 128 + 128],
                                      PPh[:, 128 * (4 * ui + ti):128 * (4 * ui + ti) + 128])
                        for ui in range(nu) for ti in range(4)], nu * 4)
                tok_xnU_used = PE.n
                if debug and ug == 0:
                    SP.wait(DV, tok_hTdv)
                    SP.dman(lambda s, nu=nu: [
                        s.dma_start(out=h_dbg[128 * ui:128 * ui + 128, :], in_=hT[:, ui * ICH * 128:(ui + 1) * ICH * 128])
                        for ui in range(nu)], nu)

                for j in range(NJ):
                    q = j % 2
                    for ic in range(ICH):
                        pe_wait_load(gi)
                        if ic == 0:
                            if j == 0:
                                PE.wait(DV, tok_hTdv)
                            if j >= 2:
                                PE.wait(DV, tok_yevac[j - 2])
                            if ug > 0 and j < 2:
                                PE.wait(AC, tok_g_ac)
                        wp = gi % 4
                        PE.mmgroup(lambda t, us=us, j=j, ic=ic, wp=wp, q=q, nu=nu: [
                            t.matmul(PP[:, 2048 * q + 512 * ui:2048 * q + 512 * ui + JT],
                                     hT[:, (ui * ICH + ic) * 128:(ui * ICH + ic) * 128 + 128],
                                     wsl[wp][:, 0:JT], start=(ic == 0), stop=(ic == ICH - 1))
                            for ui in range(nu)])
                        consumed[gi] = PE.n
                        gi += 1
                    DV.wait(PE)
                    if ug > 0 and j == 0:
                        DV.wait(AC, tok_g_ac)
                        DV.wait(SP, tok_ydump)
                    tok_yevac[j] = DV.opn(lambda v, j=j, q=q, nu=nu: [
                        v.tensor_copy(ybuf[ui][:, JT * j:JT * j + JT],
                                      PP[:, 2048 * q + 512 * ui:2048 * q + 512 * ui + JT])
                        for ui in range(nu)], nu)
                tok_ylast = tok_yevac[NJ - 1]

                if debug:
                    SP.wait(DV, tok_ylast)
                    SP.dman(lambda s, us=us: [
                        s.dma_start(out=y_dbg[128 * u:128 * u + 128, :], in_=ybuf[ui][:, :])
                        for ui, u in enumerate(us)], nu)
                    tok_ydump = SP.n

                for ui, u in enumerate(us):
                    ul = u
                    for uk in (u, u + 1):
                        if uk not in sxtok and uk <= U:
                            GP.wait(DV, sx_guard.get(uk - 2, 0))
                            row = 128 * (uk % U)
                            GP.op(lambda g, row=row, p=uk % 2: g.dma_start(
                                out=sbuf2[p][:, :], in_=s_stash[row:row + 128, :]), inc=16)
                            GP.op(lambda g, p=uk % 2, blk=uk % U: g.dma_start(
                                out=xbuf[p][:, :],
                                in_=x_in.rearrange("(m blk) d -> blk m d", blk=BLK)[blk, :, :]), inc=16)
                            sxtok[uk] = GP.n
                    AC.wait(DV, tok_ylast)
                    AC.wait(DV, tok_gbuf_free)
                    AC.op(lambda s, ui=ui: s.activation(
                        gbuf[:, :], ybuf[ui][:, :], AF.Square, accum_out=ssF[:, :]))
                    AC.selfwait()
                    AC.op(lambda s: s.activation(lnF[:, :], ssF[:, :], AF.Ln,
                                                 scale=1.0 / D, bias=epsFFN[:, :]))
                    AC.selfwait()
                    AC.op(lambda s: s.activation(rsF[:, :], lnF[:, :], AF.Exp, scale=-0.5))
                    AC.selfwait()
                    tok_g_ac = AC.op(lambda s, ui=ui: s.activation(
                        gbuf[:, :], ybuf[ui][:, :], AF.Silu, scale=rsF[:, :]))
                    DV.wait(AC, tok_g_ac)
                    DV.wait(GP, sxtok[u + 1])
                    DV.wait(GP, tok_obuf_store)
                    tok_gbuf_free = DV.op(lambda v, p=u % 2: v.tensor_tensor(
                        obuf[:, :], gbuf[:, :], sbuf2[p][:, :], op=ALU.mult))
                    DV.selfwait()
                    tok_out_dv = DV.op(lambda v, p=u % 2: v.tensor_tensor(
                        obuf[:, :], obuf[:, :], xbuf[p][:, :], op=ALU.add))
                    sx_guard[u] = tok_out_dv
                    GP.wait(DV, tok_out_dv)
                    tok_obuf_store = GP.op(lambda g, ul=ul: g.dma_start(
                        out=out_c[128 * ul:128 * ul + 128, :], in_=obuf[:, :]), inc=16)
            emit_load(len(load_fns) - 1)
            GP.selfwait()
            GP.wait(SP)
            SP.selfwait()
            flush_all()
    return nc



def shard_inputs(cfg, inputs, n_cores=8):
    c = _cfg(cfg)
    S, D, HC, BLK, SH = c["S"], c["D"], c["HC"], c["BLK"], c["SH"]
    x = np.ascontiguousarray(inputs["L0_attn_x"])
    wq, wk, wv, wo = (inputs["L0_attn_wq"], inputs["L0_attn_wk"],
                      inputs["L0_attn_wv"], inputs["L0_attn_wo"])
    maps = []
    for core in range(n_cores):
        b = core // 2
        side = core % 2
        q = np.arange(S) // BLK
        k = np.arange(S) % BLK
        src = q * BLK + (k + side * SH) % BLK
        cp = np.arange(D)
        colperm = (cp // BLK) * BLK + (cp % BLK + side * SH) % BLK
        h0 = HC * side
        m = {
            "x_in": x[b][src],
            "wq_in": wq[128 * h0:128 * (h0 + HC)].T,
            "wk_in": wk[128 * h0:128 * (h0 + HC)].T,
            "wv_in": wv[128 * h0:128 * (h0 + HC)].T,
            "wo_in": wo[:, colperm].T,
            "lq1": np.asarray(inputs["L0_attn_lambda_q1"]).reshape(1, 128),
            "lq2": np.asarray(inputs["L0_attn_lambda_q2"]).reshape(1, 128),
            "lk1": np.asarray(inputs["L0_attn_lambda_k1"]).reshape(1, 128),
            "lk2": np.asarray(inputs["L0_attn_lambda_k2"]).reshape(1, 128),
            "gup": inputs["L0_ffn_gate_up_packed"],
            "gusT": inputs["L0_ffn_gate_up_scales"].T,
            "dnp": inputs["L0_ffn_down_packed"],
            "dnsT": inputs["L0_ffn_down_scales"].T,
        }
        maps.append({k2: np.ascontiguousarray(v) for k2, v in m.items()})
    return maps


def unshard_output(cfg, outs, n_cores=8):
    c = _cfg(cfg)
    S, D, U, BLK, SH = c["S"], c["D"], c["U"], c["BLK"], c["SH"]
    B = n_cores // 2
    out = np.empty((B, S, D), dtype=np.float32)
    m = np.arange(128)
    for core in range(n_cores):
        b = core // 2
        side = core % 2
        oc = outs[core]
        for ul in range(U):
            rows = BLK * m + ul + side * SH
            out[b, rows, :] = oc[128 * ul:128 * ul + 128, :]
    return out


def kernel(**inputs):
    cfg = REAL
    in_maps = shard_inputs(cfg, inputs)
    nc = build_nc(cfg)
    res = run_bass_kernel_spmd(nc, in_maps, core_ids=list(range(8)))
    return unshard_output(cfg, [r["out_c"] for r in res.results])


# version 17
# speedup vs baseline: 2.4623x; 1.0658x over previous
import numpy as np
import ml_dtypes
from contextlib import ExitStack

import concourse.bass as bass
import concourse.mybir as mybir
from concourse.bass_utils import run_bass_kernel_spmd

F32 = mybir.dt.float32
BF16 = mybir.dt.bfloat16
F16 = mybir.dt.float16
I32 = mybir.dt.int32
AF = mybir.ActivationFunctionType
ALU = mybir.AluOpType

LN_EPS = 1e-5
RMS_EPS = 1.1920928955078125e-07
FFN_EPS = 1e-06
LAMBDA_INIT = 0.8
SM_SCALE = float(1.0 / np.sqrt(128.0))

REAL = dict(S=4096, D=2048, HG=16, HC=8, I=5632)


def _cfg(cfg):
    S, D, HG, HC, I = cfg["S"], cfg["D"], cfg["HG"], cfg["HC"], cfg["I"]
    assert D == HG * 128 and S == 2 * D and I % 512 == 0
    c = dict(cfg)
    NC = cfg.get("NC", 8)
    WD = min(512, D)
    NGUU = (2 * I) // 512
    NDU = D // WD
    c.update(
        BLK=S // 128, SH=2 * HC, DCH=D // 128, KCH=S // 128, SCH=S // 128,
        QT=512, NQT=S // 512, U=2 * HC, JT=min(512, D), NJ=D // min(512, D),
        OG=I // 512, ICH=I // 128, NGU=D // 128, NGD=I // 128, CC=(S // 2) // 128,
        NC=NC, WD=WD, NGUU=NGUU, NDU=NDU,
        S_G=-(-NGUU // NC), S_D=-(-NDU // NC),
    )
    return c


class Stream:

    CHUNK = 400

    def __init__(self, block, engname, sem, dmasem=None):
        self.block = block
        self.engname = engname
        self.sem = sem
        self.n = 0
        self.dmasem = dmasem
        self.nd = 0
        self.pend = []

    def raw(self, f):
        self.pend.append(f)

    def flush(self):
        while self.pend:
            chunk = self.pend[:self.CHUNK]
            self.pend = self.pend[self.CHUNK:]
            getattr(self.block, self.engname)(
                lambda e, chunk=chunk: [g(e) for g in chunk])

    def op(self, f, inc=1):
        def g(e):
            f(e).then_inc(self.sem, inc)
        self.raw(g)
        self.n += inc
        return self.n

    def opn(self, f, n):
        def g(e):
            inss = f(e)
            assert len(inss) == n, (self.engname, len(inss), n)
            for ins in inss:
                ins.then_inc(self.sem, 1)
        self.raw(g)
        self.n += n
        return self.n

    def dman(self, f, n):
        def g(e):
            inss = f(e)
            assert len(inss) == n, (self.engname, len(inss), n)
            for ins in inss:
                ins.then_inc(self.sem, 16)
        self.raw(g)
        self.n += 16 * n
        return self.n

    def mmgroup(self, f):
        def g(e):
            f(e)[-1].then_inc(self.sem, 1)
        self.raw(g)
        self.n += 1
        return self.n

    def dmaop(self, f):
        def g(e):
            f(e).then_inc(self.dmasem, 16)
        self.raw(g)
        self.nd += 16
        return self.nd

    def wait(self, other, tok=None):
        t = other.n if tok is None else tok
        if t > 0:
            self.raw(lambda e: e.wait_ge(other.sem, t))

    def waitd(self, other, tok):
        if tok > 0:
            self.raw(lambda e: e.wait_ge(other.dmasem, tok))

    def selfwait(self):
        if self.n > 0:
            self.raw(lambda e, n=self.n: e.wait_ge(self.sem, n))


def build_nc(cfg, debug=False):
    c = _cfg(cfg)
    S, D, HC, I = c["S"], c["D"], c["HC"], c["I"]
    DCH, KCH, SCH, QT, NQT = c["DCH"], c["KCH"], c["SCH"], c["QT"], c["NQT"]
    U, NJ, JT, OG, ICH = c["U"], c["NJ"], c["JT"], c["OG"], c["ICH"]
    NGU, NGD, CC, BLK = c["NGU"], c["NGD"], c["CC"], c["BLK"]
    NC, WD, S_G, S_D = c["NC"], c["WD"], c["S_G"], c["S_D"]
    I2 = 2 * I
    WI_GU = D // 8
    WI_DN = I // 8
    NOT_GU = I2 // 128
    NOT_DN = D // 128
    NT = QT // 128

    nc = bass.Bass("TRN2", target_bir_lowering=False)
    dk = "ExternalOutput" if debug else "Internal"

    x_in = nc.dram_tensor("x_in", [S, D], F32, kind="ExternalInput").ap()
    wq_in = nc.dram_tensor("wq_in", [D, HC * 128], F32, kind="ExternalInput").ap()
    wk_in = nc.dram_tensor("wk_in", [D, HC * 128], F32, kind="ExternalInput").ap()
    wv_in = nc.dram_tensor("wv_in", [D, HC * 128], F32, kind="ExternalInput").ap()
    wo_in = nc.dram_tensor("wo_in", [D, D], F32, kind="ExternalInput").ap()
    lq1 = nc.dram_tensor("lq1", [1, 128], F32, kind="ExternalInput").ap()
    lq2 = nc.dram_tensor("lq2", [1, 128], F32, kind="ExternalInput").ap()
    lk1 = nc.dram_tensor("lk1", [1, 128], F32, kind="ExternalInput").ap()
    lk2 = nc.dram_tensor("lk2", [1, 128], F32, kind="ExternalInput").ap()
    gup = nc.dram_tensor("gup", [I2, WI_GU], I32, kind="ExternalInput").ap()
    gusT = nc.dram_tensor("gusT", [I2, NGU], F16, kind="ExternalInput").ap()
    dnp = nc.dram_tensor("dnp", [D, WI_DN], I32, kind="ExternalInput").ap()
    dnsT = nc.dram_tensor("dnsT", [D, NGD], F16, kind="ExternalInput").ap()
    out_c = nc.dram_tensor("out_c", [U * 128, D], F32, kind="ExternalOutput").ap()

    wq_bf = nc.dram_tensor("wq_bf", [D, HC * 128], F16, kind="Internal").ap()
    wk_bf = nc.dram_tensor("wk_bf", [D, HC * 128], F16, kind="Internal").ap()
    wv_bf = nc.dram_tensor("wv_bf", [D, HC * 128], F16, kind="Internal").ap()
    wo_bf = nc.dram_tensor("wo_bf", [D, D], F16, kind="Internal").ap()
    xn_nat = nc.dram_tensor("xn_nat", [S, D], F16, kind=dk).ap()
    xnT_nat = nc.dram_tensor("xnT_nat", [D, S], F16, kind=dk).ap()
    s_stash = nc.dram_tensor("s_stash", [U * 128, D], F32, kind=dk).ap()
    wgu_dense = nc.dram_tensor("wgu_dense", [I2, D], F16, kind=dk).ap()
    wdn_dense = nc.dram_tensor("wdn_dense", [D, I], F16, kind=dk).ap()
    y_dbg = nc.dram_tensor("y_dbg", [U * 128, D], F32, kind=dk).ap() if debug else None
    h_dbg = nc.dram_tensor("h_dbg", [4 * 128, ICH * 128], F16, kind=dk).ap() if debug else None

    with ExitStack() as top:
        E = top.enter_context
        ident = E(nc.sbuf_tensor("ident", [128, 128], BF16))
        identh = E(nc.sbuf_tensor("identh", [128, 128], F16))
        identf = E(nc.sbuf_tensor("identf", [128, 128], F32))
        identi = E(nc.sbuf_tensor("identi", [128, 128], I32))
        onesb = E(nc.sbuf_tensor("onesb", [128, 1], F16))
        ones128 = E(nc.sbuf_tensor("ones128", [128, 128], F16))
        nlam128 = E(nc.sbuf_tensor("nlam128", [128, 1], F32))
        ones1 = E(nc.sbuf_tensor("ones1", [1, 128], F32))
        epsLN = E(nc.sbuf_tensor("epsLN", [128, 1], F32))
        epsRMS = E(nc.sbuf_tensor("epsRMS", [128, 1], F32))
        epsFFN = E(nc.sbuf_tensor("epsFFN", [128, 1], F32))
        lamt = E(nc.sbuf_tensor("lamt", [1, 1], F32))
        ltmp = E(nc.sbuf_tensor("ltmp", [1, 768], F32))
        lred = E(nc.sbuf_tensor("lred", [1, 4], F32))

        PP = E(nc.psum_tensor("PP", [128, 4096], F32))
        PPb = PP.bitcast(BF16)
        PPh = PP.bitcast(F16)

        sems = {k: E(nc.semaphore(k)) for k in ["pe", "ac", "dv", "gp", "sp", "ag",
                                                "acd", "spd"]}
        block = E(nc.Block())

        PE = Stream(block, "tensor", sems["pe"])
        AC = Stream(block, "scalar", sems["ac"], dmasem=sems["acd"])
        DV = Stream(block, "vector", sems["dv"])
        GP = Stream(block, "gpsimd", sems["gp"])
        SP = Stream(block, "sync", sems["sp"], dmasem=sems["spd"])

        ALLS = None

        def flush_all():
            for s in (PE, AC, DV, GP, SP):
                s.flush()

        def barrier():
            streams = [PE, AC, DV, GP, SP]
            for s in streams:
                for o in streams:
                    if o is not s:
                        s.wait(o)
            flush_all()

        GP.dman(lambda g: [
            g.dma_start(out=wq_bf, in_=wq_in),
            g.dma_start(out=wk_bf, in_=wk_in),
            g.dma_start(out=wv_bf, in_=wv_in),
            g.dma_start(out=wo_bf, in_=wo_in),
        ], 4)
        tok_wcast = GP.n
        GP.op(lambda g: g.iota(identi[:, :], pattern=[[-1, 128]], base=0, channel_multiplier=1))
        tok_iota = GP.n
        SP.dman(lambda s: [
            s.dma_start(out=ltmp[:, 0:128], in_=lq1),
            s.dma_start(out=ltmp[:, 128:256], in_=lk1),
            s.dma_start(out=ltmp[:, 256:384], in_=lq2),
            s.dma_start(out=ltmp[:, 384:512], in_=lk2),
        ], 4)
        tok_lload = SP.n

        DV.wait(GP, tok_iota)
        DV.opn(lambda v: [
            v.memset(onesb[:, :], 1.0),
            v.memset(ones128[:, :], 1.0),
            v.memset(ones1[:, :], 1.0),
            v.memset(epsLN[:, :], LN_EPS),
            v.memset(epsRMS[:, :], RMS_EPS),
            v.memset(epsFFN[:, :], FFN_EPS / 4096.0),
            v.tensor_scalar(ident[:, :], identi[:, :], 0, None, op0=ALU.is_equal),
            v.tensor_scalar(identh[:, :], identi[:, :], 0, None, op0=ALU.is_equal),
            v.tensor_scalar(identf[:, :], identi[:, :], 0, None, op0=ALU.is_equal),
        ], 9)
        DV.wait(SP, tok_lload)
        DV.wait(GP, tok_wcast)
        DV.opn(lambda v: [
            v.tensor_tensor(ltmp[:, 512:640], ltmp[:, 0:128], ltmp[:, 128:256], op=ALU.mult),
            v.tensor_tensor(ltmp[:, 640:768], ltmp[:, 256:384], ltmp[:, 384:512], op=ALU.mult),
        ], 2)
        DV.selfwait()
        DV.opn(lambda v: [
            v.tensor_reduce(lred[:, 0:1], ltmp[:, 512:640], op=ALU.add, axis=mybir.AxisListType.X),
            v.tensor_reduce(lred[:, 1:2], ltmp[:, 640:768], op=ALU.add, axis=mybir.AxisListType.X),
        ], 2)
        AC.wait(DV)
        AC.opn(lambda s: [s.activation(lred[:, 2:3], lred[:, 0:1], AF.Exp),
                          s.activation(lred[:, 3:4], lred[:, 1:2], AF.Exp)], 2)
        DV.wait(AC)
        DV.op(lambda v: v.tensor_tensor(lamt[:, :], lred[:, 2:3], lred[:, 3:4], op=ALU.subtract))
        DV.selfwait()
        DV.op(lambda v: v.tensor_scalar(lamt[:, :], lamt[:, :], LAMBDA_INIT, None, op0=ALU.add))
        PE.wait(DV)
        PE.mmgroup(lambda t: [t.matmul(PP[:, 0:1], ones1[0:1, :], lamt[0:1, 0:1], start=True, stop=True)])
        DV.wait(PE)
        DV.op(lambda v: v.tensor_scalar(nlam128[:, :], PP[:, 0:1], -1.0, None, op0=ALU.mult))
        tok_nlam = DV.n

        with ExitStack() as ph:
            Ep = ph.enter_context
            xt = [Ep(nc.sbuf_tensor(f"xt{i}", [128, D], F32)) for i in range(2)]
            xnt = [Ep(nc.sbuf_tensor(f"xnt{i}", [128, D], F16)) for i in range(2)]
            NST = max(D // 512, 1)
            stats = Ep(nc.sbuf_tensor("stats", [128, 6 * NST], F32))
            mv = Ep(nc.sbuf_tensor("mv", [128, 2], F32))
            negm = Ep(nc.sbuf_tensor("negm", [128, 1], F32))
            lnv = Ep(nc.sbuf_tensor("lnv", [128, 1], F32))
            rstd = Ep(nc.sbuf_tensor("rstd", [128, 1], F32))
            txp = [Ep(nc.sbuf_tensor(f"txp{i}", [128, 2048], F16)) for i in range(2)]

            tok_xn_dv = {}
            tok_xn_store = {}
            tok_xload = {}

            def emit_xload(si):
                if si > SCH:
                    return
                if si >= 2:
                    GP.wait(DV, tok_xn_dv[si - 2])
                row = 128 * (si % SCH)
                GP.op(lambda g, row=row, p=si % 2: g.dma_start(
                    out=xt[p][:, :], in_=x_in[row:row + 128, :]), inc=16)
                tok_xload[si] = GP.n

            emit_xload(0)
            for si in range(SCH):
                p = si % 2
                emit_xload(si + 1)
                DV.wait(GP, tok_xload[min(si + 1, SCH)])
                if si >= 2:
                    DV.wait(SP, tok_xn_store[si - 2])
                DV.opn(lambda v, p=p: [
                    v.bn_stats(stats[:, 6 * i:6 * i + 6], xt[p][:, 512 * i:512 * i + min(512, D)])
                    for i in range(NST)], NST)
                DV.selfwait()
                DV.op(lambda v: v.bn_aggr(mv[:, :], stats[:, :]))
                DV.selfwait()
                DV.op(lambda v: v.tensor_scalar(negm[:, :], mv[:, 0:1], -1.0, None, op0=ALU.mult))
                AC.wait(DV)
                AC.op(lambda s: s.activation(lnv[:, :], mv[:, 1:2], AF.Ln, bias=epsLN[:, :]))
                AC.selfwait()
                AC.op(lambda s: s.activation(rstd[:, :], lnv[:, :], AF.Exp, scale=-0.5))
                DV.wait(AC)
                tok_xn_dv[si] = DV.op(lambda v, p=p: v.tensor_scalar(
                    xnt[p][:, :], xt[p][:, :], negm[:, :], rstd[:, :], op0=ALU.add, op1=ALU.mult))
                SP.wait(DV, tok_xn_dv[si])
                tok_xn_store[si] = SP.op(lambda s, si=si, p=p: s.dma_start(
                    out=xn_nat[128 * si:128 * si + 128, :], in_=xnt[p][:, :]), inc=16)

            SP.selfwait()
            HW = min(2048, S)
            NHALF = S // HW
            for dc in range(DCH):
                for hf in range(NHALF):
                    p = (dc * NHALF + hf) % 2
                    SP.op(lambda s, dc=dc, hf=hf, p=p: s.dma_start_transpose(
                        txp[p][:, 0:HW],
                        xn_nat[HW * hf:HW * hf + HW, 128 * dc:128 * dc + 128]), inc=16)
                    SP.selfwait()
                    SP.op(lambda s, dc=dc, hf=hf, p=p: s.dma_start(
                        out=xnT_nat[128 * dc:128 * dc + 128, HW * hf:HW * hf + HW],
                        in_=txp[p][:, 0:HW]), inc=16)
            SP.selfwait()
            barrier()

        with ExitStack() as ph:
            Ep = ph.enter_context
            wqT = [Ep(nc.sbuf_tensor(f"wqT{i}", [128, DCH * 128], F16)) for i in range(2)]
            wkT = [Ep(nc.sbuf_tensor(f"wkT{i}", [128, DCH * 128], F16)) for i in range(2)]
            wvT = [Ep(nc.sbuf_tensor(f"wvT{i}", [128, DCH * 128], F16)) for i in range(2)]
            xnS = [Ep(nc.sbuf_tensor(f"xnS{i}", [128, DCH * 512], F16)) for i in range(2)]
            QTs = Ep(nc.sbuf_tensor("QTs", [128, S], F16))
            KTs = Ep(nc.sbuf_tensor("KTs", [128, S], F16))
            Vs = Ep(nc.sbuf_tensor("Vs", [128, S], F16))
            Et = [Ep(nc.sbuf_tensor(f"Et{i}", [128, 2 * QT], F16)) for i in range(4)]
            lnA1 = Ep(nc.sbuf_tensor("lnA1", [128, QT], F32))
            lnA2 = Ep(nc.sbuf_tensor("lnA2", [128, QT], F32))
            bc1 = Ep(nc.sbuf_tensor("bc1", [128, QT], F32))
            bc2 = Ep(nc.sbuf_tensor("bc2", [128, QT], F32))
            t1s = Ep(nc.sbuf_tensor("t1s", [128, QT], F32))
            t2s = Ep(nc.sbuf_tensor("t2s", [128, QT], F32))
            vcb = Ep(nc.sbuf_tensor("vcb", [128, QT], F32))
            sqs = Ep(nc.sbuf_tensor("sqs", [128, 512], F32))
            ssA = Ep(nc.sbuf_tensor("ssA", [128, 4], F32))
            lnA = Ep(nc.sbuf_tensor("lnA", [128, 4], F32))
            rsA = Ep(nc.sbuf_tensor("rsA", [128, 4], F32))
            nrm = Ep(nc.sbuf_tensor("nrm", [128, 2 * SCH * 128], F16))
            woT = [Ep(nc.sbuf_tensor(f"woT{i}", [128, D], F16)) for i in range(3)]
            sev = Ep(nc.sbuf_tensor("sev", [128, D], F32))
            pk = [Ep(nc.sbuf_tensor(f"pk{i}", [128, 256], I32)) for i in range(2)]
            bA = Ep(nc.sbuf_tensor("bA", [128, 2048], I32))
            bB = Ep(nc.sbuf_tensor("bB", [128, 2048], I32))
            bC = Ep(nc.sbuf_tensor("bC", [128, 2048], I32))
            bD = Ep(nc.sbuf_tensor("bD", [128, 2048], F32))
            Wf = Ep(nc.sbuf_tensor("Wf", [128, 2048], F32))
            dgu = [Ep(nc.sbuf_tensor(f"dgu{i}", [128, D], F16)) for i in range(2)]
            ddn0 = Ep(nc.sbuf_tensor("ddn0", [128, I], F16))
            ddn = [ddn0, ddn0]
            scg = [Ep(nc.sbuf_tensor(f"scg{i}", [128, 64], F32)) for i in range(2)]
            scd = [Ep(nc.sbuf_tensor(f"scd{i}", [128, 64], F32)) for i in range(2)]

            bAf = bA.bitcast(F32)
            bBf = bB.bitcast(F32)
            bDi = bD.bitcast(I32)
            WfR = Wf.bitcast(I32).rearrange("p (w e) -> p e w", e=8)

            def dequant_strip(nw, src_pk):
                DV.opn(lambda v: [
                    v.tensor_scalar(bA[:, 256 * j:256 * j + nw], src_pk[:, 0:nw],
                                    4 * j, 7, op0=ALU.logical_shift_right, op1=ALU.bitwise_and)
                    for j in range(8)] + [
                    v.tensor_scalar(bC[:, 256 * j:256 * j + nw], src_pk[:, 0:nw],
                                    28 - 4 * j, -2147483648,
                                    op0=ALU.logical_shift_left, op1=ALU.bitwise_and)
                    for j in range(8)], 16)
                tok_rd = DV.n
                DV.selfwait()
                DV.opn(lambda v: [
                    v.tensor_scalar(bB[:, 256 * j:256 * j + nw], bA[:, 256 * j:256 * j + nw],
                                    252.0, float(1 << 22), op0=ALU.add, op1=ALU.mult)
                    for j in range(8)], 8)
                DV.selfwait()
                DV.opn(lambda v: [
                    v.tensor_scalar(bAf[:, 256 * j:256 * j + nw], bBf[:, 256 * j:256 * j + nw],
                                    2.0, 1.0, op0=ALU.mult, op1=ALU.subtract)
                    for j in range(8)], 8)
                DV.selfwait()
                DV.opn(lambda v: [
                    v.tensor_tensor(bD[:, 256 * j:256 * j + nw], bAf[:, 256 * j:256 * j + nw],
                                    bBf[:, 256 * j:256 * j + nw], op=ALU.min)
                    for j in range(8)], 8)
                DV.selfwait()
                DV.opn(lambda v: [
                    v.tensor_tensor(WfR[:, j, 0:nw], bDi[:, 256 * j:256 * j + nw],
                                    bC[:, 256 * j:256 * j + nw], op=ALU.bitwise_or)
                    for j in range(8)], 8)
                DV.selfwait()
                return tok_rd

            pk_sched = [("g", ot, 0, WI_GU) for ot in range(NOT_GU)]
            NSTRIP_DN = (WI_DN + 255) // 256
            for ot_ in range(NOT_DN):
                for sti_ in range(NSTRIP_DN):
                    w0_ = 256 * sti_
                    pk_sched.append(("d", ot_, w0_, min(256, WI_DN - w0_)))
            pk_sched.append(("g", 0, 0, WI_GU))
            pktok = {}
            pkemit = [0]
            tok_pk_rd = {}
            tok_sc_used = {}

            def emit_pk(upto):
                while pkemit[0] <= min(upto, len(pk_sched) - 1):
                    k = pkemit[0]
                    if k >= 2:
                        GP.wait(DV, tok_pk_rd.get(k - 2, 0))
                    kind, ot_, w0_, nw_ = pk_sched[k]
                    if kind == "g":
                        GP.op(lambda g, ot_=ot_, p2=k % 2: g.dma_start(
                            out=pk[p2][:, 0:WI_GU], in_=gup[128 * ot_:128 * ot_ + 128, :]), inc=16)
                        if k < NOT_GU:
                            GP.wait(DV, tok_sc_used.get(("g", ot_ - 2), 0))
                            GP.op(lambda g, ot_=ot_, ps=ot_ % 2: g.dma_start(
                                out=scg[ps][:, 0:NGU], in_=gusT[128 * ot_:128 * ot_ + 128, :]), inc=16)
                    else:
                        GP.op(lambda g, ot_=ot_, w0_=w0_, nw_=nw_, p2=k % 2: g.dma_start(
                            out=pk[p2][:, 0:nw_], in_=dnp[128 * ot_:128 * ot_ + 128, w0_:w0_ + nw_]), inc=16)
                        if w0_ == 0:
                            GP.wait(DV, tok_sc_used.get(("d", ot_ - 2), 0))
                            GP.op(lambda g, ot_=ot_, ps=ot_ % 2: g.dma_start(
                                out=scd[ps][:, 0:NGD], in_=dnsT[128 * ot_:128 * ot_ + 128, :]), inc=16)
                    pktok[k] = GP.n
                    pkemit[0] += 1

            def pk_wait(k):
                emit_pk(k + 1)
                DV.wait(GP, pktok[min(k + 1, len(pk_sched) - 1)])

            scnt = [0]
            tok_store = {}

            def p3_steps():
              sc = scnt
              for ot in range(NOT_GU):
                p2 = sc[0] % 2
                pk_wait(sc[0])
                tok_pk_rd[sc[0]] = dequant_strip(WI_GU, pk[p2])
                if ot >= 2:
                    DV.wait(GP, tok_store.get(("g", ot - 2), 0))
                tok_sc_used[("g", ot)] = DV.opn(lambda v, p2=p2, ps=ot % 2: [
                    v.tensor_scalar(dgu[p2][:, 128 * g2:128 * g2 + 128], Wf[:, 128 * g2:128 * g2 + 128],
                                    scg[ps][:, g2:g2 + 1], None, op0=ALU.mult)
                    for g2 in range(NGU)], NGU)
                DV.selfwait()
                sc[0] += 1
                GP.wait(DV, tok_sc_used[("g", ot)])
                tok_store[("g", ot)] = GP.op(lambda g, ot=ot, p2=p2: g.dma_start(
                    out=wgu_dense[128 * ot:128 * ot + 128, :], in_=dgu[p2][:, :]), inc=16)
                yield
              for ot in range(NOT_DN):
                p2d = ot % 2
                for sti in range(NSTRIP_DN):
                    w0 = 256 * sti
                    nw = min(256, WI_DN - w0)
                    p2 = sc[0] % 2
                    pk_wait(sc[0])
                    tok_pk_rd[sc[0]] = dequant_strip(nw, pk[p2])
                    g0 = (8 * w0) // 128
                    ng = (8 * nw) // 128
                    if sti == 0 and ot >= 1:
                        DV.wait(GP, tok_store.get(("d", ot - 1), 0))
                    DV.opn(lambda v, p2d=p2d, g0=g0, ng=ng: [
                        v.tensor_scalar(ddn[p2d][:, 128 * (g0 + g2):128 * (g0 + g2) + 128],
                                        Wf[:, 128 * g2:128 * g2 + 128],
                                        scd[p2d][:, g0 + g2:g0 + g2 + 1], None, op0=ALU.mult)
                        for g2 in range(ng)], ng)
                    DV.selfwait()
                    if sti == NSTRIP_DN - 1:
                        tok_sc_used[("d", ot)] = DV.n
                    sc[0] += 1
                    yield
                GP.wait(DV)
                tok_store[("d", ot)] = GP.op(lambda g, ot=ot, p2d=p2d: g.dma_start(
                    out=wdn_dense[128 * ot:128 * ot + 128, :], in_=ddn[p2d][:, :]), inc=16)

            p3_iter = p3_steps()

            def p3_pump(n):
                for _ in range(n):
                    if next(p3_iter, "END") == "END":
                        return

            S0, S1c = 0, 1024
            AV1, AV2 = 2048, 2560
            D1C, D2C = 3072, 3584
            QC, KC, VC = 0, 512, 1024
            TRC = 1024

            xn_use = {}
            xn_tok = {}
            xcnt = 0
            tok_wT = {}
            wo_use = {}
            tok_nrm_last = {}
            tok_wo_mm = {}

            for pr in range(HC):
                pp = pr % 2
                if pr >= 2:
                    SP.wait(PE, tok_wT[("consumed", pr - 2)])
                SP.wait(GP, tok_wcast)
                SP.dman(lambda s, pr=pr, pp=pp: [
                    s.dma_start(
                        out=w_t[pp][:, 128 * dc:128 * dc + 128],
                        in_=w_b[128 * dc:128 * dc + 128, 128 * pr:128 * pr + 128])
                    for (w_t, w_b) in ((wqT, wq_bf), (wkT, wk_bf), (wvT, wv_bf))
                    for dc in range(DCH)], 3 * DCH)
                SP.op(lambda s: s.dma_start(out=ltmp[:, 0:128], in_=lq1), inc=16)
                tok_wT[pr] = SP.n

                tok_qkv_evac = 0
                for st in range(NQT):
                    sp2 = xcnt % 2
                    for k in (xcnt, xcnt + 1):
                        if k not in xn_tok and k <= HC * NQT:
                            if k >= 2:
                                GP.wait(PE, xn_use.get(k - 2, 0))
                            st_k = k % NQT
                            GP.op(lambda g, st_k=st_k, p2=k % 2: g.dma_start(
                                out=xnS[p2][:, :].rearrange("p (c s) -> p c s", c=DCH),
                                in_=xnT_nat.rearrange("(c p) s -> p c s", p=128)[:, :, QT * st_k:QT * st_k + QT]),
                            inc=16)
                            xn_tok[k] = GP.n
                    PE.wait(GP, xn_tok[min(xcnt + 1, HC * NQT)])
                    PE.wait(SP, tok_wT[pr])
                    if st == 0:
                        PE.wait(DV)
                        PE.wait(AC)
                    else:
                        PE.wait(DV, tok_qkv_evac)
                    PE.mmgroup(lambda t, pp=pp, sp2=sp2: [
                        t.matmul(PP[:, QC:QC + 512], wqT[pp][:, 128 * dc:128 * dc + 128],
                                 xnS[sp2][:, 512 * dc:512 * dc + 512],
                                 start=(dc == 0), stop=(dc == DCH - 1))
                        for dc in range(DCH)])
                    PE.mmgroup(lambda t, pp=pp, sp2=sp2: [
                        t.matmul(PP[:, KC:KC + 512], wkT[pp][:, 128 * dc:128 * dc + 128],
                                 xnS[sp2][:, 512 * dc:512 * dc + 512],
                                 start=(dc == 0), stop=(dc == DCH - 1))
                        for dc in range(DCH)])
                    PE.mmgroup(lambda t, pp=pp, sp2=sp2: [
                        t.matmul(PP[:, VC + 128 * q:VC + 128 * q + 128],
                                 xnS[sp2][:, 512 * dc + 128 * q:512 * dc + 128 * q + 128],
                                 wvT[pp][:, 128 * dc:128 * dc + 128],
                                 start=(dc == 0), stop=(dc == DCH - 1))
                        for q in range(4) for dc in range(DCH)])
                    xn_use[xcnt] = PE.n
                    xcnt += 1
                    if st == NQT - 1:
                        tok_wT[("consumed", pr)] = PE.n
                    DV.wait(PE)
                    tok_qkv_evac = DV.opn(lambda v, st=st: [
                        v.tensor_copy(QTs[:, QT * st:QT * st + QT], PP[:, QC:QC + 512]),
                        v.tensor_copy(KTs[:, QT * st:QT * st + QT], PP[:, KC:KC + 512]),
                        v.tensor_copy(Vs[:, QT * st:QT * st + QT], PP[:, VC:VC + 512]),
                    ], 3)
                tok_qkv_dv = DV.n

                for qt in range(NQT):
                    tok_scores = {}
                    tok_exp = {}
                    tok_cons = {}

                    def emit_dav(kc, tok_exp=tok_exp, tok_cons=tok_cons):
                        PE.wait(AC, tok_exp[kc])
                        ep4 = kc % 4
                        PE.mmgroup(lambda t, kc=kc, ep4=ep4: [
                            t.matmul(PP[:, D1C:D1C + QT], ones128[:, :], Et[ep4][:, 0:QT],
                                     start=(kc == 0), stop=(kc == KCH - 1)),
                            t.matmul(PP[:, D2C:D2C + QT], ones128[:, :], Et[ep4][:, QT:2 * QT],
                                     start=(kc == 0), stop=(kc == KCH - 1)),
                            t.matmul(PP[:, AV1:AV1 + QT], Vs[:, 128 * kc:128 * kc + 128],
                                     Et[ep4][:, 0:QT], start=(kc == 0), stop=(kc == KCH - 1)),
                            t.matmul(PP[:, AV2:AV2 + QT], Vs[:, 128 * kc:128 * kc + 128],
                                     Et[ep4][:, QT:2 * QT], start=(kc == 0), stop=(kc == KCH - 1)),
                        ])
                        tok_cons[kc] = PE.n

                    for kc in range(KCH):
                        sp2 = kc % 2
                        if kc == 0:
                            PE.wait(DV, tok_qkv_dv)
                            if qt > 0:
                                PE.wait(DV, tok_nrm_last[(pr, qt - 1)])
                        if kc >= 2:
                            PE.wait(AC, tok_exp[kc - 2])
                        PE.mmgroup(lambda t, kc=kc, qt=qt, sp2=sp2: [
                            t.matmul(PP[:, 1024 * sp2:1024 * sp2 + 512],
                                     KTs[0:64, 128 * kc:128 * kc + 128],
                                     QTs[0:64, QT * qt:QT * qt + QT],
                                     start=True, stop=True, tile_position=(0, 0)),
                            t.matmul(PP[:, 1024 * sp2 + 512:1024 * sp2 + 1024],
                                     KTs[64:128, 128 * kc:128 * kc + 128],
                                     QTs[64:128, QT * qt:QT * qt + QT],
                                     start=True, stop=True, tile_position=(64, 0)),
                        ])
                        tok_scores[kc] = PE.n
                        AC.wait(PE, tok_scores[kc])
                        if kc >= 4:
                            AC.wait(PE, tok_cons[kc - 4])
                        tok_exp[kc] = AC.op(lambda s, sp2=sp2, ep4=kc % 4: s.activation(
                            Et[ep4][:, :], PP[:, 1024 * sp2:1024 * sp2 + 2 * QT],
                            AF.Exp, scale=SM_SCALE))
                        if kc >= 1:
                            emit_dav(kc - 1)
                    emit_dav(KCH - 1)

                    AC.wait(PE, tok_cons[KCH - 1])
                    AC.opn(lambda s: [
                        s.activation(lnA1[:, :], PP[:, D1C:D1C + QT], AF.Ln),
                        s.activation(lnA2[:, :], PP[:, D2C:D2C + QT], AF.Ln),
                    ], 2)
                    AC.selfwait()
                    AC.opn(lambda s: [
                        s.activation(bc1[:, :], lnA1[:, :], AF.Exp, scale=-1.0),
                        s.activation(bc2[:, :], lnA2[:, :], AF.Exp, scale=-1.0),
                    ], 2)
                    tok_alpha = AC.n
                    DV.wait(AC, tok_alpha)
                    DV.wait(DV, tok_nlam)
                    DV.opn(lambda v: [
                        v.tensor_tensor(t1s[:, :], PP[:, AV1:AV1 + QT], bc1[:, :], op=ALU.mult),
                        v.scalar_tensor_tensor(t2s[:, :], PP[:, AV2:AV2 + QT], nlam128[:, :],
                                               bc2[:, :], op0=ALU.mult, op1=ALU.mult),
                    ], 2)
                    DV.selfwait()
                    tok_comb = DV.op(lambda v: v.tensor_tensor(vcb[:, :], t1s[:, :], t2s[:, :], op=ALU.add))
                    PE.wait(DV, tok_comb)
                    PE.mmgroup(lambda t: [
                        t.transpose(PP[:, TRC + 128 * ti:TRC + 128 * ti + 128],
                                    vcb[:, 128 * ti:128 * ti + 128], identf[:, :])
                        for ti in range(NT)])
                    AC.wait(PE)
                    AC.opn(lambda s: [
                        s.activation(sqs[:, 128 * ti:128 * ti + 128],
                                     PP[:, TRC + 128 * ti:TRC + 128 * ti + 128],
                                     AF.Square, accum_out=ssA[:, ti:ti + 1])
                        for ti in range(NT)], NT)
                    AC.selfwait()
                    AC.opn(lambda s: [
                        s.activation(lnA[:, ti:ti + 1], ssA[:, ti:ti + 1], AF.Ln,
                                     scale=1.0 / 128.0, bias=epsRMS[:, :])
                        for ti in range(NT)], NT)
                    AC.selfwait()
                    AC.opn(lambda s: [
                        s.activation(rsA[:, ti:ti + 1], lnA[:, ti:ti + 1], AF.Exp, scale=-0.5)
                        for ti in range(NT)], NT)
                    DV.wait(AC)
                    if qt == 0 and pr >= 2:
                        DV.wait(PE, tok_wo_mm[(pr - 2) | 1])
                    po = (pr % 2) * SCH * 128
                    tok_nrm_last[(pr, qt)] = DV.opn(lambda v, qt=qt, po=po: [
                        v.tensor_scalar(nrm[:, po + (qt * NT + ti) * 128:po + (qt * NT + ti) * 128 + 128],
                                        PP[:, TRC + 128 * ti:TRC + 128 * ti + 128],
                                        rsA[:, ti:ti + 1], 1.0 - LAMBDA_INIT,
                                        op0=ALU.mult, op1=ALU.mult)
                        for ti in range(NT)], NT)
                    p3_pump(2)

                if pr % 2 == 1 or pr == HC - 1:
                    duo = (pr - 1, pr) if pr % 2 == 1 else (pr,)
                    combos = [(pa, sh) for pa in duo for sh in (0, 1)]
                    ngrp = (len(combos) + 1) // 2
                    nwo = ngrp * CC
                    wotok = {}

                    def emit_woload(k, nwo=nwo):
                        if k > nwo or k in wotok:
                            return
                        if k >= 3:
                            SP.wait(PE, wo_use[k - 3])
                        cck = k % CC
                        SP.op(lambda s, cck=cck, wp=k % 3: s.dma_start(
                            out=woT[wp][:, :], in_=wo_bf[128 * cck:128 * cck + 128, :]), inc=16)
                        wotok[k] = SP.n

                    woci = 0
                    prev_sstore = 0
                    emit_woload(0)
                    for g0 in range(0, len(combos), 2):
                        sub = combos[g0:g0 + 2]
                        for cc in range(CC):
                            wp = woci % 3
                            emit_woload(woci + 1)
                            PE.wait(SP, wotok[min(woci + 1, nwo)])
                            if cc == 0:
                                PE.wait(DV)
                            def womms(t, sub=sub, cc=cc, wp=wp):
                                ins = []
                                for ci, (pa, sh) in enumerate(sub):
                                    po2 = (pa % 2) * SCH * 128
                                    lhs = nrm[:, po2 + (sh * CC + cc) * 128:po2 + (sh * CC + cc) * 128 + 128]
                                    for j in range(NJ):
                                        ins.append(t.matmul(
                                            PP[:, 2048 * ci + JT * j:2048 * ci + JT * j + JT],
                                            lhs, woT[wp][:, JT * j:JT * j + JT],
                                            start=(cc == 0), stop=(cc == CC - 1)))
                                return ins
                            PE.mmgroup(womms)
                            wo_use[woci] = PE.n
                            woci += 1
                        tok_wo_mm[pr] = PE.n
                        for ci, (pa, sh) in enumerate(sub):
                            ul = 2 * pa + sh
                            DV.wait(PE)
                            DV.wait(GP, prev_sstore)
                            DV.op(lambda v, ci=ci: v.tensor_copy(sev[:, :], PP[:, 2048 * ci:2048 * ci + D]))
                            GP.wait(DV)
                            prev_sstore = GP.op(lambda g, ul=ul: g.dma_start(
                                out=s_stash[128 * ul:128 * ul + 128, :], in_=sev[:, :]), inc=16)
            p3_pump(10 ** 9)
            emit_pk(len(pk_sched) - 1)
            GP.selfwait()
            barrier()

        with ExitStack() as ph:
            Ep = ph.enter_context
            xnU = Ep(nc.sbuf_tensor("xnU", [128, 4 * DCH * 128], F16))
            wsb = [Ep(nc.sbuf_tensor(f"wsb{i}", [128, 1024], F16)) for i in range(DCH)]
            wsd = [Ep(nc.sbuf_tensor(f"wsd{i}", [128, 2 * JT], F16)) for i in range(4)]
            gsil = Ep(nc.sbuf_tensor("gsil", [128, 2 * 4 * 512], F16))
            hsb = Ep(nc.sbuf_tensor("hsb", [128, 4 * 512], F16))
            hT = Ep(nc.sbuf_tensor("hT", [128, 4 * ICH * 128], F16))
            ssF = Ep(nc.sbuf_tensor("ssF", [128, 1], F32))
            lnF = Ep(nc.sbuf_tensor("lnF", [128, 1], F32))
            rsF = Ep(nc.sbuf_tensor("rsF", [128, 1], F32))
            ybuf = [Ep(nc.sbuf_tensor(f"ybuf{i}", [128, D], F32)) for i in range(4)]
            gbuf = Ep(nc.sbuf_tensor("gbuf", [128, D], F32))
            sbuf2 = [Ep(nc.sbuf_tensor(f"sbuf2{i}", [128, D], F32)) for i in range(2)]
            xbuf = [Ep(nc.sbuf_tensor(f"xbuf{i}", [128, D], F32)) for i in range(2)]
            obuf = Ep(nc.sbuf_tensor("obuf", [128, D], F32))

            NUG = (U + 3) // 4
            ogps = [list(range(p, min(p + 2, OG))) for p in range(0, OG, 2)]
            jps = [list(range(p, min(p + 2, NJ))) for p in range(0, NJ, 2)]
            NOGH = len(ogps)
            gu_tok = {}
            gu_margin = {}

            def emit_gu_load(ug_, ogh_, ph2_, dc_, guard):
                ogs_ = ogps[ogh_]
                rb_ = 512 * ogs_[0] + (I if ph2_ else 0)
                w_ = 512 * len(ogs_)
                if guard:
                    SP.wait(PE, guard)
                gu_tok[(ug_, ogh_, ph2_, dc_)] = SP.dmaop(
                    lambda s, dc_=dc_, rb_=rb_, w_=w_: s.dma_start_transpose(
                        wsb[dc_][:, 0:w_],
                        wgu_dense[rb_:rb_ + w_, 128 * dc_:128 * dc_ + 128]))

            def emit_gu_margin(ug_, ogh_, ph2_):
                ogs_ = ogps[ogh_]
                rb_ = 512 * ogs_[0] + (I if ph2_ else 0)
                w_ = 512 * len(ogs_)
                gu_margin[(ug_, ogh_, ph2_)] = SP.dmaop(
                    lambda s, rb_=rb_, w_=w_: s.dma_start_transpose(
                        wsb[DCH - 1][:, 0:w_],
                        wgu_dense[rb_:rb_ + w_, 128 * (DCH - 1):128 * DCH]))

            def gu_next_tok(ug_, ogh_, ph2_, dc_):
                if dc_ < DCH - 1:
                    return gu_tok[(ug_, ogh_, ph2_, dc_ + 1)]
                return gu_margin[(ug_, ogh_, ph2_)]

            def mk_dn_pipeline():
                tiles = [(js_, ic_) for js_ in jps for ic_ in range(ICH)]
                state = dict(tok={}, consumed={}, emitted=0)

                def emit(idx):
                    while state["emitted"] <= min(idx, len(tiles) - 1):
                        L = state["emitted"]
                        js_, ic_ = tiles[L]
                        if L >= 4:
                            SP.wait(PE, state["consumed"][L - 4])
                        w_ = JT * len(js_)
                        state["tok"][L] = SP.dmaop(
                            lambda s, L=L, js_=js_, ic_=ic_, w_=w_: s.dma_start_transpose(
                                wsd[L % 4][:, 0:w_],
                                wdn_dense[JT * js_[0]:JT * js_[0] + w_, 128 * ic_:128 * ic_ + 128]))
                        state["emitted"] += 1
                    if state["emitted"] == len(tiles) and "margin" not in state:
                        js_, ic_ = tiles[-1]
                        w_ = JT * len(js_)
                        state["margin"] = SP.dmaop(
                            lambda s, js_=js_, ic_=ic_, w_=w_: s.dma_start_transpose(
                                wsd[(len(tiles) - 1) % 4][:, 0:w_],
                                wdn_dense[JT * js_[0]:JT * js_[0] + w_, 128 * ic_:128 * ic_ + 128]))
                return state, tiles, emit

            for dc_ in range(DCH):
                emit_gu_load(0, 0, 0, dc_, 0)
            emit_gu_margin(0, 0, 0)

            tok_xnU_used = 0
            tok_g_ac = 0
            tok_gbuf_free = 0
            tok_out_dv = 0
            tok_obuf_store = 0
            sxtok = {}
            sx_guard = {}
            tok_ydump = 0

            for ug in range(NUG):
                us = list(range(4 * ug, min(4 * ug + 4, U)))
                nu = len(us)
                if ug > 0:
                    SP.wait(PE, tok_xnU_used)
                SP.dman(lambda s, us=us: [
                    s.dma_start_transpose(
                        xnU[:, (ui * DCH + dc) * 128:(ui * DCH + dc) * 128 + 128],
                        xn_nat.rearrange("(m blk) d -> blk m d", blk=BLK)[u, :, 128 * dc:128 * dc + 128])
                    for ui, u in enumerate(us) for dc in range(DCH)], nu * DCH)
                SP.op(lambda s, us=us, nu=nu: s.dma_start_transpose(
                    xnU[:, (nu * DCH - 1) * 128:nu * DCH * 128],
                    xn_nat.rearrange("(m blk) d -> blk m d", blk=BLK)[us[-1], :, 128 * (DCH - 1):128 * DCH]), inc=16)
                tok_xnU = SP.n

                tok_h_dv = 0
                tok_hTdv = 0
                tok_hT_pe = 0
                tok_yevac = {}
                for ogh, ogs in enumerate(ogps):
                    tok_gsil = {}
                    for pos, og in enumerate(ogs):
                        lastpos = pos == len(ogs) - 1
                        for dc in range(DCH):
                            PE.waitd(SP, gu_next_tok(ug, ogh, 0, dc))
                            if dc == 0:
                                PE.wait(SP, tok_xnU)
                                if pos == 0:
                                    if og > 0 or ug > 0:
                                        PE.wait(DV, tok_hTdv if og > 0 else tok_ylast)
                                else:
                                    PE.wait(DV, tok_h_dv)
                            PE.mmgroup(lambda t, us=us, dc=dc, pos=pos: [
                                t.matmul(PP[:, 2048 * pos + 512 * ui:2048 * pos + 512 * ui + 512],
                                         xnU[:, (ui * DCH + dc) * 128:(ui * DCH + dc) * 128 + 128],
                                         wsb[dc][:, 512 * pos:512 * pos + 512],
                                         start=(dc == 0), stop=(dc == DCH - 1))
                                for ui in range(len(us))])
                            if lastpos:
                                emit_gu_load(ug, ogh, 1, dc, PE.n)
                                if dc == DCH - 1:
                                    emit_gu_margin(ug, ogh, 1)
                        AC.wait(PE)
                        AC.wait(DV, tok_h_dv)
                        tok_gsil[pos] = AC.opn(lambda s, nu=nu, pos=pos: [
                            s.activation(gsil[:, 2048 * pos + 512 * ui:2048 * pos + 512 * ui + 512],
                                         PP[:, 2048 * pos + 512 * ui:2048 * pos + 512 * ui + 512],
                                         AF.Silu)
                            for ui in range(nu)], nu)
                    for pos, og in enumerate(ogs):
                        lastpos = pos == len(ogs) - 1
                        for dc in range(DCH):
                            PE.waitd(SP, gu_next_tok(ug, ogh, 1, dc))
                            if dc == 0:
                                PE.wait(AC, tok_gsil[pos])
                            PE.mmgroup(lambda t, us=us, dc=dc, pos=pos: [
                                t.matmul(PP[:, 2048 * pos + 512 * ui:2048 * pos + 512 * ui + 512],
                                         xnU[:, (ui * DCH + dc) * 128:(ui * DCH + dc) * 128 + 128],
                                         wsb[dc][:, 512 * pos:512 * pos + 512],
                                         start=(dc == 0), stop=(dc == DCH - 1))
                                for ui in range(len(us))])
                            if lastpos:
                                nxt = (ug, ogh + 1) if ogh + 1 < NOGH else (
                                    (ug + 1, 0) if ug + 1 < NUG else None)
                                if nxt is not None:
                                    emit_gu_load(nxt[0], nxt[1], 0, dc, PE.n)
                                    if dc == DCH - 1:
                                        emit_gu_margin(nxt[0], nxt[1], 0)
                        DV.wait(PE)
                        DV.wait(AC, tok_gsil[pos])
                        DV.wait(PE, tok_hT_pe)
                        tok_h_dv = DV.opn(lambda v, nu=nu, pos=pos: [
                            v.scalar_tensor_tensor(
                                hsb[:, 512 * ui:512 * ui + 512],
                                PP[:, 2048 * pos + 512 * ui:2048 * pos + 512 * ui + 512], 1.0 / 64.0,
                                gsil[:, 2048 * pos + 512 * ui:2048 * pos + 512 * ui + 512],
                                op0=ALU.mult, op1=ALU.mult)
                            for ui in range(nu)], nu)
                        PE.wait(DV, tok_h_dv)
                        if tok_hTdv:
                            PE.wait(DV, tok_hTdv)
                        PE.mmgroup(lambda t, nu=nu: [
                            t.transpose(PPh[:, 128 * (4 * ui + ti):128 * (4 * ui + ti) + 128],
                                        hsb[:, 512 * ui + 128 * ti:512 * ui + 128 * ti + 128], identh[:, :])
                            for ui in range(nu) for ti in range(4)])
                        tok_hT_pe = PE.n
                        DV.wait(PE, tok_hT_pe)
                        tok_hTdv = DV.opn(lambda v, og=og, nu=nu: [
                            v.tensor_copy(hT[:, (ui * ICH + og * 4 + ti) * 128:(ui * ICH + og * 4 + ti) * 128 + 128],
                                          PPh[:, 128 * (4 * ui + ti):128 * (4 * ui + ti) + 128])
                            for ui in range(nu) for ti in range(4)], nu * 4)
                tok_xnU_used = PE.n
                if debug and ug == 0:
                    SP.wait(DV, tok_hTdv)
                    SP.dman(lambda s, nu=nu: [
                        s.dma_start(out=h_dbg[128 * ui:128 * ui + 128, :], in_=hT[:, ui * ICH * 128:(ui + 1) * ICH * 128])
                        for ui in range(nu)], nu)

                dnstate, dntiles, dn_emit = mk_dn_pipeline()
                NDL = len(dntiles)
                for jh, js in enumerate(jps):
                    for ic in range(ICH):
                        L = jh * ICH + ic
                        dn_emit(L + 3)
                        PE.waitd(SP, dnstate["tok"][L + 1] if L + 1 < NDL else dnstate["margin"])
                        if ic == 0:
                            if jh == 0:
                                PE.wait(DV, tok_hTdv)
                            else:
                                PE.wait(DV, tok_yevac[jh - 1])
                            if ug > 0 and jh == 0:
                                PE.wait(AC, tok_g_ac)
                        wp = L % 4
                        for pos, j in enumerate(js):
                            PE.mmgroup(lambda t, j=j, ic=ic, wp=wp, pos=pos, nu=nu: [
                                t.matmul(PP[:, 2048 * (j % 2) + 512 * ui:2048 * (j % 2) + 512 * ui + JT],
                                         hT[:, (ui * ICH + ic) * 128:(ui * ICH + ic) * 128 + 128],
                                         wsd[wp][:, JT * pos:JT * pos + JT],
                                         start=(ic == 0), stop=(ic == ICH - 1))
                                for ui in range(nu)])
                        dnstate["consumed"][L] = PE.n
                    DV.wait(PE)
                    if ug > 0 and jh == 0:
                        DV.wait(AC, tok_g_ac)
                        DV.wait(SP, tok_ydump)
                    tok_yevac[jh] = DV.opn(lambda v, js=js, nu=nu: [
                        v.tensor_copy(ybuf[ui][:, JT * j:JT * j + JT],
                                      PP[:, 2048 * (j % 2) + 512 * ui:2048 * (j % 2) + 512 * ui + JT])
                        for j in js for ui in range(nu)], len(js) * nu)
                tok_ylast = tok_yevac[len(jps) - 1]

                if debug:
                    SP.wait(DV, tok_ylast)
                    SP.dman(lambda s, us=us: [
                        s.dma_start(out=y_dbg[128 * u:128 * u + 128, :], in_=ybuf[ui][:, :])
                        for ui, u in enumerate(us)], nu)
                    tok_ydump = SP.n

                for ui, u in enumerate(us):
                    ul = u
                    for uk in (u, u + 1):
                        if uk not in sxtok and uk <= U:
                            GP.wait(DV, sx_guard.get(uk - 2, 0))
                            row = 128 * (uk % U)
                            GP.op(lambda g, row=row, p=uk % 2: g.dma_start(
                                out=sbuf2[p][:, :], in_=s_stash[row:row + 128, :]), inc=16)
                            GP.op(lambda g, p=uk % 2, blk=uk % U: g.dma_start(
                                out=xbuf[p][:, :],
                                in_=x_in.rearrange("(m blk) d -> blk m d", blk=BLK)[blk, :, :]), inc=16)
                            sxtok[uk] = GP.n
                    AC.wait(DV, tok_ylast)
                    AC.wait(DV, tok_gbuf_free)
                    AC.op(lambda s, ui=ui: s.activation(
                        gbuf[:, :], ybuf[ui][:, :], AF.Square, accum_out=ssF[:, :]))
                    AC.selfwait()
                    AC.op(lambda s: s.activation(lnF[:, :], ssF[:, :], AF.Ln,
                                                 scale=1.0 / D, bias=epsFFN[:, :]))
                    AC.selfwait()
                    AC.op(lambda s: s.activation(rsF[:, :], lnF[:, :], AF.Exp, scale=-0.5))
                    AC.selfwait()
                    tok_g_ac = AC.op(lambda s, ui=ui: s.activation(
                        gbuf[:, :], ybuf[ui][:, :], AF.Silu, scale=rsF[:, :]))
                    DV.wait(AC, tok_g_ac)
                    DV.wait(GP, sxtok[u + 1])
                    DV.wait(GP, tok_obuf_store)
                    tok_gbuf_free = DV.op(lambda v, p=u % 2: v.tensor_tensor(
                        obuf[:, :], gbuf[:, :], sbuf2[p][:, :], op=ALU.mult))
                    DV.selfwait()
                    tok_out_dv = DV.op(lambda v, p=u % 2: v.tensor_tensor(
                        obuf[:, :], obuf[:, :], xbuf[p][:, :], op=ALU.add))
                    sx_guard[u] = tok_out_dv
                    GP.wait(DV, tok_out_dv)
                    tok_obuf_store = GP.op(lambda g, ul=ul: g.dma_start(
                        out=out_c[128 * ul:128 * ul + 128, :], in_=obuf[:, :]), inc=16)
            GP.selfwait()
            GP.wait(SP)
            SP.selfwait()
            flush_all()
    return nc



def shard_inputs(cfg, inputs, n_cores=8):
    c = _cfg(cfg)
    S, D, HC, BLK, SH = c["S"], c["D"], c["HC"], c["BLK"], c["SH"]
    x = np.ascontiguousarray(inputs["L0_attn_x"])
    wq, wk, wv, wo = (inputs["L0_attn_wq"], inputs["L0_attn_wk"],
                      inputs["L0_attn_wv"], inputs["L0_attn_wo"])
    maps = []
    for core in range(n_cores):
        b = core // 2
        side = core % 2
        q = np.arange(S) // BLK
        k = np.arange(S) % BLK
        src = q * BLK + (k + side * SH) % BLK
        cp = np.arange(D)
        colperm = (cp // BLK) * BLK + (cp % BLK + side * SH) % BLK
        h0 = HC * side
        m = {
            "x_in": x[b][src],
            "wq_in": wq[128 * h0:128 * (h0 + HC)].T,
            "wk_in": wk[128 * h0:128 * (h0 + HC)].T,
            "wv_in": wv[128 * h0:128 * (h0 + HC)].T,
            "wo_in": wo[:, colperm].T,
            "lq1": np.asarray(inputs["L0_attn_lambda_q1"]).reshape(1, 128),
            "lq2": np.asarray(inputs["L0_attn_lambda_q2"]).reshape(1, 128),
            "lk1": np.asarray(inputs["L0_attn_lambda_k1"]).reshape(1, 128),
            "lk2": np.asarray(inputs["L0_attn_lambda_k2"]).reshape(1, 128),
            "gup": inputs["L0_ffn_gate_up_packed"],
            "gusT": inputs["L0_ffn_gate_up_scales"].T,
            "dnp": inputs["L0_ffn_down_packed"],
            "dnsT": inputs["L0_ffn_down_scales"].T,
        }
        maps.append({k2: np.ascontiguousarray(v) for k2, v in m.items()})
    return maps


def unshard_output(cfg, outs, n_cores=8):
    c = _cfg(cfg)
    S, D, U, BLK, SH = c["S"], c["D"], c["U"], c["BLK"], c["SH"]
    B = n_cores // 2
    out = np.empty((B, S, D), dtype=np.float32)
    m = np.arange(128)
    for core in range(n_cores):
        b = core // 2
        side = core % 2
        oc = outs[core]
        for ul in range(U):
            rows = BLK * m + ul + side * SH
            out[b, rows, :] = oc[128 * ul:128 * ul + 128, :]
    return out


def kernel(**inputs):
    cfg = REAL
    in_maps = shard_inputs(cfg, inputs)
    nc = build_nc(cfg)
    res = run_bass_kernel_spmd(nc, in_maps, core_ids=list(range(8)))
    return unshard_output(cfg, [r["out_c"] for r in res.results])


# version 19
# speedup vs baseline: 2.5000x; 1.0153x over previous
import numpy as np
import ml_dtypes
from contextlib import ExitStack

import concourse.bass as bass
import concourse.mybir as mybir
from concourse.bass_utils import run_bass_kernel_spmd

F32 = mybir.dt.float32
BF16 = mybir.dt.bfloat16
F16 = mybir.dt.float16
I32 = mybir.dt.int32
AF = mybir.ActivationFunctionType
ALU = mybir.AluOpType

LN_EPS = 1e-5
RMS_EPS = 1.1920928955078125e-07
FFN_EPS = 1e-06
LAMBDA_INIT = 0.8
SM_SCALE = float(1.0 / np.sqrt(128.0))

REAL = dict(S=4096, D=2048, HG=16, HC=8, I=5632)


def _cfg(cfg):
    S, D, HG, HC, I = cfg["S"], cfg["D"], cfg["HG"], cfg["HC"], cfg["I"]
    assert D == HG * 128 and S == 2 * D and I % 512 == 0
    c = dict(cfg)
    NC = cfg.get("NC", 8)
    WD = min(512, D)
    NGUU = (2 * I) // 512
    NDU = D // WD
    c.update(
        BLK=S // 128, SH=2 * HC, DCH=D // 128, KCH=S // 128, SCH=S // 128,
        QT=512, NQT=S // 512, U=2 * HC, JT=min(512, D), NJ=D // min(512, D),
        OG=I // 512, ICH=I // 128, NGU=D // 128, NGD=I // 128, CC=(S // 2) // 128,
        NC=NC, WD=WD, NGUU=NGUU, NDU=NDU,
        S_G=-(-NGUU // NC), S_D=-(-NDU // NC),
    )
    return c


class Stream:

    CHUNK = 400

    def __init__(self, block, engname, sem, dmasem=None):
        self.block = block
        self.engname = engname
        self.sem = sem
        self.n = 0
        self.dmasem = dmasem
        self.nd = 0
        self.pend = []

    def raw(self, f):
        self.pend.append(f)

    def flush(self):
        while self.pend:
            chunk = self.pend[:self.CHUNK]
            self.pend = self.pend[self.CHUNK:]
            getattr(self.block, self.engname)(
                lambda e, chunk=chunk: [g(e) for g in chunk])

    def op(self, f, inc=1):
        def g(e):
            f(e).then_inc(self.sem, inc)
        self.raw(g)
        self.n += inc
        return self.n

    def opn(self, f, n):
        def g(e):
            inss = f(e)
            assert len(inss) == n, (self.engname, len(inss), n)
            for ins in inss:
                ins.then_inc(self.sem, 1)
        self.raw(g)
        self.n += n
        return self.n

    def dman(self, f, n):
        def g(e):
            inss = f(e)
            assert len(inss) == n, (self.engname, len(inss), n)
            for ins in inss:
                ins.then_inc(self.sem, 16)
        self.raw(g)
        self.n += 16 * n
        return self.n

    def mmgroup(self, f):
        def g(e):
            f(e)[-1].then_inc(self.sem, 1)
        self.raw(g)
        self.n += 1
        return self.n

    def dmaop(self, f):
        def g(e):
            f(e).then_inc(self.dmasem, 16)
        self.raw(g)
        self.nd += 16
        return self.nd

    def wait(self, other, tok=None):
        t = other.n if tok is None else tok
        if t > 0:
            self.raw(lambda e: e.wait_ge(other.sem, t))

    def waitd(self, other, tok):
        if tok > 0:
            self.raw(lambda e: e.wait_ge(other.dmasem, tok))

    def selfwait(self):
        if self.n > 0:
            self.raw(lambda e, n=self.n: e.wait_ge(self.sem, n))


def build_nc(cfg, debug=False):
    c = _cfg(cfg)
    S, D, HC, I = c["S"], c["D"], c["HC"], c["I"]
    DCH, KCH, SCH, QT, NQT = c["DCH"], c["KCH"], c["SCH"], c["QT"], c["NQT"]
    U, NJ, JT, OG, ICH = c["U"], c["NJ"], c["JT"], c["OG"], c["ICH"]
    NGU, NGD, CC, BLK = c["NGU"], c["NGD"], c["CC"], c["BLK"]
    NC, WD, S_G, S_D = c["NC"], c["WD"], c["S_G"], c["S_D"]
    I2 = 2 * I
    WI_GU = D // 8
    WI_DN = I // 8
    NOT_GU = I2 // 128
    NOT_DN = D // 128
    NT = QT // 128

    nc = bass.Bass("TRN2", target_bir_lowering=False)
    dk = "ExternalOutput" if debug else "Internal"

    x_in = nc.dram_tensor("x_in", [S, D], F32, kind="ExternalInput").ap()
    wq_in = nc.dram_tensor("wq_in", [D, HC * 128], F32, kind="ExternalInput").ap()
    wk_in = nc.dram_tensor("wk_in", [D, HC * 128], F32, kind="ExternalInput").ap()
    wv_in = nc.dram_tensor("wv_in", [D, HC * 128], F32, kind="ExternalInput").ap()
    wo_in = nc.dram_tensor("wo_in", [D, D], F32, kind="ExternalInput").ap()
    lq1 = nc.dram_tensor("lq1", [1, 128], F32, kind="ExternalInput").ap()
    lq2 = nc.dram_tensor("lq2", [1, 128], F32, kind="ExternalInput").ap()
    lk1 = nc.dram_tensor("lk1", [1, 128], F32, kind="ExternalInput").ap()
    lk2 = nc.dram_tensor("lk2", [1, 128], F32, kind="ExternalInput").ap()
    gup = nc.dram_tensor("gup", [I2, WI_GU], I32, kind="ExternalInput").ap()
    gusT = nc.dram_tensor("gusT", [I2, NGU], F16, kind="ExternalInput").ap()
    dnp = nc.dram_tensor("dnp", [D, WI_DN], I32, kind="ExternalInput").ap()
    dnsT = nc.dram_tensor("dnsT", [D, NGD], F16, kind="ExternalInput").ap()
    out_c = nc.dram_tensor("out_c", [U * 128, D], F32, kind="ExternalOutput").ap()

    wq_bf = nc.dram_tensor("wq_bf", [D, HC * 128], F16, kind="Internal").ap()
    wk_bf = nc.dram_tensor("wk_bf", [D, HC * 128], F16, kind="Internal").ap()
    wv_bf = nc.dram_tensor("wv_bf", [D, HC * 128], F16, kind="Internal").ap()
    wo_bf = nc.dram_tensor("wo_bf", [D, D], F16, kind="Internal").ap()
    xn_nat = nc.dram_tensor("xn_nat", [S, D], F16, kind=dk).ap()
    xnT_nat = nc.dram_tensor("xnT_nat", [D, S], F16, kind=dk).ap()
    s_stash = nc.dram_tensor("s_stash", [U * 128, D], F32, kind=dk).ap()
    wgu_dense = nc.dram_tensor("wgu_dense", [I2, D], F16, kind=dk).ap()
    wdn_dense = nc.dram_tensor("wdn_dense", [D, I], F16, kind=dk).ap()
    y_dbg = nc.dram_tensor("y_dbg", [U * 128, D], F32, kind=dk).ap() if debug else None
    h_dbg = nc.dram_tensor("h_dbg", [4 * 128, ICH * 128], F16, kind=dk).ap() if debug else None

    with ExitStack() as top:
        E = top.enter_context
        ident = E(nc.sbuf_tensor("ident", [128, 128], BF16))
        identh = E(nc.sbuf_tensor("identh", [128, 128], F16))
        identf = E(nc.sbuf_tensor("identf", [128, 128], F32))
        identi = E(nc.sbuf_tensor("identi", [128, 128], I32))
        onesb = E(nc.sbuf_tensor("onesb", [128, 1], F16))
        ones128 = E(nc.sbuf_tensor("ones128", [128, 128], F16))
        nlam128 = E(nc.sbuf_tensor("nlam128", [128, 1], F32))
        ones1 = E(nc.sbuf_tensor("ones1", [1, 128], F32))
        epsLN = E(nc.sbuf_tensor("epsLN", [128, 1], F32))
        epsRMS = E(nc.sbuf_tensor("epsRMS", [128, 1], F32))
        epsFFN = E(nc.sbuf_tensor("epsFFN", [128, 1], F32))
        lamt = E(nc.sbuf_tensor("lamt", [1, 1], F32))
        ltmp = E(nc.sbuf_tensor("ltmp", [1, 768], F32))
        lred = E(nc.sbuf_tensor("lred", [1, 4], F32))

        PP = E(nc.psum_tensor("PP", [128, 4096], F32))
        PPb = PP.bitcast(BF16)
        PPh = PP.bitcast(F16)

        sems = {k: E(nc.semaphore(k)) for k in ["pe", "ac", "dv", "gp", "sp", "ag",
                                                "acd", "spd"]}
        block = E(nc.Block())

        PE = Stream(block, "tensor", sems["pe"])
        AC = Stream(block, "scalar", sems["ac"], dmasem=sems["acd"])
        DV = Stream(block, "vector", sems["dv"])
        GP = Stream(block, "gpsimd", sems["gp"])
        SP = Stream(block, "sync", sems["sp"], dmasem=sems["spd"])

        ALLS = None

        def flush_all():
            for s in (PE, AC, DV, GP, SP):
                s.flush()

        def barrier():
            streams = [PE, AC, DV, GP, SP]
            for s in streams:
                for o in streams:
                    if o is not s:
                        s.wait(o)
            flush_all()

        GP.dman(lambda g: [
            g.dma_start(out=wq_bf, in_=wq_in),
            g.dma_start(out=wk_bf, in_=wk_in),
            g.dma_start(out=wv_bf, in_=wv_in),
            g.dma_start(out=wo_bf, in_=wo_in),
        ], 4)
        tok_wcast = GP.n
        GP.op(lambda g: g.iota(identi[:, :], pattern=[[-1, 128]], base=0, channel_multiplier=1))
        tok_iota = GP.n
        SP.dman(lambda s: [
            s.dma_start(out=ltmp[:, 0:128], in_=lq1),
            s.dma_start(out=ltmp[:, 128:256], in_=lk1),
            s.dma_start(out=ltmp[:, 256:384], in_=lq2),
            s.dma_start(out=ltmp[:, 384:512], in_=lk2),
        ], 4)
        tok_lload = SP.n

        DV.wait(GP, tok_iota)
        DV.opn(lambda v: [
            v.memset(onesb[:, :], 1.0),
            v.memset(ones128[:, :], 1.0),
            v.memset(ones1[:, :], 1.0),
            v.memset(epsLN[:, :], LN_EPS),
            v.memset(epsRMS[:, :], RMS_EPS),
            v.memset(epsFFN[:, :], FFN_EPS / 4096.0),
            v.tensor_scalar(ident[:, :], identi[:, :], 0, None, op0=ALU.is_equal),
            v.tensor_scalar(identh[:, :], identi[:, :], 0, None, op0=ALU.is_equal),
            v.tensor_scalar(identf[:, :], identi[:, :], 0, None, op0=ALU.is_equal),
        ], 9)
        DV.wait(SP, tok_lload)
        DV.wait(GP, tok_wcast)
        DV.opn(lambda v: [
            v.tensor_tensor(ltmp[:, 512:640], ltmp[:, 0:128], ltmp[:, 128:256], op=ALU.mult),
            v.tensor_tensor(ltmp[:, 640:768], ltmp[:, 256:384], ltmp[:, 384:512], op=ALU.mult),
        ], 2)
        DV.selfwait()
        DV.opn(lambda v: [
            v.tensor_reduce(lred[:, 0:1], ltmp[:, 512:640], op=ALU.add, axis=mybir.AxisListType.X),
            v.tensor_reduce(lred[:, 1:2], ltmp[:, 640:768], op=ALU.add, axis=mybir.AxisListType.X),
        ], 2)
        AC.wait(DV)
        AC.opn(lambda s: [s.activation(lred[:, 2:3], lred[:, 0:1], AF.Exp),
                          s.activation(lred[:, 3:4], lred[:, 1:2], AF.Exp)], 2)
        DV.wait(AC)
        DV.op(lambda v: v.tensor_tensor(lamt[:, :], lred[:, 2:3], lred[:, 3:4], op=ALU.subtract))
        DV.selfwait()
        DV.op(lambda v: v.tensor_scalar(lamt[:, :], lamt[:, :], LAMBDA_INIT, None, op0=ALU.add))
        PE.wait(DV)
        PE.mmgroup(lambda t: [t.matmul(PP[:, 0:1], ones1[0:1, :], lamt[0:1, 0:1], start=True, stop=True)])
        DV.wait(PE)
        DV.op(lambda v: v.tensor_scalar(nlam128[:, :], PP[:, 0:1], -1.0, None, op0=ALU.mult))
        tok_nlam = DV.n

        with ExitStack() as ph:
            Ep = ph.enter_context
            xt = [Ep(nc.sbuf_tensor(f"xt{i}", [128, D], F32)) for i in range(2)]
            xnt = [Ep(nc.sbuf_tensor(f"xnt{i}", [128, D], F16)) for i in range(2)]
            NST = max(D // 512, 1)
            stats = Ep(nc.sbuf_tensor("stats", [128, 6 * NST], F32))
            mv = Ep(nc.sbuf_tensor("mv", [128, 2], F32))
            negm = Ep(nc.sbuf_tensor("negm", [128, 1], F32))
            lnv = Ep(nc.sbuf_tensor("lnv", [128, 1], F32))
            rstd = Ep(nc.sbuf_tensor("rstd", [128, 1], F32))
            txp = [Ep(nc.sbuf_tensor(f"txp{i}", [128, 2048], F16)) for i in range(2)]

            tok_xn_dv = {}
            tok_xn_store = {}
            tok_xload = {}

            def emit_xload(si):
                if si > SCH:
                    return
                if si >= 2:
                    GP.wait(DV, tok_xn_dv[si - 2])
                row = 128 * (si % SCH)
                GP.op(lambda g, row=row, p=si % 2: g.dma_start(
                    out=xt[p][:, :], in_=x_in[row:row + 128, :]), inc=16)
                tok_xload[si] = GP.n

            emit_xload(0)
            for si in range(SCH):
                p = si % 2
                emit_xload(si + 1)
                DV.wait(GP, tok_xload[min(si + 1, SCH)])
                if si >= 2:
                    DV.wait(SP, tok_xn_store[si - 2])
                DV.opn(lambda v, p=p: [
                    v.bn_stats(stats[:, 6 * i:6 * i + 6], xt[p][:, 512 * i:512 * i + min(512, D)])
                    for i in range(NST)], NST)
                DV.selfwait()
                DV.op(lambda v: v.bn_aggr(mv[:, :], stats[:, :]))
                DV.selfwait()
                DV.op(lambda v: v.tensor_scalar(negm[:, :], mv[:, 0:1], -1.0, None, op0=ALU.mult))
                AC.wait(DV)
                AC.op(lambda s: s.activation(lnv[:, :], mv[:, 1:2], AF.Ln, bias=epsLN[:, :]))
                AC.selfwait()
                AC.op(lambda s: s.activation(rstd[:, :], lnv[:, :], AF.Exp, scale=-0.5))
                DV.wait(AC)
                tok_xn_dv[si] = DV.op(lambda v, p=p: v.tensor_scalar(
                    xnt[p][:, :], xt[p][:, :], negm[:, :], rstd[:, :], op0=ALU.add, op1=ALU.mult))
                SP.wait(DV, tok_xn_dv[si])
                tok_xn_store[si] = SP.op(lambda s, si=si, p=p: s.dma_start(
                    out=xn_nat[128 * si:128 * si + 128, :], in_=xnt[p][:, :]), inc=16)

            SP.selfwait()
            HW = min(2048, S)
            NHALF = S // HW
            for dc in range(DCH):
                for hf in range(NHALF):
                    p = (dc * NHALF + hf) % 2
                    SP.op(lambda s, dc=dc, hf=hf, p=p: s.dma_start_transpose(
                        txp[p][:, 0:HW],
                        xn_nat[HW * hf:HW * hf + HW, 128 * dc:128 * dc + 128]), inc=16)
                    SP.selfwait()
                    SP.op(lambda s, dc=dc, hf=hf, p=p: s.dma_start(
                        out=xnT_nat[128 * dc:128 * dc + 128, HW * hf:HW * hf + HW],
                        in_=txp[p][:, 0:HW]), inc=16)
            SP.selfwait()
            barrier()

        with ExitStack() as ph:
            Ep = ph.enter_context
            wqT = [Ep(nc.sbuf_tensor(f"wqT{i}", [128, DCH * 128], F16)) for i in range(2)]
            wkT = [Ep(nc.sbuf_tensor(f"wkT{i}", [128, DCH * 128], F16)) for i in range(2)]
            wvT = [Ep(nc.sbuf_tensor(f"wvT{i}", [128, DCH * 128], F16)) for i in range(2)]
            xnS = [Ep(nc.sbuf_tensor(f"xnS{i}", [128, DCH * 512], F16)) for i in range(2)]
            QTs = Ep(nc.sbuf_tensor("QTs", [128, S], F16))
            KTs = Ep(nc.sbuf_tensor("KTs", [128, S], F16))
            Vs = Ep(nc.sbuf_tensor("Vs", [128, S], F16))
            Et = [Ep(nc.sbuf_tensor(f"Et{i}", [128, 2 * QT], F16)) for i in range(4)]
            lnA1 = Ep(nc.sbuf_tensor("lnA1", [128, QT], F32))
            lnA2 = Ep(nc.sbuf_tensor("lnA2", [128, QT], F32))
            bc1 = Ep(nc.sbuf_tensor("bc1", [128, QT], F32))
            bc2 = Ep(nc.sbuf_tensor("bc2", [128, QT], F32))
            t1s = Ep(nc.sbuf_tensor("t1s", [128, QT], F32))
            t2s = Ep(nc.sbuf_tensor("t2s", [128, QT], F32))
            vcb = Ep(nc.sbuf_tensor("vcb", [128, QT], F32))
            sqs = Ep(nc.sbuf_tensor("sqs", [128, 512], F32))
            ssA = Ep(nc.sbuf_tensor("ssA", [128, 4], F32))
            lnA = Ep(nc.sbuf_tensor("lnA", [128, 4], F32))
            rsA = Ep(nc.sbuf_tensor("rsA", [128, 4], F32))
            nrm = Ep(nc.sbuf_tensor("nrm", [128, 2 * SCH * 128], F16))
            woT = [Ep(nc.sbuf_tensor(f"woT{i}", [128, D], F16)) for i in range(3)]
            sev = Ep(nc.sbuf_tensor("sev", [128, D], F32))
            pk = [Ep(nc.sbuf_tensor(f"pk{i}", [128, 256], I32)) for i in range(2)]
            bA = Ep(nc.sbuf_tensor("bA", [128, 2048], I32))
            bB = Ep(nc.sbuf_tensor("bB", [128, 2048], I32))
            bC = Ep(nc.sbuf_tensor("bC", [128, 2048], I32))
            bD = Ep(nc.sbuf_tensor("bD", [128, 2048], F32))
            Wf = Ep(nc.sbuf_tensor("Wf", [128, 2048], F32))
            dgu = [Ep(nc.sbuf_tensor(f"dgu{i}", [128, D], F16)) for i in range(2)]
            ddn0 = Ep(nc.sbuf_tensor("ddn0", [128, I], F16))
            ddn = [ddn0, ddn0]
            scg = [Ep(nc.sbuf_tensor(f"scg{i}", [128, 64], F32)) for i in range(2)]
            scd = [Ep(nc.sbuf_tensor(f"scd{i}", [128, 64], F32)) for i in range(2)]

            bAf = bA.bitcast(F32)
            bBf = bB.bitcast(F32)
            bDi = bD.bitcast(I32)
            WfR = Wf.bitcast(I32).rearrange("p (w e) -> p e w", e=8)

            def dequant_strip(nw, src_pk):
                DV.opn(lambda v: [
                    v.tensor_scalar(bA[:, 256 * j:256 * j + nw], src_pk[:, 0:nw],
                                    4 * j, 7, op0=ALU.logical_shift_right, op1=ALU.bitwise_and)
                    for j in range(8)] + [
                    v.tensor_scalar(bC[:, 256 * j:256 * j + nw], src_pk[:, 0:nw],
                                    28 - 4 * j, -2147483648,
                                    op0=ALU.logical_shift_left, op1=ALU.bitwise_and)
                    for j in range(8)], 16)
                tok_rd = DV.n
                DV.selfwait()
                DV.opn(lambda v: [
                    v.tensor_scalar(bB[:, 256 * j:256 * j + nw], bA[:, 256 * j:256 * j + nw],
                                    252.0, float(1 << 22), op0=ALU.add, op1=ALU.mult)
                    for j in range(8)], 8)
                DV.selfwait()
                DV.opn(lambda v: [
                    v.tensor_scalar(bAf[:, 256 * j:256 * j + nw], bBf[:, 256 * j:256 * j + nw],
                                    2.0, 1.0, op0=ALU.mult, op1=ALU.subtract)
                    for j in range(8)], 8)
                DV.selfwait()
                DV.opn(lambda v: [
                    v.tensor_tensor(bD[:, 256 * j:256 * j + nw], bAf[:, 256 * j:256 * j + nw],
                                    bBf[:, 256 * j:256 * j + nw], op=ALU.min)
                    for j in range(8)], 8)
                DV.selfwait()
                DV.opn(lambda v: [
                    v.tensor_tensor(WfR[:, j, 0:nw], bDi[:, 256 * j:256 * j + nw],
                                    bC[:, 256 * j:256 * j + nw], op=ALU.bitwise_or)
                    for j in range(8)], 8)
                DV.selfwait()
                return tok_rd

            pk_sched = [("g", ot, 0, WI_GU) for ot in range(NOT_GU)]
            NSTRIP_DN = (WI_DN + 255) // 256
            for ot_ in range(NOT_DN):
                for sti_ in range(NSTRIP_DN):
                    w0_ = 256 * sti_
                    pk_sched.append(("d", ot_, w0_, min(256, WI_DN - w0_)))
            pk_sched.append(("g", 0, 0, WI_GU))
            pktok = {}
            pkemit = [0]
            tok_pk_rd = {}
            tok_sc_used = {}

            def emit_pk(upto):
                while pkemit[0] <= min(upto, len(pk_sched) - 1):
                    k = pkemit[0]
                    if k >= 2:
                        GP.wait(DV, tok_pk_rd.get(k - 2, 0))
                    kind, ot_, w0_, nw_ = pk_sched[k]
                    if kind == "g":
                        GP.op(lambda g, ot_=ot_, p2=k % 2: g.dma_start(
                            out=pk[p2][:, 0:WI_GU], in_=gup[128 * ot_:128 * ot_ + 128, :]), inc=16)
                        if k < NOT_GU:
                            GP.wait(DV, tok_sc_used.get(("g", ot_ - 2), 0))
                            GP.op(lambda g, ot_=ot_, ps=ot_ % 2: g.dma_start(
                                out=scg[ps][:, 0:NGU], in_=gusT[128 * ot_:128 * ot_ + 128, :]), inc=16)
                    else:
                        GP.op(lambda g, ot_=ot_, w0_=w0_, nw_=nw_, p2=k % 2: g.dma_start(
                            out=pk[p2][:, 0:nw_], in_=dnp[128 * ot_:128 * ot_ + 128, w0_:w0_ + nw_]), inc=16)
                        if w0_ == 0:
                            GP.wait(DV, tok_sc_used.get(("d", ot_ - 2), 0))
                            GP.op(lambda g, ot_=ot_, ps=ot_ % 2: g.dma_start(
                                out=scd[ps][:, 0:NGD], in_=dnsT[128 * ot_:128 * ot_ + 128, :]), inc=16)
                    pktok[k] = GP.n
                    pkemit[0] += 1

            def pk_wait(k):
                emit_pk(k + 1)
                DV.wait(GP, pktok[min(k + 1, len(pk_sched) - 1)])

            scnt = [0]
            tok_store = {}

            def p3_steps():
              sc = scnt
              for ot in range(NOT_GU):
                p2 = sc[0] % 2
                pk_wait(sc[0])
                tok_pk_rd[sc[0]] = dequant_strip(WI_GU, pk[p2])
                if ot >= 2:
                    DV.wait(GP, tok_store.get(("g", ot - 2), 0))
                tok_sc_used[("g", ot)] = DV.opn(lambda v, p2=p2, ps=ot % 2: [
                    v.tensor_scalar(dgu[p2][:, 128 * g2:128 * g2 + 128], Wf[:, 128 * g2:128 * g2 + 128],
                                    scg[ps][:, g2:g2 + 1], None, op0=ALU.mult)
                    for g2 in range(NGU)], NGU)
                DV.selfwait()
                sc[0] += 1
                GP.wait(DV, tok_sc_used[("g", ot)])
                tok_store[("g", ot)] = GP.op(lambda g, ot=ot, p2=p2: g.dma_start(
                    out=wgu_dense[128 * ot:128 * ot + 128, :], in_=dgu[p2][:, :]), inc=16)
                yield
              for ot in range(NOT_DN):
                p2d = ot % 2
                for sti in range(NSTRIP_DN):
                    w0 = 256 * sti
                    nw = min(256, WI_DN - w0)
                    p2 = sc[0] % 2
                    pk_wait(sc[0])
                    tok_pk_rd[sc[0]] = dequant_strip(nw, pk[p2])
                    g0 = (8 * w0) // 128
                    ng = (8 * nw) // 128
                    if sti == 0 and ot >= 1:
                        DV.wait(GP, tok_store.get(("d", ot - 1), 0))
                    DV.opn(lambda v, p2d=p2d, g0=g0, ng=ng: [
                        v.tensor_scalar(ddn[p2d][:, 128 * (g0 + g2):128 * (g0 + g2) + 128],
                                        Wf[:, 128 * g2:128 * g2 + 128],
                                        scd[p2d][:, g0 + g2:g0 + g2 + 1], None, op0=ALU.mult)
                        for g2 in range(ng)], ng)
                    DV.selfwait()
                    if sti == NSTRIP_DN - 1:
                        tok_sc_used[("d", ot)] = DV.n
                    sc[0] += 1
                    yield
                GP.wait(DV)
                tok_store[("d", ot)] = GP.op(lambda g, ot=ot, p2d=p2d: g.dma_start(
                    out=wdn_dense[128 * ot:128 * ot + 128, :], in_=ddn[p2d][:, :]), inc=16)

            p3_iter = p3_steps()

            def p3_pump(n):
                for _ in range(n):
                    if next(p3_iter, "END") == "END":
                        return

            S0, S1c = 0, 1024
            AV1, AV2 = 2048, 2560
            D1C, D2C = 3072, 3584
            QC, KC, VC = 0, 512, 1024
            TRC = 1024

            xn_use = {}
            xn_tok = {}
            xcnt = 0
            tok_wT = {}
            wo_use = {}
            tok_nrm_last = {}
            tok_wo_mm = {}

            wT_emitted = set()

            def emit_wT(pr_):
                if pr_ >= HC or pr_ in wT_emitted:
                    return
                wT_emitted.add(pr_)
                pp_ = pr_ % 2
                if pr_ >= 2:
                    SP.wait(PE, tok_wT[("consumed", pr_ - 2)])
                SP.wait(GP, tok_wcast)
                SP.dman(lambda s, pr_=pr_, pp_=pp_: [
                    s.dma_start(
                        out=w_t[pp_][:, 128 * dc:128 * dc + 128],
                        in_=w_b[128 * dc:128 * dc + 128, 128 * pr_:128 * pr_ + 128])
                    for (w_t, w_b) in ((wqT, wq_bf), (wkT, wk_bf), (wvT, wv_bf))
                    for dc in range(DCH)], 3 * DCH)
                SP.op(lambda s: s.dma_start(out=ltmp[:, 0:128], in_=lq1), inc=16)
                tok_wT[pr_] = SP.n

            for pr in range(HC):
                pp = pr % 2
                emit_wT(pr)
                emit_wT(pr + 1)

                tok_qkv_evac = 0
                for st in range(NQT):
                    sp2 = xcnt % 2
                    for k in (xcnt, xcnt + 1):
                        if k not in xn_tok and k <= HC * NQT:
                            if k >= 2:
                                GP.wait(PE, xn_use.get(k - 2, 0))
                            st_k = k % NQT
                            GP.op(lambda g, st_k=st_k, p2=k % 2: g.dma_start(
                                out=xnS[p2][:, :].rearrange("p (c s) -> p c s", c=DCH),
                                in_=xnT_nat.rearrange("(c p) s -> p c s", p=128)[:, :, QT * st_k:QT * st_k + QT]),
                            inc=16)
                            xn_tok[k] = GP.n
                    PE.wait(GP, xn_tok[min(xcnt + 1, HC * NQT)])
                    PE.wait(SP, tok_wT[pr])
                    if st == 0:
                        PE.wait(DV)
                        PE.wait(AC)
                    else:
                        PE.wait(DV, tok_qkv_evac)
                    PE.mmgroup(lambda t, pp=pp, sp2=sp2: [
                        t.matmul(PP[:, QC:QC + 512], wqT[pp][:, 128 * dc:128 * dc + 128],
                                 xnS[sp2][:, 512 * dc:512 * dc + 512],
                                 start=(dc == 0), stop=(dc == DCH - 1))
                        for dc in range(DCH)])
                    PE.mmgroup(lambda t, pp=pp, sp2=sp2: [
                        t.matmul(PP[:, KC:KC + 512], wkT[pp][:, 128 * dc:128 * dc + 128],
                                 xnS[sp2][:, 512 * dc:512 * dc + 512],
                                 start=(dc == 0), stop=(dc == DCH - 1))
                        for dc in range(DCH)])
                    PE.mmgroup(lambda t, pp=pp, sp2=sp2: [
                        t.matmul(PP[:, VC + 128 * q:VC + 128 * q + 128],
                                 xnS[sp2][:, 512 * dc + 128 * q:512 * dc + 128 * q + 128],
                                 wvT[pp][:, 128 * dc:128 * dc + 128],
                                 start=(dc == 0), stop=(dc == DCH - 1))
                        for q in range(4) for dc in range(DCH)])
                    xn_use[xcnt] = PE.n
                    xcnt += 1
                    if st == NQT - 1:
                        tok_wT[("consumed", pr)] = PE.n
                    DV.wait(PE)
                    tok_qkv_evac = DV.opn(lambda v, st=st: [
                        v.tensor_copy(QTs[:, QT * st:QT * st + QT], PP[:, QC:QC + 512]),
                        v.tensor_copy(KTs[:, QT * st:QT * st + QT], PP[:, KC:KC + 512]),
                        v.tensor_copy(Vs[:, QT * st:QT * st + QT], PP[:, VC:VC + 512]),
                    ], 3)
                tok_qkv_dv = DV.n

                for qt in range(NQT):
                    tok_scores = {}
                    tok_exp = {}
                    tok_cons = {}

                    def emit_dav(kc, tok_exp=tok_exp, tok_cons=tok_cons):
                        PE.wait(AC, tok_exp[kc])
                        ep4 = kc % 4
                        PE.mmgroup(lambda t, kc=kc, ep4=ep4: [
                            t.matmul(PP[:, D1C:D1C + QT], ones128[:, :], Et[ep4][:, 0:QT],
                                     start=(kc == 0), stop=(kc == KCH - 1)),
                            t.matmul(PP[:, D2C:D2C + QT], ones128[:, :], Et[ep4][:, QT:2 * QT],
                                     start=(kc == 0), stop=(kc == KCH - 1)),
                            t.matmul(PP[:, AV1:AV1 + QT], Vs[:, 128 * kc:128 * kc + 128],
                                     Et[ep4][:, 0:QT], start=(kc == 0), stop=(kc == KCH - 1)),
                            t.matmul(PP[:, AV2:AV2 + QT], Vs[:, 128 * kc:128 * kc + 128],
                                     Et[ep4][:, QT:2 * QT], start=(kc == 0), stop=(kc == KCH - 1)),
                        ])
                        tok_cons[kc] = PE.n

                    for kc in range(KCH):
                        sp2 = kc % 2
                        if kc == 0:
                            PE.wait(DV, tok_qkv_dv)
                            if qt > 0:
                                PE.wait(DV, tok_nrm_last[(pr, qt - 1)])
                        if kc >= 2:
                            PE.wait(AC, tok_exp[kc - 2])
                        PE.mmgroup(lambda t, kc=kc, qt=qt, sp2=sp2: [
                            t.matmul(PP[:, 1024 * sp2:1024 * sp2 + 512],
                                     KTs[0:64, 128 * kc:128 * kc + 128],
                                     QTs[0:64, QT * qt:QT * qt + QT],
                                     start=True, stop=True, tile_position=(0, 0)),
                            t.matmul(PP[:, 1024 * sp2 + 512:1024 * sp2 + 1024],
                                     KTs[64:128, 128 * kc:128 * kc + 128],
                                     QTs[64:128, QT * qt:QT * qt + QT],
                                     start=True, stop=True, tile_position=(64, 0)),
                        ])
                        tok_scores[kc] = PE.n
                        AC.wait(PE, tok_scores[kc])
                        if kc >= 4:
                            AC.wait(PE, tok_cons[kc - 4])
                        tok_exp[kc] = AC.op(lambda s, sp2=sp2, ep4=kc % 4: s.activation(
                            Et[ep4][:, :], PP[:, 1024 * sp2:1024 * sp2 + 2 * QT],
                            AF.Exp, scale=SM_SCALE))
                        if kc >= 1:
                            emit_dav(kc - 1)
                    emit_dav(KCH - 1)

                    AC.wait(PE, tok_cons[KCH - 1])
                    AC.opn(lambda s: [
                        s.activation(lnA1[:, :], PP[:, D1C:D1C + QT], AF.Ln),
                        s.activation(lnA2[:, :], PP[:, D2C:D2C + QT], AF.Ln),
                    ], 2)
                    AC.selfwait()
                    AC.opn(lambda s: [
                        s.activation(bc1[:, :], lnA1[:, :], AF.Exp, scale=-1.0),
                        s.activation(bc2[:, :], lnA2[:, :], AF.Exp, scale=-1.0),
                    ], 2)
                    tok_alpha = AC.n
                    DV.wait(AC, tok_alpha)
                    DV.wait(DV, tok_nlam)
                    DV.opn(lambda v: [
                        v.tensor_tensor(t1s[:, :], PP[:, AV1:AV1 + QT], bc1[:, :], op=ALU.mult),
                        v.scalar_tensor_tensor(t2s[:, :], PP[:, AV2:AV2 + QT], nlam128[:, :],
                                               bc2[:, :], op0=ALU.mult, op1=ALU.mult),
                    ], 2)
                    DV.selfwait()
                    tok_comb = DV.op(lambda v: v.tensor_tensor(vcb[:, :], t1s[:, :], t2s[:, :], op=ALU.add))
                    PE.wait(DV, tok_comb)
                    PE.mmgroup(lambda t: [
                        t.transpose(PP[:, TRC + 128 * ti:TRC + 128 * ti + 128],
                                    vcb[:, 128 * ti:128 * ti + 128], identf[:, :])
                        for ti in range(NT)])
                    AC.wait(PE)
                    AC.opn(lambda s: [
                        s.activation(sqs[:, 128 * ti:128 * ti + 128],
                                     PP[:, TRC + 128 * ti:TRC + 128 * ti + 128],
                                     AF.Square, accum_out=ssA[:, ti:ti + 1])
                        for ti in range(NT)], NT)
                    AC.selfwait()
                    AC.opn(lambda s: [
                        s.activation(lnA[:, ti:ti + 1], ssA[:, ti:ti + 1], AF.Ln,
                                     scale=1.0 / 128.0, bias=epsRMS[:, :])
                        for ti in range(NT)], NT)
                    AC.selfwait()
                    AC.opn(lambda s: [
                        s.activation(rsA[:, ti:ti + 1], lnA[:, ti:ti + 1], AF.Exp, scale=-0.5)
                        for ti in range(NT)], NT)
                    DV.wait(AC)
                    if qt == 0 and pr >= 2:
                        DV.wait(PE, tok_wo_mm[(pr - 2) | 1])
                    po = (pr % 2) * SCH * 128
                    tok_nrm_last[(pr, qt)] = DV.opn(lambda v, qt=qt, po=po: [
                        v.tensor_scalar(nrm[:, po + (qt * NT + ti) * 128:po + (qt * NT + ti) * 128 + 128],
                                        PP[:, TRC + 128 * ti:TRC + 128 * ti + 128],
                                        rsA[:, ti:ti + 1], 1.0 - LAMBDA_INIT,
                                        op0=ALU.mult, op1=ALU.mult)
                        for ti in range(NT)], NT)
                    p3_pump(2)

                if pr % 2 == 1 or pr == HC - 1:
                    duo = (pr - 1, pr) if pr % 2 == 1 else (pr,)
                    combos = [(pa, sh) for pa in duo for sh in (0, 1)]
                    ngrp = (len(combos) + 1) // 2
                    nwo = ngrp * CC
                    wotok = {}

                    def emit_woload(k, nwo=nwo):
                        if k > nwo or k in wotok:
                            return
                        if k >= 3:
                            SP.wait(PE, wo_use[k - 3])
                        cck = k % CC
                        SP.op(lambda s, cck=cck, wp=k % 3: s.dma_start(
                            out=woT[wp][:, :], in_=wo_bf[128 * cck:128 * cck + 128, :]), inc=16)
                        wotok[k] = SP.n

                    woci = 0
                    prev_sstore = 0
                    emit_woload(0)
                    for g0 in range(0, len(combos), 2):
                        sub = combos[g0:g0 + 2]
                        for cc in range(CC):
                            wp = woci % 3
                            emit_woload(woci + 1)
                            PE.wait(SP, wotok[min(woci + 1, nwo)])
                            if cc == 0:
                                PE.wait(DV)
                            def womms(t, sub=sub, cc=cc, wp=wp):
                                ins = []
                                for ci, (pa, sh) in enumerate(sub):
                                    po2 = (pa % 2) * SCH * 128
                                    lhs = nrm[:, po2 + (sh * CC + cc) * 128:po2 + (sh * CC + cc) * 128 + 128]
                                    for j in range(NJ):
                                        ins.append(t.matmul(
                                            PP[:, 2048 * ci + JT * j:2048 * ci + JT * j + JT],
                                            lhs, woT[wp][:, JT * j:JT * j + JT],
                                            start=(cc == 0), stop=(cc == CC - 1)))
                                return ins
                            PE.mmgroup(womms)
                            wo_use[woci] = PE.n
                            woci += 1
                        tok_wo_mm[pr] = PE.n
                        for ci, (pa, sh) in enumerate(sub):
                            ul = 2 * pa + sh
                            DV.wait(PE)
                            DV.wait(GP, prev_sstore)
                            DV.op(lambda v, ci=ci: v.tensor_copy(sev[:, :], PP[:, 2048 * ci:2048 * ci + D]))
                            GP.wait(DV)
                            prev_sstore = GP.op(lambda g, ul=ul: g.dma_start(
                                out=s_stash[128 * ul:128 * ul + 128, :], in_=sev[:, :]), inc=16)
            p3_pump(10 ** 9)
            emit_pk(len(pk_sched) - 1)
            GP.selfwait()
            barrier()

        with ExitStack() as ph:
            Ep = ph.enter_context
            xnU = [Ep(nc.sbuf_tensor(f"xnU{i}", [128, 4 * DCH * 128], F16)) for i in range(2)]
            wsb = [Ep(nc.sbuf_tensor(f"wsb{i}", [128, 1024], F16)) for i in range(DCH)]
            wsd = [Ep(nc.sbuf_tensor(f"wsd{i}", [128, 2 * JT], F16)) for i in range(4)]
            gsil = Ep(nc.sbuf_tensor("gsil", [128, 2 * 4 * 512], F16))
            hsb = Ep(nc.sbuf_tensor("hsb", [128, 4 * 512], F16))
            hT = Ep(nc.sbuf_tensor("hT", [128, 4 * ICH * 128], F16))
            ssF = Ep(nc.sbuf_tensor("ssF", [128, 1], F32))
            lnF = Ep(nc.sbuf_tensor("lnF", [128, 1], F32))
            rsF = Ep(nc.sbuf_tensor("rsF", [128, 1], F32))
            ybuf = [Ep(nc.sbuf_tensor(f"ybuf{i}", [128, D], F32)) for i in range(4)]
            gbuf = Ep(nc.sbuf_tensor("gbuf", [128, D], F32))
            sbuf2 = [Ep(nc.sbuf_tensor(f"sbuf2{i}", [128, D], F32)) for i in range(2)]
            xbuf = [Ep(nc.sbuf_tensor(f"xbuf{i}", [128, D], F32)) for i in range(2)]

            NUG = (U + 3) // 4
            ogps = [list(range(p, min(p + 2, OG))) for p in range(0, OG, 2)]
            jps = [list(range(p, min(p + 2, NJ))) for p in range(0, NJ, 2)]
            NOGH = len(ogps)
            gu_tok = {}
            gu_margin = {}

            def emit_gu_load(ug_, ogh_, ph2_, dc_, guard):
                ogs_ = ogps[ogh_]
                rb_ = 512 * ogs_[0] + (I if ph2_ else 0)
                w_ = 512 * len(ogs_)
                if guard:
                    SP.wait(PE, guard)
                gu_tok[(ug_, ogh_, ph2_, dc_)] = SP.dmaop(
                    lambda s, dc_=dc_, rb_=rb_, w_=w_: s.dma_start_transpose(
                        wsb[dc_][:, 0:w_],
                        wgu_dense[rb_:rb_ + w_, 128 * dc_:128 * dc_ + 128]))

            def emit_gu_margin(ug_, ogh_, ph2_):
                ogs_ = ogps[ogh_]
                rb_ = 512 * ogs_[0] + (I if ph2_ else 0)
                w_ = 512 * len(ogs_)
                gu_margin[(ug_, ogh_, ph2_)] = SP.dmaop(
                    lambda s, rb_=rb_, w_=w_: s.dma_start_transpose(
                        wsb[DCH - 1][:, 0:w_],
                        wgu_dense[rb_:rb_ + w_, 128 * (DCH - 1):128 * DCH]))

            def gu_next_tok(ug_, ogh_, ph2_, dc_):
                if dc_ < DCH - 1:
                    return gu_tok[(ug_, ogh_, ph2_, dc_ + 1)]
                return gu_margin[(ug_, ogh_, ph2_)]

            def mk_dn_pipeline():
                tiles = [(js_, ic_) for js_ in jps for ic_ in range(ICH)]
                state = dict(tok={}, consumed={}, emitted=0)

                def emit(idx):
                    while state["emitted"] <= min(idx, len(tiles) - 1):
                        L = state["emitted"]
                        js_, ic_ = tiles[L]
                        if L >= 4:
                            SP.wait(PE, state["consumed"][L - 4])
                        w_ = JT * len(js_)
                        state["tok"][L] = SP.dmaop(
                            lambda s, L=L, js_=js_, ic_=ic_, w_=w_: s.dma_start_transpose(
                                wsd[L % 4][:, 0:w_],
                                wdn_dense[JT * js_[0]:JT * js_[0] + w_, 128 * ic_:128 * ic_ + 128]))
                        state["emitted"] += 1
                    if state["emitted"] == len(tiles) and "margin" not in state:
                        js_, ic_ = tiles[-1]
                        w_ = JT * len(js_)
                        state["margin"] = SP.dmaop(
                            lambda s, js_=js_, ic_=ic_, w_=w_: s.dma_start_transpose(
                                wsd[(len(tiles) - 1) % 4][:, 0:w_],
                                wdn_dense[JT * js_[0]:JT * js_[0] + w_, 128 * ic_:128 * ic_ + 128]))
                return state, tiles, emit

            xnu_pend = []
            burst_tok = {}

            def queue_xnu(ug_):
                if ug_ >= NUG:
                    return
                us_ = list(range(4 * ug_, min(4 * ug_ + 4, U)))
                nu_ = len(us_)
                xb = xnU[ug_ % 2]

                def tile(dc_, ui_, u_):
                    SP.op(lambda s: s.dma_start_transpose(
                        xb[:, (dc_ * nu_ + ui_) * 128:(dc_ * nu_ + ui_) * 128 + 128],
                        xn_nat.rearrange("(m blk) d -> blk m d", blk=BLK)[u_, :, 128 * dc_:128 * dc_ + 128]),
                        inc=16)

                for dc_ in range(DCH):
                    for ui_, u_ in enumerate(us_):
                        xnu_pend.append(lambda dc_=dc_, ui_=ui_, u_=u_: tile(dc_, ui_, u_))

                def margin():
                    tile(DCH - 1, nu_ - 1, us_[-1])
                    burst_tok[ug_] = SP.n
                xnu_pend.append(margin)

            def pop_xnu():
                if xnu_pend:
                    xnu_pend.pop(0)()

            for dc_ in range(DCH):
                emit_gu_load(0, 0, 0, dc_, 0)
            emit_gu_margin(0, 0, 0)
            queue_xnu(0)
            while xnu_pend:
                pop_xnu()
            queue_xnu(1)

            tok_xnU_used = 0
            tok_g_ac = 0
            tok_gbuf_free = 0
            tok_out_dv = 0
            tok_obuf_store = 0
            sxtok = {}
            sx_guard = {}
            tok_ydump = 0

            for ug in range(NUG):
                us = list(range(4 * ug, min(4 * ug + 4, U)))
                nu = len(us)
                assert ug in burst_tok, ug
                tok_xnU = burst_tok[ug]
                if ug > 0:
                    queue_xnu(ug + 1)

                tok_h_dv = 0
                tok_hTdv = 0
                tok_hT_pe = 0
                tok_yevac = {}
                for ogh, ogs in enumerate(ogps):
                    tok_gsil = {}
                    for pos, og in enumerate(ogs):
                        lastpos = pos == len(ogs) - 1
                        for dc in range(DCH):
                            PE.waitd(SP, gu_next_tok(ug, ogh, 0, dc))
                            if dc == 0:
                                PE.wait(SP, tok_xnU)
                                if pos == 0:
                                    if og > 0 or ug > 0:
                                        PE.wait(DV, tok_hTdv if og > 0 else tok_ylast)
                                else:
                                    PE.wait(DV, tok_h_dv)
                            PE.mmgroup(lambda t, us=us, dc=dc, pos=pos, pug=ug % 2: [
                                t.matmul(PP[:, 2048 * pos + 512 * ui:2048 * pos + 512 * ui + 512],
                                         xnU[pug][:, (dc * len(us) + ui) * 128:(dc * len(us) + ui) * 128 + 128],
                                         wsb[dc][:, 512 * pos:512 * pos + 512],
                                         start=(dc == 0), stop=(dc == DCH - 1))
                                for ui in range(len(us))])
                            if lastpos:
                                emit_gu_load(ug, ogh, 1, dc, PE.n)
                                pop_xnu()
                                if dc == DCH - 1:
                                    emit_gu_margin(ug, ogh, 1)
                        AC.wait(PE)
                        AC.wait(DV, tok_h_dv)
                        tok_gsil[pos] = AC.opn(lambda s, nu=nu, pos=pos: [
                            s.activation(gsil[:, 2048 * pos + 512 * ui:2048 * pos + 512 * ui + 512],
                                         PP[:, 2048 * pos + 512 * ui:2048 * pos + 512 * ui + 512],
                                         AF.Silu)
                            for ui in range(nu)], nu)
                    for pos, og in enumerate(ogs):
                        lastpos = pos == len(ogs) - 1
                        for dc in range(DCH):
                            PE.waitd(SP, gu_next_tok(ug, ogh, 1, dc))
                            if dc == 0:
                                PE.wait(AC, tok_gsil[pos])
                            PE.mmgroup(lambda t, us=us, dc=dc, pos=pos, pug=ug % 2: [
                                t.matmul(PP[:, 2048 * pos + 512 * ui:2048 * pos + 512 * ui + 512],
                                         xnU[pug][:, (dc * len(us) + ui) * 128:(dc * len(us) + ui) * 128 + 128],
                                         wsb[dc][:, 512 * pos:512 * pos + 512],
                                         start=(dc == 0), stop=(dc == DCH - 1))
                                for ui in range(len(us))])
                            if lastpos:
                                nxt = (ug, ogh + 1) if ogh + 1 < NOGH else (
                                    (ug + 1, 0) if ug + 1 < NUG else None)
                                if nxt is not None:
                                    emit_gu_load(nxt[0], nxt[1], 0, dc, PE.n)
                                    if dc == DCH - 1:
                                        emit_gu_margin(nxt[0], nxt[1], 0)
                                pop_xnu()
                        DV.wait(PE)
                        DV.wait(AC, tok_gsil[pos])
                        DV.wait(PE, tok_hT_pe)
                        tok_h_dv = DV.opn(lambda v, nu=nu, pos=pos: [
                            v.scalar_tensor_tensor(
                                hsb[:, 512 * ui:512 * ui + 512],
                                PP[:, 2048 * pos + 512 * ui:2048 * pos + 512 * ui + 512], 1.0 / 64.0,
                                gsil[:, 2048 * pos + 512 * ui:2048 * pos + 512 * ui + 512],
                                op0=ALU.mult, op1=ALU.mult)
                            for ui in range(nu)], nu)
                        PE.wait(DV, tok_h_dv)
                        if tok_hTdv:
                            PE.wait(DV, tok_hTdv)
                        PE.mmgroup(lambda t, nu=nu: [
                            t.transpose(PPh[:, 128 * (4 * ui + ti):128 * (4 * ui + ti) + 128],
                                        hsb[:, 512 * ui + 128 * ti:512 * ui + 128 * ti + 128], identh[:, :])
                            for ui in range(nu) for ti in range(4)])
                        tok_hT_pe = PE.n
                        DV.wait(PE, tok_hT_pe)
                        tok_hTdv = DV.opn(lambda v, og=og, nu=nu: [
                            v.tensor_copy(hT[:, (ui * ICH + og * 4 + ti) * 128:(ui * ICH + og * 4 + ti) * 128 + 128],
                                          PPh[:, 128 * (4 * ui + ti):128 * (4 * ui + ti) + 128])
                            for ui in range(nu) for ti in range(4)], nu * 4)
                tok_xnU_used = PE.n
                if debug and ug == 0:
                    SP.wait(DV, tok_hTdv)
                    SP.dman(lambda s, nu=nu: [
                        s.dma_start(out=h_dbg[128 * ui:128 * ui + 128, :], in_=hT[:, ui * ICH * 128:(ui + 1) * ICH * 128])
                        for ui in range(nu)], nu)

                dnstate, dntiles, dn_emit = mk_dn_pipeline()
                NDL = len(dntiles)
                for jh, js in enumerate(jps):
                    for ic in range(ICH):
                        L = jh * ICH + ic
                        dn_emit(L + 3)
                        PE.waitd(SP, dnstate["tok"][L + 1] if L + 1 < NDL else dnstate["margin"])
                        if ic == 0:
                            if jh == 0:
                                PE.wait(DV, tok_hTdv)
                            else:
                                PE.wait(DV, tok_yevac[jh - 1])
                            if ug > 0 and jh == 0:
                                PE.wait(AC, tok_g_ac)
                        wp = L % 4
                        for pos, j in enumerate(js):
                            PE.mmgroup(lambda t, j=j, ic=ic, wp=wp, pos=pos, nu=nu: [
                                t.matmul(PP[:, 2048 * (j % 2) + 512 * ui:2048 * (j % 2) + 512 * ui + JT],
                                         hT[:, (ui * ICH + ic) * 128:(ui * ICH + ic) * 128 + 128],
                                         wsd[wp][:, JT * pos:JT * pos + JT],
                                         start=(ic == 0), stop=(ic == ICH - 1))
                                for ui in range(nu)])
                        dnstate["consumed"][L] = PE.n
                    DV.wait(PE)
                    if ug > 0 and jh == 0:
                        DV.wait(AC, tok_g_ac)
                        DV.wait(SP, tok_ydump)
                    tok_yevac[jh] = DV.opn(lambda v, js=js, nu=nu: [
                        v.tensor_copy(ybuf[ui][:, JT * j:JT * j + JT],
                                      PP[:, 2048 * (j % 2) + 512 * ui:2048 * (j % 2) + 512 * ui + JT])
                        for j in js for ui in range(nu)], len(js) * nu)
                tok_ylast = tok_yevac[len(jps) - 1]

                if debug:
                    SP.wait(DV, tok_ylast)
                    SP.dman(lambda s, us=us: [
                        s.dma_start(out=y_dbg[128 * u:128 * u + 128, :], in_=ybuf[ui][:, :])
                        for ui, u in enumerate(us)], nu)
                    tok_ydump = SP.n

                for ui, u in enumerate(us):
                    ul = u
                    for uk in (u, u + 1):
                        if uk not in sxtok and uk <= U:
                            GP.wait(DV, sx_guard.get(uk - 2, 0))
                            row = 128 * (uk % U)
                            GP.op(lambda g, row=row, p=uk % 2: g.dma_start(
                                out=sbuf2[p][:, :], in_=s_stash[row:row + 128, :]), inc=16)
                            GP.op(lambda g, p=uk % 2, blk=uk % U: g.dma_start(
                                out=xbuf[p][:, :],
                                in_=x_in.rearrange("(m blk) d -> blk m d", blk=BLK)[blk, :, :]), inc=16)
                            sxtok[uk] = GP.n
                    AC.wait(DV, tok_ylast)
                    AC.wait(GP, tok_obuf_store)
                    AC.op(lambda s, ui=ui: s.activation(
                        gbuf[:, :], ybuf[ui][:, :], AF.Square, accum_out=ssF[:, :]))
                    AC.selfwait()
                    AC.op(lambda s: s.activation(lnF[:, :], ssF[:, :], AF.Ln,
                                                 scale=1.0 / D, bias=epsFFN[:, :]))
                    AC.selfwait()
                    AC.op(lambda s: s.activation(rsF[:, :], lnF[:, :], AF.Exp, scale=-0.5))
                    AC.selfwait()
                    tok_g_ac = AC.op(lambda s, ui=ui: s.activation(
                        gbuf[:, :], ybuf[ui][:, :], AF.Silu, scale=rsF[:, :]))
                    DV.wait(AC, tok_g_ac)
                    DV.wait(GP, sxtok[u + 1])
                    tok_gbuf_free = DV.op(lambda v, p=u % 2: v.tensor_tensor(
                        gbuf[:, :], gbuf[:, :], sbuf2[p][:, :], op=ALU.mult))
                    DV.selfwait()
                    tok_out_dv = DV.op(lambda v, p=u % 2: v.tensor_tensor(
                        gbuf[:, :], gbuf[:, :], xbuf[p][:, :], op=ALU.add))
                    sx_guard[u] = tok_out_dv
                    GP.wait(DV, tok_out_dv)
                    tok_obuf_store = GP.op(lambda g, ul=ul: g.dma_start(
                        out=out_c[128 * ul:128 * ul + 128, :], in_=gbuf[:, :]), inc=16)
            GP.selfwait()
            GP.wait(SP)
            SP.selfwait()
            flush_all()
    return nc



def shard_inputs(cfg, inputs, n_cores=8):
    c = _cfg(cfg)
    S, D, HC, BLK, SH = c["S"], c["D"], c["HC"], c["BLK"], c["SH"]
    x = np.ascontiguousarray(inputs["L0_attn_x"])
    wq, wk, wv, wo = (inputs["L0_attn_wq"], inputs["L0_attn_wk"],
                      inputs["L0_attn_wv"], inputs["L0_attn_wo"])
    maps = []
    for core in range(n_cores):
        b = core // 2
        side = core % 2
        q = np.arange(S) // BLK
        k = np.arange(S) % BLK
        src = q * BLK + (k + side * SH) % BLK
        cp = np.arange(D)
        colperm = (cp // BLK) * BLK + (cp % BLK + side * SH) % BLK
        h0 = HC * side
        m = {
            "x_in": x[b][src],
            "wq_in": wq[128 * h0:128 * (h0 + HC)].T,
            "wk_in": wk[128 * h0:128 * (h0 + HC)].T,
            "wv_in": wv[128 * h0:128 * (h0 + HC)].T,
            "wo_in": wo[:, colperm].T,
            "lq1": np.asarray(inputs["L0_attn_lambda_q1"]).reshape(1, 128),
            "lq2": np.asarray(inputs["L0_attn_lambda_q2"]).reshape(1, 128),
            "lk1": np.asarray(inputs["L0_attn_lambda_k1"]).reshape(1, 128),
            "lk2": np.asarray(inputs["L0_attn_lambda_k2"]).reshape(1, 128),
            "gup": inputs["L0_ffn_gate_up_packed"],
            "gusT": inputs["L0_ffn_gate_up_scales"].T,
            "dnp": inputs["L0_ffn_down_packed"],
            "dnsT": inputs["L0_ffn_down_scales"].T,
        }
        maps.append({k2: np.ascontiguousarray(v) for k2, v in m.items()})
    return maps


def unshard_output(cfg, outs, n_cores=8):
    c = _cfg(cfg)
    S, D, U, BLK, SH = c["S"], c["D"], c["U"], c["BLK"], c["SH"]
    B = n_cores // 2
    out = np.empty((B, S, D), dtype=np.float32)
    m = np.arange(128)
    for core in range(n_cores):
        b = core // 2
        side = core % 2
        oc = outs[core]
        for ul in range(U):
            rows = BLK * m + ul + side * SH
            out[b, rows, :] = oc[128 * ul:128 * ul + 128, :]
    return out


def kernel(**inputs):
    cfg = REAL
    in_maps = shard_inputs(cfg, inputs)
    nc = build_nc(cfg)
    res = run_bass_kernel_spmd(nc, in_maps, core_ids=list(range(8)))
    return unshard_output(cfg, [r["out_c"] for r in res.results])


# version 20
# speedup vs baseline: 2.5707x; 1.0283x over previous
import numpy as np
import ml_dtypes
from contextlib import ExitStack

import concourse.bass as bass
import concourse.mybir as mybir
from concourse.bass_utils import run_bass_kernel_spmd

F32 = mybir.dt.float32
BF16 = mybir.dt.bfloat16
F16 = mybir.dt.float16
I32 = mybir.dt.int32
AF = mybir.ActivationFunctionType
ALU = mybir.AluOpType

LN_EPS = 1e-5
RMS_EPS = 1.1920928955078125e-07
FFN_EPS = 1e-06
LAMBDA_INIT = 0.8
SM_SCALE = float(1.0 / np.sqrt(128.0))

REAL = dict(S=4096, D=2048, HG=16, HC=8, I=5632)


def _cfg(cfg):
    S, D, HG, HC, I = cfg["S"], cfg["D"], cfg["HG"], cfg["HC"], cfg["I"]
    assert D == HG * 128 and S == 2 * D and I % 512 == 0
    c = dict(cfg)
    NC = cfg.get("NC", 8)
    WD = min(512, D)
    NGUU = (2 * I) // 512
    NDU = D // WD
    c.update(
        BLK=S // 128, SH=2 * HC, DCH=D // 128, KCH=S // 128, SCH=S // 128,
        QT=512, NQT=S // 512, U=2 * HC, JT=min(512, D), NJ=D // min(512, D),
        OG=I // 512, ICH=I // 128, NGU=D // 128, NGD=I // 128, CC=(S // 2) // 128,
        NC=NC, WD=WD, NGUU=NGUU, NDU=NDU,
        S_G=-(-NGUU // NC), S_D=-(-NDU // NC),
    )
    return c


class Stream:

    CHUNK = 400

    def __init__(self, block, engname, sem, dmasem=None):
        self.block = block
        self.engname = engname
        self.sem = sem
        self.n = 0
        self.dmasem = dmasem
        self.nd = 0
        self.pend = []

    def raw(self, f):
        self.pend.append(f)

    def flush(self):
        while self.pend:
            chunk = self.pend[:self.CHUNK]
            self.pend = self.pend[self.CHUNK:]
            getattr(self.block, self.engname)(
                lambda e, chunk=chunk: [g(e) for g in chunk])

    def op(self, f, inc=1):
        def g(e):
            f(e).then_inc(self.sem, inc)
        self.raw(g)
        self.n += inc
        return self.n

    def opn(self, f, n):
        def g(e):
            inss = f(e)
            assert len(inss) == n, (self.engname, len(inss), n)
            for ins in inss:
                ins.then_inc(self.sem, 1)
        self.raw(g)
        self.n += n
        return self.n

    def dman(self, f, n):
        def g(e):
            inss = f(e)
            assert len(inss) == n, (self.engname, len(inss), n)
            for ins in inss:
                ins.then_inc(self.sem, 16)
        self.raw(g)
        self.n += 16 * n
        return self.n

    def mmgroup(self, f):
        def g(e):
            f(e)[-1].then_inc(self.sem, 1)
        self.raw(g)
        self.n += 1
        return self.n

    def dmaop(self, f):
        def g(e):
            f(e).then_inc(self.dmasem, 16)
        self.raw(g)
        self.nd += 16
        return self.nd

    def wait(self, other, tok=None):
        t = other.n if tok is None else tok
        if t > 0:
            self.raw(lambda e: e.wait_ge(other.sem, t))

    def waitd(self, other, tok):
        if tok > 0:
            self.raw(lambda e: e.wait_ge(other.dmasem, tok))

    def selfwait(self):
        if self.n > 0:
            self.raw(lambda e, n=self.n: e.wait_ge(self.sem, n))


def build_nc(cfg, debug=False):
    c = _cfg(cfg)
    S, D, HC, I = c["S"], c["D"], c["HC"], c["I"]
    DCH, KCH, SCH, QT, NQT = c["DCH"], c["KCH"], c["SCH"], c["QT"], c["NQT"]
    U, NJ, JT, OG, ICH = c["U"], c["NJ"], c["JT"], c["OG"], c["ICH"]
    NGU, NGD, CC, BLK = c["NGU"], c["NGD"], c["CC"], c["BLK"]
    NC, WD, S_G, S_D = c["NC"], c["WD"], c["S_G"], c["S_D"]
    I2 = 2 * I
    WI_GU = D // 8
    WI_DN = I // 8
    NOT_GU = I2 // 128
    NOT_DN = D // 128
    NT = QT // 128

    nc = bass.Bass("TRN2", target_bir_lowering=False)
    dk = "ExternalOutput" if debug else "Internal"

    x_in = nc.dram_tensor("x_in", [S, D], F32, kind="ExternalInput").ap()
    wq_in = nc.dram_tensor("wq_in", [D, HC * 128], F32, kind="ExternalInput").ap()
    wk_in = nc.dram_tensor("wk_in", [D, HC * 128], F32, kind="ExternalInput").ap()
    wv_in = nc.dram_tensor("wv_in", [D, HC * 128], F32, kind="ExternalInput").ap()
    wo_in = nc.dram_tensor("wo_in", [D, D], F32, kind="ExternalInput").ap()
    lq1 = nc.dram_tensor("lq1", [1, 128], F32, kind="ExternalInput").ap()
    lq2 = nc.dram_tensor("lq2", [1, 128], F32, kind="ExternalInput").ap()
    lk1 = nc.dram_tensor("lk1", [1, 128], F32, kind="ExternalInput").ap()
    lk2 = nc.dram_tensor("lk2", [1, 128], F32, kind="ExternalInput").ap()
    gup = nc.dram_tensor("gup", [I2, WI_GU], I32, kind="ExternalInput").ap()
    gusT = nc.dram_tensor("gusT", [I2, NGU], F16, kind="ExternalInput").ap()
    dnp = nc.dram_tensor("dnp", [D, WI_DN], I32, kind="ExternalInput").ap()
    dnsT = nc.dram_tensor("dnsT", [D, NGD], F16, kind="ExternalInput").ap()
    out_c = nc.dram_tensor("out_c", [U * 128, D], F32, kind="ExternalOutput").ap()

    wq_bf = nc.dram_tensor("wq_bf", [D, HC * 128], F16, kind="Internal").ap()
    wk_bf = nc.dram_tensor("wk_bf", [D, HC * 128], F16, kind="Internal").ap()
    wv_bf = nc.dram_tensor("wv_bf", [D, HC * 128], F16, kind="Internal").ap()
    wo_bf = nc.dram_tensor("wo_bf", [D, D], F16, kind="Internal").ap()
    xn_nat = nc.dram_tensor("xn_nat", [S, D], F16, kind=dk).ap()
    xnT_nat = nc.dram_tensor("xnT_nat", [D, S], F16, kind=dk).ap()
    s_stash = nc.dram_tensor("s_stash", [U * 128, D], F32, kind=dk).ap()
    wgu_dense = nc.dram_tensor("wgu_dense", [I2, D], F16, kind=dk).ap()
    wdn_dense = nc.dram_tensor("wdn_dense", [D, I], F16, kind=dk).ap()
    y_dbg = nc.dram_tensor("y_dbg", [U * 128, D], F32, kind=dk).ap() if debug else None
    h_dbg = nc.dram_tensor("h_dbg", [4 * 128, ICH * 128], F16, kind=dk).ap() if debug else None

    with ExitStack() as top:
        E = top.enter_context
        ident = E(nc.sbuf_tensor("ident", [128, 128], BF16))
        identh = E(nc.sbuf_tensor("identh", [128, 128], F16))
        identf = E(nc.sbuf_tensor("identf", [128, 128], F32))
        identi = E(nc.sbuf_tensor("identi", [128, 128], I32))
        onesb = E(nc.sbuf_tensor("onesb", [128, 1], F16))
        ones128 = E(nc.sbuf_tensor("ones128", [128, 128], F16))
        nlam128 = E(nc.sbuf_tensor("nlam128", [128, 1], F32))
        ones1 = E(nc.sbuf_tensor("ones1", [1, 128], F32))
        epsLN = E(nc.sbuf_tensor("epsLN", [128, 1], F32))
        epsRMS = E(nc.sbuf_tensor("epsRMS", [128, 1], F32))
        epsFFN = E(nc.sbuf_tensor("epsFFN", [128, 1], F32))
        lamt = E(nc.sbuf_tensor("lamt", [1, 1], F32))
        ltmp = E(nc.sbuf_tensor("ltmp", [1, 768], F32))
        lred = E(nc.sbuf_tensor("lred", [1, 4], F32))

        PP = E(nc.psum_tensor("PP", [128, 4096], F32))
        PPb = PP.bitcast(BF16)
        PPh = PP.bitcast(F16)

        sems = {k: E(nc.semaphore(k)) for k in ["pe", "ac", "dv", "gp", "sp", "ag",
                                                "acd", "spd"]}
        block = E(nc.Block())

        PE = Stream(block, "tensor", sems["pe"])
        AC = Stream(block, "scalar", sems["ac"], dmasem=sems["acd"])
        DV = Stream(block, "vector", sems["dv"])
        GP = Stream(block, "gpsimd", sems["gp"])
        SP = Stream(block, "sync", sems["sp"], dmasem=sems["spd"])

        ALLS = None

        def flush_all():
            for s in (PE, AC, DV, GP, SP):
                s.flush()

        def barrier():
            streams = [PE, AC, DV, GP, SP]
            for s in streams:
                for o in streams:
                    if o is not s:
                        s.wait(o)
            flush_all()

        GP.op(lambda g: g.iota(identi[:, :], pattern=[[-1, 128]], base=0, channel_multiplier=1))
        tok_iota = GP.n
        GP.dman(lambda g: [
            g.dma_start(out=wq_bf, in_=wq_in),
            g.dma_start(out=wk_bf, in_=wk_in),
            g.dma_start(out=wv_bf, in_=wv_in),
            g.dma_start(out=wo_bf, in_=wo_in),
        ], 4)
        tok_wcast = GP.n
        SP.dman(lambda s: [
            s.dma_start(out=ltmp[:, 0:128], in_=lq1),
            s.dma_start(out=ltmp[:, 128:256], in_=lk1),
            s.dma_start(out=ltmp[:, 256:384], in_=lq2),
            s.dma_start(out=ltmp[:, 384:512], in_=lk2),
        ], 4)
        tok_lload = SP.n

        DV.wait(GP, tok_iota)
        DV.opn(lambda v: [
            v.memset(onesb[:, :], 1.0),
            v.memset(ones128[:, :], 1.0),
            v.memset(ones1[:, :], 1.0),
            v.memset(epsLN[:, :], LN_EPS),
            v.memset(epsRMS[:, :], RMS_EPS),
            v.memset(epsFFN[:, :], FFN_EPS / 4096.0),
            v.tensor_scalar(ident[:, :], identi[:, :], 0, None, op0=ALU.is_equal),
            v.tensor_scalar(identh[:, :], identi[:, :], 0, None, op0=ALU.is_equal),
            v.tensor_scalar(identf[:, :], identi[:, :], 0, None, op0=ALU.is_equal),
        ], 9)
        DV.wait(SP, tok_lload)
        DV.wait(GP, tok_wcast)
        DV.opn(lambda v: [
            v.tensor_tensor(ltmp[:, 512:640], ltmp[:, 0:128], ltmp[:, 128:256], op=ALU.mult),
            v.tensor_tensor(ltmp[:, 640:768], ltmp[:, 256:384], ltmp[:, 384:512], op=ALU.mult),
        ], 2)
        DV.selfwait()
        DV.opn(lambda v: [
            v.tensor_reduce(lred[:, 0:1], ltmp[:, 512:640], op=ALU.add, axis=mybir.AxisListType.X),
            v.tensor_reduce(lred[:, 1:2], ltmp[:, 640:768], op=ALU.add, axis=mybir.AxisListType.X),
        ], 2)
        AC.wait(DV)
        AC.opn(lambda s: [s.activation(lred[:, 2:3], lred[:, 0:1], AF.Exp),
                          s.activation(lred[:, 3:4], lred[:, 1:2], AF.Exp)], 2)
        DV.wait(AC)
        DV.op(lambda v: v.tensor_tensor(lamt[:, :], lred[:, 2:3], lred[:, 3:4], op=ALU.subtract))
        DV.selfwait()
        DV.op(lambda v: v.tensor_scalar(lamt[:, :], lamt[:, :], LAMBDA_INIT, None, op0=ALU.add))
        PE.wait(DV)
        PE.mmgroup(lambda t: [t.matmul(PP[:, 0:1], ones1[0:1, :], lamt[0:1, 0:1], start=True, stop=True)])
        DV.wait(PE)
        DV.op(lambda v: v.tensor_scalar(nlam128[:, :], PP[:, 0:1], -1.0, None, op0=ALU.mult))
        tok_nlam = DV.n

        with ExitStack() as ph:
            Ep = ph.enter_context
            xt = [Ep(nc.sbuf_tensor(f"xt{i}", [128, D], F32)) for i in range(2)]
            xnt = [Ep(nc.sbuf_tensor(f"xnt{i}", [128, D], F16)) for i in range(2)]
            NST = max(D // 512, 1)
            stats = Ep(nc.sbuf_tensor("stats", [128, 6 * NST], F32))
            mv = Ep(nc.sbuf_tensor("mv", [128, 2], F32))
            negm = Ep(nc.sbuf_tensor("negm", [128, 1], F32))
            lnv = Ep(nc.sbuf_tensor("lnv", [128, 1], F32))
            rstd = Ep(nc.sbuf_tensor("rstd", [128, 1], F32))
            txp = [Ep(nc.sbuf_tensor(f"txp{i}", [128, 2048], F16)) for i in range(2)]

            tok_xn_dv = {}
            tok_xn_store = {}
            tok_xload = {}

            def emit_xload(si):
                if si > SCH:
                    return
                if si >= 2:
                    GP.wait(DV, tok_xn_dv[si - 2])
                row = 128 * (si % SCH)
                GP.op(lambda g, row=row, p=si % 2: g.dma_start(
                    out=xt[p][:, :], in_=x_in[row:row + 128, :]), inc=16)
                tok_xload[si] = GP.n

            emit_xload(0)
            for si in range(SCH):
                p = si % 2
                emit_xload(si + 1)
                DV.wait(GP, tok_xload[min(si + 1, SCH)])
                if si >= 2:
                    DV.wait(SP, tok_xn_store[si - 2])
                DV.opn(lambda v, p=p: [
                    v.bn_stats(stats[:, 6 * i:6 * i + 6], xt[p][:, 512 * i:512 * i + min(512, D)])
                    for i in range(NST)], NST)
                DV.selfwait()
                DV.op(lambda v: v.bn_aggr(mv[:, :], stats[:, :]))
                DV.selfwait()
                DV.op(lambda v: v.tensor_scalar(negm[:, :], mv[:, 0:1], -1.0, None, op0=ALU.mult))
                AC.wait(DV)
                AC.op(lambda s: s.activation(lnv[:, :], mv[:, 1:2], AF.Ln, bias=epsLN[:, :]))
                AC.selfwait()
                AC.op(lambda s: s.activation(rstd[:, :], lnv[:, :], AF.Exp, scale=-0.5))
                DV.wait(AC)
                tok_xn_dv[si] = DV.op(lambda v, p=p: v.tensor_scalar(
                    xnt[p][:, :], xt[p][:, :], negm[:, :], rstd[:, :], op0=ALU.add, op1=ALU.mult))
                SP.wait(DV, tok_xn_dv[si])
                tok_xn_store[si] = SP.op(lambda s, si=si, p=p: s.dma_start(
                    out=xn_nat[128 * si:128 * si + 128, :], in_=xnt[p][:, :]), inc=16)

            SP.selfwait()
            HW = min(2048, S)
            NHALF = S // HW
            for dc in range(DCH):
                for hf in range(NHALF):
                    p = (dc * NHALF + hf) % 2
                    SP.op(lambda s, dc=dc, hf=hf, p=p: s.dma_start_transpose(
                        txp[p][:, 0:HW],
                        xn_nat[HW * hf:HW * hf + HW, 128 * dc:128 * dc + 128]), inc=16)
                    SP.selfwait()
                    SP.op(lambda s, dc=dc, hf=hf, p=p: s.dma_start(
                        out=xnT_nat[128 * dc:128 * dc + 128, HW * hf:HW * hf + HW],
                        in_=txp[p][:, 0:HW]), inc=16)
            SP.selfwait()
            barrier()

        with ExitStack() as ph:
            Ep = ph.enter_context
            wqT = [Ep(nc.sbuf_tensor(f"wqT{i}", [128, DCH * 128], F16)) for i in range(2)]
            wkT = [Ep(nc.sbuf_tensor(f"wkT{i}", [128, DCH * 128], F16)) for i in range(2)]
            wvT = [Ep(nc.sbuf_tensor(f"wvT{i}", [128, DCH * 128], F16)) for i in range(2)]
            xnS = [Ep(nc.sbuf_tensor(f"xnS{i}", [128, DCH * 512], F16)) for i in range(2)]
            QTs = Ep(nc.sbuf_tensor("QTs", [128, S], F16))
            KTs = Ep(nc.sbuf_tensor("KTs", [128, S], F16))
            Vs = Ep(nc.sbuf_tensor("Vs", [128, S], F16))
            Et = [Ep(nc.sbuf_tensor(f"Et{i}", [128, 2 * QT], F16)) for i in range(4)]
            lnA1 = Ep(nc.sbuf_tensor("lnA1", [128, QT], F32))
            lnA2 = Ep(nc.sbuf_tensor("lnA2", [128, QT], F32))
            bc1 = Ep(nc.sbuf_tensor("bc1", [128, QT], F32))
            bc2 = Ep(nc.sbuf_tensor("bc2", [128, QT], F32))
            t1s = Ep(nc.sbuf_tensor("t1s", [128, QT], F32))
            t2s = Ep(nc.sbuf_tensor("t2s", [128, QT], F32))
            vcb = Ep(nc.sbuf_tensor("vcb", [128, QT], F32))
            sqs = Ep(nc.sbuf_tensor("sqs", [128, 512], F32))
            ssA = Ep(nc.sbuf_tensor("ssA", [128, 4], F32))
            lnA = Ep(nc.sbuf_tensor("lnA", [128, 4], F32))
            rsA = Ep(nc.sbuf_tensor("rsA", [128, 4], F32))
            nrm = Ep(nc.sbuf_tensor("nrm", [128, 2 * SCH * 128], F16))
            woT = [Ep(nc.sbuf_tensor(f"woT{i}", [128, D], F16)) for i in range(3)]
            sev = Ep(nc.sbuf_tensor("sev", [128, D], F32))
            pk = [Ep(nc.sbuf_tensor(f"pk{i}", [128, 256], I32)) for i in range(2)]
            bA = Ep(nc.sbuf_tensor("bA", [128, 2048], I32))
            bB = Ep(nc.sbuf_tensor("bB", [128, 2048], I32))
            bC = Ep(nc.sbuf_tensor("bC", [128, 2048], I32))
            bD = Ep(nc.sbuf_tensor("bD", [128, 2048], F32))
            Wf = Ep(nc.sbuf_tensor("Wf", [128, 2048], F32))
            dgu = [Ep(nc.sbuf_tensor(f"dgu{i}", [128, D], F16)) for i in range(2)]
            ddn0 = Ep(nc.sbuf_tensor("ddn0", [128, I], F16))
            ddn = [ddn0, ddn0]
            scg = [Ep(nc.sbuf_tensor(f"scg{i}", [128, 64], F32)) for i in range(2)]
            scd = [Ep(nc.sbuf_tensor(f"scd{i}", [128, 64], F32)) for i in range(2)]

            bAf = bA.bitcast(F32)
            bBf = bB.bitcast(F32)
            bDi = bD.bitcast(I32)
            WfR = Wf.bitcast(I32).rearrange("p (w e) -> p e w", e=8)

            def dequant_strip(nw, src_pk):
                DV.opn(lambda v: [
                    v.tensor_scalar(bA[:, 256 * j:256 * j + nw], src_pk[:, 0:nw],
                                    4 * j, 7, op0=ALU.logical_shift_right, op1=ALU.bitwise_and)
                    for j in range(8)] + [
                    v.tensor_scalar(bC[:, 256 * j:256 * j + nw], src_pk[:, 0:nw],
                                    28 - 4 * j, -2147483648,
                                    op0=ALU.logical_shift_left, op1=ALU.bitwise_and)
                    for j in range(8)], 16)
                tok_rd = DV.n
                DV.selfwait()
                DV.opn(lambda v: [
                    v.tensor_scalar(bB[:, 256 * j:256 * j + nw], bA[:, 256 * j:256 * j + nw],
                                    252.0, float(1 << 22), op0=ALU.add, op1=ALU.mult)
                    for j in range(8)], 8)
                DV.selfwait()
                DV.opn(lambda v: [
                    v.tensor_scalar(bAf[:, 256 * j:256 * j + nw], bBf[:, 256 * j:256 * j + nw],
                                    2.0, 1.0, op0=ALU.mult, op1=ALU.subtract)
                    for j in range(8)], 8)
                DV.selfwait()
                DV.opn(lambda v: [
                    v.tensor_tensor(bD[:, 256 * j:256 * j + nw], bAf[:, 256 * j:256 * j + nw],
                                    bBf[:, 256 * j:256 * j + nw], op=ALU.min)
                    for j in range(8)], 8)
                DV.selfwait()
                DV.opn(lambda v: [
                    v.tensor_tensor(WfR[:, j, 0:nw], bDi[:, 256 * j:256 * j + nw],
                                    bC[:, 256 * j:256 * j + nw], op=ALU.bitwise_or)
                    for j in range(8)], 8)
                DV.selfwait()
                return tok_rd

            pk_sched = [("g", ot, 0, WI_GU) for ot in range(NOT_GU)]
            NSTRIP_DN = (WI_DN + 255) // 256
            for ot_ in range(NOT_DN):
                for sti_ in range(NSTRIP_DN):
                    w0_ = 256 * sti_
                    pk_sched.append(("d", ot_, w0_, min(256, WI_DN - w0_)))
            pk_sched.append(("g", 0, 0, WI_GU))
            pktok = {}
            pkemit = [0]
            tok_pk_rd = {}
            tok_sc_used = {}

            def emit_pk(upto):
                while pkemit[0] <= min(upto, len(pk_sched) - 1):
                    k = pkemit[0]
                    if k >= 2:
                        GP.wait(DV, tok_pk_rd.get(k - 2, 0))
                    kind, ot_, w0_, nw_ = pk_sched[k]
                    if kind == "g":
                        GP.op(lambda g, ot_=ot_, p2=k % 2: g.dma_start(
                            out=pk[p2][:, 0:WI_GU], in_=gup[128 * ot_:128 * ot_ + 128, :]), inc=16)
                        if k < NOT_GU:
                            GP.wait(DV, tok_sc_used.get(("g", ot_ - 2), 0))
                            GP.op(lambda g, ot_=ot_, ps=ot_ % 2: g.dma_start(
                                out=scg[ps][:, 0:NGU], in_=gusT[128 * ot_:128 * ot_ + 128, :]), inc=16)
                    else:
                        GP.op(lambda g, ot_=ot_, w0_=w0_, nw_=nw_, p2=k % 2: g.dma_start(
                            out=pk[p2][:, 0:nw_], in_=dnp[128 * ot_:128 * ot_ + 128, w0_:w0_ + nw_]), inc=16)
                        if w0_ == 0:
                            GP.wait(DV, tok_sc_used.get(("d", ot_ - 2), 0))
                            GP.op(lambda g, ot_=ot_, ps=ot_ % 2: g.dma_start(
                                out=scd[ps][:, 0:NGD], in_=dnsT[128 * ot_:128 * ot_ + 128, :]), inc=16)
                    pktok[k] = GP.n
                    pkemit[0] += 1

            def pk_wait(k):
                emit_pk(k + 1)
                DV.wait(GP, pktok[min(k + 1, len(pk_sched) - 1)])

            scnt = [0]
            tok_store = {}

            def p3_steps():
              sc = scnt
              for ot in range(NOT_GU):
                p2 = sc[0] % 2
                pk_wait(sc[0])
                tok_pk_rd[sc[0]] = dequant_strip(WI_GU, pk[p2])
                if ot >= 2:
                    DV.wait(GP, tok_store.get(("g", ot - 2), 0))
                tok_sc_used[("g", ot)] = DV.opn(lambda v, p2=p2, ps=ot % 2: [
                    v.tensor_scalar(dgu[p2][:, 128 * g2:128 * g2 + 128], Wf[:, 128 * g2:128 * g2 + 128],
                                    scg[ps][:, g2:g2 + 1], None, op0=ALU.mult)
                    for g2 in range(NGU)], NGU)
                DV.selfwait()
                sc[0] += 1
                GP.wait(DV, tok_sc_used[("g", ot)])
                tok_store[("g", ot)] = GP.op(lambda g, ot=ot, p2=p2: g.dma_start(
                    out=wgu_dense[128 * ot:128 * ot + 128, :], in_=dgu[p2][:, :]), inc=16)
                yield
              for ot in range(NOT_DN):
                p2d = ot % 2
                for sti in range(NSTRIP_DN):
                    w0 = 256 * sti
                    nw = min(256, WI_DN - w0)
                    p2 = sc[0] % 2
                    pk_wait(sc[0])
                    tok_pk_rd[sc[0]] = dequant_strip(nw, pk[p2])
                    g0 = (8 * w0) // 128
                    ng = (8 * nw) // 128
                    if sti == 0 and ot >= 1:
                        DV.wait(GP, tok_store.get(("d", ot - 1), 0))
                    DV.opn(lambda v, p2d=p2d, g0=g0, ng=ng: [
                        v.tensor_scalar(ddn[p2d][:, 128 * (g0 + g2):128 * (g0 + g2) + 128],
                                        Wf[:, 128 * g2:128 * g2 + 128],
                                        scd[p2d][:, g0 + g2:g0 + g2 + 1], None, op0=ALU.mult)
                        for g2 in range(ng)], ng)
                    DV.selfwait()
                    if sti == NSTRIP_DN - 1:
                        tok_sc_used[("d", ot)] = DV.n
                    sc[0] += 1
                    yield
                GP.wait(DV)
                tok_store[("d", ot)] = GP.op(lambda g, ot=ot, p2d=p2d: g.dma_start(
                    out=wdn_dense[128 * ot:128 * ot + 128, :], in_=ddn[p2d][:, :]), inc=16)

            p3_iter = p3_steps()

            def p3_pump(n):
                for _ in range(n):
                    if next(p3_iter, "END") == "END":
                        return

            S0, S1c = 0, 1024
            AV1, AV2 = 2048, 2560
            D1C, D2C = 3072, 3584
            QC, KC, VC = 0, 512, 1024
            TRC = 1024

            xn_use = {}
            xn_tok = {}
            xcnt = 0
            tok_wT = {}
            wo_use = {}
            tok_nrm_last = {}
            tok_wo_mm = {}

            for pr in range(HC):
                pp = pr % 2
                if pr >= 2:
                    SP.wait(PE, tok_wT[("consumed", pr - 2)])
                SP.wait(GP, tok_wcast)
                SP.dman(lambda s, pr=pr, pp=pp: [
                    s.dma_start(
                        out=w_t[pp][:, 128 * dc:128 * dc + 128],
                        in_=w_b[128 * dc:128 * dc + 128, 128 * pr:128 * pr + 128])
                    for (w_t, w_b) in ((wqT, wq_bf), (wkT, wk_bf), (wvT, wv_bf))
                    for dc in range(DCH)], 3 * DCH)
                SP.op(lambda s: s.dma_start(out=ltmp[:, 0:128], in_=lq1), inc=16)
                tok_wT[pr] = SP.n

                tok_qkv_evac = 0
                for st in range(NQT):
                    sp2 = xcnt % 2
                    for k in (xcnt, xcnt + 1):
                        if k not in xn_tok and k <= HC * NQT:
                            if k >= 2:
                                GP.wait(PE, xn_use.get(k - 2, 0))
                            st_k = k % NQT
                            GP.op(lambda g, st_k=st_k, p2=k % 2: g.dma_start(
                                out=xnS[p2][:, :].rearrange("p (c s) -> p c s", c=DCH),
                                in_=xnT_nat.rearrange("(c p) s -> p c s", p=128)[:, :, QT * st_k:QT * st_k + QT]),
                            inc=16)
                            xn_tok[k] = GP.n
                    PE.wait(GP, xn_tok[min(xcnt + 1, HC * NQT)])
                    PE.wait(SP, tok_wT[pr])
                    if st == 0:
                        PE.wait(DV)
                        PE.wait(AC)
                    else:
                        PE.wait(DV, tok_qkv_evac)
                    PE.mmgroup(lambda t, pp=pp, sp2=sp2: [
                        t.matmul(PP[:, QC:QC + 512], wqT[pp][:, 128 * dc:128 * dc + 128],
                                 xnS[sp2][:, 512 * dc:512 * dc + 512],
                                 start=(dc == 0), stop=(dc == DCH - 1))
                        for dc in range(DCH)])
                    PE.mmgroup(lambda t, pp=pp, sp2=sp2: [
                        t.matmul(PP[:, KC:KC + 512], wkT[pp][:, 128 * dc:128 * dc + 128],
                                 xnS[sp2][:, 512 * dc:512 * dc + 512],
                                 start=(dc == 0), stop=(dc == DCH - 1))
                        for dc in range(DCH)])
                    PE.mmgroup(lambda t, pp=pp, sp2=sp2: [
                        t.matmul(PP[:, VC + 128 * q:VC + 128 * q + 128],
                                 xnS[sp2][:, 512 * dc + 128 * q:512 * dc + 128 * q + 128],
                                 wvT[pp][:, 128 * dc:128 * dc + 128],
                                 start=(dc == 0), stop=(dc == DCH - 1))
                        for q in range(4) for dc in range(DCH)])
                    xn_use[xcnt] = PE.n
                    xcnt += 1
                    if st == NQT - 1:
                        tok_wT[("consumed", pr)] = PE.n
                    DV.wait(PE)
                    tok_qkv_evac = DV.opn(lambda v, st=st: [
                        v.tensor_copy(QTs[:, QT * st:QT * st + QT], PP[:, QC:QC + 512]),
                        v.tensor_copy(KTs[:, QT * st:QT * st + QT], PP[:, KC:KC + 512]),
                        v.tensor_copy(Vs[:, QT * st:QT * st + QT], PP[:, VC:VC + 512]),
                    ], 3)
                tok_qkv_dv = DV.n

                for qt in range(NQT):
                    tok_scores = {}
                    tok_exp = {}
                    tok_cons = {}

                    def emit_dav(kc, tok_exp=tok_exp, tok_cons=tok_cons):
                        PE.wait(AC, tok_exp[kc])
                        ep4 = kc % 4
                        PE.mmgroup(lambda t, kc=kc, ep4=ep4: [
                            t.matmul(PP[:, D1C:D1C + QT], ones128[:, :], Et[ep4][:, 0:QT],
                                     start=(kc == 0), stop=(kc == KCH - 1)),
                            t.matmul(PP[:, D2C:D2C + QT], ones128[:, :], Et[ep4][:, QT:2 * QT],
                                     start=(kc == 0), stop=(kc == KCH - 1)),
                            t.matmul(PP[:, AV1:AV1 + QT], Vs[:, 128 * kc:128 * kc + 128],
                                     Et[ep4][:, 0:QT], start=(kc == 0), stop=(kc == KCH - 1)),
                            t.matmul(PP[:, AV2:AV2 + QT], Vs[:, 128 * kc:128 * kc + 128],
                                     Et[ep4][:, QT:2 * QT], start=(kc == 0), stop=(kc == KCH - 1)),
                        ])
                        tok_cons[kc] = PE.n

                    for kc in range(KCH):
                        sp2 = kc % 2
                        if kc == 0:
                            PE.wait(DV, tok_qkv_dv)
                            if qt > 0:
                                PE.wait(DV, tok_nrm_last[(pr, qt - 1)])
                        if kc >= 2:
                            PE.wait(AC, tok_exp[kc - 2])
                        PE.mmgroup(lambda t, kc=kc, qt=qt, sp2=sp2: [
                            t.matmul(PP[:, 1024 * sp2:1024 * sp2 + 512],
                                     KTs[0:64, 128 * kc:128 * kc + 128],
                                     QTs[0:64, QT * qt:QT * qt + QT],
                                     start=True, stop=True, tile_position=(0, 0)),
                            t.matmul(PP[:, 1024 * sp2 + 512:1024 * sp2 + 1024],
                                     KTs[64:128, 128 * kc:128 * kc + 128],
                                     QTs[64:128, QT * qt:QT * qt + QT],
                                     start=True, stop=True, tile_position=(64, 0)),
                        ])
                        tok_scores[kc] = PE.n
                        AC.wait(PE, tok_scores[kc])
                        if kc >= 4:
                            AC.wait(PE, tok_cons[kc - 4])
                        tok_exp[kc] = AC.op(lambda s, sp2=sp2, ep4=kc % 4: s.activation(
                            Et[ep4][:, :], PP[:, 1024 * sp2:1024 * sp2 + 2 * QT],
                            AF.Exp, scale=SM_SCALE))
                        if kc >= 1:
                            emit_dav(kc - 1)
                    emit_dav(KCH - 1)

                    AC.wait(PE, tok_cons[KCH - 1])
                    AC.opn(lambda s: [
                        s.activation(lnA1[:, :], PP[:, D1C:D1C + QT], AF.Ln),
                        s.activation(lnA2[:, :], PP[:, D2C:D2C + QT], AF.Ln),
                    ], 2)
                    AC.selfwait()
                    AC.opn(lambda s: [
                        s.activation(bc1[:, :], lnA1[:, :], AF.Exp, scale=-1.0),
                        s.activation(bc2[:, :], lnA2[:, :], AF.Exp, scale=-1.0),
                    ], 2)
                    tok_alpha = AC.n
                    DV.wait(AC, tok_alpha)
                    DV.wait(DV, tok_nlam)
                    DV.opn(lambda v: [
                        v.tensor_tensor(t1s[:, :], PP[:, AV1:AV1 + QT], bc1[:, :], op=ALU.mult),
                        v.scalar_tensor_tensor(t2s[:, :], PP[:, AV2:AV2 + QT], nlam128[:, :],
                                               bc2[:, :], op0=ALU.mult, op1=ALU.mult),
                    ], 2)
                    DV.selfwait()
                    tok_comb = DV.op(lambda v: v.tensor_tensor(vcb[:, :], t1s[:, :], t2s[:, :], op=ALU.add))
                    PE.wait(DV, tok_comb)
                    PE.mmgroup(lambda t: [
                        t.transpose(PP[:, TRC + 128 * ti:TRC + 128 * ti + 128],
                                    vcb[:, 128 * ti:128 * ti + 128], identf[:, :])
                        for ti in range(NT)])
                    AC.wait(PE)
                    AC.opn(lambda s: [
                        s.activation(sqs[:, 128 * ti:128 * ti + 128],
                                     PP[:, TRC + 128 * ti:TRC + 128 * ti + 128],
                                     AF.Square, accum_out=ssA[:, ti:ti + 1])
                        for ti in range(NT)], NT)
                    AC.selfwait()
                    AC.opn(lambda s: [
                        s.activation(lnA[:, ti:ti + 1], ssA[:, ti:ti + 1], AF.Ln,
                                     scale=1.0 / 128.0, bias=epsRMS[:, :])
                        for ti in range(NT)], NT)
                    AC.selfwait()
                    AC.opn(lambda s: [
                        s.activation(rsA[:, ti:ti + 1], lnA[:, ti:ti + 1], AF.Exp, scale=-0.5)
                        for ti in range(NT)], NT)
                    DV.wait(AC)
                    if qt == 0 and pr >= 2:
                        DV.wait(PE, tok_wo_mm[(pr - 2) | 1])
                    po = (pr % 2) * SCH * 128
                    tok_nrm_last[(pr, qt)] = DV.opn(lambda v, qt=qt, po=po: [
                        v.tensor_scalar(nrm[:, po + (qt * NT + ti) * 128:po + (qt * NT + ti) * 128 + 128],
                                        PP[:, TRC + 128 * ti:TRC + 128 * ti + 128],
                                        rsA[:, ti:ti + 1], 1.0 - LAMBDA_INIT,
                                        op0=ALU.mult, op1=ALU.mult)
                        for ti in range(NT)], NT)
                    p3_pump(2)

                if pr % 2 == 1 or pr == HC - 1:
                    duo = (pr - 1, pr) if pr % 2 == 1 else (pr,)
                    combos = [(pa, sh) for pa in duo for sh in (0, 1)]
                    ngrp = (len(combos) + 1) // 2
                    nwo = ngrp * CC
                    wotok = {}

                    def emit_woload(k, nwo=nwo):
                        if k > nwo or k in wotok:
                            return
                        if k >= 3:
                            SP.wait(PE, wo_use[k - 3])
                        cck = k % CC
                        SP.op(lambda s, cck=cck, wp=k % 3: s.dma_start(
                            out=woT[wp][:, :], in_=wo_bf[128 * cck:128 * cck + 128, :]), inc=16)
                        wotok[k] = SP.n

                    woci = 0
                    prev_sstore = 0
                    emit_woload(0)
                    for g0 in range(0, len(combos), 2):
                        sub = combos[g0:g0 + 2]
                        for cc in range(CC):
                            wp = woci % 3
                            emit_woload(woci + 1)
                            PE.wait(SP, wotok[min(woci + 1, nwo)])
                            if cc == 0:
                                PE.wait(DV)
                            def womms(t, sub=sub, cc=cc, wp=wp):
                                ins = []
                                for ci, (pa, sh) in enumerate(sub):
                                    po2 = (pa % 2) * SCH * 128
                                    lhs = nrm[:, po2 + (sh * CC + cc) * 128:po2 + (sh * CC + cc) * 128 + 128]
                                    for j in range(NJ):
                                        ins.append(t.matmul(
                                            PP[:, 2048 * ci + JT * j:2048 * ci + JT * j + JT],
                                            lhs, woT[wp][:, JT * j:JT * j + JT],
                                            start=(cc == 0), stop=(cc == CC - 1)))
                                return ins
                            PE.mmgroup(womms)
                            wo_use[woci] = PE.n
                            woci += 1
                        tok_wo_mm[pr] = PE.n
                        for ci, (pa, sh) in enumerate(sub):
                            ul = 2 * pa + sh
                            DV.wait(PE)
                            DV.wait(GP, prev_sstore)
                            DV.op(lambda v, ci=ci: v.tensor_copy(sev[:, :], PP[:, 2048 * ci:2048 * ci + D]))
                            GP.wait(DV)
                            prev_sstore = GP.op(lambda g, ul=ul: g.dma_start(
                                out=s_stash[128 * ul:128 * ul + 128, :], in_=sev[:, :]), inc=16)
            p3_pump(10 ** 9)
            emit_pk(len(pk_sched) - 1)
            GP.selfwait()
            barrier()

        with ExitStack() as ph:
            Ep = ph.enter_context
            xnU = Ep(nc.sbuf_tensor("xnU", [128, 4 * DCH * 128], F16))
            wsb = [Ep(nc.sbuf_tensor(f"wsb{i}", [128, 1024], F16)) for i in range(DCH)]
            wsd = [Ep(nc.sbuf_tensor(f"wsd{i}", [128, 2 * JT], F16)) for i in range(4)]
            gsil = Ep(nc.sbuf_tensor("gsil", [128, 2 * 4 * 512], F16))
            hsb = Ep(nc.sbuf_tensor("hsb", [128, 4 * 512], F16))
            hT = Ep(nc.sbuf_tensor("hT", [128, 4 * ICH * 128], F16))
            ssF = Ep(nc.sbuf_tensor("ssF", [128, 1], F32))
            lnF = Ep(nc.sbuf_tensor("lnF", [128, 1], F32))
            rsF = Ep(nc.sbuf_tensor("rsF", [128, 1], F32))
            ybuf = [Ep(nc.sbuf_tensor(f"ybuf{i}", [128, D], F32)) for i in range(4)]
            gbuf = Ep(nc.sbuf_tensor("gbuf", [128, D], F32))
            sbuf2 = [Ep(nc.sbuf_tensor(f"sbuf2{i}", [128, D], F32)) for i in range(2)]
            xbuf = [Ep(nc.sbuf_tensor(f"xbuf{i}", [128, D], F32)) for i in range(2)]
            obuf = Ep(nc.sbuf_tensor("obuf", [128, D], F32))

            NUG = (U + 3) // 4
            ogps = [list(range(p, min(p + 2, OG))) for p in range(0, OG, 2)]
            jps = [list(range(p, min(p + 2, NJ))) for p in range(0, NJ, 2)]
            NOGH = len(ogps)
            gu_tok = {}
            gu_margin = {}

            def emit_gu_load(ug_, ogh_, ph2_, dc_, guard):
                ogs_ = ogps[ogh_]
                rb_ = 512 * ogs_[0] + (I if ph2_ else 0)
                w_ = 512 * len(ogs_)
                if guard:
                    SP.wait(PE, guard)
                gu_tok[(ug_, ogh_, ph2_, dc_)] = SP.dmaop(
                    lambda s, dc_=dc_, rb_=rb_, w_=w_: s.dma_start_transpose(
                        wsb[dc_][:, 0:w_],
                        wgu_dense[rb_:rb_ + w_, 128 * dc_:128 * dc_ + 128]))

            def emit_gu_margin(ug_, ogh_, ph2_):
                ogs_ = ogps[ogh_]
                rb_ = 512 * ogs_[0] + (I if ph2_ else 0)
                w_ = 512 * len(ogs_)
                gu_margin[(ug_, ogh_, ph2_)] = SP.dmaop(
                    lambda s, rb_=rb_, w_=w_: s.dma_start_transpose(
                        wsb[DCH - 1][:, 0:w_],
                        wgu_dense[rb_:rb_ + w_, 128 * (DCH - 1):128 * DCH]))

            def gu_next_tok(ug_, ogh_, ph2_, dc_):
                if dc_ < DCH - 1:
                    return gu_tok[(ug_, ogh_, ph2_, dc_ + 1)]
                return gu_margin[(ug_, ogh_, ph2_)]

            def mk_dn_pipeline():
                tiles = [(js_, ic_) for js_ in jps for ic_ in range(ICH)]
                state = dict(tok={}, consumed={}, emitted=0)

                def emit(idx):
                    while state["emitted"] <= min(idx, len(tiles) - 1):
                        L = state["emitted"]
                        js_, ic_ = tiles[L]
                        if L >= 4:
                            SP.wait(PE, state["consumed"][L - 4])
                        w_ = JT * len(js_)
                        state["tok"][L] = SP.dmaop(
                            lambda s, L=L, js_=js_, ic_=ic_, w_=w_: s.dma_start_transpose(
                                wsd[L % 4][:, 0:w_],
                                wdn_dense[JT * js_[0]:JT * js_[0] + w_, 128 * ic_:128 * ic_ + 128]))
                        state["emitted"] += 1
                    if state["emitted"] == len(tiles) and "margin" not in state:
                        js_, ic_ = tiles[-1]
                        w_ = JT * len(js_)
                        state["margin"] = SP.dmaop(
                            lambda s, js_=js_, ic_=ic_, w_=w_: s.dma_start_transpose(
                                wsd[(len(tiles) - 1) % 4][:, 0:w_],
                                wdn_dense[JT * js_[0]:JT * js_[0] + w_, 128 * ic_:128 * ic_ + 128]))
                return state, tiles, emit

            for dc_ in range(DCH):
                emit_gu_load(0, 0, 0, dc_, 0)
            emit_gu_margin(0, 0, 0)

            tok_xnU_used = 0
            tok_g_ac = 0
            tok_gbuf_free = 0
            tok_out_dv = 0
            tok_obuf_store = 0
            sxtok = {}
            sx_guard = {}
            tok_ydump = 0

            for ug in range(NUG):
                us = list(range(4 * ug, min(4 * ug + 4, U)))
                nu = len(us)
                if ug > 0:
                    SP.wait(PE, tok_xnU_used)
                SP.dman(lambda s, us=us: [
                    s.dma_start_transpose(
                        xnU[:, (ui * DCH + dc) * 128:(ui * DCH + dc) * 128 + 128],
                        xn_nat.rearrange("(m blk) d -> blk m d", blk=BLK)[u, :, 128 * dc:128 * dc + 128])
                    for ui, u in enumerate(us) for dc in range(DCH)], nu * DCH)
                SP.op(lambda s, us=us, nu=nu: s.dma_start_transpose(
                    xnU[:, (nu * DCH - 1) * 128:nu * DCH * 128],
                    xn_nat.rearrange("(m blk) d -> blk m d", blk=BLK)[us[-1], :, 128 * (DCH - 1):128 * DCH]), inc=16)
                tok_xnU = SP.n

                tok_h_dv = 0
                tok_hTdv = 0
                tok_hT_pe = 0
                tok_yevac = {}
                for ogh, ogs in enumerate(ogps):
                    tok_gsil = {}
                    for pos, og in enumerate(ogs):
                        lastpos = pos == len(ogs) - 1
                        for dc in range(DCH):
                            PE.waitd(SP, gu_next_tok(ug, ogh, 0, dc))
                            if dc == 0:
                                PE.wait(SP, tok_xnU)
                                if pos == 0:
                                    if og > 0 or ug > 0:
                                        PE.wait(DV, tok_hTdv if og > 0 else tok_ylast)
                                else:
                                    PE.wait(DV, tok_h_dv)
                            PE.mmgroup(lambda t, us=us, dc=dc, pos=pos: [
                                t.matmul(PP[:, 2048 * pos + 512 * ui:2048 * pos + 512 * ui + 512],
                                         xnU[:, (ui * DCH + dc) * 128:(ui * DCH + dc) * 128 + 128],
                                         wsb[dc][:, 512 * pos:512 * pos + 512],
                                         start=(dc == 0), stop=(dc == DCH - 1))
                                for ui in range(len(us))])
                            if lastpos:
                                emit_gu_load(ug, ogh, 1, dc, PE.n)
                                if dc == DCH - 1:
                                    emit_gu_margin(ug, ogh, 1)
                        AC.wait(PE)
                        AC.wait(DV, tok_h_dv)
                        tok_gsil[pos] = AC.opn(lambda s, nu=nu, pos=pos: [
                            s.activation(gsil[:, 2048 * pos + 512 * ui:2048 * pos + 512 * ui + 512],
                                         PP[:, 2048 * pos + 512 * ui:2048 * pos + 512 * ui + 512],
                                         AF.Silu)
                            for ui in range(nu)], nu)
                    for pos, og in enumerate(ogs):
                        lastpos = pos == len(ogs) - 1
                        for dc in range(DCH):
                            PE.waitd(SP, gu_next_tok(ug, ogh, 1, dc))
                            if dc == 0:
                                PE.wait(AC, tok_gsil[pos])
                            PE.mmgroup(lambda t, us=us, dc=dc, pos=pos: [
                                t.matmul(PP[:, 2048 * pos + 512 * ui:2048 * pos + 512 * ui + 512],
                                         xnU[:, (ui * DCH + dc) * 128:(ui * DCH + dc) * 128 + 128],
                                         wsb[dc][:, 512 * pos:512 * pos + 512],
                                         start=(dc == 0), stop=(dc == DCH - 1))
                                for ui in range(len(us))])
                            if lastpos:
                                nxt = (ug, ogh + 1) if ogh + 1 < NOGH else (
                                    (ug + 1, 0) if ug + 1 < NUG else None)
                                if nxt is not None:
                                    emit_gu_load(nxt[0], nxt[1], 0, dc, PE.n)
                                    if dc == DCH - 1:
                                        emit_gu_margin(nxt[0], nxt[1], 0)
                        DV.wait(PE)
                        DV.wait(AC, tok_gsil[pos])
                        DV.wait(PE, tok_hT_pe)
                        tok_h_dv = DV.opn(lambda v, nu=nu, pos=pos: [
                            v.scalar_tensor_tensor(
                                hsb[:, 512 * ui:512 * ui + 512],
                                PP[:, 2048 * pos + 512 * ui:2048 * pos + 512 * ui + 512], 1.0 / 64.0,
                                gsil[:, 2048 * pos + 512 * ui:2048 * pos + 512 * ui + 512],
                                op0=ALU.mult, op1=ALU.mult)
                            for ui in range(nu)], nu)
                        PE.wait(DV, tok_h_dv)
                        if tok_hTdv:
                            PE.wait(DV, tok_hTdv)
                        PE.mmgroup(lambda t, nu=nu: [
                            t.transpose(PPh[:, 128 * (4 * ui + ti):128 * (4 * ui + ti) + 128],
                                        hsb[:, 512 * ui + 128 * ti:512 * ui + 128 * ti + 128], identh[:, :])
                            for ui in range(nu) for ti in range(4)])
                        tok_hT_pe = PE.n
                        DV.wait(PE, tok_hT_pe)
                        tok_hTdv = DV.opn(lambda v, og=og, nu=nu: [
                            v.tensor_copy(hT[:, (ui * ICH + og * 4 + ti) * 128:(ui * ICH + og * 4 + ti) * 128 + 128],
                                          PPh[:, 128 * (4 * ui + ti):128 * (4 * ui + ti) + 128])
                            for ui in range(nu) for ti in range(4)], nu * 4)
                tok_xnU_used = PE.n
                if debug and ug == 0:
                    SP.wait(DV, tok_hTdv)
                    SP.dman(lambda s, nu=nu: [
                        s.dma_start(out=h_dbg[128 * ui:128 * ui + 128, :], in_=hT[:, ui * ICH * 128:(ui + 1) * ICH * 128])
                        for ui in range(nu)], nu)

                dnstate, dntiles, dn_emit = mk_dn_pipeline()
                NDL = len(dntiles)
                for jh, js in enumerate(jps):
                    for ic in range(ICH):
                        L = jh * ICH + ic
                        dn_emit(L + 3)
                        PE.waitd(SP, dnstate["tok"][L + 1] if L + 1 < NDL else dnstate["margin"])
                        if ic == 0:
                            if jh == 0:
                                PE.wait(DV, tok_hTdv)
                            else:
                                PE.wait(DV, tok_yevac[jh - 1])
                            if ug > 0 and jh == 0:
                                PE.wait(AC, tok_g_ac)
                        wp = L % 4
                        for pos, j in enumerate(js):
                            PE.mmgroup(lambda t, j=j, ic=ic, wp=wp, pos=pos, nu=nu: [
                                t.matmul(PP[:, 2048 * (j % 2) + 512 * ui:2048 * (j % 2) + 512 * ui + JT],
                                         hT[:, (ui * ICH + ic) * 128:(ui * ICH + ic) * 128 + 128],
                                         wsd[wp][:, JT * pos:JT * pos + JT],
                                         start=(ic == 0), stop=(ic == ICH - 1))
                                for ui in range(nu)])
                        dnstate["consumed"][L] = PE.n
                    DV.wait(PE)
                    if ug > 0 and jh == 0:
                        DV.wait(AC, tok_g_ac)
                        DV.wait(SP, tok_ydump)
                    tok_yevac[jh] = DV.opn(lambda v, js=js, nu=nu: [
                        v.tensor_copy(ybuf[ui][:, JT * j:JT * j + JT],
                                      PP[:, 2048 * (j % 2) + 512 * ui:2048 * (j % 2) + 512 * ui + JT])
                        for j in js for ui in range(nu)], len(js) * nu)
                tok_ylast = tok_yevac[len(jps) - 1]

                if debug:
                    SP.wait(DV, tok_ylast)
                    SP.dman(lambda s, us=us: [
                        s.dma_start(out=y_dbg[128 * u:128 * u + 128, :], in_=ybuf[ui][:, :])
                        for ui, u in enumerate(us)], nu)
                    tok_ydump = SP.n

                for ui, u in enumerate(us):
                    ul = u
                    for uk in (u, u + 1):
                        if uk not in sxtok and uk <= U:
                            GP.wait(DV, sx_guard.get(uk - 2, 0))
                            row = 128 * (uk % U)
                            GP.op(lambda g, row=row, p=uk % 2: g.dma_start(
                                out=sbuf2[p][:, :], in_=s_stash[row:row + 128, :]), inc=16)
                            GP.op(lambda g, p=uk % 2, blk=uk % U: g.dma_start(
                                out=xbuf[p][:, :],
                                in_=x_in.rearrange("(m blk) d -> blk m d", blk=BLK)[blk, :, :]), inc=16)
                            sxtok[uk] = GP.n
                    AC.wait(DV, tok_ylast)
                    AC.wait(DV, tok_gbuf_free)
                    AC.op(lambda s, ui=ui: s.activation(
                        gbuf[:, :], ybuf[ui][:, :], AF.Square, accum_out=ssF[:, :]))
                    AC.selfwait()
                    AC.op(lambda s: s.activation(lnF[:, :], ssF[:, :], AF.Ln,
                                                 scale=1.0 / D, bias=epsFFN[:, :]))
                    AC.selfwait()
                    AC.op(lambda s: s.activation(rsF[:, :], lnF[:, :], AF.Exp, scale=-0.5))
                    AC.selfwait()
                    tok_g_ac = AC.op(lambda s, ui=ui: s.activation(
                        gbuf[:, :], ybuf[ui][:, :], AF.Silu, scale=rsF[:, :]))
                    DV.wait(AC, tok_g_ac)
                    DV.wait(GP, sxtok[u + 1])
                    DV.wait(GP, tok_obuf_store)
                    tok_gbuf_free = DV.op(lambda v, p=u % 2: v.tensor_tensor(
                        obuf[:, :], gbuf[:, :], sbuf2[p][:, :], op=ALU.mult))
                    DV.selfwait()
                    tok_out_dv = DV.op(lambda v, p=u % 2: v.tensor_tensor(
                        obuf[:, :], obuf[:, :], xbuf[p][:, :], op=ALU.add))
                    sx_guard[u] = tok_out_dv
                    GP.wait(DV, tok_out_dv)
                    tok_obuf_store = GP.op(lambda g, ul=ul: g.dma_start(
                        out=out_c[128 * ul:128 * ul + 128, :], in_=obuf[:, :]), inc=16)
            GP.selfwait()
            GP.wait(SP)
            SP.selfwait()
            flush_all()
    return nc



def shard_inputs(cfg, inputs, n_cores=8):
    c = _cfg(cfg)
    S, D, HC, BLK, SH = c["S"], c["D"], c["HC"], c["BLK"], c["SH"]
    x = np.ascontiguousarray(inputs["L0_attn_x"])
    wq, wk, wv, wo = (inputs["L0_attn_wq"], inputs["L0_attn_wk"],
                      inputs["L0_attn_wv"], inputs["L0_attn_wo"])
    maps = []
    for core in range(n_cores):
        b = core // 2
        side = core % 2
        q = np.arange(S) // BLK
        k = np.arange(S) % BLK
        src = q * BLK + (k + side * SH) % BLK
        cp = np.arange(D)
        colperm = (cp // BLK) * BLK + (cp % BLK + side * SH) % BLK
        h0 = HC * side
        m = {
            "x_in": x[b][src],
            "wq_in": wq[128 * h0:128 * (h0 + HC)].T,
            "wk_in": wk[128 * h0:128 * (h0 + HC)].T,
            "wv_in": wv[128 * h0:128 * (h0 + HC)].T,
            "wo_in": wo[:, colperm].T,
            "lq1": np.asarray(inputs["L0_attn_lambda_q1"]).reshape(1, 128),
            "lq2": np.asarray(inputs["L0_attn_lambda_q2"]).reshape(1, 128),
            "lk1": np.asarray(inputs["L0_attn_lambda_k1"]).reshape(1, 128),
            "lk2": np.asarray(inputs["L0_attn_lambda_k2"]).reshape(1, 128),
            "gup": inputs["L0_ffn_gate_up_packed"],
            "gusT": inputs["L0_ffn_gate_up_scales"].T,
            "dnp": inputs["L0_ffn_down_packed"],
            "dnsT": inputs["L0_ffn_down_scales"].T,
        }
        maps.append({k2: np.ascontiguousarray(v) for k2, v in m.items()})
    return maps


def unshard_output(cfg, outs, n_cores=8):
    c = _cfg(cfg)
    S, D, U, BLK, SH = c["S"], c["D"], c["U"], c["BLK"], c["SH"]
    B = n_cores // 2
    out = np.empty((B, S, D), dtype=np.float32)
    m = np.arange(128)
    for core in range(n_cores):
        b = core // 2
        side = core % 2
        oc = outs[core]
        for ul in range(U):
            rows = BLK * m + ul + side * SH
            out[b, rows, :] = oc[128 * ul:128 * ul + 128, :]
    return out


def kernel(**inputs):
    cfg = REAL
    in_maps = shard_inputs(cfg, inputs)
    nc = build_nc(cfg)
    res = run_bass_kernel_spmd(nc, in_maps, core_ids=list(range(8)))
    return unshard_output(cfg, [r["out_c"] for r in res.results])
